# version 05
# baseline (speedup 1.0000x reference)
import numpy as np
import ml_dtypes
import concourse.bass as bass
import concourse.mybir as mybir
from concourse.bass_utils import run_bass_kernel_spmd

F32 = mybir.dt.float32
BF16 = mybir.dt.bfloat16
I32 = mybir.dt.int32
AF = mybir.ActivationFunctionType
ALU = mybir.AluOpType
AX = mybir.AxisListType
BFNP = ml_dtypes.bfloat16

S, DM, NH, DH, INNER = 2048, 1024, 16, 64, 4096
FFN_TTS = range(2)
FFN_T1_STOP = None
FUSED = True


def host_constants():
    c = {}
    c["identf"] = np.eye(128, dtype=np.float32)
    jj = np.arange(128)[:, None]
    uu = np.arange(2432)[None, :] - 384
    d = uu - jj
    allowed = ((d >= 0) & (d <= 127)) | ((d >= 132) & ((d - 128) % 4 == 0))
    c["mask0"] = allowed.astype(BFNP)
    uu1 = np.arange(896)[None, :] - 384
    c["mask1"] = ((uu1 - jj) >= 0).astype(BFNP)
    r = np.arange(0, DH, 2, dtype=np.float64)
    invf = 1.0 / (10000.0 ** (r / DH))
    fr = np.arange(S, dtype=np.float64)[:, None] * invf[None, :]
    cos = np.cos(fr); sin = np.sin(fr)
    C = np.repeat(cos, 2, axis=1)
    S2 = np.empty((S, DH)); S2[:, 0::2] = -sin; S2[:, 1::2] = sin
    c["ropeC"] = np.tile(C.T, (2, 1)).astype(BFNP)
    c["ropeS"] = np.tile(S2.T, (2, 1)).astype(BFNP)
    return c

_HC = host_constants()
_SHUF_PAIR = []
for _i in range(0, 32, 2):
    _SHUF_PAIR.extend([_i + 1, _i])


QSEM = {"sp": "qsp", "act": "qact", "pool": "qpool"}


def _apkeys(x):
    if not hasattr(x, "tensor"):
        return (x,)
    t = x.tensor
    nm = t.name
    dims = list(x.ap)
    try:
        import bass_rust
        is_dram = x.space == getattr(bass_rust, "MemorySpace").DRAM
    except Exception:
        is_dram = str(getattr(x, "space", "")).find("DRAM") >= 0
    lo = int(x.offset)
    if not is_dram:
        fw = 1
        for d in list(t.shape)[1:]:
            fw *= int(d)
        lo = lo % fw
    hi = lo
    use = dims if is_dram else dims[1:] if len(dims) > 1 else []
    for st, ct in use:
        hi += (int(ct) - 1) * int(st)
    hi += 1
    gran = 65536 if is_dram else 512
    return tuple((nm, i) for i in range(lo // gran, (hi + gran - 1) // gran))


class Prog:
    def __init__(self, nc):
        self.nc = nc
        self.sem = {}
        for e in ("pe", "dve", "act", "pool"):
            self.sem[e] = nc.alloc_semaphore("S" + e)
        self.cnt = {e: 0 for e in self.sem}
        self.q = {e: [] for e in ("pe", "dve", "act", "pool", "sp")}
        self.hwm = {e: {} for e in self.q}
        self.res = {}

    def _stream(self, name):
        key = ("dma", name)
        if key not in self.sem:
            self.sem[key] = self.nc.alloc_semaphore("Sd_" + name)
            self.cnt[key] = 0
        return key

    def _expand(self, items):
        out = []
        for it in items:
            out.extend(_apkeys(it))
        return out

    def _need(self, eng, own_sem, reads, writes):
        need = {}

        def add(s, v):
            if s == own_sem and v > self.cnt.get(own_sem, 0):
                return
            need[s] = max(need.get(s, 0), v)

        for k in reads:
            r = self.res.get(k)
            if r and r["w"] is not None:
                add(*r["w"])
        for k in writes:
            r = self.res.get(k)
            if r:
                if r["w"] is not None:
                    add(*r["w"])
                for s, v in r["r"].items():
                    add(s, v)
        hw = self.hwm[eng]
        waits = []
        for s, v in sorted(need.items(), key=lambda kv: str(kv[0])):
            if hw.get(s, 0) < v:
                waits.append((s, v))
                hw[s] = v
        return waits

    def _record(self, own_sem, tick, reads, writes):
        for k in writes:
            self.res[k] = {"w": (own_sem, tick), "r": {}}
        for k in reads:
            r = self.res.get(k)
            if r is None:
                r = {"w": None, "r": {}}
                self.res[k] = r
            r["r"][own_sem] = max(r["r"].get(own_sem, 0), tick)

    def op(self, eng, fn, reads=(), writes=(), inc=True):
        import sys as _s
        f = _s._getframe(1)
        self._dbg = "%s:%d" % (f.f_code.co_name, f.f_lineno)
        own = eng
        reads = self._expand(reads)
        writes = self._expand(writes)
        waits = self._need(eng, own, reads, writes)
        if inc:
            self.cnt[eng] += 1
            tick = self.cnt[eng]
        else:
            tick = self.cnt[eng] + 1
        self.q[eng].append((waits, fn, own if inc else None, self._dbg))
        self._record(own, tick, reads, writes)

    def dma(self, qeng, dst, src, stream, reads=(), writes=(), **kw):
        own = self._stream(stream)
        reads = self._expand(tuple(reads) + (src,))
        writes = self._expand(tuple(writes) + (dst,))
        waits = self._need(qeng, own, reads, writes)
        self.cnt[own] += 1
        tick = 16 * self.cnt[own]
        import sys as _s
        f = _s._getframe(1)
        dbg = "%s:%d" % (f.f_code.co_name, f.f_lineno)
        self.q[qeng].append(
            (waits, lambda e, d=dst, s=src, k=dict(kw): e.dma_start(d, s, **k), own, dbg))
        self._record(own, tick, reads, writes)

    def seal_tensor(self, name, stream):
        key = ("dma", stream)
        tick = 16 * self.cnt[key]
        for k, r in self.res.items():
            if isinstance(k, tuple) and len(k) == 2 and k[0] == name:
                r["w"] = (key, tick)

    def check_deadlock(self):
        pos = {e: 0 for e in self.q}
        val = {s: 0 for s in self.cnt}
        progressed = True
        while progressed:
            progressed = False
            for e, lst in self.q.items():
                while pos[e] < len(lst):
                    waits, _fn, inc_sem = lst[pos[e]][0], lst[pos[e]][1], lst[pos[e]][2]
                    if all(val.get(s, 0) >= v for s, v in waits):
                        if inc_sem is not None:
                            val[inc_sem] = val.get(inc_sem, 0) + (
                                16 if isinstance(inc_sem, tuple) else 1)
                        pos[e] += 1
                        progressed = True
                    else:
                        break
        stuck = {e: pos[e] for e in self.q if pos[e] < len(self.q[e])}
        if stuck:
            msg = []
            for e, p in stuck.items():
                waits, _f, _i, dbg = self.q[e][p]
                bad = [(s, v, val.get(s, 0)) for s, v in waits if val.get(s, 0) < v]
                msg.append(f"{e}@{p} [{dbg}] needs {bad}")
            raise RuntimeError("build-time deadlock: " + " | ".join(msg))

    def emit(self):
        nc = self.nc
        sems = self.sem
        final = dict(self.cnt)

        def mk(lst, eng_name):
            def body(engine):
                for waits, fn, inc_sem, _dbg in lst:
                    for s, v in waits:
                        engine.wait_ge(sems[s], v)
                    ins = fn(engine)
                    if inc_sem is not None:
                        amt = 16 if isinstance(inc_sem, tuple) else 1
                        ins.then_inc(sems[inc_sem], amt)
                if eng_name == "pool":
                    for s, v in final.items():
                        tgt = v * 16 if isinstance(s, tuple) else v
                        if tgt > 0:
                            engine.wait_ge(sems[s], tgt)
            return body

        self.check_deadlock()
        with nc.Block() as blk:
            blk.sync(mk(self.q["sp"], "sp"))
            blk.tensor(mk(self.q["pe"], "pe"))
            blk.vector(mk(self.q["dve"], "dve"))
            blk.scalar(mk(self.q["act"], "act"))
            blk.gpsimd(mk(self.q["pool"], "pool"))



class Bufs:
    def __init__(self, nc):
        A = nc.alloc_sbuf_tensor
        self.X = A("X", [128, 16384], F32).ap()
        self.HB = A("HB", [128, 33024], BF16).ap()
        self.XQ = A("XQb", [128, 8192], BF16).ap()
        self.QT = A("QTb", [128, 4096], BF16).ap()
        self.WB = A("WBb", [128, 4096], BF16).ap()
        self.WR = A("WRb", [128, 1024], F32).ap()
        self.BC = A("BCb", [128, 2048], F32).ap()
        self.EV = A("EVb", [128, 1280], F32).ap()
        self.ACC = A("ACCb", [128, 1024], BF16).ap()
        self.RT = A("RTb", [128, 2048], F32).ap()
        self.RI = A("RIb", [128, 1024], I32).ap()
        self.CC = A("CCb", [128, 2048], BF16).ap()
        self.AUX = A("AUXb", [128, 4992], BF16).ap()
        self.MASK = self.AUX
        self.RC = self.AUX[:, 896:2944]
        self.RS = self.AUX[:, 2944:4992]
        self.COLS = A("COLSb", [128, 160], F32).ap()
        self.CB = A("CBb", [128, 128], F32).ap()
        self.IDF = A("IDFb", [128, 128], F32).ap()
        self.ONE1 = A("ONE1b", [128, 1], BF16).ap()
        self.KCONST = A("KCONSTb", [128, 2], F32).ap()
        self.MWC = A("MWCb", [128, 10], F32).ap()
        self.BIA = A("BIAb", [128, 48], F32).ap()
        self.SCR = A("SCRb", [1, 32], F32).ap()
        self.pb = [nc.alloc_psum_tensor(f"pb{i}", [128, 512], F32).ap() for i in range(8)]

    def col(self, v, idx, n=1):
        return self.COLS[:, v * 16 + idx: v * 16 + idx + n]

    def xs(self, c, qt):
        return self.X[:, c * 2048 + qt * 512: c * 2048 + qt * 512 + 512]


(VAR_M, VAR_RSTD, VAR_S1, VAR_S2, VAR_MAXX, VAR_MINX,
 VAR_ACOL, VAR_INVSX, VAR_SH, VAR_TMP) = range(10)
KC_EPS, KC_EIGHTH = 0, 1


def emit_setup(P, B, dins):
    P.dma("sp", B.IDF, dins["identf"].ap(), "ldid")
    P.op("dve", lambda e: e.memset(B.CB, 0.0), writes=[B.CB])
    P.op("dve", lambda e: e.memset(B.ONE1, 1.0), writes=[B.ONE1])
    P.op("dve", lambda e: e.memset(B.KCONST[:, 0:1], 1e-5), writes=[B.KCONST])
    P.op("dve", lambda e: e.memset(B.KCONST[:, 1:2], 0.125), writes=[B.KCONST])
    P.op("dve", lambda e: e.memset(B.COLS, 0.0), writes=[B.COLS])



def emit_weight_prep(P, B, wT, scratch, widx, K, M, scr_dram):
    nkc, nmb = K // 128, M // 1024
    wsrc = wT.ap().rearrange("(k p) m -> k p m", p=128)
    wdst = scratch.ap().rearrange("(k p) m -> k p m", p=128)
    acc = B.col(VAR_TMP, 0)
    tmpc = B.col(VAR_TMP, 1)
    P.op("dve", lambda e: e.memset(acc, 0.0), writes=[acc])
    bi = 0
    for kc in range(nkc):
        for mb in range(nmb):
            wr = B.WR[:, 0:1024] if bi % 2 == 0 else B.RT[:, 1024:2048]
            tm2 = B.col(VAR_TMP, 1 + bi % 2)
            P.dma("sp", wr, wsrc[kc, :, mb * 1024:(mb + 1) * 1024],
                  "wr%d" % (bi % 2))
            P.op("dve", lambda e, wr=wr, tm2=tm2: e.reduce_sum(
                tm2, wr, AX.X, apply_absolute_value=True),
                reads=[wr], writes=[tm2])
            P.op("dve", lambda e, tm2=tm2: e.tensor_add(acc, acc, tm2),
                 reads=[tm2, acc], writes=[acc])
            bi += 1
    P.op("pe", lambda e: e.transpose(B.pb[7][0:1, 0:128], acc, B.IDF),
         reads=[acc, B.IDF], writes=[B.pb[7]])
    P.op("dve", lambda e: e.tensor_copy(B.EV[0:1, 0:128], B.pb[7][0:1, 0:128]),
         reads=[B.pb[7]], writes=[B.EV[0:1, 0:128]])
    P.op("dve", lambda e: e.reduce_sum(B.EV[0:1, 512:513], B.EV[0:1, 0:128], AX.X),
         reads=[B.EV[0:1, 0:128]], writes=[B.EV[0:1, 512:513]])
    P.op("dve", lambda e: e.tensor_scalar(
        B.SCR[0:1, widx:widx + 1], B.EV[0:1, 512:513], 1.0 / (K * M), 1e-5,
        ALU.mult, ALU.max),
        reads=[B.EV[0:1, 512:513]], writes=[B.SCR[0:1, widx:widx + 1]])
    P.op("dve", lambda e: e.reciprocal(B.SCR[0:1, 16 + widx:17 + widx],
                                       B.SCR[0:1, widx:widx + 1]),
         reads=[B.SCR[0:1, widx:widx + 1]], writes=[B.SCR[0:1, 16 + widx:17 + widx]])
    P.dma("pool", scr_dram.ap()[0:1, widx:widx + 1], B.SCR[0:1, widx:widx + 1],
          "scs%d" % widx)
    P.dma("pool", scr_dram.ap()[0:1, 16 + widx:17 + widx],
          B.SCR[0:1, 16 + widx:17 + widx], "scs%d" % widx)
    emit_load_scales(P, B, scr_dram, [widx])
    bi = 0
    for kc in range(nkc):
        for mb in range(nmb):
            sl = bi % 2
            wr = B.WR[:, 0:1024] if sl == 0 else B.RT[:, 1024:2048]
            P.dma("sp", wr, wsrc[kc, :, mb * 1024:(mb + 1) * 1024], "wr%d" % sl)
            vv = B.EV[:, 0:1024] if sl == 0 else B.BC[:, 0:1024]
            P.op("dve", lambda e, wr=wr, vv=vv: e.tensor_scalar_mul(
                vv, wr, B.MWC[:, 5 + widx:6 + widx]),
                reads=[wr, B.MWC], writes=[vv])
            c1 = B.CC[:, sl * 1024:(sl + 1) * 1024]
            P.op("dve", lambda e, vv=vv, c1=c1: e.tensor_single_scalar(
                c1, vv, 0.5, ALU.is_gt), reads=[vv], writes=[c1])
            c2 = B.QT[:, sl * 1024:(sl + 1) * 1024]
            P.op("pool", lambda e, vv=vv, c2=c2: e.tensor_single_scalar(
                c2, vv, -0.5, ALU.is_lt), reads=[vv], writes=[c2])
            wb = B.WB[:, sl * 1024:(sl + 1) * 1024]
            P.op("dve", lambda e, c1=c1, c2=c2, wb=wb: e.tensor_sub(wb, c1, c2),
                 reads=[c1, c2], writes=[wb])
            P.dma("pool", wdst[kc, :, mb * 1024:(mb + 1) * 1024], wb, "tst%d" % widx)
            bi += 1


def emit_load_scales(P, B, scr_dram, widxs):
    for w in widxs:
        P.dma("sp", B.MWC[:, w:w + 1],
              scr_dram.ap()[0:1, w:w + 1].broadcast_to([128, 1]), "mwc")
        P.dma("sp", B.MWC[:, 5 + w:6 + w],
              scr_dram.ap()[0:1, 16 + w:17 + w].broadcast_to([128, 1]), "mwc")


def emit_load_bias(P, B, dins):
    P.dma("sp", B.BIA[:, 0:32], dins["b1c"].ap(), "bias")
    P.dma("sp", B.BIA[:, 32:40], dins["b2c"].ap(), "bias")
    P.op("dve", lambda e: e.tensor_scalar_mul(B.BIA[:, 40:48], B.BIA[:, 32:40], 0.5),
         reads=[B.BIA[:, 32:40]], writes=[B.BIA[:, 40:48]])



def emit_stats(P, B, ffn):
    for qt in range(4):
        for c in range(8):
            xsl = B.xs(c, qt)
            slot = c % 2
            xb = B.QT[:, slot * 512:(slot + 1) * 512]
            sq = B.QT[:, 1024 + slot * 512: 1024 + (slot + 1) * 512]
            P.op("pool", lambda e, xb=xb, xsl=xsl: e.tensor_copy(xb, xsl),
                 reads=[xsl], writes=[xb])
            P.op("act", lambda e, sq=sq, xsl=xsl: e.activation(sq, xsl, AF.Square),
                 reads=[xsl], writes=[sq])
            P.op("pe", lambda e, xb=xb, c=c: e.matmul(
                B.pb[5][0:1, :], B.ONE1, xb, start=(c == 0), stop=(c == 7)),
                reads=[xb, B.ONE1], writes=[B.pb[5]])
            P.op("pe", lambda e, sq=sq, c=c: e.matmul(
                B.pb[6][0:1, :], B.ONE1, sq, start=(c == 0), stop=(c == 7)),
                reads=[sq, B.ONE1], writes=[B.pb[6]])
        r1 = B.EV[0:1, 0:512]
        r2 = B.EV[0:1, 512:1024]
        P.op("dve", lambda e: e.tensor_copy(r1, B.pb[5][0:1, :]),
             reads=[B.pb[5]], writes=[r1])
        P.op("dve", lambda e: e.tensor_copy(r2, B.pb[6][0:1, :]),
             reads=[B.pb[6]], writes=[r2])
        for g in range(4):
            P.op("pe", lambda e, g=g: e.transpose(
                B.pb[7][:, g:g + 1], B.EV[0:1, g * 128:(g + 1) * 128], B.IDF[0:1, 0:1]),
                reads=[r1, B.IDF], writes=[B.pb[7]], inc=False)
            P.op("pe", lambda e, g=g: e.transpose(
                B.pb[7][:, 4 + g:5 + g],
                B.EV[0:1, 512 + g * 128:512 + (g + 1) * 128], B.IDF[0:1, 0:1]),
                reads=[r2, B.IDF], writes=[B.pb[7]])
        P.op("dve", lambda e, qt=qt: e.tensor_copy(
            B.col(VAR_S1, qt * 4, 4), B.pb[7][:, 0:4]),
            reads=[B.pb[7]], writes=[B.col(VAR_S1, qt * 4, 4)])
        P.op("dve", lambda e, qt=qt: e.tensor_copy(
            B.col(VAR_S2, qt * 4, 4), B.pb[7][:, 4:8]),
            reads=[B.pb[7]], writes=[B.col(VAR_S2, qt * 4, 4)])
        if ffn:
            ea = B.EV[:, 0:512]
            eb = B.EV[:, 512:1024]
            ec = B.EV[:, 768:1280]
            P.op("dve", lambda e, qt=qt: e.tensor_max(ea, B.xs(0, qt), B.xs(1, qt)),
                 reads=[B.xs(0, qt), B.xs(1, qt), r1], writes=[ea])
            P.op("dve", lambda e, qt=qt: e.tensor_tensor(
                eb, B.xs(0, qt), B.xs(1, qt), ALU.min),
                reads=[B.xs(0, qt), B.xs(1, qt), r2], writes=[eb])
            for c in range(2, 8):
                P.op("dve", lambda e, c=c, qt=qt: e.tensor_max(ea, ea, B.xs(c, qt)),
                     reads=[B.xs(c, qt), ea], writes=[ea])
                P.op("dve", lambda e, c=c, qt=qt: e.tensor_tensor(
                    eb, eb, B.xs(c, qt), ALU.min),
                    reads=[B.xs(c, qt), eb], writes=[eb])
            for g in range(4):
                P.op("pe", lambda e, g=g: e.transpose(
                    B.pb[7][:, g * 128:(g + 1) * 128], ea[:, g * 128:(g + 1) * 128],
                    B.IDF),
                    reads=[ea, B.IDF], writes=[B.pb[7]])
            for g in range(4):
                P.op("dve", lambda e, g=g, qt=qt: e.reduce_max(
                    B.col(VAR_MAXX, qt * 4 + g), B.pb[7][:, g * 128:(g + 1) * 128],
                    AX.X),
                    reads=[B.pb[7]], writes=[B.col(VAR_MAXX, qt * 4 + g)])
            for g in range(4):
                P.op("pe", lambda e, g=g: e.transpose(
                    B.pb[7][:, g * 128:(g + 1) * 128], eb[:, g * 128:(g + 1) * 128],
                    B.IDF),
                    reads=[eb, B.IDF], writes=[B.pb[7]])
            for g in range(4):
                P.op("dve", lambda e, g=g, qt=qt: e.tensor_reduce(
                    B.col(VAR_MINX, qt * 4 + g), B.pb[7][:, g * 128:(g + 1) * 128],
                    AX.X, ALU.min),
                    reads=[B.pb[7]], writes=[B.col(VAR_MINX, qt * 4 + g)])
    m16 = B.col(VAR_M, 0, 16)
    rst = B.col(VAR_RSTD, 0, 16)
    tmp = B.col(VAR_TMP, 0, 16)
    s1 = B.col(VAR_S1, 0, 16)
    s2 = B.col(VAR_S2, 0, 16)
    P.op("dve", lambda e: e.tensor_scalar_mul(m16, s1, 1.0 / 1024.0),
         reads=[s1], writes=[m16])
    P.op("dve", lambda e: e.tensor_mul(tmp, m16, m16), reads=[m16], writes=[tmp])
    P.op("dve", lambda e: e.scalar_tensor_tensor(
        rst, s2, 1.0 / 1024.0, tmp, ALU.mult, ALU.subtract),
        reads=[s2, tmp], writes=[rst])
    P.op("act", lambda e: e.activation(rst, rst, AF.Sqrt, bias=B.KCONST[:, 0:1]),
         reads=[rst, B.KCONST], writes=[rst])
    P.op("dve", lambda e: e.reciprocal(rst, rst), reads=[rst], writes=[rst])
    if ffn:
        mx = B.col(VAR_MAXX, 0, 16)
        mn = B.col(VAR_MINX, 0, 16)
        a16 = B.col(VAR_ACOL, 0, 16)
        iv = B.col(VAR_INVSX, 0, 16)
        P.op("dve", lambda e: e.tensor_sub(mx, mx, m16), reads=[mx, m16], writes=[mx])
        P.op("dve", lambda e: e.tensor_sub(mn, m16, mn), reads=[mn, m16], writes=[mn])
        P.op("dve", lambda e: e.tensor_max(mx, mx, mn), reads=[mx, mn], writes=[mx])
        P.op("dve", lambda e: e.reciprocal(a16, mx), reads=[mx], writes=[a16])
        P.op("dve", lambda e: e.tensor_scalar_mul(a16, a16, 127.0),
             reads=[a16], writes=[a16])
        P.op("dve", lambda e: e.tensor_mul(iv, rst, mx), reads=[rst, mx], writes=[iv])
        P.op("dve", lambda e: e.tensor_scalar(iv, iv, 1e-5, 1.0 / 127.0,
                                              ALU.max, ALU.mult),
             reads=[iv], writes=[iv])


def emit_bcast(P, B, colap, slot, qt, pbi=4):
    pb = B.pb[pbi]
    for g in range(4):
        cs = colap[:, qt * 4 + g: qt * 4 + g + 1]
        P.op("act", lambda e, cs=cs: e.activation(B.CB, B.CB, AF.Identity,
                                                  bias=cs, scale=0.0),
             reads=[cs, B.CB], writes=[B.CB])
        P.op("pe", lambda e, g=g, pb=pb: e.transpose(
            pb[:, g * 128:(g + 1) * 128], B.CB, B.IDF),
            reads=[B.CB, B.IDF], writes=[pb])
    bc = B.BC[:, slot * 512:(slot + 1) * 512]
    P.op("dve", lambda e, bc=bc, pb=pb: e.tensor_copy(bc, pb),
         reads=[pb], writes=[bc])
    return bc


def emit_round(P, B, src, dst, ro, n=512):
    yi = B.RI[:, ro:ro + n]
    yf = B.RT[:, 1024 + ro:1024 + ro + n]
    c1 = B.CC[:, ro:ro + n]
    c2 = B.CC[:, 1024 + ro:1024 + ro + n]
    P.op("dve", lambda e: e.tensor_scalar(yi, src, 1.0, None, ALU.mult),
         reads=[src], writes=[yi])
    P.op("pool", lambda e: e.tensor_copy(yf, yi), reads=[yi], writes=[yf])
    P.op("dve", lambda e: e.scalar_tensor_tensor(
        c1, src, 0.5, yf, ALU.subtract, ALU.is_gt),
        reads=[src, yf], writes=[c1])
    P.op("dve", lambda e: e.scalar_tensor_tensor(
        c2, src, 0.5, yf, ALU.add, ALU.is_lt),
        reads=[src, yf], writes=[c2])
    P.op("dve", lambda e: e.tensor_add(yf, yf, c1), reads=[yf, c1], writes=[yf])
    P.op("dve", lambda e: e.tensor_sub(dst, yf, c2), reads=[yf, c2], writes=[dst])



def emit_ffn(P, B, tern_g, tern_u, tern_d, wg_i, wu_i, wd_i, l1):
    emit_stats(P, B, ffn=True)
    g3 = tern_g.ap().rearrange("(k p) m -> k p m", p=128)
    u3 = tern_u.ap().rearrange("(k p) m -> k p m", p=128) if tern_u is not None else None
    d3 = tern_d.ap().rearrange("(k p) m -> k p m", p=128)
    cgc = B.col(VAR_TMP, 0, 16)
    P.op("dve", lambda e: e.tensor_scalar_mul(
        cgc, B.col(VAR_INVSX, 0, 16), B.MWC[:, wg_i:wg_i + 1]),
        reads=[B.col(VAR_INVSX, 0, 16), B.MWC], writes=[cgc])
    for tt in FFN_TTS:
        for sub in range(2):
            qt = tt * 2 + sub
            bM = emit_bcast(P, B, B.col(VAR_M, 0, 16), 0, qt)
            bA = emit_bcast(P, B, B.col(VAR_ACOL, 0, 16), 1, qt)
            for c in range(8):
                xsl = B.xs(c, qt)
                ro = (c % 2) * 512
                xm = B.EV[:, ro:ro + 512]
                t = B.RT[:, ro:ro + 512]
                P.op("dve", lambda e, xsl=xsl, xm=xm, bM=bM: e.tensor_sub(xm, xsl, bM),
                     reads=[xsl, bM], writes=[xm])
                P.op("pool", lambda e, xm=xm, t=t, bA=bA: e.tensor_mul(t, xm, bA),
                     reads=[xm, bA], writes=[t])
                xq = B.XQ[:, c * 1024 + sub * 512: c * 1024 + sub * 512 + 512]
                emit_round(P, B, t, xq, ro)
        if tt == 1 and FFN_T1_STOP == "xq":
            continue
        bCg = [emit_bcast(P, B, cgc, 2, tt * 2), emit_bcast(P, B, cgc, 3, tt * 2 + 1)]
        am0 = B.ACC[:, 0:1024]
        an0 = B.CC[:, 1024:2048]
        P.op("dve", lambda e: e.memset(am0, -1e30), writes=[am0])
        P.op("pool", lambda e: e.memset(an0, 1e30), writes=[an0])
        for m in range(32):
            ws = m % 2
            gblk = B.WB[:, ws * 1024: ws * 1024 + 1024]
            P.dma("sp", gblk.rearrange("p (k m) -> p k m", k=8),
                  g3[:, :, m * 128:(m + 1) * 128].rearrange("k p m -> p k m"),
                  "wg%d" % ws)
            if not l1:
                ublk = B.WB[:, 2048 + ws * 1024: 2048 + ws * 1024 + 1024]
                P.dma("sp", ublk.rearrange("p (k m) -> p k m", k=8),
                      u3[:, :, m * 128:(m + 1) * 128].rearrange("k p m -> p k m"),
                      "wu%d" % ws)
            if True:
                for sub in range(2):
                    bgi = (m % 2) * 4 + sub * 2
                    bui = bgi + 1
                    pg = B.pb[bgi]
                    for k in range(8):
                        wt = gblk[:, k * 128:(k + 1) * 128]
                        xq = B.XQ[:, k * 1024 + sub * 512: k * 1024 + sub * 512 + 512]
                        P.op("pe", lambda e, wt=wt, xq=xq, k=k, pg=pg: e.matmul(
                            pg, wt, xq, start=(k == 0), stop=(k == 7)),
                            reads=[wt, xq], writes=[pg])
                    if not l1:
                        pu = B.pb[bui]
                        for k in range(8):
                            wt = ublk[:, k * 128:(k + 1) * 128]
                            xq = B.XQ[:, k * 1024 + sub * 512: k * 1024 + sub * 512 + 512]
                            P.op("pe", lambda e, wt=wt, xq=xq, k=k, pu=pu: e.matmul(
                                pu, wt, xq, start=(k == 0), stop=(k == 7)),
                                reads=[wt, xq], writes=[pu])
                    gs = B.EV[:, sub * 512: sub * 512 + 512]
                    P.op("dve", lambda e, pg=pg, gs=gs, sub=sub: e.tensor_mul(
                        gs, pg, bCg[sub]), reads=[pg, bCg[sub]], writes=[gs])
                    hsl = B.HB[:, m * 1024 + sub * 512: m * 1024 + sub * 512 + 512]
                    if l1:
                        b1c = B.BIA[:, m:m + 1]
                        P.op("act", lambda e, gs=gs, hsl=hsl, b1c=b1c: e.activation(
                            hsl, gs, AF.Gelu, bias=b1c),
                            reads=[gs, B.BIA], writes=[hsl])
                    else:
                        P.op("act", lambda e, gs=gs: e.activation(gs, gs, AF.Silu),
                             reads=[gs], writes=[gs])
                        pu = B.pb[bui]
                        P.op("dve", lambda e, gs=gs, hsl=hsl, pu=pu: e.tensor_mul(
                            hsl, gs, pu), reads=[gs, pu], writes=[hsl])
                    am = B.ACC[:, sub * 512:sub * 512 + 512]
                    an = B.CC[:, 1024 + sub * 512:1024 + sub * 512 + 512]
                    P.op("dve", lambda e, am=am, hsl=hsl: e.tensor_max(am, am, hsl),
                         reads=[hsl, am], writes=[am])
                    P.op("dve", lambda e, an=an, hsl=hsl: e.tensor_tensor(
                        an, an, hsl, ALU.min), reads=[hsl, an], writes=[an])
        if tt == 1 and FFN_T1_STOP == "gu":
            continue
        shc = B.col(VAR_SH, tt * 8, 8)
        negm = B.col(VAR_MAXX, tt * 8, 8)
        for half in range(2):
            rmax = B.RT[:, 0:512]
            rmin = B.RT[:, 512:1024]
            P.op("dve", lambda e, half=half, rmax=rmax: e.tensor_copy(
                rmax, B.ACC[:, half * 512:(half + 1) * 512]),
                reads=[am0], writes=[rmax])
            P.op("dve", lambda e, half=half, rmin=rmin: e.tensor_copy(
                rmin, B.CC[:, 1024 + half * 512: 1024 + (half + 1) * 512]),
                reads=[an0], writes=[rmin])
            for g in range(4):
                P.op("pe", lambda e, g=g, rmax=rmax: e.transpose(
                    B.pb[7][:, g * 128:(g + 1) * 128],
                    rmax[:, g * 128:(g + 1) * 128], B.IDF),
                    reads=[rmax, B.IDF], writes=[B.pb[7]])
            for g in range(4):
                gg = half * 4 + g
                P.op("dve", lambda e, g=g, gg=gg: e.reduce_max(
                    B.col(VAR_SH, tt * 8 + gg), B.pb[7][:, g * 128:(g + 1) * 128],
                    AX.X),
                    reads=[B.pb[7]], writes=[B.col(VAR_SH, tt * 8 + gg)])
            for g in range(4):
                P.op("pe", lambda e, g=g, rmin=rmin: e.transpose(
                    B.pb[7][:, g * 128:(g + 1) * 128],
                    rmin[:, g * 128:(g + 1) * 128], B.IDF),
                    reads=[rmin, B.IDF], writes=[B.pb[7]])
            for g in range(4):
                gg = half * 4 + g
                P.op("dve", lambda e, g=g, gg=gg: e.tensor_reduce(
                    B.col(VAR_MAXX, tt * 8 + gg),
                    B.pb[7][:, g * 128:(g + 1) * 128], AX.X, ALU.min,
                    negate=True),
                    reads=[B.pb[7]], writes=[B.col(VAR_MAXX, tt * 8 + gg)])
        if tt == 1 and FFN_T1_STOP == "tailmid":
            continue
        P.op("dve", lambda e, shc=shc, negm=negm: e.tensor_max(shc, shc, negm),
             reads=[shc, negm], writes=[shc])
        P.op("dve", lambda e, shc=shc: e.tensor_scalar_max(shc, shc, 1e-5),
             reads=[shc], writes=[shc])
        shs = B.col(VAR_MINX, tt * 8, 8)
        P.op("dve", lambda e, shs=shs, shc=shc: e.reciprocal(shs, shc),
             reads=[shc], writes=[shs])
        P.op("dve", lambda e, shs=shs: e.tensor_scalar_mul(shs, shs, 127.0),
             reads=[shs], writes=[shs])
        sd = negm
        if l1:
            P.op("dve", lambda e, sd=sd, shc=shc: e.tensor_scalar_mul(
                sd, shc, 1.0 / 127.0), reads=[shc], writes=[sd])
        else:
            iv8 = B.col(VAR_INVSX, tt * 8, 8)
            P.op("dve", lambda e, sd=sd, iv8=iv8, shc=shc: e.tensor_mul(sd, iv8, shc),
                 reads=[iv8, shc], writes=[sd])
            P.op("dve", lambda e, sd=sd: e.tensor_scalar_mul(
                sd, sd, B.MWC[:, wu_i:wu_i + 1]),
                 reads=[sd, B.MWC], writes=[sd])
            P.op("dve", lambda e, sd=sd: e.tensor_scalar(
                sd, sd, 1e-5, 1.0 / 127.0, ALU.max, ALU.mult),
                 reads=[sd], writes=[sd])
        P.op("dve", lambda e, sd=sd: e.tensor_scalar_mul(
            sd, sd, B.MWC[:, wd_i:wd_i + 1]),
             reads=[sd, B.MWC], writes=[sd])
        P.op("dve", lambda e, sd=sd: e.tensor_scalar_mul(sd, sd, 0.5),
             reads=[sd], writes=[sd])
        if tt == 1 and FFN_T1_STOP == "tailsd":
            continue
        bSh = [emit_bcast(P, B, B.col(VAR_MINX, 0, 16), 0, tt * 2),
               emit_bcast(P, B, B.col(VAR_MINX, 0, 16), 1, tt * 2 + 1)]
        bSd = [emit_bcast(P, B, B.col(VAR_MAXX, 0, 16), 2, tt * 2),
               emit_bcast(P, B, B.col(VAR_MAXX, 0, 16), 3, tt * 2 + 1)]
        if tt == 1 and FFN_T1_STOP == "tail":
            continue
        for m in range(32):
            for sub in range(2):
                hsl = B.HB[:, m * 1024 + sub * 512: m * 1024 + sub * 512 + 512]
                t = B.RT[:, sub * 512: sub * 512 + 512]
                P.op("pool", lambda e, hsl=hsl, t=t, sub=sub: e.tensor_mul(
                    t, hsl, bSh[sub]), reads=[hsl, bSh[sub]], writes=[t])
            hfull = B.HB[:, m * 1024: m * 1024 + 1024]
            tfull = B.RT[:, 0:1024]
            emit_round(P, B, tfull, hfull, 0, n=1024)
        if tt == 1 and FFN_T1_STOP == "round":
            continue
        for sub in range(2):
            qt = tt * 2 + sub
            for k in range(32):
                ws = k % 2
                wdb = B.WB[:, ws * 1024: ws * 1024 + 1024]
                P.dma("sp", wdb, d3[k, :, :], "wg%d" % ws)
                hq = B.HB[:, k * 1024 + sub * 512: k * 1024 + sub * 512 + 512]
                for m8 in range(8):
                    wt = wdb[:, m8 * 128:(m8 + 1) * 128]
                    P.op("pe", lambda e, wt=wt, hq=hq, k=k, m8=m8: e.matmul(
                        B.pb[m8], wt, hq, start=(k == 0), stop=(k == 31)),
                        reads=[wt, hq], writes=[B.pb[m8]])
            for m8 in range(8):
                ev = B.EV[:, (m8 % 2) * 512:(m8 % 2) * 512 + 512]
                P.op("dve", lambda e, m8=m8, ev=ev, sub=sub: e.tensor_mul(
                    ev, B.pb[m8], bSd[sub]),
                    reads=[B.pb[m8], bSd[sub]], writes=[ev])
                if l1:
                    b2h = B.BIA[:, 40 + m8:41 + m8]
                    P.op("pool", lambda e, ev=ev, b2h=b2h: e.tensor_scalar_add(
                        ev, ev, b2h), reads=[ev, B.BIA], writes=[ev])
                xsl = B.xs(m8, qt)
                P.op("dve", lambda e, xsl=xsl, ev=ev: e.tensor_add(xsl, xsl, ev),
                     reads=[ev, xsl], writes=[xsl])



def emit_attn(P, B, layer, qkvT, woT, dscr):
    emit_stats(P, B, ffn=False)
    l1 = layer == 1
    qoff = 8 if l1 else 0
    koff = 0 if l1 else 8
    voff = 16
    qkv = qkvT.ap().rearrange("(k p) m -> k p m", p=128)
    wo = woT.ap().rearrange("(k p) m -> k p m", p=128)
    VOFF = 16384
    vones = B.HB[:, VOFF:VOFF + 16640].rearrange("p (s c) -> p s c", c=65)[:, :, 64:65]
    P.op("dve", lambda e: e.memset(vones, 1.0), writes=[B.HB[:, VOFF:VOFF + 16640]])

    def build_u(qt):
        bM = emit_bcast(P, B, B.col(VAR_M, 0, 16), 0, qt)
        bR = emit_bcast(P, B, B.col(VAR_RSTD, 0, 16), 1, qt)
        for c in range(8):
            xsl = B.xs(c, qt)
            ro = (c % 2) * 512
            xm = B.EV[:, ro:ro + 512]
            P.op("dve", lambda e, xsl=xsl, xm=xm, bM=bM: e.tensor_sub(xm, xsl, bM),
                 reads=[xsl, bM], writes=[xm])
            u = B.XQ[:, c * 512:(c + 1) * 512]
            P.op("dve", lambda e, xm=xm, u=u, bR=bR: e.tensor_mul(u, xm, bR),
                 reads=[xm, bR], writes=[u])

    def proj_one(qt, m, bi):
        ws = bi % 2
        wr = B.WR[:, 0:1024] if ws == 0 else B.RT[:, 0:1024]
        P.dma("sp", wr.rearrange("p (k m) -> p k m", k=8),
              qkv[:, :, m * 128:(m + 1) * 128].rearrange("k p m -> p k m"), "wr")
        wb = B.WB[:, 2048 + ws * 1024: 2048 + (ws + 1) * 1024]
        P.op("pool", lambda e, wb=wb, wr=wr: e.tensor_copy(wb, wr),
             reads=[wr], writes=[wb])
        pb = B.pb[ws]
        for k in range(8):
            wt = wb[:, k * 128:(k + 1) * 128]
            u = B.XQ[:, k * 512:(k + 1) * 512]
            P.op("pe", lambda e, wt=wt, u=u, k=k, pb=pb: e.matmul(
                pb, wt, u, start=(k == 0), stop=(k == 7)),
                reads=[wt, u], writes=[pb])
        return pb

    def rope_evac(pb, toff, dst):
        ea = B.BC[:, 1536:2048]
        ebb = B.EV[:, 0:512]
        P.op("dve", lambda e: e.tensor_copy(ea, pb), reads=[pb], writes=[ea])
        P.op("dve", lambda e: e.stream_shuffle(ebb, ea, _SHUF_PAIR),
             reads=[ea], writes=[ebb])
        P.op("dve", lambda e: e.tensor_mul(ea, ea, B.RC[:, toff:toff + 512]),
             reads=[ea, B.RC], writes=[ea])
        P.op("pool", lambda e: e.tensor_mul(ebb, ebb, B.RS[:, toff:toff + 512]),
             reads=[ebb, B.RS], writes=[ebb])
        P.op("dve", lambda e: e.tensor_add(dst, ea, ebb),
             reads=[ea, ebb], writes=[dst])

    for qt in range(4):
        build_u(qt)
        for j in range(8):
            pb = proj_one(qt, koff + j, j)
            dst = B.HB[:, j * 2048 + qt * 512: j * 2048 + qt * 512 + 512]
            if l1:
                rope_evac(pb, qt * 512, dst)
            else:
                P.op("dve", lambda e, dst=dst, pb=pb: e.tensor_copy(dst, pb),
                     reads=[pb], writes=[dst])
        for j in range(8):
            pb = proj_one(qt, voff + j, j)
            ev = B.EV[:, 0:512]
            P.op("dve", lambda e, ev=ev, pb=pb: e.tensor_copy(ev, pb),
                 reads=[pb], writes=[ev])
            for g in range(4):
                kc = qt * 4 + g
                pt = B.pb[6 + (g % 2)]
                P.op("pe", lambda e, g=g, pt=pt: e.transpose(
                    pt[:, 0:128], ev[:, g * 128:(g + 1) * 128], B.IDF),
                    reads=[ev, B.IDF], writes=[pt])
                for hh in range(2):
                    h = 2 * j + hh
                    slot = VOFF + (kc * 16 + h) * 65
                    dst = B.HB[:, slot:slot + 64]
                    P.op("dve", lambda e, dst=dst, pt=pt, hh=hh: e.tensor_copy(
                        dst, pt[:, hh * 64:hh * 64 + 64]),
                        reads=[pt], writes=[dst])
    for qt in range(4):
        build_u(qt)
        for j in range(8):
            pb = proj_one(qt, qoff + j, j)
            dst = B.QT[:, j * 512:(j + 1) * 512]
            if l1:
                rope_evac(pb, qt * 512, dst)
            else:
                P.op("dve", lambda e, dst=dst, pb=pb: e.tensor_copy(dst, pb),
                     reads=[pb], writes=[dst])
        nkc = qt * 4 + 4
        for h in range(NH):
            qslice = B.QT[64 * (h % 2):64 * (h % 2) + 64,
                          (h // 2) * 512:(h // 2) * 512 + 512]
            pav = B.pb[5]
            for kc in range(nkc):
                ps = B.pb[2 + (kc % 2)]
                ksl = B.HB[64 * (h % 2):64 * (h % 2) + 64,
                           (h // 2) * 2048 + kc * 128: (h // 2) * 2048 + kc * 128 + 128]
                P.op("pe", lambda e, ksl=ksl, qslice=qslice, ps=ps: e.matmul(
                    ps, ksl, qslice, start=True, stop=True),
                    reads=[ksl, qslice], writes=[ps])
                p0 = B.CC[:, (kc % 2) * 512:(kc % 2) * 512 + 512]
                P.op("act", lambda e, ps=ps, p0=p0: e.activation(
                    p0, ps, AF.Exp, scale=B.KCONST[:, 1:2]),
                    reads=[ps, B.KCONST], writes=[p0])
                if (not l1) or (kc >= qt * 4):
                    moff = qt * 512 - kc * 128 + 384
                    msl = B.MASK[:, moff:moff + 512]
                    P.op("pool", lambda e, p0=p0, msl=msl: e.tensor_mul(p0, p0, msl),
                         reads=[p0, msl], writes=[p0])
                vsl = B.HB[:, VOFF + (kc * 16 + h) * 65: VOFF + (kc * 16 + h) * 65 + 65]
                P.op("pe", lambda e, vsl=vsl, p0=p0, kc=kc, nkc=nkc, pav=pav: e.matmul(
                    pav[0:65, :], vsl, p0, start=(kc == 0), stop=(kc == nkc - 1)),
                    reads=[vsl, p0], writes=[pav])
            drow = B.EV[64:65, 512:1024]
            P.op("dve", lambda e, pav=pav: e.reciprocal(drow, pav[64:65, 0:512]),
                 reads=[pav], writes=[drow])
            dsl = dscr.ap()[h % 4: h % 4 + 1, :]
            P.dma("act", dsl, drow, "dsc%d" % (h % 4))
            db = B.BC[0:64, 1536:2048]
            P.dma("sp", db, dsl.broadcast_to([64, 512]), "dbc")
            if h % 2 == 0:
                ao = B.XQ[0:64, (h // 2) * 512:(h // 2) * 512 + 512]
                P.op("dve", lambda e, ao=ao, db=db, pav=pav: e.tensor_mul(
                    ao, pav[0:64, :], db), reads=[pav, db], writes=[ao])
            else:
                stg = B.CC[0:64, 1024:1536]
                P.op("dve", lambda e, stg=stg, db=db, pav=pav: e.tensor_mul(
                    stg, pav[0:64, :], db), reads=[pav, db], writes=[stg])
                ao = B.XQ[64:128, (h // 2) * 512:(h // 2) * 512 + 512]
                P.dma("pool", ao, stg, "aomv")
        for mo in range(8):
            ws = mo % 2
            wr = B.WR[:, 0:1024] if ws == 0 else B.RT[:, 0:1024]
            P.dma("sp", wr.rearrange("p (k m) -> p k m", k=8),
                  wo[:, :, mo * 128:(mo + 1) * 128].rearrange("k p m -> p k m"), "wr")
            wb = B.WB[:, 2048 + ws * 1024: 2048 + (ws + 1) * 1024]
            P.op("pool", lambda e, wb=wb, wr=wr: e.tensor_copy(wb, wr),
                 reads=[wr], writes=[wb])
            po = B.pb[6 + ws]
            for k in range(8):
                wt = wb[:, k * 128:(k + 1) * 128]
                ao = B.XQ[:, k * 512:(k + 1) * 512]
                P.op("pe", lambda e, wt=wt, ao=ao, k=k, po=po: e.matmul(
                    po, wt, ao, start=(k == 0), stop=(k == 7)),
                    reads=[wt, ao], writes=[po])
            xsl = B.xs(mo, qt)
            P.op("dve", lambda e, xsl=xsl, po=po: e.tensor_add(xsl, xsl, po),
                 reads=[po, xsl], writes=[xsl])



def emit_input_tm(P, B, x_tm):
    xv = x_tm.ap().rearrange("(n p) d -> n p d", p=128)
    for nt in range(16):
        io = B.BC[:, 0:1024]
        P.dma("sp", io, xv[nt], "xin")
        for c in range(8):
            pt = B.pb[c % 2]
            P.op("pe", lambda e, c=c, io=io, pt=pt: e.transpose(
                pt[:, 0:128], io[:, c * 128:(c + 1) * 128], B.IDF),
                reads=[io, B.IDF], writes=[pt])
            dst = B.X[:, c * 2048 + nt * 128: c * 2048 + (nt + 1) * 128]
            P.op("dve", lambda e, dst=dst, pt=pt: e.tensor_copy(dst, pt[:, 0:128]),
                 reads=[pt], writes=[dst])


def emit_output_tm(P, B, y_tm):
    yv = y_tm.ap().rearrange("(n p) d -> n p d", p=128)
    for nt in range(16):
        io = B.BC[:, 1024:2048]
        for c in range(8):
            src = B.X[:, c * 2048 + nt * 128: c * 2048 + (nt + 1) * 128]
            pt = B.pb[2 + c % 2]
            P.op("pe", lambda e, src=src, pt=pt: e.transpose(
                pt[:, 0:128], src, B.IDF),
                reads=[src, B.IDF], writes=[pt])
            P.op("dve", lambda e, c=c, io=io, pt=pt: e.tensor_copy(
                io[:, c * 128:(c + 1) * 128], pt[:, 0:128]),
                reads=[pt], writes=[io[:, c * 128:(c + 1) * 128]])
        P.dma("pool", yv[nt], io, "yo")


def emit_input_fm(P, B, x_fm):
    xv = x_fm.ap().rearrange("(c p) n -> c p n", p=128)
    for c in range(8):
        P.dma("sp", B.X[:, c * 2048:(c + 1) * 2048], xv[c], "xin%d" % c)


def emit_output_fm(P, B, y_fm):
    yv = y_fm.ap().rearrange("(c p) n -> c p n", p=128)
    for c in range(8):
        P.dma("pool", yv[c], B.X[:, c * 2048:(c + 1) * 2048], "yo")



W_G0, W_U0, W_D0, W_1, W_2 = 0, 1, 2, 3, 4
TKEYS = ["wg3", "wu3", "wd3", "w13", "w23"]
TSHAPES = {"wg3": (1024, 4096), "wu3": (1024, 4096), "wd3": (4096, 1024),
           "w13": (1024, 4096), "w23": (4096, 1024)}
WIDX = {"wg3": W_G0, "wu3": W_U0, "wd3": W_D0, "w13": W_1, "w23": W_2}


def build_prep():
    nc = bass.Bass("TRN2", target_bir_lowering=False, debug=False)
    dins = {"identf": nc.dram_tensor("identf", [128, 128], F32, kind="ExternalInput")}
    wts = {}
    for k in TKEYS:
        K, M = TSHAPES[k]
        wts[k] = (nc.dram_tensor(k + "_in", [K, M], F32, kind="ExternalInput"),
                  nc.dram_tensor(k, [K, M], BF16, kind="ExternalOutput"))
    scales = nc.dram_tensor("scales", [1, 32], F32, kind="ExternalOutput")
    P = Prog(nc)
    B = Bufs(nc)
    emit_setup(P, B, dins)
    for k in TKEYS:
        K, M = TSHAPES[k]
        emit_weight_prep(P, B, wts[k][0], wts[k][1], WIDX[k], K, M, scales)
    P.emit()
    nc.finalize()
    return nc


def build_main(stages, x_in, x_out):
    nc = bass.Bass("TRN2", target_bir_lowering=False, debug=False)
    dins = {"identf": nc.dram_tensor("identf", [128, 128], F32, kind="ExternalInput")}
    need = set()
    for st in stages:
        if st[0] == "ffn":
            need.add("scales")
            need |= ({"wg3", "wu3", "wd3"} if st[1] == 0 else {"w13", "w23", "bias"})
        else:
            need |= {"attn%d" % st[1]}
            if st[1] == 1:
                need.add("rope")
    tern = {}
    for k in TKEYS:
        if k in need:
            K, M = TSHAPES[k]
            tern[k] = nc.dram_tensor(k, [K, M], BF16, kind="ExternalInput")
    scales = (nc.dram_tensor("scales", [1, 32], F32, kind="ExternalInput")
              if "scales" in need else None)
    if "bias" in need:
        dins["b1c"] = nc.dram_tensor("b1c", [128, 32], F32, kind="ExternalInput")
        dins["b2c"] = nc.dram_tensor("b2c", [128, 8], F32, kind="ExternalInput")
    for L in (0, 1):
        if ("attn%d" % L) in need:
            w = _HC["mask%d" % L].shape[1]
            dins["mask%d" % L] = nc.dram_tensor("mask%d" % L, [128, w], BF16,
                                                kind="ExternalInput")
            dins["qkvT%d" % L] = nc.dram_tensor("qkvT%d" % L, [1024, 3072], F32,
                                                kind="ExternalInput")
            dins["woT%d" % L] = nc.dram_tensor("woT%d" % L, [1024, 1024], F32,
                                               kind="ExternalInput")
    if "rope" in need:
        dins["ropeC"] = nc.dram_tensor("ropeC", [128, 2048], BF16, kind="ExternalInput")
        dins["ropeS"] = nc.dram_tensor("ropeS", [128, 2048], BF16, kind="ExternalInput")
    x_d = nc.dram_tensor("x", [2048, 1024] if x_in == "tm" else [1024, 2048],
                         F32, kind="ExternalInput")
    y_d = nc.dram_tensor("y", [2048, 1024] if x_out == "tm" else [1024, 2048],
                         F32, kind="ExternalOutput")
    dscr = nc.dram_tensor("dscr", [4, 512], F32, kind="Internal")

    P = Prog(nc)
    B = Bufs(nc)
    emit_setup(P, B, dins)
    if scales is not None:
        emit_load_scales(P, B, scales, list(range(5)))
    if "bias" in need:
        emit_load_bias(P, B, dins)
    if "rope" in need:
        P.dma("sp", B.RC, dins["ropeC"].ap(), "ropec")
        P.dma("sp", B.RS, dins["ropeS"].ap(), "ropes")
    if x_in == "tm":
        emit_input_tm(P, B, x_d)
    else:
        emit_input_fm(P, B, x_d)
    for st in stages:
        if st[0] == "ffn":
            if st[1] == 0:
                emit_ffn(P, B, tern["wg3"], tern["wu3"], tern["wd3"],
                         W_G0, W_U0, W_D0, l1=False)
            else:
                emit_ffn(P, B, tern["w13"], None, tern["w23"],
                         W_1, None, W_2, l1=True)
        else:
            L = st[1]
            m = _HC["mask%d" % L]
            P.dma("sp", B.MASK[:, 0:m.shape[1]], dins["mask%d" % L].ap(), "mask")
            emit_attn(P, B, L, dins["qkvT%d" % L], dins["woT%d" % L], dscr)
    if x_out == "tm":
        emit_output_tm(P, B, y_d)
    else:
        emit_output_fm(P, B, y_d)
    P.emit()
    nc.finalize()
    return nc


def build_fused():
    nc = bass.Bass("TRN2", target_bir_lowering=False, debug=False)
    dins = {"identf": nc.dram_tensor("identf", [128, 128], F32, kind="ExternalInput")}
    win = {}
    tern = {}
    for k in TKEYS:
        K, M = TSHAPES[k]
        win[k] = nc.dram_tensor(k + "_in", [K, M], F32, kind="ExternalInput")
        tern[k] = nc.dram_tensor(k, [K, M], BF16, kind="Internal")
    scales = nc.dram_tensor("scales", [1, 32], F32, kind="Internal")
    dins["b1c"] = nc.dram_tensor("b1c", [128, 32], F32, kind="ExternalInput")
    dins["b2c"] = nc.dram_tensor("b2c", [128, 8], F32, kind="ExternalInput")
    for L in (0, 1):
        w = _HC["mask%d" % L].shape[1]
        dins["mask%d" % L] = nc.dram_tensor("mask%d" % L, [128, w], BF16,
                                            kind="ExternalInput")
        dins["qkvT%d" % L] = nc.dram_tensor("qkvT%d" % L, [1024, 3072], F32,
                                            kind="ExternalInput")
        dins["woT%d" % L] = nc.dram_tensor("woT%d" % L, [1024, 1024], F32,
                                           kind="ExternalInput")
    dins["ropeC"] = nc.dram_tensor("ropeC", [128, 2048], BF16, kind="ExternalInput")
    dins["ropeS"] = nc.dram_tensor("ropeS", [128, 2048], BF16, kind="ExternalInput")
    x_d = nc.dram_tensor("x", [2048, 1024], F32, kind="ExternalInput")
    y_d = nc.dram_tensor("y", [2048, 1024], F32, kind="ExternalOutput")
    dscr = nc.dram_tensor("dscr", [4, 512], F32, kind="Internal")

    P = Prog(nc)
    B = Bufs(nc)
    emit_setup(P, B, dins)
    emit_load_bias(P, B, dins)
    emit_input_tm(P, B, x_d)
    for k in TKEYS:
        K, M = TSHAPES[k]
        emit_weight_prep(P, B, win[k], tern[k], WIDX[k], K, M, scales)
        P.seal_tensor(k, "tst%d" % WIDX[k])
    for st in [("ffn", 0), ("attn", 0), ("ffn", 0), ("ffn", 1), ("attn", 1),
               ("ffn", 1)]:
        if st[0] == "ffn":
            if st[1] == 0:
                emit_ffn(P, B, tern["wg3"], tern["wu3"], tern["wd3"],
                         W_G0, W_U0, W_D0, l1=False)
            else:
                emit_ffn(P, B, tern["w13"], None, tern["w23"],
                         W_1, None, W_2, l1=True)
        else:
            L = st[1]
            m = _HC["mask%d" % L]
            P.dma("sp", B.MASK[:, 0:m.shape[1]], dins["mask%d" % L].ap(), "mask")
            if L == 1:
                P.dma("sp", B.RC, dins["ropeC"].ap(), "ropec")
                P.dma("sp", B.RS, dins["ropeS"].ap(), "ropes")
            emit_attn(P, B, L, dins["qkvT%d" % L], dins["woT%d" % L], dscr)
    emit_output_tm(P, B, y_d)
    P.emit()
    nc.finalize()
    return nc



_NC_CACHE = {}


def _get(key, fn):
    if key not in _NC_CACHE:
        _NC_CACHE[key] = fn()
    return _NC_CACHE[key]


def kernel(L0_attn_x, L0_attn_qkv, L0_attn_wo, L0_ffn_w_gate, L0_ffn_w_up,
           L0_ffn_w_down, L1_attn_W_KQV, L1_attn_W_O, L1_ffn_W1, L1_ffn_b1,
           L1_ffn_W2, L1_ffn_b2):
    x = np.ascontiguousarray(np.asarray(L0_attn_x, np.float32))
    tw = lambda w: np.ascontiguousarray(np.asarray(w, np.float32).T)
    wgT, wuT, wdT = tw(L0_ffn_w_gate), tw(L0_ffn_w_up), tw(L0_ffn_w_down)
    w1T, w2T = tw(L1_ffn_W1), tw(L1_ffn_W2)
    qkvT0, woT0 = tw(L0_attn_qkv), tw(L0_attn_wo)
    qkvT1, woT1 = tw(L1_attn_W_KQV), tw(L1_attn_W_O)
    b1c = np.ascontiguousarray(np.asarray(L1_ffn_b1, np.float32).reshape(32, 128).T)
    b2c = np.ascontiguousarray(np.asarray(L1_ffn_b2, np.float32).reshape(8, 128).T)
    consts = _HC

    if FUSED:
        ncf = _get("fused", build_fused)
        in_maps = []
        for i in range(8):
            mm = {"identf": consts["identf"], "b1c": b1c, "b2c": b2c,
                  "ropeC": consts["ropeC"], "ropeS": consts["ropeS"],
                  "mask0": np.ascontiguousarray(consts["mask0"]),
                  "mask1": np.ascontiguousarray(consts["mask1"]),
                  "qkvT0": qkvT0, "woT0": woT0, "qkvT1": qkvT1, "woT1": woT1,
                  "x": np.ascontiguousarray(x[i])}
            mm.update({k + "_in": w for k, w in
                       zip(TKEYS, [wgT, wuT, wdT, w1T, w2T])})
            in_maps.append(mm)
        res = run_bass_kernel_spmd(ncf, in_maps, core_ids=list(range(8)))
        return np.stack([res.results[i]["y"] for i in range(8)], axis=0).astype(np.float32)

    ncp = _get("prep", build_prep)
    inp = {"identf": consts["identf"]}
    inp.update({k + "_in": w for k, w in zip(TKEYS, [wgT, wuT, wdT, w1T, w2T])})
    resp = run_bass_kernel_spmd(ncp, [inp] * 8, core_ids=list(range(8)))
    tern = {k: resp.results[0][k] for k in TKEYS}
    scales = resp.results[0]["scales"]

    common = {"identf": consts["identf"]}
    stage_cfgs = [
        ([("ffn", 0)], "tm", "fm"),
        ([("attn", 0)], "fm", "fm"),
        ([("ffn", 0)], "fm", "fm"),
        ([("ffn", 1)], "fm", "fm"),
        ([("attn", 1)], "fm", "fm"),
        ([("ffn", 1)], "fm", "tm"),
    ]
    cur = [np.ascontiguousarray(x[i]) for i in range(8)]
    for si, (stages, xin, xout) in enumerate(stage_cfgs):
        ncm = _get(("main", si), lambda stages=stages, xin=xin, xout=xout:
                   build_main(stages, xin, xout))
        in_maps = []
        for i in range(8):
            m = dict(common)
            st = stages[0]
            if st[0] == "ffn":
                m["scales"] = scales
                if st[1] == 0:
                    m.update({k: tern[k] for k in ("wg3", "wu3", "wd3")})
                else:
                    m.update({k: tern[k] for k in ("w13", "w23")})
                    m.update({"b1c": b1c, "b2c": b2c})
            else:
                L = st[1]
                m["mask%d" % L] = np.ascontiguousarray(consts["mask%d" % L])
                m["qkvT%d" % L] = qkvT0 if L == 0 else qkvT1
                m["woT%d" % L] = woT0 if L == 0 else woT1
                if L == 1:
                    m["ropeC"] = consts["ropeC"]
                    m["ropeS"] = consts["ropeS"]
            m["x"] = np.ascontiguousarray(cur[i])
            in_maps.append(m)
        res = run_bass_kernel_spmd(ncm, in_maps, core_ids=list(range(8)))
        cur = [res.results[i]["y"] for i in range(8)]
    return np.stack(cur, axis=0).astype(np.float32)


# version 06
# speedup vs baseline: 1.0347x; 1.0347x over previous
import numpy as np
import ml_dtypes
import concourse.bass as bass
import concourse.mybir as mybir
from concourse.bass_utils import run_bass_kernel_spmd

F32 = mybir.dt.float32
BF16 = mybir.dt.bfloat16
I32 = mybir.dt.int32
AF = mybir.ActivationFunctionType
ALU = mybir.AluOpType
AX = mybir.AxisListType
BFNP = ml_dtypes.bfloat16

S, DM, NH, DH, INNER = 2048, 1024, 16, 64, 4096
FFN_TTS = range(2)
FFN_T1_STOP = None
FUSED = True


def host_constants():
    c = {}
    c["identf"] = np.eye(128, dtype=np.float32)
    jj = np.arange(128)[:, None]
    uu = np.arange(2432)[None, :] - 384
    d = uu - jj
    allowed = ((d >= 0) & (d <= 127)) | ((d >= 132) & ((d - 128) % 4 == 0))
    c["mask0"] = allowed.astype(BFNP)
    uu1 = np.arange(896)[None, :] - 384
    c["mask1"] = ((uu1 - jj) >= 0).astype(BFNP)
    r = np.arange(0, DH, 2, dtype=np.float64)
    invf = 1.0 / (10000.0 ** (r / DH))
    fr = np.arange(S, dtype=np.float64)[:, None] * invf[None, :]
    cos = np.cos(fr); sin = np.sin(fr)
    C = np.repeat(cos, 2, axis=1)
    S2 = np.empty((S, DH)); S2[:, 0::2] = -sin; S2[:, 1::2] = sin
    c["ropeC"] = np.tile(C.T, (2, 1)).astype(BFNP)
    c["ropeS"] = np.tile(S2.T, (2, 1)).astype(BFNP)
    return c

_HC = host_constants()
_SHUF_PAIR = []
for _i in range(0, 32, 2):
    _SHUF_PAIR.extend([_i + 1, _i])


QSEM = {"sp": "qsp", "act": "qact", "pool": "qpool"}


def _apkeys(x):
    if not hasattr(x, "tensor"):
        return (x,)
    t = x.tensor
    nm = t.name
    dims = list(x.ap)
    try:
        import bass_rust
        is_dram = x.space == getattr(bass_rust, "MemorySpace").DRAM
    except Exception:
        is_dram = str(getattr(x, "space", "")).find("DRAM") >= 0
    lo = int(x.offset)
    if not is_dram:
        fw = 1
        for d in list(t.shape)[1:]:
            fw *= int(d)
        lo = lo % fw
    hi = lo
    use = dims if is_dram else dims[1:] if len(dims) > 1 else []
    for st, ct in use:
        hi += (int(ct) - 1) * int(st)
    hi += 1
    gran = 65536 if is_dram else 512
    return tuple((nm, i) for i in range(lo // gran, (hi + gran - 1) // gran))


class Prog:
    def __init__(self, nc):
        self.nc = nc
        self.sem = {}
        for e in ("pe", "dve", "act", "pool"):
            self.sem[e] = nc.alloc_semaphore("S" + e)
        self.cnt = {e: 0 for e in self.sem}
        self.q = {e: [] for e in ("pe", "dve", "act", "pool", "sp")}
        self.hwm = {e: {} for e in self.q}
        self.res = {}

    def _stream(self, name):
        key = ("dma", name)
        if key not in self.sem:
            self.sem[key] = self.nc.alloc_semaphore("Sd_" + name)
            self.cnt[key] = 0
        return key

    def _expand(self, items):
        out = []
        for it in items:
            out.extend(_apkeys(it))
        return out

    def _need(self, eng, own_sem, reads, writes):
        need = {}

        def add(s, v):
            if s == own_sem and v > self.cnt.get(own_sem, 0):
                return
            need[s] = max(need.get(s, 0), v)

        for k in reads:
            r = self.res.get(k)
            if r and r["w"] is not None:
                add(*r["w"])
        for k in writes:
            r = self.res.get(k)
            if r:
                if r["w"] is not None:
                    add(*r["w"])
                for s, v in r["r"].items():
                    add(s, v)
        hw = self.hwm[eng]
        waits = []
        for s, v in sorted(need.items(), key=lambda kv: str(kv[0])):
            if hw.get(s, 0) < v:
                waits.append((s, v))
                hw[s] = v
        return waits

    def _record(self, own_sem, tick, reads, writes):
        for k in writes:
            self.res[k] = {"w": (own_sem, tick), "r": {}}
        for k in reads:
            r = self.res.get(k)
            if r is None:
                r = {"w": None, "r": {}}
                self.res[k] = r
            r["r"][own_sem] = max(r["r"].get(own_sem, 0), tick)

    def op(self, eng, fn, reads=(), writes=(), inc=True):
        import sys as _s
        f = _s._getframe(1)
        self._dbg = "%s:%d" % (f.f_code.co_name, f.f_lineno)
        own = eng
        reads = self._expand(reads)
        writes = self._expand(writes)
        waits = self._need(eng, own, reads, writes)
        if inc:
            self.cnt[eng] += 1
            tick = self.cnt[eng]
        else:
            tick = self.cnt[eng] + 1
        self.q[eng].append((waits, fn, own if inc else None, self._dbg))
        self._record(own, tick, reads, writes)

    def dma(self, qeng, dst, src, stream, reads=(), writes=(), **kw):
        own = self._stream(stream)
        reads = self._expand(tuple(reads) + (src,))
        writes = self._expand(tuple(writes) + (dst,))
        waits = self._need(qeng, own, reads, writes)
        self.cnt[own] += 1
        tick = 16 * self.cnt[own]
        import sys as _s
        f = _s._getframe(1)
        dbg = "%s:%d" % (f.f_code.co_name, f.f_lineno)
        self.q[qeng].append(
            (waits, lambda e, d=dst, s=src, k=dict(kw): e.dma_start(d, s, **k), own, dbg))
        self._record(own, tick, reads, writes)

    def seal_tensor(self, name, stream):
        key = ("dma", stream)
        tick = 16 * self.cnt[key]
        for k, r in self.res.items():
            if isinstance(k, tuple) and len(k) == 2 and k[0] == name:
                r["w"] = (key, tick)

    def check_deadlock(self):
        pos = {e: 0 for e in self.q}
        val = {s: 0 for s in self.cnt}
        progressed = True
        while progressed:
            progressed = False
            for e, lst in self.q.items():
                while pos[e] < len(lst):
                    waits, _fn, inc_sem = lst[pos[e]][0], lst[pos[e]][1], lst[pos[e]][2]
                    if all(val.get(s, 0) >= v for s, v in waits):
                        if inc_sem is not None:
                            val[inc_sem] = val.get(inc_sem, 0) + (
                                16 if isinstance(inc_sem, tuple) else 1)
                        pos[e] += 1
                        progressed = True
                    else:
                        break
        stuck = {e: pos[e] for e in self.q if pos[e] < len(self.q[e])}
        if stuck:
            msg = []
            for e, p in stuck.items():
                waits, _f, _i, dbg = self.q[e][p]
                bad = [(s, v, val.get(s, 0)) for s, v in waits if val.get(s, 0) < v]
                msg.append(f"{e}@{p} [{dbg}] needs {bad}")
            raise RuntimeError("build-time deadlock: " + " | ".join(msg))

    def emit(self):
        nc = self.nc
        sems = self.sem
        final = dict(self.cnt)

        def mk(lst, eng_name):
            def body(engine):
                for waits, fn, inc_sem, _dbg in lst:
                    for s, v in waits:
                        engine.wait_ge(sems[s], v)
                    ins = fn(engine)
                    if inc_sem is not None:
                        amt = 16 if isinstance(inc_sem, tuple) else 1
                        ins.then_inc(sems[inc_sem], amt)
                if eng_name == "pool":
                    for s, v in final.items():
                        tgt = v * 16 if isinstance(s, tuple) else v
                        if tgt > 0:
                            engine.wait_ge(sems[s], tgt)
            return body

        self.check_deadlock()
        with nc.Block() as blk:
            blk.sync(mk(self.q["sp"], "sp"))
            blk.tensor(mk(self.q["pe"], "pe"))
            blk.vector(mk(self.q["dve"], "dve"))
            blk.scalar(mk(self.q["act"], "act"))
            blk.gpsimd(mk(self.q["pool"], "pool"))



class Bufs:
    def __init__(self, nc):
        A = nc.alloc_sbuf_tensor
        self.X = A("X", [128, 16384], F32).ap()
        self.HB = A("HB", [128, 33024], BF16).ap()
        self.XQ = A("XQb", [128, 8192], BF16).ap()
        self.QT = A("QTb", [128, 4096], BF16).ap()
        self.WB = A("WBb", [128, 4096], BF16).ap()
        self.WR = A("WRb", [128, 1024], F32).ap()
        self.BC = A("BCb", [128, 2048], F32).ap()
        self.EV = A("EVb", [128, 1280], F32).ap()
        self.ACC = A("ACCb", [128, 1024], BF16).ap()
        self.RT = A("RTb", [128, 2048], F32).ap()
        self.RI = A("RIb", [128, 1024], I32).ap()
        self.CC = A("CCb", [128, 2048], BF16).ap()
        self.AUX = A("AUXb", [128, 4992], BF16).ap()
        self.MASK = self.AUX
        self.RC = self.AUX[:, 896:2944]
        self.RS = self.AUX[:, 2944:4992]
        self.COLS = A("COLSb", [128, 160], F32).ap()
        self.CB = A("CBb", [128, 128], F32).ap()
        self.IDF = A("IDFb", [128, 128], F32).ap()
        self.ONE1 = A("ONE1b", [128, 1], BF16).ap()
        self.KCONST = A("KCONSTb", [128, 2], F32).ap()
        self.MWC = A("MWCb", [128, 10], F32).ap()
        self.BIA = A("BIAb", [128, 48], F32).ap()
        self.SCR = A("SCRb", [1, 32], F32).ap()
        self.pb = [nc.alloc_psum_tensor(f"pb{i}", [128, 512], F32).ap() for i in range(8)]

    def col(self, v, idx, n=1):
        return self.COLS[:, v * 16 + idx: v * 16 + idx + n]

    def xs(self, c, qt):
        return self.X[:, c * 2048 + qt * 512: c * 2048 + qt * 512 + 512]


(VAR_M, VAR_RSTD, VAR_S1, VAR_S2, VAR_MAXX, VAR_MINX,
 VAR_ACOL, VAR_INVSX, VAR_SH, VAR_TMP) = range(10)
KC_EPS, KC_EIGHTH = 0, 1


def emit_setup(P, B, dins):
    P.dma("sp", B.IDF, dins["identf"].ap(), "ldid")
    P.op("dve", lambda e: e.memset(B.CB, 0.0), writes=[B.CB])
    P.op("dve", lambda e: e.memset(B.ONE1, 1.0), writes=[B.ONE1])
    P.op("dve", lambda e: e.memset(B.KCONST[:, 0:1], 1e-5), writes=[B.KCONST])
    P.op("dve", lambda e: e.memset(B.KCONST[:, 1:2], 0.125), writes=[B.KCONST])
    P.op("dve", lambda e: e.memset(B.COLS, 0.0), writes=[B.COLS])



def emit_weight_prep(P, B, wT, scratch, widx, K, M, scr_dram):
    nkc, nmb = K // 128, M // 1024
    wsrc = wT.ap().rearrange("(k p) m -> k p m", p=128)
    wdst = scratch.ap().rearrange("(k p) m -> k p m", p=128)
    acc = B.col(VAR_TMP, 0)
    tmpc = B.col(VAR_TMP, 1)
    P.op("dve", lambda e: e.memset(acc, 0.0), writes=[acc])
    bi = 0
    for kc in range(nkc):
        for mb in range(nmb):
            wr = B.WR[:, 0:1024] if bi % 2 == 0 else B.RT[:, 1024:2048]
            tm2 = B.col(VAR_TMP, 1 + bi % 2)
            P.dma("sp", wr, wsrc[kc, :, mb * 1024:(mb + 1) * 1024],
                  "wr%d" % (bi % 2))
            P.op("dve", lambda e, wr=wr, tm2=tm2: e.reduce_sum(
                tm2, wr, AX.X, apply_absolute_value=True),
                reads=[wr], writes=[tm2])
            P.op("dve", lambda e, tm2=tm2: e.tensor_add(acc, acc, tm2),
                 reads=[tm2, acc], writes=[acc])
            bi += 1
    P.op("pe", lambda e: e.transpose(B.pb[7][0:1, 0:128], acc, B.IDF),
         reads=[acc, B.IDF], writes=[B.pb[7]])
    P.op("dve", lambda e: e.tensor_copy(B.EV[0:1, 0:128], B.pb[7][0:1, 0:128]),
         reads=[B.pb[7]], writes=[B.EV[0:1, 0:128]])
    P.op("dve", lambda e: e.reduce_sum(B.EV[0:1, 512:513], B.EV[0:1, 0:128], AX.X),
         reads=[B.EV[0:1, 0:128]], writes=[B.EV[0:1, 512:513]])
    P.op("dve", lambda e: e.tensor_scalar(
        B.SCR[0:1, widx:widx + 1], B.EV[0:1, 512:513], 1.0 / (K * M), 1e-5,
        ALU.mult, ALU.max),
        reads=[B.EV[0:1, 512:513]], writes=[B.SCR[0:1, widx:widx + 1]])
    P.op("dve", lambda e: e.reciprocal(B.SCR[0:1, 16 + widx:17 + widx],
                                       B.SCR[0:1, widx:widx + 1]),
         reads=[B.SCR[0:1, widx:widx + 1]], writes=[B.SCR[0:1, 16 + widx:17 + widx]])
    P.dma("pool", scr_dram.ap()[0:1, widx:widx + 1], B.SCR[0:1, widx:widx + 1],
          "scs%d" % widx)
    P.dma("pool", scr_dram.ap()[0:1, 16 + widx:17 + widx],
          B.SCR[0:1, 16 + widx:17 + widx], "scs%d" % widx)
    emit_load_scales(P, B, scr_dram, [widx])
    bi = 0
    for kc in range(nkc):
        for mb in range(nmb):
            sl = bi % 2
            wr = B.WR[:, 0:1024] if sl == 0 else B.RT[:, 1024:2048]
            P.dma("sp", wr, wsrc[kc, :, mb * 1024:(mb + 1) * 1024], "wr%d" % sl)
            vv = B.EV[:, 0:1024] if sl == 0 else B.BC[:, 0:1024]
            P.op("dve", lambda e, wr=wr, vv=vv: e.tensor_scalar_mul(
                vv, wr, B.MWC[:, 5 + widx:6 + widx]),
                reads=[wr, B.MWC], writes=[vv])
            c1 = B.CC[:, sl * 1024:(sl + 1) * 1024]
            P.op("dve", lambda e, vv=vv, c1=c1: e.tensor_single_scalar(
                c1, vv, 0.5, ALU.is_gt), reads=[vv], writes=[c1])
            c2 = B.QT[:, sl * 1024:(sl + 1) * 1024]
            P.op("pool", lambda e, vv=vv, c2=c2: e.tensor_single_scalar(
                c2, vv, -0.5, ALU.is_lt), reads=[vv], writes=[c2])
            wb = B.WB[:, sl * 1024:(sl + 1) * 1024]
            P.op("dve", lambda e, c1=c1, c2=c2, wb=wb: e.tensor_sub(wb, c1, c2),
                 reads=[c1, c2], writes=[wb])
            P.dma("pool", wdst[kc, :, mb * 1024:(mb + 1) * 1024], wb, "tst%d" % widx)
            bi += 1


def emit_load_scales(P, B, scr_dram, widxs):
    for w in widxs:
        P.dma("sp", B.MWC[:, w:w + 1],
              scr_dram.ap()[0:1, w:w + 1].broadcast_to([128, 1]), "mwc")
        P.dma("sp", B.MWC[:, 5 + w:6 + w],
              scr_dram.ap()[0:1, 16 + w:17 + w].broadcast_to([128, 1]), "mwc")


def emit_load_bias(P, B, dins):
    P.dma("sp", B.BIA[:, 0:32], dins["b1c"].ap(), "bias")
    P.dma("sp", B.BIA[:, 32:40], dins["b2c"].ap(), "bias")
    P.op("dve", lambda e: e.tensor_scalar_mul(B.BIA[:, 40:48], B.BIA[:, 32:40], 0.5),
         reads=[B.BIA[:, 32:40]], writes=[B.BIA[:, 40:48]])



def emit_stats(P, B, ffn):
    for qt in range(4):
        for c in range(8):
            xsl = B.xs(c, qt)
            slot = c % 2
            xb = B.QT[:, slot * 512:(slot + 1) * 512]
            sq = B.QT[:, 1024 + slot * 512: 1024 + (slot + 1) * 512]
            P.op("dve", lambda e, xb=xb, xsl=xsl: e.tensor_copy(xb, xsl),
                 reads=[xsl], writes=[xb])
            P.op("act", lambda e, sq=sq, xsl=xsl: e.activation(sq, xsl, AF.Square),
                 reads=[xsl], writes=[sq])
            P.op("pe", lambda e, xb=xb, c=c: e.matmul(
                B.pb[5][0:1, :], B.ONE1, xb, start=(c == 0), stop=(c == 7)),
                reads=[xb, B.ONE1], writes=[B.pb[5]])
            P.op("pe", lambda e, sq=sq, c=c: e.matmul(
                B.pb[6][0:1, :], B.ONE1, sq, start=(c == 0), stop=(c == 7)),
                reads=[sq, B.ONE1], writes=[B.pb[6]])
        r1 = B.EV[0:1, 0:512]
        r2 = B.EV[0:1, 512:1024]
        P.op("dve", lambda e: e.tensor_copy(r1, B.pb[5][0:1, :]),
             reads=[B.pb[5]], writes=[r1])
        P.op("dve", lambda e: e.tensor_copy(r2, B.pb[6][0:1, :]),
             reads=[B.pb[6]], writes=[r2])
        for g in range(4):
            P.op("pe", lambda e, g=g: e.transpose(
                B.pb[7][:, g:g + 1], B.EV[0:1, g * 128:(g + 1) * 128], B.IDF[0:1, 0:1]),
                reads=[r1, B.IDF], writes=[B.pb[7]], inc=False)
            P.op("pe", lambda e, g=g: e.transpose(
                B.pb[7][:, 4 + g:5 + g],
                B.EV[0:1, 512 + g * 128:512 + (g + 1) * 128], B.IDF[0:1, 0:1]),
                reads=[r2, B.IDF], writes=[B.pb[7]])
        P.op("dve", lambda e, qt=qt: e.tensor_copy(
            B.col(VAR_S1, qt * 4, 4), B.pb[7][:, 0:4]),
            reads=[B.pb[7]], writes=[B.col(VAR_S1, qt * 4, 4)])
        P.op("dve", lambda e, qt=qt: e.tensor_copy(
            B.col(VAR_S2, qt * 4, 4), B.pb[7][:, 4:8]),
            reads=[B.pb[7]], writes=[B.col(VAR_S2, qt * 4, 4)])
        if ffn:
            ea = B.EV[:, 0:512]
            eb = B.EV[:, 512:1024]
            xview = B.X.rearrange("p (c t) -> p t c", c=8)[:, qt * 512:(qt + 1) * 512, :]
            xkeys = [B.xs(c2_, qt) for c2_ in range(8)] + [r1, r2]
            P.op("dve", lambda e, xview=xview: e.reduce_max(ea, xview, AX.X),
                 reads=xkeys, writes=[ea])
            P.op("dve", lambda e, xview=xview: e.tensor_reduce(
                eb, xview, AX.X, ALU.min), reads=xkeys, writes=[eb])
            for g in range(4):
                P.op("pe", lambda e, g=g: e.transpose(
                    B.pb[7][:, g * 128:(g + 1) * 128], ea[:, g * 128:(g + 1) * 128],
                    B.IDF),
                    reads=[ea, B.IDF], writes=[B.pb[7]])
            for g in range(4):
                P.op("dve", lambda e, g=g, qt=qt: e.reduce_max(
                    B.col(VAR_MAXX, qt * 4 + g), B.pb[7][:, g * 128:(g + 1) * 128],
                    AX.X),
                    reads=[B.pb[7]], writes=[B.col(VAR_MAXX, qt * 4 + g)])
            for g in range(4):
                P.op("pe", lambda e, g=g: e.transpose(
                    B.pb[7][:, g * 128:(g + 1) * 128], eb[:, g * 128:(g + 1) * 128],
                    B.IDF),
                    reads=[eb, B.IDF], writes=[B.pb[7]])
            for g in range(4):
                P.op("dve", lambda e, g=g, qt=qt: e.tensor_reduce(
                    B.col(VAR_MINX, qt * 4 + g), B.pb[7][:, g * 128:(g + 1) * 128],
                    AX.X, ALU.min),
                    reads=[B.pb[7]], writes=[B.col(VAR_MINX, qt * 4 + g)])
    m16 = B.col(VAR_M, 0, 16)
    rst = B.col(VAR_RSTD, 0, 16)
    tmp = B.col(VAR_TMP, 0, 16)
    s1 = B.col(VAR_S1, 0, 16)
    s2 = B.col(VAR_S2, 0, 16)
    P.op("dve", lambda e: e.tensor_scalar_mul(m16, s1, 1.0 / 1024.0),
         reads=[s1], writes=[m16])
    P.op("dve", lambda e: e.tensor_mul(tmp, m16, m16), reads=[m16], writes=[tmp])
    P.op("dve", lambda e: e.scalar_tensor_tensor(
        rst, s2, 1.0 / 1024.0, tmp, ALU.mult, ALU.subtract),
        reads=[s2, tmp], writes=[rst])
    P.op("act", lambda e: e.activation(rst, rst, AF.Sqrt, bias=B.KCONST[:, 0:1]),
         reads=[rst, B.KCONST], writes=[rst])
    P.op("dve", lambda e: e.reciprocal(rst, rst), reads=[rst], writes=[rst])
    if ffn:
        mx = B.col(VAR_MAXX, 0, 16)
        mn = B.col(VAR_MINX, 0, 16)
        a16 = B.col(VAR_ACOL, 0, 16)
        iv = B.col(VAR_INVSX, 0, 16)
        P.op("dve", lambda e: e.tensor_sub(mx, mx, m16), reads=[mx, m16], writes=[mx])
        P.op("dve", lambda e: e.tensor_sub(mn, m16, mn), reads=[mn, m16], writes=[mn])
        P.op("dve", lambda e: e.tensor_max(mx, mx, mn), reads=[mx, mn], writes=[mx])
        P.op("dve", lambda e: e.reciprocal(a16, mx), reads=[mx], writes=[a16])
        P.op("dve", lambda e: e.tensor_scalar_mul(a16, a16, 127.0),
             reads=[a16], writes=[a16])
        P.op("dve", lambda e: e.tensor_mul(iv, rst, mx), reads=[rst, mx], writes=[iv])
        P.op("dve", lambda e: e.tensor_scalar(iv, iv, 1e-5, 1.0 / 127.0,
                                              ALU.max, ALU.mult),
             reads=[iv], writes=[iv])


def emit_bcast(P, B, colap, slot, qt, pbi=4):
    pb = B.pb[pbi]
    for g in range(4):
        cs = colap[:, qt * 4 + g: qt * 4 + g + 1]
        P.op("act", lambda e, cs=cs: e.activation(B.CB, B.CB, AF.Identity,
                                                  bias=cs, scale=0.0),
             reads=[cs, B.CB], writes=[B.CB])
        P.op("pe", lambda e, g=g, pb=pb: e.transpose(
            pb[:, g * 128:(g + 1) * 128], B.CB, B.IDF),
            reads=[B.CB, B.IDF], writes=[pb])
    bc = B.BC[:, slot * 512:(slot + 1) * 512]
    P.op("dve", lambda e, bc=bc, pb=pb: e.tensor_copy(bc, pb),
         reads=[pb], writes=[bc])
    return bc


def emit_round(P, B, src, dst, ro, n=512):
    yi = B.RI[:, ro:ro + n]
    yf = B.RT[:, 1024 + ro:1024 + ro + n]
    c1 = B.CC[:, ro:ro + n]
    c2 = B.CC[:, 1024 + ro:1024 + ro + n]
    P.op("dve", lambda e: e.tensor_scalar(yi, src, 1.0, None, ALU.mult),
         reads=[src], writes=[yi])
    P.op("dve", lambda e: e.tensor_copy(yf, yi), reads=[yi], writes=[yf])
    P.op("dve", lambda e: e.scalar_tensor_tensor(
        c1, src, 0.5, yf, ALU.subtract, ALU.is_gt),
        reads=[src, yf], writes=[c1])
    P.op("dve", lambda e: e.scalar_tensor_tensor(
        c2, src, 0.5, yf, ALU.add, ALU.is_lt),
        reads=[src, yf], writes=[c2])
    P.op("dve", lambda e: e.tensor_add(yf, yf, c1), reads=[yf, c1], writes=[yf])
    P.op("dve", lambda e: e.tensor_sub(dst, yf, c2), reads=[yf, c2], writes=[dst])



def emit_ffn(P, B, tern_g, tern_u, tern_d, wg_i, wu_i, wd_i, l1):
    emit_stats(P, B, ffn=True)
    g3 = tern_g.ap().rearrange("(k p) m -> k p m", p=128)
    u3 = tern_u.ap().rearrange("(k p) m -> k p m", p=128) if tern_u is not None else None
    d3 = tern_d.ap().rearrange("(k p) m -> k p m", p=128)
    cgc = B.col(VAR_TMP, 0, 16)
    P.op("dve", lambda e: e.tensor_scalar_mul(
        cgc, B.col(VAR_INVSX, 0, 16), B.MWC[:, wg_i:wg_i + 1]),
        reads=[B.col(VAR_INVSX, 0, 16), B.MWC], writes=[cgc])
    for tt in FFN_TTS:
        for sub in range(2):
            qt = tt * 2 + sub
            bM = emit_bcast(P, B, B.col(VAR_M, 0, 16), 0, qt)
            bA = emit_bcast(P, B, B.col(VAR_ACOL, 0, 16), 1, qt)
            for c in range(8):
                xsl = B.xs(c, qt)
                ro = (c % 2) * 512
                xm = B.EV[:, ro:ro + 512]
                t = B.RT[:, ro:ro + 512]
                P.op("dve", lambda e, xsl=xsl, xm=xm, bM=bM: e.tensor_sub(xm, xsl, bM),
                     reads=[xsl, bM], writes=[xm])
                P.op("dve", lambda e, xm=xm, t=t, bA=bA: e.tensor_mul(t, xm, bA),
                     reads=[xm, bA], writes=[t])
                xq = B.XQ[:, c * 1024 + sub * 512: c * 1024 + sub * 512 + 512]
                emit_round(P, B, t, xq, ro)
        if tt == 1 and FFN_T1_STOP == "xq":
            continue
        bCg = [emit_bcast(P, B, cgc, 2, tt * 2), emit_bcast(P, B, cgc, 3, tt * 2 + 1)]
        am0 = B.ACC[:, 0:1024]
        an0 = B.CC[:, 1024:2048]
        for m in range(32):
            ws = m % 2
            gblk = B.WB[:, ws * 1024: ws * 1024 + 1024]
            P.dma("sp", gblk.rearrange("p (k m) -> p k m", k=8),
                  g3[:, :, m * 128:(m + 1) * 128].rearrange("k p m -> p k m"),
                  "wg%d" % ws)
            if not l1:
                ublk = B.WB[:, 2048 + ws * 1024: 2048 + ws * 1024 + 1024]
                P.dma("sp", ublk.rearrange("p (k m) -> p k m", k=8),
                      u3[:, :, m * 128:(m + 1) * 128].rearrange("k p m -> p k m"),
                      "wu%d" % ws)
            if True:
                for sub in range(2):
                    bgi = (m % 2) * 4 + sub * 2
                    bui = bgi + 1
                    pg = B.pb[bgi]
                    for k in range(8):
                        wt = gblk[:, k * 128:(k + 1) * 128]
                        xq = B.XQ[:, k * 1024 + sub * 512: k * 1024 + sub * 512 + 512]
                        P.op("pe", lambda e, wt=wt, xq=xq, k=k, pg=pg: e.matmul(
                            pg, wt, xq, start=(k == 0), stop=(k == 7)),
                            reads=[wt, xq], writes=[pg])
                    if not l1:
                        pu = B.pb[bui]
                        for k in range(8):
                            wt = ublk[:, k * 128:(k + 1) * 128]
                            xq = B.XQ[:, k * 1024 + sub * 512: k * 1024 + sub * 512 + 512]
                            P.op("pe", lambda e, wt=wt, xq=xq, k=k, pu=pu: e.matmul(
                                pu, wt, xq, start=(k == 0), stop=(k == 7)),
                                reads=[wt, xq], writes=[pu])
                    gs = B.EV[:, sub * 512: sub * 512 + 512]
                    P.op("dve", lambda e, pg=pg, gs=gs, sub=sub: e.tensor_mul(
                        gs, pg, bCg[sub]), reads=[pg, bCg[sub]], writes=[gs])
                    hsl = B.HB[:, m * 1024 + sub * 512: m * 1024 + sub * 512 + 512]
                    if l1:
                        b1c = B.BIA[:, m:m + 1]
                        P.op("act", lambda e, gs=gs, hsl=hsl, b1c=b1c: e.activation(
                            hsl, gs, AF.Gelu, bias=b1c),
                            reads=[gs, B.BIA], writes=[hsl])
                    else:
                        P.op("act", lambda e, gs=gs: e.activation(gs, gs, AF.Silu),
                             reads=[gs], writes=[gs])
                        pu = B.pb[bui]
                        P.op("dve", lambda e, gs=gs, hsl=hsl, pu=pu: e.tensor_mul(
                            hsl, gs, pu), reads=[gs, pu], writes=[hsl])

        if tt == 1 and FFN_T1_STOP == "gu":
            continue
        hview = B.HB[:, 0:32768].rearrange("p (m t) -> p t m", m=32)
        hkeys = [B.HB[:, 0:32768]]
        P.op("dve", lambda e, hview=hview: e.reduce_max(am0, hview, AX.X),
             reads=hkeys, writes=[am0])
        P.op("dve", lambda e, hview=hview: e.tensor_reduce(an0, hview, AX.X, ALU.min),
             reads=hkeys, writes=[an0])
        shc = B.col(VAR_SH, tt * 8, 8)
        negm = B.col(VAR_MAXX, tt * 8, 8)
        for half in range(2):
            rmax = B.RT[:, 0:512]
            rmin = B.RT[:, 512:1024]
            P.op("dve", lambda e, half=half, rmax=rmax: e.tensor_copy(
                rmax, B.ACC[:, half * 512:(half + 1) * 512]),
                reads=[am0], writes=[rmax])
            P.op("dve", lambda e, half=half, rmin=rmin: e.tensor_copy(
                rmin, B.CC[:, 1024 + half * 512: 1024 + (half + 1) * 512]),
                reads=[an0], writes=[rmin])
            for g in range(4):
                P.op("pe", lambda e, g=g, rmax=rmax: e.transpose(
                    B.pb[7][:, g * 128:(g + 1) * 128],
                    rmax[:, g * 128:(g + 1) * 128], B.IDF),
                    reads=[rmax, B.IDF], writes=[B.pb[7]])
            for g in range(4):
                gg = half * 4 + g
                P.op("dve", lambda e, g=g, gg=gg: e.reduce_max(
                    B.col(VAR_SH, tt * 8 + gg), B.pb[7][:, g * 128:(g + 1) * 128],
                    AX.X),
                    reads=[B.pb[7]], writes=[B.col(VAR_SH, tt * 8 + gg)])
            for g in range(4):
                P.op("pe", lambda e, g=g, rmin=rmin: e.transpose(
                    B.pb[7][:, g * 128:(g + 1) * 128],
                    rmin[:, g * 128:(g + 1) * 128], B.IDF),
                    reads=[rmin, B.IDF], writes=[B.pb[7]])
            for g in range(4):
                gg = half * 4 + g
                P.op("dve", lambda e, g=g, gg=gg: e.tensor_reduce(
                    B.col(VAR_MAXX, tt * 8 + gg),
                    B.pb[7][:, g * 128:(g + 1) * 128], AX.X, ALU.min,
                    negate=True),
                    reads=[B.pb[7]], writes=[B.col(VAR_MAXX, tt * 8 + gg)])
        if tt == 1 and FFN_T1_STOP == "tailmid":
            continue
        P.op("dve", lambda e, shc=shc, negm=negm: e.tensor_max(shc, shc, negm),
             reads=[shc, negm], writes=[shc])
        P.op("dve", lambda e, shc=shc: e.tensor_scalar_max(shc, shc, 1e-5),
             reads=[shc], writes=[shc])
        shs = B.col(VAR_MINX, tt * 8, 8)
        P.op("dve", lambda e, shs=shs, shc=shc: e.reciprocal(shs, shc),
             reads=[shc], writes=[shs])
        P.op("dve", lambda e, shs=shs: e.tensor_scalar_mul(shs, shs, 127.0),
             reads=[shs], writes=[shs])
        sd = negm
        if l1:
            P.op("dve", lambda e, sd=sd, shc=shc: e.tensor_scalar_mul(
                sd, shc, 1.0 / 127.0), reads=[shc], writes=[sd])
        else:
            iv8 = B.col(VAR_INVSX, tt * 8, 8)
            P.op("dve", lambda e, sd=sd, iv8=iv8, shc=shc: e.tensor_mul(sd, iv8, shc),
                 reads=[iv8, shc], writes=[sd])
            P.op("dve", lambda e, sd=sd: e.tensor_scalar_mul(
                sd, sd, B.MWC[:, wu_i:wu_i + 1]),
                 reads=[sd, B.MWC], writes=[sd])
            P.op("dve", lambda e, sd=sd: e.tensor_scalar(
                sd, sd, 1e-5, 1.0 / 127.0, ALU.max, ALU.mult),
                 reads=[sd], writes=[sd])
        P.op("dve", lambda e, sd=sd: e.tensor_scalar_mul(
            sd, sd, B.MWC[:, wd_i:wd_i + 1]),
             reads=[sd, B.MWC], writes=[sd])
        P.op("dve", lambda e, sd=sd: e.tensor_scalar_mul(sd, sd, 0.5),
             reads=[sd], writes=[sd])
        if tt == 1 and FFN_T1_STOP == "tailsd":
            continue
        bSh = [emit_bcast(P, B, B.col(VAR_MINX, 0, 16), 0, tt * 2),
               emit_bcast(P, B, B.col(VAR_MINX, 0, 16), 1, tt * 2 + 1)]
        bSd = [emit_bcast(P, B, B.col(VAR_MAXX, 0, 16), 2, tt * 2),
               emit_bcast(P, B, B.col(VAR_MAXX, 0, 16), 3, tt * 2 + 1)]
        if tt == 1 and FFN_T1_STOP == "tail":
            continue
        for m in range(32):
            for sub in range(2):
                hsl = B.HB[:, m * 1024 + sub * 512: m * 1024 + sub * 512 + 512]
                t = B.RT[:, sub * 512: sub * 512 + 512]
                P.op("dve", lambda e, hsl=hsl, t=t, sub=sub: e.tensor_mul(
                    t, hsl, bSh[sub]), reads=[hsl, bSh[sub]], writes=[t])
            hfull = B.HB[:, m * 1024: m * 1024 + 1024]
            tfull = B.RT[:, 0:1024]
            emit_round(P, B, tfull, hfull, 0, n=1024)
        if tt == 1 and FFN_T1_STOP == "round":
            continue
        for sub in range(2):
            qt = tt * 2 + sub
            for k in range(32):
                ws = k % 2
                wdb = B.WB[:, ws * 1024: ws * 1024 + 1024]
                P.dma("sp", wdb, d3[k, :, :], "wg%d" % ws)
                hq = B.HB[:, k * 1024 + sub * 512: k * 1024 + sub * 512 + 512]
                for m8 in range(8):
                    wt = wdb[:, m8 * 128:(m8 + 1) * 128]
                    P.op("pe", lambda e, wt=wt, hq=hq, k=k, m8=m8: e.matmul(
                        B.pb[m8], wt, hq, start=(k == 0), stop=(k == 31)),
                        reads=[wt, hq], writes=[B.pb[m8]])
            for m8 in range(8):
                ev = B.EV[:, (m8 % 2) * 512:(m8 % 2) * 512 + 512]
                P.op("dve", lambda e, m8=m8, ev=ev, sub=sub: e.tensor_mul(
                    ev, B.pb[m8], bSd[sub]),
                    reads=[B.pb[m8], bSd[sub]], writes=[ev])
                if l1:
                    b2h = B.BIA[:, 40 + m8:41 + m8]
                    P.op("pool", lambda e, ev=ev, b2h=b2h: e.tensor_scalar_add(
                        ev, ev, b2h), reads=[ev, B.BIA], writes=[ev])
                xsl = B.xs(m8, qt)
                P.op("dve", lambda e, xsl=xsl, ev=ev: e.tensor_add(xsl, xsl, ev),
                     reads=[ev, xsl], writes=[xsl])



def emit_attn(P, B, layer, qkvT, woT, dscr):
    emit_stats(P, B, ffn=False)
    l1 = layer == 1
    qoff = 8 if l1 else 0
    koff = 0 if l1 else 8
    voff = 16
    qkv = qkvT.ap().rearrange("(k p) m -> k p m", p=128)
    wo = woT.ap().rearrange("(k p) m -> k p m", p=128)
    VOFF = 16384
    vones = B.HB[:, VOFF:VOFF + 16640].rearrange("p (s c) -> p s c", c=65)[:, :, 64:65]
    P.op("dve", lambda e: e.memset(vones, 1.0), writes=[B.HB[:, VOFF:VOFF + 16640]])

    def build_u(qt):
        bM = emit_bcast(P, B, B.col(VAR_M, 0, 16), 0, qt)
        bR = emit_bcast(P, B, B.col(VAR_RSTD, 0, 16), 1, qt)
        for c in range(8):
            xsl = B.xs(c, qt)
            ro = (c % 2) * 512
            xm = B.EV[:, ro:ro + 512]
            P.op("dve", lambda e, xsl=xsl, xm=xm, bM=bM: e.tensor_sub(xm, xsl, bM),
                 reads=[xsl, bM], writes=[xm])
            u = B.XQ[:, c * 512:(c + 1) * 512]
            P.op("dve", lambda e, xm=xm, u=u, bR=bR: e.tensor_mul(u, xm, bR),
                 reads=[xm, bR], writes=[u])

    def proj_one(qt, m, bi):
        ws = bi % 2
        wr = B.WR[:, 0:1024] if ws == 0 else B.RT[:, 0:1024]
        P.dma("sp", wr.rearrange("p (k m) -> p k m", k=8),
              qkv[:, :, m * 128:(m + 1) * 128].rearrange("k p m -> p k m"), "wr")
        wb = B.WB[:, 2048 + ws * 1024: 2048 + (ws + 1) * 1024]
        P.op("dve", lambda e, wb=wb, wr=wr: e.tensor_copy(wb, wr),
             reads=[wr], writes=[wb])
        pb = B.pb[ws]
        for k in range(8):
            wt = wb[:, k * 128:(k + 1) * 128]
            u = B.XQ[:, k * 512:(k + 1) * 512]
            P.op("pe", lambda e, wt=wt, u=u, k=k, pb=pb: e.matmul(
                pb, wt, u, start=(k == 0), stop=(k == 7)),
                reads=[wt, u], writes=[pb])
        return pb

    def rope_evac(pb, toff, dst):
        ea = B.BC[:, 1536:2048]
        ebb = B.EV[:, 0:512]
        P.op("dve", lambda e: e.tensor_copy(ea, pb), reads=[pb], writes=[ea])
        P.op("dve", lambda e: e.stream_shuffle(ebb, ea, _SHUF_PAIR),
             reads=[ea], writes=[ebb])
        P.op("dve", lambda e: e.tensor_mul(ea, ea, B.RC[:, toff:toff + 512]),
             reads=[ea, B.RC], writes=[ea])
        P.op("pool", lambda e: e.tensor_mul(ebb, ebb, B.RS[:, toff:toff + 512]),
             reads=[ebb, B.RS], writes=[ebb])
        P.op("dve", lambda e: e.tensor_add(dst, ea, ebb),
             reads=[ea, ebb], writes=[dst])

    for qt in range(4):
        build_u(qt)
        for j in range(8):
            pb = proj_one(qt, koff + j, j)
            dst = B.HB[:, j * 2048 + qt * 512: j * 2048 + qt * 512 + 512]
            if l1:
                rope_evac(pb, qt * 512, dst)
            else:
                P.op("dve", lambda e, dst=dst, pb=pb: e.tensor_copy(dst, pb),
                     reads=[pb], writes=[dst])
        for j in range(8):
            pb = proj_one(qt, voff + j, j)
            ev = B.EV[:, 0:512]
            P.op("dve", lambda e, ev=ev, pb=pb: e.tensor_copy(ev, pb),
                 reads=[pb], writes=[ev])
            for g in range(4):
                kc = qt * 4 + g
                pt = B.pb[6 + (g % 2)]
                P.op("pe", lambda e, g=g, pt=pt: e.transpose(
                    pt[:, 0:128], ev[:, g * 128:(g + 1) * 128], B.IDF),
                    reads=[ev, B.IDF], writes=[pt])
                for hh in range(2):
                    h = 2 * j + hh
                    slot = VOFF + (kc * 16 + h) * 65
                    dst = B.HB[:, slot:slot + 64]
                    P.op("dve", lambda e, dst=dst, pt=pt, hh=hh: e.tensor_copy(
                        dst, pt[:, hh * 64:hh * 64 + 64]),
                        reads=[pt], writes=[dst])
    for qt in range(4):
        build_u(qt)
        for j in range(8):
            pb = proj_one(qt, qoff + j, j)
            dst = B.QT[:, j * 512:(j + 1) * 512]
            if l1:
                rope_evac(pb, qt * 512, dst)
            else:
                P.op("dve", lambda e, dst=dst, pb=pb: e.tensor_copy(dst, pb),
                     reads=[pb], writes=[dst])
        nkc = qt * 4 + 4
        for h in range(NH):
            qslice = B.QT[64 * (h % 2):64 * (h % 2) + 64,
                          (h // 2) * 512:(h // 2) * 512 + 512]
            pav = B.pb[5]
            for kc in range(nkc):
                ps = B.pb[2 + (kc % 2)]
                ksl = B.HB[64 * (h % 2):64 * (h % 2) + 64,
                           (h // 2) * 2048 + kc * 128: (h // 2) * 2048 + kc * 128 + 128]
                P.op("pe", lambda e, ksl=ksl, qslice=qslice, ps=ps: e.matmul(
                    ps, ksl, qslice, start=True, stop=True),
                    reads=[ksl, qslice], writes=[ps])
                p0 = B.CC[:, (kc % 2) * 512:(kc % 2) * 512 + 512]
                P.op("act", lambda e, ps=ps, p0=p0: e.activation(
                    p0, ps, AF.Exp, scale=B.KCONST[:, 1:2]),
                    reads=[ps, B.KCONST], writes=[p0])
                if (not l1) or (kc >= qt * 4):
                    moff = qt * 512 - kc * 128 + 384
                    msl = B.MASK[:, moff:moff + 512]
                    P.op("dve", lambda e, p0=p0, msl=msl: e.tensor_mul(p0, p0, msl),
                         reads=[p0, msl], writes=[p0])
                vsl = B.HB[:, VOFF + (kc * 16 + h) * 65: VOFF + (kc * 16 + h) * 65 + 65]
                P.op("pe", lambda e, vsl=vsl, p0=p0, kc=kc, nkc=nkc, pav=pav: e.matmul(
                    pav[0:65, :], vsl, p0, start=(kc == 0), stop=(kc == nkc - 1)),
                    reads=[vsl, p0], writes=[pav])
            drow = B.EV[64:65, 512:1024]
            P.op("dve", lambda e, pav=pav: e.reciprocal(drow, pav[64:65, 0:512]),
                 reads=[pav], writes=[drow])
            dsl = dscr.ap()[h % 4: h % 4 + 1, :]
            P.dma("act", dsl, drow, "dsc%d" % (h % 4))
            db = B.BC[0:64, 1536:2048]
            P.dma("sp", db, dsl.broadcast_to([64, 512]), "dbc")
            if h % 2 == 0:
                ao = B.XQ[0:64, (h // 2) * 512:(h // 2) * 512 + 512]
                P.op("dve", lambda e, ao=ao, db=db, pav=pav: e.tensor_mul(
                    ao, pav[0:64, :], db), reads=[pav, db], writes=[ao])
            else:
                stg = B.CC[0:64, 1024:1536]
                P.op("dve", lambda e, stg=stg, db=db, pav=pav: e.tensor_mul(
                    stg, pav[0:64, :], db), reads=[pav, db], writes=[stg])
                ao = B.XQ[64:128, (h // 2) * 512:(h // 2) * 512 + 512]
                P.dma("pool", ao, stg, "aomv")
        for mo in range(8):
            ws = mo % 2
            wr = B.WR[:, 0:1024] if ws == 0 else B.RT[:, 0:1024]
            P.dma("sp", wr.rearrange("p (k m) -> p k m", k=8),
                  wo[:, :, mo * 128:(mo + 1) * 128].rearrange("k p m -> p k m"), "wr")
            wb = B.WB[:, 2048 + ws * 1024: 2048 + (ws + 1) * 1024]
            P.op("dve", lambda e, wb=wb, wr=wr: e.tensor_copy(wb, wr),
                 reads=[wr], writes=[wb])
            po = B.pb[6 + ws]
            for k in range(8):
                wt = wb[:, k * 128:(k + 1) * 128]
                ao = B.XQ[:, k * 512:(k + 1) * 512]
                P.op("pe", lambda e, wt=wt, ao=ao, k=k, po=po: e.matmul(
                    po, wt, ao, start=(k == 0), stop=(k == 7)),
                    reads=[wt, ao], writes=[po])
            xsl = B.xs(mo, qt)
            P.op("dve", lambda e, xsl=xsl, po=po: e.tensor_add(xsl, xsl, po),
                 reads=[po, xsl], writes=[xsl])



def emit_input_tm(P, B, x_tm):
    xv = x_tm.ap().rearrange("(n p) d -> n p d", p=128)
    for nt in range(16):
        io = B.BC[:, 0:1024]
        P.dma("sp", io, xv[nt], "xin")
        for c in range(8):
            pt = B.pb[c % 2]
            P.op("pe", lambda e, c=c, io=io, pt=pt: e.transpose(
                pt[:, 0:128], io[:, c * 128:(c + 1) * 128], B.IDF),
                reads=[io, B.IDF], writes=[pt])
            dst = B.X[:, c * 2048 + nt * 128: c * 2048 + (nt + 1) * 128]
            P.op("dve", lambda e, dst=dst, pt=pt: e.tensor_copy(dst, pt[:, 0:128]),
                 reads=[pt], writes=[dst])


def emit_output_tm(P, B, y_tm):
    yv = y_tm.ap().rearrange("(n p) d -> n p d", p=128)
    for nt in range(16):
        io = B.BC[:, 1024:2048]
        for c in range(8):
            src = B.X[:, c * 2048 + nt * 128: c * 2048 + (nt + 1) * 128]
            pt = B.pb[2 + c % 2]
            P.op("pe", lambda e, src=src, pt=pt: e.transpose(
                pt[:, 0:128], src, B.IDF),
                reads=[src, B.IDF], writes=[pt])
            P.op("dve", lambda e, c=c, io=io, pt=pt: e.tensor_copy(
                io[:, c * 128:(c + 1) * 128], pt[:, 0:128]),
                reads=[pt], writes=[io[:, c * 128:(c + 1) * 128]])
        P.dma("pool", yv[nt], io, "yo")


def emit_input_fm(P, B, x_fm):
    xv = x_fm.ap().rearrange("(c p) n -> c p n", p=128)
    for c in range(8):
        P.dma("sp", B.X[:, c * 2048:(c + 1) * 2048], xv[c], "xin%d" % c)


def emit_output_fm(P, B, y_fm):
    yv = y_fm.ap().rearrange("(c p) n -> c p n", p=128)
    for c in range(8):
        P.dma("pool", yv[c], B.X[:, c * 2048:(c + 1) * 2048], "yo")



W_G0, W_U0, W_D0, W_1, W_2 = 0, 1, 2, 3, 4
TKEYS = ["wg3", "wu3", "wd3", "w13", "w23"]
TSHAPES = {"wg3": (1024, 4096), "wu3": (1024, 4096), "wd3": (4096, 1024),
           "w13": (1024, 4096), "w23": (4096, 1024)}
WIDX = {"wg3": W_G0, "wu3": W_U0, "wd3": W_D0, "w13": W_1, "w23": W_2}


def build_prep():
    nc = bass.Bass("TRN2", target_bir_lowering=False, debug=False)
    dins = {"identf": nc.dram_tensor("identf", [128, 128], F32, kind="ExternalInput")}
    wts = {}
    for k in TKEYS:
        K, M = TSHAPES[k]
        wts[k] = (nc.dram_tensor(k + "_in", [K, M], F32, kind="ExternalInput"),
                  nc.dram_tensor(k, [K, M], BF16, kind="ExternalOutput"))
    scales = nc.dram_tensor("scales", [1, 32], F32, kind="ExternalOutput")
    P = Prog(nc)
    B = Bufs(nc)
    emit_setup(P, B, dins)
    for k in TKEYS:
        K, M = TSHAPES[k]
        emit_weight_prep(P, B, wts[k][0], wts[k][1], WIDX[k], K, M, scales)
    P.emit()
    nc.finalize()
    return nc


def build_main(stages, x_in, x_out):
    nc = bass.Bass("TRN2", target_bir_lowering=False, debug=False)
    dins = {"identf": nc.dram_tensor("identf", [128, 128], F32, kind="ExternalInput")}
    need = set()
    for st in stages:
        if st[0] == "ffn":
            need.add("scales")
            need |= ({"wg3", "wu3", "wd3"} if st[1] == 0 else {"w13", "w23", "bias"})
        else:
            need |= {"attn%d" % st[1]}
            if st[1] == 1:
                need.add("rope")
    tern = {}
    for k in TKEYS:
        if k in need:
            K, M = TSHAPES[k]
            tern[k] = nc.dram_tensor(k, [K, M], BF16, kind="ExternalInput")
    scales = (nc.dram_tensor("scales", [1, 32], F32, kind="ExternalInput")
              if "scales" in need else None)
    if "bias" in need:
        dins["b1c"] = nc.dram_tensor("b1c", [128, 32], F32, kind="ExternalInput")
        dins["b2c"] = nc.dram_tensor("b2c", [128, 8], F32, kind="ExternalInput")
    for L in (0, 1):
        if ("attn%d" % L) in need:
            w = _HC["mask%d" % L].shape[1]
            dins["mask%d" % L] = nc.dram_tensor("mask%d" % L, [128, w], BF16,
                                                kind="ExternalInput")
            dins["qkvT%d" % L] = nc.dram_tensor("qkvT%d" % L, [1024, 3072], F32,
                                                kind="ExternalInput")
            dins["woT%d" % L] = nc.dram_tensor("woT%d" % L, [1024, 1024], F32,
                                               kind="ExternalInput")
    if "rope" in need:
        dins["ropeC"] = nc.dram_tensor("ropeC", [128, 2048], BF16, kind="ExternalInput")
        dins["ropeS"] = nc.dram_tensor("ropeS", [128, 2048], BF16, kind="ExternalInput")
    x_d = nc.dram_tensor("x", [2048, 1024] if x_in == "tm" else [1024, 2048],
                         F32, kind="ExternalInput")
    y_d = nc.dram_tensor("y", [2048, 1024] if x_out == "tm" else [1024, 2048],
                         F32, kind="ExternalOutput")
    dscr = nc.dram_tensor("dscr", [4, 512], F32, kind="Internal")

    P = Prog(nc)
    B = Bufs(nc)
    emit_setup(P, B, dins)
    if scales is not None:
        emit_load_scales(P, B, scales, list(range(5)))
    if "bias" in need:
        emit_load_bias(P, B, dins)
    if "rope" in need:
        P.dma("sp", B.RC, dins["ropeC"].ap(), "ropec")
        P.dma("sp", B.RS, dins["ropeS"].ap(), "ropes")
    if x_in == "tm":
        emit_input_tm(P, B, x_d)
    else:
        emit_input_fm(P, B, x_d)
    for st in stages:
        if st[0] == "ffn":
            if st[1] == 0:
                emit_ffn(P, B, tern["wg3"], tern["wu3"], tern["wd3"],
                         W_G0, W_U0, W_D0, l1=False)
            else:
                emit_ffn(P, B, tern["w13"], None, tern["w23"],
                         W_1, None, W_2, l1=True)
        else:
            L = st[1]
            m = _HC["mask%d" % L]
            P.dma("sp", B.MASK[:, 0:m.shape[1]], dins["mask%d" % L].ap(), "mask")
            emit_attn(P, B, L, dins["qkvT%d" % L], dins["woT%d" % L], dscr)
    if x_out == "tm":
        emit_output_tm(P, B, y_d)
    else:
        emit_output_fm(P, B, y_d)
    P.emit()
    nc.finalize()
    return nc


def build_fused():
    nc = bass.Bass("TRN2", target_bir_lowering=False, debug=False)
    dins = {"identf": nc.dram_tensor("identf", [128, 128], F32, kind="ExternalInput")}
    win = {}
    tern = {}
    for k in TKEYS:
        K, M = TSHAPES[k]
        win[k] = nc.dram_tensor(k + "_in", [K, M], F32, kind="ExternalInput")
        tern[k] = nc.dram_tensor(k, [K, M], BF16, kind="Internal")
    scales = nc.dram_tensor("scales", [1, 32], F32, kind="Internal")
    dins["b1c"] = nc.dram_tensor("b1c", [128, 32], F32, kind="ExternalInput")
    dins["b2c"] = nc.dram_tensor("b2c", [128, 8], F32, kind="ExternalInput")
    for L in (0, 1):
        w = _HC["mask%d" % L].shape[1]
        dins["mask%d" % L] = nc.dram_tensor("mask%d" % L, [128, w], BF16,
                                            kind="ExternalInput")
        dins["qkvT%d" % L] = nc.dram_tensor("qkvT%d" % L, [1024, 3072], F32,
                                            kind="ExternalInput")
        dins["woT%d" % L] = nc.dram_tensor("woT%d" % L, [1024, 1024], F32,
                                           kind="ExternalInput")
    dins["ropeC"] = nc.dram_tensor("ropeC", [128, 2048], BF16, kind="ExternalInput")
    dins["ropeS"] = nc.dram_tensor("ropeS", [128, 2048], BF16, kind="ExternalInput")
    x_d = nc.dram_tensor("x", [2048, 1024], F32, kind="ExternalInput")
    y_d = nc.dram_tensor("y", [2048, 1024], F32, kind="ExternalOutput")
    dscr = nc.dram_tensor("dscr", [4, 512], F32, kind="Internal")

    P = Prog(nc)
    B = Bufs(nc)
    emit_setup(P, B, dins)
    emit_load_bias(P, B, dins)
    emit_input_tm(P, B, x_d)
    for k in TKEYS:
        K, M = TSHAPES[k]
        emit_weight_prep(P, B, win[k], tern[k], WIDX[k], K, M, scales)
        P.seal_tensor(k, "tst%d" % WIDX[k])
    for st in [("ffn", 0), ("attn", 0), ("ffn", 0), ("ffn", 1), ("attn", 1),
               ("ffn", 1)]:
        if st[0] == "ffn":
            if st[1] == 0:
                emit_ffn(P, B, tern["wg3"], tern["wu3"], tern["wd3"],
                         W_G0, W_U0, W_D0, l1=False)
            else:
                emit_ffn(P, B, tern["w13"], None, tern["w23"],
                         W_1, None, W_2, l1=True)
        else:
            L = st[1]
            m = _HC["mask%d" % L]
            P.dma("sp", B.MASK[:, 0:m.shape[1]], dins["mask%d" % L].ap(), "mask")
            if L == 1:
                P.dma("sp", B.RC, dins["ropeC"].ap(), "ropec")
                P.dma("sp", B.RS, dins["ropeS"].ap(), "ropes")
            emit_attn(P, B, L, dins["qkvT%d" % L], dins["woT%d" % L], dscr)
    emit_output_tm(P, B, y_d)
    P.emit()
    nc.finalize()
    return nc



_NC_CACHE = {}


def _get(key, fn):
    if key not in _NC_CACHE:
        _NC_CACHE[key] = fn()
    return _NC_CACHE[key]


def kernel(L0_attn_x, L0_attn_qkv, L0_attn_wo, L0_ffn_w_gate, L0_ffn_w_up,
           L0_ffn_w_down, L1_attn_W_KQV, L1_attn_W_O, L1_ffn_W1, L1_ffn_b1,
           L1_ffn_W2, L1_ffn_b2):
    x = np.ascontiguousarray(np.asarray(L0_attn_x, np.float32))
    tw = lambda w: np.ascontiguousarray(np.asarray(w, np.float32).T)
    wgT, wuT, wdT = tw(L0_ffn_w_gate), tw(L0_ffn_w_up), tw(L0_ffn_w_down)
    w1T, w2T = tw(L1_ffn_W1), tw(L1_ffn_W2)
    qkvT0, woT0 = tw(L0_attn_qkv), tw(L0_attn_wo)
    qkvT1, woT1 = tw(L1_attn_W_KQV), tw(L1_attn_W_O)
    b1c = np.ascontiguousarray(np.asarray(L1_ffn_b1, np.float32).reshape(32, 128).T)
    b2c = np.ascontiguousarray(np.asarray(L1_ffn_b2, np.float32).reshape(8, 128).T)
    consts = _HC

    if FUSED:
        ncf = _get("fused", build_fused)
        in_maps = []
        for i in range(8):
            mm = {"identf": consts["identf"], "b1c": b1c, "b2c": b2c,
                  "ropeC": consts["ropeC"], "ropeS": consts["ropeS"],
                  "mask0": np.ascontiguousarray(consts["mask0"]),
                  "mask1": np.ascontiguousarray(consts["mask1"]),
                  "qkvT0": qkvT0, "woT0": woT0, "qkvT1": qkvT1, "woT1": woT1,
                  "x": np.ascontiguousarray(x[i])}
            mm.update({k + "_in": w for k, w in
                       zip(TKEYS, [wgT, wuT, wdT, w1T, w2T])})
            in_maps.append(mm)
        res = run_bass_kernel_spmd(ncf, in_maps, core_ids=list(range(8)))
        return np.stack([res.results[i]["y"] for i in range(8)], axis=0).astype(np.float32)

    ncp = _get("prep", build_prep)
    inp = {"identf": consts["identf"]}
    inp.update({k + "_in": w for k, w in zip(TKEYS, [wgT, wuT, wdT, w1T, w2T])})
    resp = run_bass_kernel_spmd(ncp, [inp] * 8, core_ids=list(range(8)))
    tern = {k: resp.results[0][k] for k in TKEYS}
    scales = resp.results[0]["scales"]

    common = {"identf": consts["identf"]}
    stage_cfgs = [
        ([("ffn", 0)], "tm", "fm"),
        ([("attn", 0)], "fm", "fm"),
        ([("ffn", 0)], "fm", "fm"),
        ([("ffn", 1)], "fm", "fm"),
        ([("attn", 1)], "fm", "fm"),
        ([("ffn", 1)], "fm", "tm"),
    ]
    cur = [np.ascontiguousarray(x[i]) for i in range(8)]
    for si, (stages, xin, xout) in enumerate(stage_cfgs):
        ncm = _get(("main", si), lambda stages=stages, xin=xin, xout=xout:
                   build_main(stages, xin, xout))
        in_maps = []
        for i in range(8):
            m = dict(common)
            st = stages[0]
            if st[0] == "ffn":
                m["scales"] = scales
                if st[1] == 0:
                    m.update({k: tern[k] for k in ("wg3", "wu3", "wd3")})
                else:
                    m.update({k: tern[k] for k in ("w13", "w23")})
                    m.update({"b1c": b1c, "b2c": b2c})
            else:
                L = st[1]
                m["mask%d" % L] = np.ascontiguousarray(consts["mask%d" % L])
                m["qkvT%d" % L] = qkvT0 if L == 0 else qkvT1
                m["woT%d" % L] = woT0 if L == 0 else woT1
                if L == 1:
                    m["ropeC"] = consts["ropeC"]
                    m["ropeS"] = consts["ropeS"]
            m["x"] = np.ascontiguousarray(cur[i])
            in_maps.append(m)
        res = run_bass_kernel_spmd(ncm, in_maps, core_ids=list(range(8)))
        cur = [res.results[i]["y"] for i in range(8)]
    return np.stack(cur, axis=0).astype(np.float32)


# version 07
# speedup vs baseline: 1.0407x; 1.0059x over previous
import numpy as np
import ml_dtypes
import concourse.bass as bass
import concourse.mybir as mybir
from concourse.bass_utils import run_bass_kernel_spmd

F32 = mybir.dt.float32
BF16 = mybir.dt.bfloat16
I32 = mybir.dt.int32
AF = mybir.ActivationFunctionType
ALU = mybir.AluOpType
AX = mybir.AxisListType
BFNP = ml_dtypes.bfloat16

S, DM, NH, DH, INNER = 2048, 1024, 16, 64, 4096
FFN_TTS = range(2)
FFN_T1_STOP = None
FUSED = True


def host_constants():
    c = {}
    c["identf"] = np.eye(128, dtype=np.float32)
    jj = np.arange(128)[:, None]
    uu = np.arange(2432)[None, :] - 384
    d = uu - jj
    allowed = ((d >= 0) & (d <= 127)) | ((d >= 132) & ((d - 128) % 4 == 0))
    c["mask0"] = allowed.astype(BFNP)
    uu1 = np.arange(896)[None, :] - 384
    c["mask1"] = ((uu1 - jj) >= 0).astype(BFNP)
    r = np.arange(0, DH, 2, dtype=np.float64)
    invf = 1.0 / (10000.0 ** (r / DH))
    fr = np.arange(S, dtype=np.float64)[:, None] * invf[None, :]
    cos = np.cos(fr); sin = np.sin(fr)
    C = np.repeat(cos, 2, axis=1)
    S2 = np.empty((S, DH)); S2[:, 0::2] = -sin; S2[:, 1::2] = sin
    c["ropeC"] = np.tile(C.T, (2, 1)).astype(BFNP)
    c["ropeS"] = np.tile(S2.T, (2, 1)).astype(BFNP)
    return c

_HC = host_constants()
_SHUF_PAIR = []
for _i in range(0, 32, 2):
    _SHUF_PAIR.extend([_i + 1, _i])


QSEM = {"sp": "qsp", "act": "qact", "pool": "qpool"}


def _apkeys(x):
    if not hasattr(x, "tensor"):
        return (x,)
    t = x.tensor
    nm = t.name
    dims = list(x.ap)
    try:
        import bass_rust
        is_dram = x.space == getattr(bass_rust, "MemorySpace").DRAM
    except Exception:
        is_dram = str(getattr(x, "space", "")).find("DRAM") >= 0
    lo = int(x.offset)
    if not is_dram:
        fw = 1
        for d in list(t.shape)[1:]:
            fw *= int(d)
        lo = lo % fw
    hi = lo
    use = dims if is_dram else dims[1:] if len(dims) > 1 else []
    for st, ct in use:
        hi += (int(ct) - 1) * int(st)
    hi += 1
    gran = 65536 if is_dram else 512
    return tuple((nm, i) for i in range(lo // gran, (hi + gran - 1) // gran))


class Prog:
    def __init__(self, nc):
        self.nc = nc
        self.sem = {}
        for e in ("pe", "dve", "act", "pool"):
            self.sem[e] = nc.alloc_semaphore("S" + e)
        self.cnt = {e: 0 for e in self.sem}
        self.q = {e: [] for e in ("pe", "dve", "act", "pool", "sp")}
        self.hwm = {e: {} for e in self.q}
        self.res = {}

    def _stream(self, name):
        key = ("dma", name)
        if key not in self.sem:
            self.sem[key] = self.nc.alloc_semaphore("Sd_" + name)
            self.cnt[key] = 0
        return key

    def _expand(self, items):
        out = []
        for it in items:
            out.extend(_apkeys(it))
        return out

    def _need(self, eng, own_sem, reads, writes):
        need = {}

        def add(s, v):
            if s == own_sem and v > self.cnt.get(own_sem, 0):
                return
            need[s] = max(need.get(s, 0), v)

        for k in reads:
            r = self.res.get(k)
            if r and r["w"] is not None:
                add(*r["w"])
        for k in writes:
            r = self.res.get(k)
            if r:
                if r["w"] is not None:
                    add(*r["w"])
                for s, v in r["r"].items():
                    add(s, v)
        hw = self.hwm[eng]
        waits = []
        for s, v in sorted(need.items(), key=lambda kv: str(kv[0])):
            if hw.get(s, 0) < v:
                waits.append((s, v))
                hw[s] = v
        return waits

    def _record(self, own_sem, tick, reads, writes):
        for k in writes:
            self.res[k] = {"w": (own_sem, tick), "r": {}}
        for k in reads:
            r = self.res.get(k)
            if r is None:
                r = {"w": None, "r": {}}
                self.res[k] = r
            r["r"][own_sem] = max(r["r"].get(own_sem, 0), tick)

    def op(self, eng, fn, reads=(), writes=(), inc=True):
        import sys as _s
        f = _s._getframe(1)
        self._dbg = "%s:%d" % (f.f_code.co_name, f.f_lineno)
        own = eng
        reads = self._expand(reads)
        writes = self._expand(writes)
        waits = self._need(eng, own, reads, writes)
        if inc:
            self.cnt[eng] += 1
            tick = self.cnt[eng]
        else:
            tick = self.cnt[eng] + 1
        self.q[eng].append((waits, fn, own if inc else None, self._dbg))
        self._record(own, tick, reads, writes)

    def dma(self, qeng, dst, src, stream, reads=(), writes=(), **kw):
        own = self._stream(stream)
        reads = self._expand(tuple(reads) + (src,))
        writes = self._expand(tuple(writes) + (dst,))
        waits = self._need(qeng, own, reads, writes)
        self.cnt[own] += 1
        tick = 16 * self.cnt[own]
        import sys as _s
        f = _s._getframe(1)
        dbg = "%s:%d" % (f.f_code.co_name, f.f_lineno)
        self.q[qeng].append(
            (waits, lambda e, d=dst, s=src, k=dict(kw): e.dma_start(d, s, **k), own, dbg))
        self._record(own, tick, reads, writes)

    def seal_tensor(self, name, stream):
        key = ("dma", stream)
        tick = 16 * self.cnt[key]
        for k, r in self.res.items():
            if isinstance(k, tuple) and len(k) == 2 and k[0] == name:
                r["w"] = (key, tick)

    def check_deadlock(self):
        pos = {e: 0 for e in self.q}
        val = {s: 0 for s in self.cnt}
        progressed = True
        while progressed:
            progressed = False
            for e, lst in self.q.items():
                while pos[e] < len(lst):
                    waits, _fn, inc_sem = lst[pos[e]][0], lst[pos[e]][1], lst[pos[e]][2]
                    if all(val.get(s, 0) >= v for s, v in waits):
                        if inc_sem is not None:
                            val[inc_sem] = val.get(inc_sem, 0) + (
                                16 if isinstance(inc_sem, tuple) else 1)
                        pos[e] += 1
                        progressed = True
                    else:
                        break
        stuck = {e: pos[e] for e in self.q if pos[e] < len(self.q[e])}
        if stuck:
            msg = []
            for e, p in stuck.items():
                waits, _f, _i, dbg = self.q[e][p]
                bad = [(s, v, val.get(s, 0)) for s, v in waits if val.get(s, 0) < v]
                msg.append(f"{e}@{p} [{dbg}] needs {bad}")
            raise RuntimeError("build-time deadlock: " + " | ".join(msg))

    def emit(self):
        nc = self.nc
        sems = self.sem
        final = dict(self.cnt)

        def mk(lst, eng_name):
            def body(engine):
                for waits, fn, inc_sem, _dbg in lst:
                    for s, v in waits:
                        engine.wait_ge(sems[s], v)
                    ins = fn(engine)
                    if inc_sem is not None:
                        amt = 16 if isinstance(inc_sem, tuple) else 1
                        ins.then_inc(sems[inc_sem], amt)
                if eng_name == "pool":
                    for s, v in final.items():
                        tgt = v * 16 if isinstance(s, tuple) else v
                        if tgt > 0:
                            engine.wait_ge(sems[s], tgt)
            return body

        self.check_deadlock()
        with nc.Block() as blk:
            blk.sync(mk(self.q["sp"], "sp"))
            blk.tensor(mk(self.q["pe"], "pe"))
            blk.vector(mk(self.q["dve"], "dve"))
            blk.scalar(mk(self.q["act"], "act"))
            blk.gpsimd(mk(self.q["pool"], "pool"))



class Bufs:
    def __init__(self, nc):
        A = nc.alloc_sbuf_tensor
        self.X = A("X", [128, 16384], F32).ap()
        self.HB = A("HB", [128, 33024], BF16).ap()
        self.XQ = A("XQb", [128, 8192], BF16).ap()
        self.QT = A("QTb", [128, 4096], BF16).ap()
        self.WB = A("WBb", [128, 4096], BF16).ap()
        self.WR = A("WRb", [128, 1024], F32).ap()
        self.BC = A("BCb", [128, 2048], F32).ap()
        self.EV = A("EVb", [128, 1280], F32).ap()
        self.ACC = A("ACCb", [128, 1024], BF16).ap()
        self.RT = A("RTb", [128, 2048], F32).ap()
        self.RI = A("RIb", [128, 1024], I32).ap()
        self.CC = A("CCb", [128, 2048], BF16).ap()
        self.AUX = A("AUXb", [128, 4992], BF16).ap()
        self.MASK = self.AUX
        self.RC = self.AUX[:, 896:2944]
        self.RS = self.AUX[:, 2944:4992]
        self.COLS = A("COLSb", [128, 160], F32).ap()
        self.CB = A("CBb", [128, 128], F32).ap()
        self.IDF = A("IDFb", [128, 128], F32).ap()
        self.ONE1 = A("ONE1b", [128, 1], BF16).ap()
        self.KCONST = A("KCONSTb", [128, 2], F32).ap()
        self.MWC = A("MWCb", [128, 10], F32).ap()
        self.BIA = A("BIAb", [128, 48], F32).ap()
        self.SCR = A("SCRb", [1, 32], F32).ap()
        self.pb = [nc.alloc_psum_tensor(f"pb{i}", [128, 512], F32).ap() for i in range(8)]

    def col(self, v, idx, n=1):
        return self.COLS[:, v * 16 + idx: v * 16 + idx + n]

    def xs(self, c, qt):
        return self.X[:, c * 2048 + qt * 512: c * 2048 + qt * 512 + 512]


(VAR_M, VAR_RSTD, VAR_S1, VAR_S2, VAR_MAXX, VAR_MINX,
 VAR_ACOL, VAR_INVSX, VAR_SH, VAR_TMP) = range(10)
KC_EPS, KC_EIGHTH = 0, 1


def emit_setup(P, B, dins):
    P.dma("sp", B.IDF, dins["identf"].ap(), "ldid")
    P.op("dve", lambda e: e.memset(B.CB, 0.0), writes=[B.CB])
    P.op("dve", lambda e: e.memset(B.ONE1, 1.0), writes=[B.ONE1])
    P.op("dve", lambda e: e.memset(B.KCONST[:, 0:1], 1e-5), writes=[B.KCONST])
    P.op("dve", lambda e: e.memset(B.KCONST[:, 1:2], 0.125), writes=[B.KCONST])
    P.op("dve", lambda e: e.memset(B.COLS, 0.0), writes=[B.COLS])



def emit_weight_prep(P, B, wT, scratch, widx, K, M, scr_dram, tiled=False):
    if tiled:
        nkc, nmb = M // 128, K // 1024
        wsrc = wT.ap()
        wdst = scratch.ap()
    else:
        nkc, nmb = K // 128, M // 1024
        wsrc = wT.ap().rearrange("(k p) m -> k p m", p=128)
        wdst = scratch.ap().rearrange("(k p) m -> k p m", p=128)
    acc = B.col(VAR_TMP, 0)
    tmpc = B.col(VAR_TMP, 1)
    P.op("dve", lambda e: e.memset(acc, 0.0), writes=[acc])
    bi = 0
    for kc in range(nkc):
        for mb in range(nmb):
            wr = B.WR[:, 0:1024] if bi % 2 == 0 else B.RT[:, 1024:2048]
            tm2 = B.col(VAR_TMP, 1 + bi % 2)
            P.dma("sp", wr, wsrc[kc, :, mb * 1024:(mb + 1) * 1024],
                  "wr%d" % (bi % 2))
            P.op("dve", lambda e, wr=wr, tm2=tm2: e.reduce_sum(
                tm2, wr, AX.X, apply_absolute_value=True),
                reads=[wr], writes=[tm2])
            P.op("dve", lambda e, tm2=tm2: e.tensor_add(acc, acc, tm2),
                 reads=[tm2, acc], writes=[acc])
            bi += 1
    P.op("pe", lambda e: e.transpose(B.pb[7][0:1, 0:128], acc, B.IDF),
         reads=[acc, B.IDF], writes=[B.pb[7]])
    P.op("dve", lambda e: e.tensor_copy(B.EV[0:1, 0:128], B.pb[7][0:1, 0:128]),
         reads=[B.pb[7]], writes=[B.EV[0:1, 0:128]])
    P.op("dve", lambda e: e.reduce_sum(B.EV[0:1, 512:513], B.EV[0:1, 0:128], AX.X),
         reads=[B.EV[0:1, 0:128]], writes=[B.EV[0:1, 512:513]])
    P.op("dve", lambda e: e.tensor_scalar(
        B.SCR[0:1, widx:widx + 1], B.EV[0:1, 512:513], 1.0 / (K * M), 1e-5,
        ALU.mult, ALU.max),
        reads=[B.EV[0:1, 512:513]], writes=[B.SCR[0:1, widx:widx + 1]])
    P.op("dve", lambda e: e.reciprocal(B.SCR[0:1, 16 + widx:17 + widx],
                                       B.SCR[0:1, widx:widx + 1]),
         reads=[B.SCR[0:1, widx:widx + 1]], writes=[B.SCR[0:1, 16 + widx:17 + widx]])
    P.dma("pool", scr_dram.ap()[0:1, widx:widx + 1], B.SCR[0:1, widx:widx + 1],
          "scs%d" % widx)
    P.dma("pool", scr_dram.ap()[0:1, 16 + widx:17 + widx],
          B.SCR[0:1, 16 + widx:17 + widx], "scs%d" % widx)
    emit_load_scales(P, B, scr_dram, [widx])
    bi = 0
    for kc in range(nkc):
        for mb in range(nmb):
            sl = bi % 2
            wr = B.WR[:, 0:1024] if sl == 0 else B.RT[:, 1024:2048]
            P.dma("sp", wr, wsrc[kc, :, mb * 1024:(mb + 1) * 1024], "wr%d" % sl)
            vv = B.EV[:, 0:1024] if sl == 0 else B.BC[:, 0:1024]
            P.op("dve", lambda e, wr=wr, vv=vv: e.tensor_scalar_mul(
                vv, wr, B.MWC[:, 5 + widx:6 + widx]),
                reads=[wr, B.MWC], writes=[vv])
            c1 = B.CC[:, sl * 1024:(sl + 1) * 1024]
            P.op("dve", lambda e, vv=vv, c1=c1: e.tensor_single_scalar(
                c1, vv, 0.5, ALU.is_gt), reads=[vv], writes=[c1])
            c2 = B.QT[:, sl * 1024:(sl + 1) * 1024]
            P.op("pool", lambda e, vv=vv, c2=c2: e.tensor_single_scalar(
                c2, vv, -0.5, ALU.is_lt), reads=[vv], writes=[c2])
            wb = B.WB[:, sl * 1024:(sl + 1) * 1024]
            P.op("dve", lambda e, c1=c1, c2=c2, wb=wb: e.tensor_sub(wb, c1, c2),
                 reads=[c1, c2], writes=[wb])
            P.dma("pool", wdst[kc, :, mb * 1024:(mb + 1) * 1024], wb, "tst%d" % widx)
            bi += 1


def emit_load_scales(P, B, scr_dram, widxs):
    for w in widxs:
        P.dma("sp", B.MWC[:, w:w + 1],
              scr_dram.ap()[0:1, w:w + 1].broadcast_to([128, 1]), "mwc")
        P.dma("sp", B.MWC[:, 5 + w:6 + w],
              scr_dram.ap()[0:1, 16 + w:17 + w].broadcast_to([128, 1]), "mwc")


def emit_load_bias(P, B, dins):
    P.dma("sp", B.BIA[:, 0:32], dins["b1c"].ap(), "bias")
    P.dma("sp", B.BIA[:, 32:40], dins["b2c"].ap(), "bias")
    P.op("dve", lambda e: e.tensor_scalar_mul(B.BIA[:, 40:48], B.BIA[:, 32:40], 0.5),
         reads=[B.BIA[:, 32:40]], writes=[B.BIA[:, 40:48]])



def emit_stats(P, B, ffn):
    for qt in range(4):
        for c in range(8):
            xsl = B.xs(c, qt)
            slot = c % 2
            xb = B.QT[:, slot * 512:(slot + 1) * 512]
            sq = B.QT[:, 1024 + slot * 512: 1024 + (slot + 1) * 512]
            P.op("dve", lambda e, xb=xb, xsl=xsl: e.tensor_copy(xb, xsl),
                 reads=[xsl], writes=[xb])
            P.op("act", lambda e, sq=sq, xsl=xsl: e.activation(sq, xsl, AF.Square),
                 reads=[xsl], writes=[sq])
            P.op("pe", lambda e, xb=xb, c=c: e.matmul(
                B.pb[5][0:1, :], B.ONE1, xb, start=(c == 0), stop=(c == 7)),
                reads=[xb, B.ONE1], writes=[B.pb[5]])
            P.op("pe", lambda e, sq=sq, c=c: e.matmul(
                B.pb[6][0:1, :], B.ONE1, sq, start=(c == 0), stop=(c == 7)),
                reads=[sq, B.ONE1], writes=[B.pb[6]])
        r1 = B.EV[0:1, 0:512]
        r2 = B.EV[0:1, 512:1024]
        P.op("dve", lambda e: e.tensor_copy(r1, B.pb[5][0:1, :]),
             reads=[B.pb[5]], writes=[r1])
        P.op("dve", lambda e: e.tensor_copy(r2, B.pb[6][0:1, :]),
             reads=[B.pb[6]], writes=[r2])
        for g in range(4):
            P.op("pe", lambda e, g=g: e.transpose(
                B.pb[7][:, g:g + 1], B.EV[0:1, g * 128:(g + 1) * 128], B.IDF[0:1, 0:1]),
                reads=[r1, B.IDF], writes=[B.pb[7]], inc=False)
            P.op("pe", lambda e, g=g: e.transpose(
                B.pb[7][:, 4 + g:5 + g],
                B.EV[0:1, 512 + g * 128:512 + (g + 1) * 128], B.IDF[0:1, 0:1]),
                reads=[r2, B.IDF], writes=[B.pb[7]])
        P.op("dve", lambda e, qt=qt: e.tensor_copy(
            B.col(VAR_S1, qt * 4, 4), B.pb[7][:, 0:4]),
            reads=[B.pb[7]], writes=[B.col(VAR_S1, qt * 4, 4)])
        P.op("dve", lambda e, qt=qt: e.tensor_copy(
            B.col(VAR_S2, qt * 4, 4), B.pb[7][:, 4:8]),
            reads=[B.pb[7]], writes=[B.col(VAR_S2, qt * 4, 4)])
        if ffn:
            ea = B.EV[:, 0:512]
            eb = B.EV[:, 512:1024]
            xview = B.X.rearrange("p (c t) -> p t c", c=8)[:, qt * 512:(qt + 1) * 512, :]
            xkeys = [B.xs(c2_, qt) for c2_ in range(8)] + [r1, r2]
            P.op("dve", lambda e, xview=xview: e.reduce_max(ea, xview, AX.X),
                 reads=xkeys, writes=[ea])
            P.op("dve", lambda e, xview=xview: e.tensor_reduce(
                eb, xview, AX.X, ALU.min), reads=xkeys, writes=[eb])
            for g in range(4):
                P.op("pe", lambda e, g=g: e.transpose(
                    B.pb[7][:, g * 128:(g + 1) * 128], ea[:, g * 128:(g + 1) * 128],
                    B.IDF),
                    reads=[ea, B.IDF], writes=[B.pb[7]])
            for g in range(4):
                P.op("dve", lambda e, g=g, qt=qt: e.reduce_max(
                    B.col(VAR_MAXX, qt * 4 + g), B.pb[7][:, g * 128:(g + 1) * 128],
                    AX.X),
                    reads=[B.pb[7]], writes=[B.col(VAR_MAXX, qt * 4 + g)])
            for g in range(4):
                P.op("pe", lambda e, g=g: e.transpose(
                    B.pb[7][:, g * 128:(g + 1) * 128], eb[:, g * 128:(g + 1) * 128],
                    B.IDF),
                    reads=[eb, B.IDF], writes=[B.pb[7]])
            for g in range(4):
                P.op("dve", lambda e, g=g, qt=qt: e.tensor_reduce(
                    B.col(VAR_MINX, qt * 4 + g), B.pb[7][:, g * 128:(g + 1) * 128],
                    AX.X, ALU.min),
                    reads=[B.pb[7]], writes=[B.col(VAR_MINX, qt * 4 + g)])
    m16 = B.col(VAR_M, 0, 16)
    rst = B.col(VAR_RSTD, 0, 16)
    tmp = B.col(VAR_TMP, 0, 16)
    s1 = B.col(VAR_S1, 0, 16)
    s2 = B.col(VAR_S2, 0, 16)
    P.op("dve", lambda e: e.tensor_scalar_mul(m16, s1, 1.0 / 1024.0),
         reads=[s1], writes=[m16])
    P.op("dve", lambda e: e.tensor_mul(tmp, m16, m16), reads=[m16], writes=[tmp])
    P.op("dve", lambda e: e.scalar_tensor_tensor(
        rst, s2, 1.0 / 1024.0, tmp, ALU.mult, ALU.subtract),
        reads=[s2, tmp], writes=[rst])
    P.op("act", lambda e: e.activation(rst, rst, AF.Sqrt, bias=B.KCONST[:, 0:1]),
         reads=[rst, B.KCONST], writes=[rst])
    P.op("dve", lambda e: e.reciprocal(rst, rst), reads=[rst], writes=[rst])
    if ffn:
        mx = B.col(VAR_MAXX, 0, 16)
        mn = B.col(VAR_MINX, 0, 16)
        a16 = B.col(VAR_ACOL, 0, 16)
        iv = B.col(VAR_INVSX, 0, 16)
        P.op("dve", lambda e: e.tensor_sub(mx, mx, m16), reads=[mx, m16], writes=[mx])
        P.op("dve", lambda e: e.tensor_sub(mn, m16, mn), reads=[mn, m16], writes=[mn])
        P.op("dve", lambda e: e.tensor_max(mx, mx, mn), reads=[mx, mn], writes=[mx])
        P.op("dve", lambda e: e.reciprocal(a16, mx), reads=[mx], writes=[a16])
        P.op("dve", lambda e: e.tensor_scalar_mul(a16, a16, 127.0),
             reads=[a16], writes=[a16])
        P.op("dve", lambda e: e.tensor_mul(iv, rst, mx), reads=[rst, mx], writes=[iv])
        P.op("dve", lambda e: e.tensor_scalar(iv, iv, 1e-5, 1.0 / 127.0,
                                              ALU.max, ALU.mult),
             reads=[iv], writes=[iv])


def emit_bcast(P, B, colap, slot, qt, pbi=4):
    pb = B.pb[pbi]
    for g in range(4):
        cs = colap[:, qt * 4 + g: qt * 4 + g + 1]
        P.op("act", lambda e, cs=cs: e.activation(B.CB, B.CB, AF.Identity,
                                                  bias=cs, scale=0.0),
             reads=[cs, B.CB], writes=[B.CB])
        P.op("pe", lambda e, g=g, pb=pb: e.transpose(
            pb[:, g * 128:(g + 1) * 128], B.CB, B.IDF),
            reads=[B.CB, B.IDF], writes=[pb])
    bc = B.BC[:, slot * 512:(slot + 1) * 512]
    P.op("dve", lambda e, bc=bc, pb=pb: e.tensor_copy(bc, pb),
         reads=[pb], writes=[bc])
    return bc


def emit_round(P, B, src, dst, ro, n=512):
    yi = B.RI[:, ro:ro + n]
    yf = B.RT[:, 1024 + ro:1024 + ro + n]
    c1 = B.CC[:, ro:ro + n]
    c2 = B.CC[:, 1024 + ro:1024 + ro + n]
    P.op("dve", lambda e: e.tensor_scalar(yi, src, 1.0, None, ALU.mult),
         reads=[src], writes=[yi])
    P.op("dve", lambda e: e.tensor_copy(yf, yi), reads=[yi], writes=[yf])
    P.op("dve", lambda e: e.scalar_tensor_tensor(
        c1, src, 0.5, yf, ALU.subtract, ALU.is_gt),
        reads=[src, yf], writes=[c1])
    P.op("dve", lambda e: e.scalar_tensor_tensor(
        c2, src, 0.5, yf, ALU.add, ALU.is_lt),
        reads=[src, yf], writes=[c2])
    P.op("dve", lambda e: e.tensor_add(yf, yf, c1), reads=[yf, c1], writes=[yf])
    P.op("dve", lambda e: e.tensor_sub(dst, yf, c2), reads=[yf, c2], writes=[dst])



def emit_ffn(P, B, tern_g, tern_u, tern_d, wg_i, wu_i, wd_i, l1):
    emit_stats(P, B, ffn=True)
    g3 = tern_g.ap()
    u3 = tern_u.ap() if tern_u is not None else None
    d3 = tern_d.ap().rearrange("(k p) m -> k p m", p=128)
    cgc = B.col(VAR_TMP, 0, 16)
    P.op("dve", lambda e: e.tensor_scalar_mul(
        cgc, B.col(VAR_INVSX, 0, 16), B.MWC[:, wg_i:wg_i + 1]),
        reads=[B.col(VAR_INVSX, 0, 16), B.MWC], writes=[cgc])
    for tt in FFN_TTS:
        for sub in range(2):
            qt = tt * 2 + sub
            bM = emit_bcast(P, B, B.col(VAR_M, 0, 16), 0, qt)
            bA = emit_bcast(P, B, B.col(VAR_ACOL, 0, 16), 1, qt)
            for c in range(8):
                xsl = B.xs(c, qt)
                ro = (c % 2) * 512
                xm = B.EV[:, ro:ro + 512]
                t = B.RT[:, ro:ro + 512]
                P.op("dve", lambda e, xsl=xsl, xm=xm, bM=bM: e.tensor_sub(xm, xsl, bM),
                     reads=[xsl, bM], writes=[xm])
                P.op("dve", lambda e, xm=xm, t=t, bA=bA: e.tensor_mul(t, xm, bA),
                     reads=[xm, bA], writes=[t])
                xq = B.XQ[:, c * 1024 + sub * 512: c * 1024 + sub * 512 + 512]
                emit_round(P, B, t, xq, ro)
        if tt == 1 and FFN_T1_STOP == "xq":
            continue
        bCg = [emit_bcast(P, B, cgc, 2, tt * 2), emit_bcast(P, B, cgc, 3, tt * 2 + 1)]
        am0 = B.ACC[:, 0:1024]
        an0 = B.CC[:, 1024:2048]
        for m in range(32):
            ws = m % 2
            gblk = B.WB[:, ws * 1024: ws * 1024 + 1024]
            P.dma("sp", gblk, g3[m], "wg%d" % ws)
            if not l1:
                ublk = B.WB[:, 2048 + ws * 1024: 2048 + ws * 1024 + 1024]
                P.dma("sp", ublk, u3[m], "wu%d" % ws)
            if True:
                for sub in range(2):
                    bgi = (m % 2) * 4 + sub * 2
                    bui = bgi + 1
                    pg = B.pb[bgi]
                    for k in range(8):
                        wt = gblk[:, k * 128:(k + 1) * 128]
                        xq = B.XQ[:, k * 1024 + sub * 512: k * 1024 + sub * 512 + 512]
                        P.op("pe", lambda e, wt=wt, xq=xq, k=k, pg=pg: e.matmul(
                            pg, wt, xq, start=(k == 0), stop=(k == 7)),
                            reads=[wt, xq], writes=[pg])
                    if not l1:
                        pu = B.pb[bui]
                        for k in range(8):
                            wt = ublk[:, k * 128:(k + 1) * 128]
                            xq = B.XQ[:, k * 1024 + sub * 512: k * 1024 + sub * 512 + 512]
                            P.op("pe", lambda e, wt=wt, xq=xq, k=k, pu=pu: e.matmul(
                                pu, wt, xq, start=(k == 0), stop=(k == 7)),
                                reads=[wt, xq], writes=[pu])
                    gs = B.EV[:, sub * 512: sub * 512 + 512]
                    P.op("dve", lambda e, pg=pg, gs=gs, sub=sub: e.tensor_mul(
                        gs, pg, bCg[sub]), reads=[pg, bCg[sub]], writes=[gs])
                    hsl = B.HB[:, m * 1024 + sub * 512: m * 1024 + sub * 512 + 512]
                    if l1:
                        b1c = B.BIA[:, m:m + 1]
                        P.op("act", lambda e, gs=gs, hsl=hsl, b1c=b1c: e.activation(
                            hsl, gs, AF.Gelu, bias=b1c),
                            reads=[gs, B.BIA], writes=[hsl])
                    else:
                        P.op("act", lambda e, gs=gs: e.activation(gs, gs, AF.Silu),
                             reads=[gs], writes=[gs])
                        pu = B.pb[bui]
                        P.op("dve", lambda e, gs=gs, hsl=hsl, pu=pu: e.tensor_mul(
                            hsl, gs, pu), reads=[gs, pu], writes=[hsl])

        if tt == 1 and FFN_T1_STOP == "gu":
            continue
        hview = B.HB[:, 0:32768].rearrange("p (m t) -> p t m", m=32)
        hkeys = [B.HB[:, 0:32768]]
        P.op("dve", lambda e, hview=hview: e.reduce_max(am0, hview, AX.X),
             reads=hkeys, writes=[am0])
        P.op("dve", lambda e, hview=hview: e.tensor_reduce(an0, hview, AX.X, ALU.min),
             reads=hkeys, writes=[an0])
        shc = B.col(VAR_SH, tt * 8, 8)
        negm = B.col(VAR_MAXX, tt * 8, 8)
        for half in range(2):
            rmax = B.RT[:, 0:512]
            rmin = B.RT[:, 512:1024]
            P.op("dve", lambda e, half=half, rmax=rmax: e.tensor_copy(
                rmax, B.ACC[:, half * 512:(half + 1) * 512]),
                reads=[am0], writes=[rmax])
            P.op("dve", lambda e, half=half, rmin=rmin: e.tensor_copy(
                rmin, B.CC[:, 1024 + half * 512: 1024 + (half + 1) * 512]),
                reads=[an0], writes=[rmin])
            for g in range(4):
                P.op("pe", lambda e, g=g, rmax=rmax: e.transpose(
                    B.pb[7][:, g * 128:(g + 1) * 128],
                    rmax[:, g * 128:(g + 1) * 128], B.IDF),
                    reads=[rmax, B.IDF], writes=[B.pb[7]])
            for g in range(4):
                gg = half * 4 + g
                P.op("dve", lambda e, g=g, gg=gg: e.reduce_max(
                    B.col(VAR_SH, tt * 8 + gg), B.pb[7][:, g * 128:(g + 1) * 128],
                    AX.X),
                    reads=[B.pb[7]], writes=[B.col(VAR_SH, tt * 8 + gg)])
            for g in range(4):
                P.op("pe", lambda e, g=g, rmin=rmin: e.transpose(
                    B.pb[7][:, g * 128:(g + 1) * 128],
                    rmin[:, g * 128:(g + 1) * 128], B.IDF),
                    reads=[rmin, B.IDF], writes=[B.pb[7]])
            for g in range(4):
                gg = half * 4 + g
                P.op("dve", lambda e, g=g, gg=gg: e.tensor_reduce(
                    B.col(VAR_MAXX, tt * 8 + gg),
                    B.pb[7][:, g * 128:(g + 1) * 128], AX.X, ALU.min,
                    negate=True),
                    reads=[B.pb[7]], writes=[B.col(VAR_MAXX, tt * 8 + gg)])
        if tt == 1 and FFN_T1_STOP == "tailmid":
            continue
        P.op("dve", lambda e, shc=shc, negm=negm: e.tensor_max(shc, shc, negm),
             reads=[shc, negm], writes=[shc])
        P.op("dve", lambda e, shc=shc: e.tensor_scalar_max(shc, shc, 1e-5),
             reads=[shc], writes=[shc])
        shs = B.col(VAR_MINX, tt * 8, 8)
        P.op("dve", lambda e, shs=shs, shc=shc: e.reciprocal(shs, shc),
             reads=[shc], writes=[shs])
        P.op("dve", lambda e, shs=shs: e.tensor_scalar_mul(shs, shs, 127.0),
             reads=[shs], writes=[shs])
        sd = negm
        if l1:
            P.op("dve", lambda e, sd=sd, shc=shc: e.tensor_scalar_mul(
                sd, shc, 1.0 / 127.0), reads=[shc], writes=[sd])
        else:
            iv8 = B.col(VAR_INVSX, tt * 8, 8)
            P.op("dve", lambda e, sd=sd, iv8=iv8, shc=shc: e.tensor_mul(sd, iv8, shc),
                 reads=[iv8, shc], writes=[sd])
            P.op("dve", lambda e, sd=sd: e.tensor_scalar_mul(
                sd, sd, B.MWC[:, wu_i:wu_i + 1]),
                 reads=[sd, B.MWC], writes=[sd])
            P.op("dve", lambda e, sd=sd: e.tensor_scalar(
                sd, sd, 1e-5, 1.0 / 127.0, ALU.max, ALU.mult),
                 reads=[sd], writes=[sd])
        P.op("dve", lambda e, sd=sd: e.tensor_scalar_mul(
            sd, sd, B.MWC[:, wd_i:wd_i + 1]),
             reads=[sd, B.MWC], writes=[sd])
        P.op("dve", lambda e, sd=sd: e.tensor_scalar_mul(sd, sd, 0.5),
             reads=[sd], writes=[sd])
        if tt == 1 and FFN_T1_STOP == "tailsd":
            continue
        bSh = [emit_bcast(P, B, B.col(VAR_MINX, 0, 16), 0, tt * 2),
               emit_bcast(P, B, B.col(VAR_MINX, 0, 16), 1, tt * 2 + 1)]
        bSd = [emit_bcast(P, B, B.col(VAR_MAXX, 0, 16), 2, tt * 2),
               emit_bcast(P, B, B.col(VAR_MAXX, 0, 16), 3, tt * 2 + 1)]
        if tt == 1 and FFN_T1_STOP == "tail":
            continue
        for m in range(32):
            for sub in range(2):
                hsl = B.HB[:, m * 1024 + sub * 512: m * 1024 + sub * 512 + 512]
                t = B.RT[:, sub * 512: sub * 512 + 512]
                P.op("dve", lambda e, hsl=hsl, t=t, sub=sub: e.tensor_mul(
                    t, hsl, bSh[sub]), reads=[hsl, bSh[sub]], writes=[t])
            hfull = B.HB[:, m * 1024: m * 1024 + 1024]
            tfull = B.RT[:, 0:1024]
            emit_round(P, B, tfull, hfull, 0, n=1024)
        if tt == 1 and FFN_T1_STOP == "round":
            continue
        for sub in range(2):
            qt = tt * 2 + sub
            for k in range(32):
                ws = k % 2
                wdb = B.WB[:, ws * 1024: ws * 1024 + 1024]
                P.dma("sp", wdb, d3[k, :, :], "wg%d" % ws)
                hq = B.HB[:, k * 1024 + sub * 512: k * 1024 + sub * 512 + 512]
                for m8 in range(8):
                    wt = wdb[:, m8 * 128:(m8 + 1) * 128]
                    P.op("pe", lambda e, wt=wt, hq=hq, k=k, m8=m8: e.matmul(
                        B.pb[m8], wt, hq, start=(k == 0), stop=(k == 31)),
                        reads=[wt, hq], writes=[B.pb[m8]])
            for m8 in range(8):
                ev = B.EV[:, (m8 % 2) * 512:(m8 % 2) * 512 + 512]
                P.op("dve", lambda e, m8=m8, ev=ev, sub=sub: e.tensor_mul(
                    ev, B.pb[m8], bSd[sub]),
                    reads=[B.pb[m8], bSd[sub]], writes=[ev])
                if l1:
                    b2h = B.BIA[:, 40 + m8:41 + m8]
                    P.op("pool", lambda e, ev=ev, b2h=b2h: e.tensor_scalar_add(
                        ev, ev, b2h), reads=[ev, B.BIA], writes=[ev])
                xsl = B.xs(m8, qt)
                P.op("dve", lambda e, xsl=xsl, ev=ev: e.tensor_add(xsl, xsl, ev),
                     reads=[ev, xsl], writes=[xsl])



def emit_attn(P, B, layer, qkvT, woT, dscr):
    emit_stats(P, B, ffn=False)
    l1 = layer == 1
    qoff = 8 if l1 else 0
    koff = 0 if l1 else 8
    voff = 16
    qkv = qkvT.ap()
    wo = woT.ap()
    VOFF = 16384
    vones = B.HB[:, VOFF:VOFF + 16640].rearrange("p (s c) -> p s c", c=65)[:, :, 64:65]
    P.op("dve", lambda e: e.memset(vones, 1.0), writes=[B.HB[:, VOFF:VOFF + 16640]])

    def build_u(qt):
        bM = emit_bcast(P, B, B.col(VAR_M, 0, 16), 0, qt)
        bR = emit_bcast(P, B, B.col(VAR_RSTD, 0, 16), 1, qt)
        for c in range(8):
            xsl = B.xs(c, qt)
            ro = (c % 2) * 512
            xm = B.EV[:, ro:ro + 512]
            P.op("dve", lambda e, xsl=xsl, xm=xm, bM=bM: e.tensor_sub(xm, xsl, bM),
                 reads=[xsl, bM], writes=[xm])
            u = B.XQ[:, c * 512:(c + 1) * 512]
            P.op("dve", lambda e, xm=xm, u=u, bR=bR: e.tensor_mul(u, xm, bR),
                 reads=[xm, bR], writes=[u])

    def proj_one(qt, m, bi):
        ws = bi % 2
        wr = B.WR[:, 0:1024] if ws == 0 else B.RT[:, 0:1024]
        P.dma("sp", wr, qkv[m], "wr%d" % ws)
        wb = B.WB[:, 2048 + ws * 1024: 2048 + (ws + 1) * 1024]
        P.op("dve", lambda e, wb=wb, wr=wr: e.tensor_copy(wb, wr),
             reads=[wr], writes=[wb])
        pb = B.pb[ws]
        for k in range(8):
            wt = wb[:, k * 128:(k + 1) * 128]
            u = B.XQ[:, k * 512:(k + 1) * 512]
            P.op("pe", lambda e, wt=wt, u=u, k=k, pb=pb: e.matmul(
                pb, wt, u, start=(k == 0), stop=(k == 7)),
                reads=[wt, u], writes=[pb])
        return pb

    def rope_evac(pb, toff, dst):
        ea = B.BC[:, 1536:2048]
        ebb = B.EV[:, 0:512]
        P.op("dve", lambda e: e.tensor_copy(ea, pb), reads=[pb], writes=[ea])
        P.op("dve", lambda e: e.stream_shuffle(ebb, ea, _SHUF_PAIR),
             reads=[ea], writes=[ebb])
        P.op("dve", lambda e: e.tensor_mul(ea, ea, B.RC[:, toff:toff + 512]),
             reads=[ea, B.RC], writes=[ea])
        P.op("pool", lambda e: e.tensor_mul(ebb, ebb, B.RS[:, toff:toff + 512]),
             reads=[ebb, B.RS], writes=[ebb])
        P.op("dve", lambda e: e.tensor_add(dst, ea, ebb),
             reads=[ea, ebb], writes=[dst])

    for qt in range(4):
        build_u(qt)
        for j in range(8):
            pb = proj_one(qt, koff + j, j)
            dst = B.HB[:, j * 2048 + qt * 512: j * 2048 + qt * 512 + 512]
            if l1:
                rope_evac(pb, qt * 512, dst)
            else:
                P.op("dve", lambda e, dst=dst, pb=pb: e.tensor_copy(dst, pb),
                     reads=[pb], writes=[dst])
        for j in range(8):
            pb = proj_one(qt, voff + j, j)
            ev = B.EV[:, 0:512]
            P.op("dve", lambda e, ev=ev, pb=pb: e.tensor_copy(ev, pb),
                 reads=[pb], writes=[ev])
            for g in range(4):
                kc = qt * 4 + g
                pt = B.pb[6 + (g % 2)]
                P.op("pe", lambda e, g=g, pt=pt: e.transpose(
                    pt[:, 0:128], ev[:, g * 128:(g + 1) * 128], B.IDF),
                    reads=[ev, B.IDF], writes=[pt])
                for hh in range(2):
                    h = 2 * j + hh
                    slot = VOFF + (kc * 16 + h) * 65
                    dst = B.HB[:, slot:slot + 64]
                    P.op("dve", lambda e, dst=dst, pt=pt, hh=hh: e.tensor_copy(
                        dst, pt[:, hh * 64:hh * 64 + 64]),
                        reads=[pt], writes=[dst])
    for qt in range(4):
        build_u(qt)
        for j in range(8):
            pb = proj_one(qt, qoff + j, j)
            dst = B.QT[:, j * 512:(j + 1) * 512]
            if l1:
                rope_evac(pb, qt * 512, dst)
            else:
                P.op("dve", lambda e, dst=dst, pb=pb: e.tensor_copy(dst, pb),
                     reads=[pb], writes=[dst])
        nkc = qt * 4 + 4
        for h in range(NH):
            qslice = B.QT[64 * (h % 2):64 * (h % 2) + 64,
                          (h // 2) * 512:(h // 2) * 512 + 512]
            pav = B.pb[5]
            for kc in range(nkc):
                ps = B.pb[2 + (kc % 2)]
                ksl = B.HB[64 * (h % 2):64 * (h % 2) + 64,
                           (h // 2) * 2048 + kc * 128: (h // 2) * 2048 + kc * 128 + 128]
                P.op("pe", lambda e, ksl=ksl, qslice=qslice, ps=ps: e.matmul(
                    ps, ksl, qslice, start=True, stop=True),
                    reads=[ksl, qslice], writes=[ps])
                p0 = B.CC[:, (kc % 2) * 512:(kc % 2) * 512 + 512]
                P.op("act", lambda e, ps=ps, p0=p0: e.activation(
                    p0, ps, AF.Exp, scale=B.KCONST[:, 1:2]),
                    reads=[ps, B.KCONST], writes=[p0])
                if (not l1) or (kc >= qt * 4):
                    moff = qt * 512 - kc * 128 + 384
                    msl = B.MASK[:, moff:moff + 512]
                    P.op("dve", lambda e, p0=p0, msl=msl: e.tensor_mul(p0, p0, msl),
                         reads=[p0, msl], writes=[p0])
                vsl = B.HB[:, VOFF + (kc * 16 + h) * 65: VOFF + (kc * 16 + h) * 65 + 65]
                P.op("pe", lambda e, vsl=vsl, p0=p0, kc=kc, nkc=nkc, pav=pav: e.matmul(
                    pav[0:65, :], vsl, p0, start=(kc == 0), stop=(kc == nkc - 1)),
                    reads=[vsl, p0], writes=[pav])
            drow = B.EV[64:65, 512:1024]
            P.op("dve", lambda e, pav=pav: e.reciprocal(drow, pav[64:65, 0:512]),
                 reads=[pav], writes=[drow])
            dsl = dscr.ap()[h % 4: h % 4 + 1, :]
            P.dma("act", dsl, drow, "dsc%d" % (h % 4))
            db = B.BC[0:64, 1536:2048]
            P.dma("sp", db, dsl.broadcast_to([64, 512]), "dbc")
            if h % 2 == 0:
                ao = B.XQ[0:64, (h // 2) * 512:(h // 2) * 512 + 512]
                P.op("dve", lambda e, ao=ao, db=db, pav=pav: e.tensor_mul(
                    ao, pav[0:64, :], db), reads=[pav, db], writes=[ao])
            else:
                stg = B.CC[0:64, 1024:1536]
                P.op("dve", lambda e, stg=stg, db=db, pav=pav: e.tensor_mul(
                    stg, pav[0:64, :], db), reads=[pav, db], writes=[stg])
                ao = B.XQ[64:128, (h // 2) * 512:(h // 2) * 512 + 512]
                P.dma("pool", ao, stg, "aomv")
        for mo in range(8):
            ws = mo % 2
            wr = B.WR[:, 0:1024] if ws == 0 else B.RT[:, 0:1024]
            P.dma("sp", wr, wo[mo], "wr%d" % ws)
            wb = B.WB[:, 2048 + ws * 1024: 2048 + (ws + 1) * 1024]
            P.op("dve", lambda e, wb=wb, wr=wr: e.tensor_copy(wb, wr),
                 reads=[wr], writes=[wb])
            po = B.pb[6 + ws]
            for k in range(8):
                wt = wb[:, k * 128:(k + 1) * 128]
                ao = B.XQ[:, k * 512:(k + 1) * 512]
                P.op("pe", lambda e, wt=wt, ao=ao, k=k, po=po: e.matmul(
                    po, wt, ao, start=(k == 0), stop=(k == 7)),
                    reads=[wt, ao], writes=[po])
            xsl = B.xs(mo, qt)
            P.op("dve", lambda e, xsl=xsl, po=po: e.tensor_add(xsl, xsl, po),
                 reads=[po, xsl], writes=[xsl])



def emit_input_tm(P, B, x_tm):
    xv = x_tm.ap().rearrange("(n p) d -> n p d", p=128)
    for nt in range(16):
        io = B.BC[:, 0:1024]
        P.dma("sp", io, xv[nt], "xin")
        for c in range(8):
            pt = B.pb[c % 2]
            P.op("pe", lambda e, c=c, io=io, pt=pt: e.transpose(
                pt[:, 0:128], io[:, c * 128:(c + 1) * 128], B.IDF),
                reads=[io, B.IDF], writes=[pt])
            dst = B.X[:, c * 2048 + nt * 128: c * 2048 + (nt + 1) * 128]
            P.op("dve", lambda e, dst=dst, pt=pt: e.tensor_copy(dst, pt[:, 0:128]),
                 reads=[pt], writes=[dst])


def emit_output_tm(P, B, y_tm):
    yv = y_tm.ap().rearrange("(n p) d -> n p d", p=128)
    for nt in range(16):
        io = B.BC[:, 1024:2048]
        for c in range(8):
            src = B.X[:, c * 2048 + nt * 128: c * 2048 + (nt + 1) * 128]
            pt = B.pb[2 + c % 2]
            P.op("pe", lambda e, src=src, pt=pt: e.transpose(
                pt[:, 0:128], src, B.IDF),
                reads=[src, B.IDF], writes=[pt])
            P.op("dve", lambda e, c=c, io=io, pt=pt: e.tensor_copy(
                io[:, c * 128:(c + 1) * 128], pt[:, 0:128]),
                reads=[pt], writes=[io[:, c * 128:(c + 1) * 128]])
        P.dma("pool", yv[nt], io, "yo")


def emit_input_fm(P, B, x_fm):
    xv = x_fm.ap().rearrange("(c p) n -> c p n", p=128)
    for c in range(8):
        P.dma("sp", B.X[:, c * 2048:(c + 1) * 2048], xv[c], "xin%d" % c)


def emit_output_fm(P, B, y_fm):
    yv = y_fm.ap().rearrange("(c p) n -> c p n", p=128)
    for c in range(8):
        P.dma("pool", yv[c], B.X[:, c * 2048:(c + 1) * 2048], "yo")



W_G0, W_U0, W_D0, W_1, W_2 = 0, 1, 2, 3, 4
TKEYS = ["wg3", "wu3", "wd3", "w13", "w23"]
TSHAPES = {"wg3": (1024, 4096), "wu3": (1024, 4096), "wd3": (4096, 1024),
           "w13": (1024, 4096), "w23": (4096, 1024)}
TILED = {"wg3": True, "wu3": True, "wd3": False, "w13": True, "w23": False}


def _tshape(k):
    K, M = TSHAPES[k]
    return [M // 128, 128, K] if TILED[k] else [K, M]
WIDX = {"wg3": W_G0, "wu3": W_U0, "wd3": W_D0, "w13": W_1, "w23": W_2}


def build_prep():
    nc = bass.Bass("TRN2", target_bir_lowering=False, debug=False)
    dins = {"identf": nc.dram_tensor("identf", [128, 128], F32, kind="ExternalInput")}
    wts = {}
    for k in TKEYS:
        wts[k] = (nc.dram_tensor(k + "_in", _tshape(k), F32, kind="ExternalInput"),
                  nc.dram_tensor(k, _tshape(k), BF16, kind="ExternalOutput"))
    scales = nc.dram_tensor("scales", [1, 32], F32, kind="ExternalOutput")
    P = Prog(nc)
    B = Bufs(nc)
    emit_setup(P, B, dins)
    for k in TKEYS:
        K, M = TSHAPES[k]
        emit_weight_prep(P, B, wts[k][0], wts[k][1], WIDX[k], K, M, scales,
                         tiled=TILED[k])
    P.emit()
    nc.finalize()
    return nc


def build_main(stages, x_in, x_out):
    nc = bass.Bass("TRN2", target_bir_lowering=False, debug=False)
    dins = {"identf": nc.dram_tensor("identf", [128, 128], F32, kind="ExternalInput")}
    need = set()
    for st in stages:
        if st[0] == "ffn":
            need.add("scales")
            need |= ({"wg3", "wu3", "wd3"} if st[1] == 0 else {"w13", "w23", "bias"})
        else:
            need |= {"attn%d" % st[1]}
            if st[1] == 1:
                need.add("rope")
    tern = {}
    for k in TKEYS:
        if k in need:
            tern[k] = nc.dram_tensor(k, _tshape(k), BF16, kind="ExternalInput")
    scales = (nc.dram_tensor("scales", [1, 32], F32, kind="ExternalInput")
              if "scales" in need else None)
    if "bias" in need:
        dins["b1c"] = nc.dram_tensor("b1c", [128, 32], F32, kind="ExternalInput")
        dins["b2c"] = nc.dram_tensor("b2c", [128, 8], F32, kind="ExternalInput")
    for L in (0, 1):
        if ("attn%d" % L) in need:
            w = _HC["mask%d" % L].shape[1]
            dins["mask%d" % L] = nc.dram_tensor("mask%d" % L, [128, w], BF16,
                                                kind="ExternalInput")
            dins["qkvT%d" % L] = nc.dram_tensor("qkvT%d" % L, [1024, 3072], F32,
                                                kind="ExternalInput")
            dins["woT%d" % L] = nc.dram_tensor("woT%d" % L, [1024, 1024], F32,
                                               kind="ExternalInput")
    if "rope" in need:
        dins["ropeC"] = nc.dram_tensor("ropeC", [128, 2048], BF16, kind="ExternalInput")
        dins["ropeS"] = nc.dram_tensor("ropeS", [128, 2048], BF16, kind="ExternalInput")
    x_d = nc.dram_tensor("x", [2048, 1024] if x_in == "tm" else [1024, 2048],
                         F32, kind="ExternalInput")
    y_d = nc.dram_tensor("y", [2048, 1024] if x_out == "tm" else [1024, 2048],
                         F32, kind="ExternalOutput")
    dscr = nc.dram_tensor("dscr", [4, 512], F32, kind="Internal")

    P = Prog(nc)
    B = Bufs(nc)
    emit_setup(P, B, dins)
    if scales is not None:
        emit_load_scales(P, B, scales, list(range(5)))
    if "bias" in need:
        emit_load_bias(P, B, dins)
    if "rope" in need:
        P.dma("sp", B.RC, dins["ropeC"].ap(), "ropec")
        P.dma("sp", B.RS, dins["ropeS"].ap(), "ropes")
    if x_in == "tm":
        emit_input_tm(P, B, x_d)
    else:
        emit_input_fm(P, B, x_d)
    for st in stages:
        if st[0] == "ffn":
            if st[1] == 0:
                emit_ffn(P, B, tern["wg3"], tern["wu3"], tern["wd3"],
                         W_G0, W_U0, W_D0, l1=False)
            else:
                emit_ffn(P, B, tern["w13"], None, tern["w23"],
                         W_1, None, W_2, l1=True)
        else:
            L = st[1]
            m = _HC["mask%d" % L]
            P.dma("sp", B.MASK[:, 0:m.shape[1]], dins["mask%d" % L].ap(), "mask")
            emit_attn(P, B, L, dins["qkvT%d" % L], dins["woT%d" % L], dscr)
    if x_out == "tm":
        emit_output_tm(P, B, y_d)
    else:
        emit_output_fm(P, B, y_d)
    P.emit()
    nc.finalize()
    return nc


def build_fused():
    nc = bass.Bass("TRN2", target_bir_lowering=False, debug=False)
    dins = {"identf": nc.dram_tensor("identf", [128, 128], F32, kind="ExternalInput")}
    win = {}
    tern = {}
    for k in TKEYS:
        win[k] = nc.dram_tensor(k + "_in", _tshape(k), F32, kind="ExternalInput")
        tern[k] = nc.dram_tensor(k, _tshape(k), BF16, kind="Internal")
    scales = nc.dram_tensor("scales", [1, 32], F32, kind="Internal")
    dins["b1c"] = nc.dram_tensor("b1c", [128, 32], F32, kind="ExternalInput")
    dins["b2c"] = nc.dram_tensor("b2c", [128, 8], F32, kind="ExternalInput")
    for L in (0, 1):
        w = _HC["mask%d" % L].shape[1]
        dins["mask%d" % L] = nc.dram_tensor("mask%d" % L, [128, w], BF16,
                                            kind="ExternalInput")
        dins["qkvT%d" % L] = nc.dram_tensor("qkvT%d" % L, [24, 128, 1024], F32,
                                            kind="ExternalInput")
        dins["woT%d" % L] = nc.dram_tensor("woT%d" % L, [8, 128, 1024], F32,
                                           kind="ExternalInput")
    dins["ropeC"] = nc.dram_tensor("ropeC", [128, 2048], BF16, kind="ExternalInput")
    dins["ropeS"] = nc.dram_tensor("ropeS", [128, 2048], BF16, kind="ExternalInput")
    x_d = nc.dram_tensor("x", [2048, 1024], F32, kind="ExternalInput")
    y_d = nc.dram_tensor("y", [2048, 1024], F32, kind="ExternalOutput")
    dscr = nc.dram_tensor("dscr", [4, 512], F32, kind="Internal")

    P = Prog(nc)
    B = Bufs(nc)
    emit_setup(P, B, dins)
    emit_load_bias(P, B, dins)
    emit_input_tm(P, B, x_d)
    for k in TKEYS:
        K, M = TSHAPES[k]
        emit_weight_prep(P, B, win[k], tern[k], WIDX[k], K, M, scales,
                         tiled=TILED[k])
        P.seal_tensor(k, "tst%d" % WIDX[k])
    for st in [("ffn", 0), ("attn", 0), ("ffn", 0), ("ffn", 1), ("attn", 1),
               ("ffn", 1)]:
        if st[0] == "ffn":
            if st[1] == 0:
                emit_ffn(P, B, tern["wg3"], tern["wu3"], tern["wd3"],
                         W_G0, W_U0, W_D0, l1=False)
            else:
                emit_ffn(P, B, tern["w13"], None, tern["w23"],
                         W_1, None, W_2, l1=True)
        else:
            L = st[1]
            m = _HC["mask%d" % L]
            P.dma("sp", B.MASK[:, 0:m.shape[1]], dins["mask%d" % L].ap(), "mask")
            if L == 1:
                P.dma("sp", B.RC, dins["ropeC"].ap(), "ropec")
                P.dma("sp", B.RS, dins["ropeS"].ap(), "ropes")
            emit_attn(P, B, L, dins["qkvT%d" % L], dins["woT%d" % L], dscr)
    emit_output_tm(P, B, y_d)
    P.emit()
    nc.finalize()
    return nc



_NC_CACHE = {}


def _get(key, fn):
    if key not in _NC_CACHE:
        _NC_CACHE[key] = fn()
    return _NC_CACHE[key]


def kernel(L0_attn_x, L0_attn_qkv, L0_attn_wo, L0_ffn_w_gate, L0_ffn_w_up,
           L0_ffn_w_down, L1_attn_W_KQV, L1_attn_W_O, L1_ffn_W1, L1_ffn_b1,
           L1_ffn_W2, L1_ffn_b2):
    x = np.ascontiguousarray(np.asarray(L0_attn_x, np.float32))
    tw = lambda w: np.ascontiguousarray(np.asarray(w, np.float32).T)

    def tile5(wT):
        K, M = wT.shape
        return np.ascontiguousarray(
            wT.reshape(K // 128, 128, M // 128, 128).transpose(2, 1, 0, 3)
            .reshape(M // 128, 128, K))

    wgT, wuT, wdT = tile5(tw(L0_ffn_w_gate)), tile5(tw(L0_ffn_w_up)), tw(L0_ffn_w_down)
    w1T, w2T = tile5(tw(L1_ffn_W1)), tw(L1_ffn_W2)
    qkvT0, woT0 = tile5(tw(L0_attn_qkv)), tile5(tw(L0_attn_wo))
    qkvT1, woT1 = tile5(tw(L1_attn_W_KQV)), tile5(tw(L1_attn_W_O))
    b1c = np.ascontiguousarray(np.asarray(L1_ffn_b1, np.float32).reshape(32, 128).T)
    b2c = np.ascontiguousarray(np.asarray(L1_ffn_b2, np.float32).reshape(8, 128).T)
    consts = _HC

    if FUSED:
        ncf = _get("fused", build_fused)
        in_maps = []
        for i in range(8):
            mm = {"identf": consts["identf"], "b1c": b1c, "b2c": b2c,
                  "ropeC": consts["ropeC"], "ropeS": consts["ropeS"],
                  "mask0": np.ascontiguousarray(consts["mask0"]),
                  "mask1": np.ascontiguousarray(consts["mask1"]),
                  "qkvT0": qkvT0, "woT0": woT0, "qkvT1": qkvT1, "woT1": woT1,
                  "x": np.ascontiguousarray(x[i])}
            mm.update({k + "_in": w for k, w in
                       zip(TKEYS, [wgT, wuT, wdT, w1T, w2T])})
            in_maps.append(mm)
        res = run_bass_kernel_spmd(ncf, in_maps, core_ids=list(range(8)))
        return np.stack([res.results[i]["y"] for i in range(8)], axis=0).astype(np.float32)

    ncp = _get("prep", build_prep)
    inp = {"identf": consts["identf"]}
    inp.update({k + "_in": w for k, w in zip(TKEYS, [wgT, wuT, wdT, w1T, w2T])})
    resp = run_bass_kernel_spmd(ncp, [inp] * 8, core_ids=list(range(8)))
    tern = {k: resp.results[0][k] for k in TKEYS}
    scales = resp.results[0]["scales"]

    common = {"identf": consts["identf"]}
    stage_cfgs = [
        ([("ffn", 0)], "tm", "fm"),
        ([("attn", 0)], "fm", "fm"),
        ([("ffn", 0)], "fm", "fm"),
        ([("ffn", 1)], "fm", "fm"),
        ([("attn", 1)], "fm", "fm"),
        ([("ffn", 1)], "fm", "tm"),
    ]
    cur = [np.ascontiguousarray(x[i]) for i in range(8)]
    for si, (stages, xin, xout) in enumerate(stage_cfgs):
        ncm = _get(("main", si), lambda stages=stages, xin=xin, xout=xout:
                   build_main(stages, xin, xout))
        in_maps = []
        for i in range(8):
            m = dict(common)
            st = stages[0]
            if st[0] == "ffn":
                m["scales"] = scales
                if st[1] == 0:
                    m.update({k: tern[k] for k in ("wg3", "wu3", "wd3")})
                else:
                    m.update({k: tern[k] for k in ("w13", "w23")})
                    m.update({"b1c": b1c, "b2c": b2c})
            else:
                L = st[1]
                m["mask%d" % L] = np.ascontiguousarray(consts["mask%d" % L])
                m["qkvT%d" % L] = qkvT0 if L == 0 else qkvT1
                m["woT%d" % L] = woT0 if L == 0 else woT1
                if L == 1:
                    m["ropeC"] = consts["ropeC"]
                    m["ropeS"] = consts["ropeS"]
            m["x"] = np.ascontiguousarray(cur[i])
            in_maps.append(m)
        res = run_bass_kernel_spmd(ncm, in_maps, core_ids=list(range(8)))
        cur = [res.results[i]["y"] for i in range(8)]
    return np.stack(cur, axis=0).astype(np.float32)


# version 08
# speedup vs baseline: 1.2672x; 1.2176x over previous
import numpy as np
import ml_dtypes
import concourse.bass as bass
import concourse.mybir as mybir
from concourse.bass_utils import run_bass_kernel_spmd

F32 = mybir.dt.float32
BF16 = mybir.dt.bfloat16
I32 = mybir.dt.int32
AF = mybir.ActivationFunctionType
ALU = mybir.AluOpType
AX = mybir.AxisListType
BFNP = ml_dtypes.bfloat16

S, DM, NH, DH, INNER = 2048, 1024, 16, 64, 4096
FFN_TTS = range(2)
FFN_T1_STOP = None
FUSED = True


def host_constants():
    c = {}
    c["identf"] = np.eye(128, dtype=np.float32)
    jj = np.arange(128)[:, None]
    uu = np.arange(2432)[None, :] - 384
    d = uu - jj
    allowed = ((d >= 0) & (d <= 127)) | ((d >= 132) & ((d - 128) % 4 == 0))
    c["mask0"] = allowed.astype(BFNP)
    uu1 = np.arange(896)[None, :] - 384
    c["mask1"] = ((uu1 - jj) >= 0).astype(BFNP)
    r = np.arange(0, DH, 2, dtype=np.float64)
    invf = 1.0 / (10000.0 ** (r / DH))
    fr = np.arange(S, dtype=np.float64)[:, None] * invf[None, :]
    cos = np.cos(fr); sin = np.sin(fr)
    C = np.repeat(cos, 2, axis=1)
    S2 = np.empty((S, DH)); S2[:, 0::2] = -sin; S2[:, 1::2] = sin
    c["ropeC"] = np.tile(C.T, (2, 1)).astype(BFNP)
    c["ropeS"] = np.tile(S2.T, (2, 1)).astype(BFNP)
    return c

_HC = host_constants()
_SHUF_PAIR = []
for _i in range(0, 32, 2):
    _SHUF_PAIR.extend([_i + 1, _i])


QSEM = {"sp": "qsp", "act": "qact", "pool": "qpool"}


def _apkeys(x):
    if not hasattr(x, "tensor"):
        return (x,)
    t = x.tensor
    nm = t.name
    dims = list(x.ap)
    try:
        import bass_rust
        is_dram = x.space == getattr(bass_rust, "MemorySpace").DRAM
    except Exception:
        is_dram = str(getattr(x, "space", "")).find("DRAM") >= 0
    lo = int(x.offset)
    if not is_dram:
        fw = 1
        for d in list(t.shape)[1:]:
            fw *= int(d)
        lo = lo % fw
    hi = lo
    use = dims if is_dram else dims[1:] if len(dims) > 1 else []
    for st, ct in use:
        hi += (int(ct) - 1) * int(st)
    hi += 1
    gran = 65536 if is_dram else 512
    return tuple((nm, i) for i in range(lo // gran, (hi + gran - 1) // gran))


class Prog:
    def __init__(self, nc):
        self.nc = nc
        self.sem = {}
        for e in ("pe", "dve", "act", "pool"):
            self.sem[e] = nc.alloc_semaphore("S" + e)
        self.cnt = {e: 0 for e in self.sem}
        self.q = {e: [] for e in ("pe", "dve", "act", "pool", "sp")}
        self.hwm = {e: {} for e in self.q}
        self.res = {}

    def _stream(self, name):
        key = ("dma", name)
        if key not in self.sem:
            self.sem[key] = self.nc.alloc_semaphore("Sd_" + name)
            self.cnt[key] = 0
        return key

    def _expand(self, items):
        out = []
        for it in items:
            out.extend(_apkeys(it))
        return out

    def _need(self, eng, own_sem, reads, writes):
        need = {}

        def add(s, v):
            if s == own_sem and v > self.cnt.get(own_sem, 0):
                return
            need[s] = max(need.get(s, 0), v)

        for k in reads:
            r = self.res.get(k)
            if r and r["w"] is not None:
                add(*r["w"])
        for k in writes:
            r = self.res.get(k)
            if r:
                if r["w"] is not None:
                    add(*r["w"])
                for s, v in r["r"].items():
                    add(s, v)
        hw = self.hwm[eng]
        waits = []
        for s, v in sorted(need.items(), key=lambda kv: str(kv[0])):
            if hw.get(s, 0) < v:
                waits.append((s, v))
                hw[s] = v
        return waits

    def _record(self, own_sem, tick, reads, writes):
        for k in writes:
            self.res[k] = {"w": (own_sem, tick), "r": {}}
        for k in reads:
            r = self.res.get(k)
            if r is None:
                r = {"w": None, "r": {}}
                self.res[k] = r
            r["r"][own_sem] = max(r["r"].get(own_sem, 0), tick)

    def op(self, eng, fn, reads=(), writes=(), inc=True):
        import sys as _s
        f = _s._getframe(1)
        self._dbg = "%s:%d" % (f.f_code.co_name, f.f_lineno)
        own = eng
        reads = self._expand(reads)
        writes = self._expand(writes)
        waits = self._need(eng, own, reads, writes)
        if inc:
            self.cnt[eng] += 1
            tick = self.cnt[eng]
        else:
            tick = self.cnt[eng] + 1
        self.q[eng].append((waits, fn, own if inc else None, self._dbg))
        self._record(own, tick, reads, writes)

    def dma(self, qeng, dst, src, stream, reads=(), writes=(), **kw):
        own = self._stream(stream)
        reads = self._expand(tuple(reads) + (src,))
        writes = self._expand(tuple(writes) + (dst,))
        waits = self._need(qeng, own, reads, writes)
        self.cnt[own] += 1
        tick = 16 * self.cnt[own]
        import sys as _s
        f = _s._getframe(1)
        dbg = "%s:%d" % (f.f_code.co_name, f.f_lineno)
        self.q[qeng].append(
            (waits, lambda e, d=dst, s=src, k=dict(kw): e.dma_start(d, s, **k), own, dbg))
        self._record(own, tick, reads, writes)

    def seal_tensor(self, name, stream):
        key = ("dma", stream)
        tick = 16 * self.cnt[key]
        for k, r in self.res.items():
            if isinstance(k, tuple) and len(k) == 2 and k[0] == name:
                r["w"] = (key, tick)

    def check_deadlock(self):
        pos = {e: 0 for e in self.q}
        val = {s: 0 for s in self.cnt}
        progressed = True
        while progressed:
            progressed = False
            for e, lst in self.q.items():
                while pos[e] < len(lst):
                    waits, _fn, inc_sem = lst[pos[e]][0], lst[pos[e]][1], lst[pos[e]][2]
                    if all(val.get(s, 0) >= v for s, v in waits):
                        if inc_sem is not None:
                            val[inc_sem] = val.get(inc_sem, 0) + (
                                16 if isinstance(inc_sem, tuple) else 1)
                        pos[e] += 1
                        progressed = True
                    else:
                        break
        stuck = {e: pos[e] for e in self.q if pos[e] < len(self.q[e])}
        if stuck:
            msg = []
            for e, p in stuck.items():
                waits, _f, _i, dbg = self.q[e][p]
                bad = [(s, v, val.get(s, 0)) for s, v in waits if val.get(s, 0) < v]
                msg.append(f"{e}@{p} [{dbg}] needs {bad}")
            raise RuntimeError("build-time deadlock: " + " | ".join(msg))

    def emit(self):
        nc = self.nc
        sems = self.sem
        final = dict(self.cnt)

        def mk(lst, eng_name):
            def body(engine):
                for waits, fn, inc_sem, _dbg in lst:
                    for s, v in waits:
                        engine.wait_ge(sems[s], v)
                    ins = fn(engine)
                    if inc_sem is not None:
                        amt = 16 if isinstance(inc_sem, tuple) else 1
                        ins.then_inc(sems[inc_sem], amt)
                if eng_name == "pool":
                    for s, v in final.items():
                        tgt = v * 16 if isinstance(s, tuple) else v
                        if tgt > 0:
                            engine.wait_ge(sems[s], tgt)
            return body

        self.check_deadlock()
        with nc.Block() as blk:
            blk.sync(mk(self.q["sp"], "sp"))
            blk.tensor(mk(self.q["pe"], "pe"))
            blk.vector(mk(self.q["dve"], "dve"))
            blk.scalar(mk(self.q["act"], "act"))
            blk.gpsimd(mk(self.q["pool"], "pool"))



class Bufs:
    def __init__(self, nc):
        A = nc.alloc_sbuf_tensor
        self.X = A("X", [128, 16384], F32).ap()
        self.HB = A("HB", [128, 33024], BF16).ap()
        self.XQ = A("XQb", [128, 8192], BF16).ap()
        self.QT = A("QTb", [128, 4096], BF16).ap()
        self.WB = A("WBb", [128, 4096], BF16).ap()
        self.WR = A("WRb", [128, 1024], F32).ap()
        self.BC = A("BCb", [128, 2048], F32).ap()
        self.EV = A("EVb", [128, 1280], F32).ap()
        self.ACC = A("ACCb", [128, 1024], BF16).ap()
        self.RT = A("RTb", [128, 2048], F32).ap()
        self.RI = A("RIb", [128, 1024], I32).ap()
        self.CC = A("CCb", [128, 2048], BF16).ap()
        self.AUX = A("AUXb", [128, 4992], BF16).ap()
        self.MASK = self.AUX
        self.RC = self.AUX[:, 896:2944]
        self.RS = self.AUX[:, 2944:4992]
        self.COLS = A("COLSb", [128, 160], F32).ap()
        self.CB = A("CBb", [128, 128], F32).ap()
        self.IDF = A("IDFb", [128, 128], F32).ap()
        self.ONE1 = A("ONE1b", [128, 1], BF16).ap()
        self.KCONST = A("KCONSTb", [128, 2], F32).ap()
        self.MWC = A("MWCb", [128, 10], F32).ap()
        self.BIA = A("BIAb", [128, 48], F32).ap()
        self.SCR = A("SCRb", [1, 32], F32).ap()
        self.pbw = [nc.alloc_psum_tensor(f"pw{i}", [128, 1024], F32).ap() for i in range(4)]
        self.pb = [self.pbw[i // 2][:, (i % 2) * 512:(i % 2) * 512 + 512] for i in range(8)]

    def col(self, v, idx, n=1):
        return self.COLS[:, v * 16 + idx: v * 16 + idx + n]

    def xs(self, c, qt):
        return self.X[:, c * 2048 + qt * 512: c * 2048 + qt * 512 + 512]


(VAR_M, VAR_RSTD, VAR_S1, VAR_S2, VAR_MAXX, VAR_MINX,
 VAR_ACOL, VAR_INVSX, VAR_SH, VAR_TMP) = range(10)
KC_EPS, KC_EIGHTH = 0, 1


def emit_setup(P, B, dins):
    P.dma("sp", B.IDF, dins["identf"].ap(), "ldid")
    P.op("dve", lambda e: e.memset(B.CB, 0.0), writes=[B.CB])
    P.op("dve", lambda e: e.memset(B.ONE1, 1.0), writes=[B.ONE1])
    P.op("dve", lambda e: e.memset(B.KCONST[:, 0:1], 1e-5), writes=[B.KCONST])
    P.op("dve", lambda e: e.memset(B.KCONST[:, 1:2], 0.125), writes=[B.KCONST])
    P.op("dve", lambda e: e.memset(B.COLS, 0.0), writes=[B.COLS])



def emit_weight_prep(P, B, wT, scratch, widx, K, M, scr_dram, tiled=False):
    if tiled:
        nkc, nmb = M // 128, K // 1024
        wsrc = wT.ap()
        wdst = scratch.ap()
    else:
        nkc, nmb = K // 128, M // 1024
        wsrc = wT.ap().rearrange("(k p) m -> k p m", p=128)
        wdst = scratch.ap().rearrange("(k p) m -> k p m", p=128)
    acc = B.col(VAR_TMP, 0)
    tmpc = B.col(VAR_TMP, 1)
    P.op("dve", lambda e: e.memset(acc, 0.0), writes=[acc])
    bi = 0
    for kc in range(nkc):
        for mb in range(nmb):
            wr = B.WR[:, 0:1024] if bi % 2 == 0 else B.RT[:, 1024:2048]
            tm2 = B.col(VAR_TMP, 1 + bi % 2)
            P.dma("sp", wr, wsrc[kc, :, mb * 1024:(mb + 1) * 1024],
                  "wr%d" % (bi % 2))
            P.op("dve", lambda e, wr=wr, tm2=tm2: e.reduce_sum(
                tm2, wr, AX.X, apply_absolute_value=True),
                reads=[wr], writes=[tm2])
            P.op("dve", lambda e, tm2=tm2: e.tensor_add(acc, acc, tm2),
                 reads=[tm2, acc], writes=[acc])
            bi += 1
    P.op("pe", lambda e: e.transpose(B.pb[7][0:1, 0:128], acc, B.IDF),
         reads=[acc, B.IDF], writes=[B.pb[7]])
    P.op("dve", lambda e: e.tensor_copy(B.EV[0:1, 0:128], B.pb[7][0:1, 0:128]),
         reads=[B.pb[7]], writes=[B.EV[0:1, 0:128]])
    P.op("dve", lambda e: e.reduce_sum(B.EV[0:1, 512:513], B.EV[0:1, 0:128], AX.X),
         reads=[B.EV[0:1, 0:128]], writes=[B.EV[0:1, 512:513]])
    P.op("dve", lambda e: e.tensor_scalar(
        B.SCR[0:1, widx:widx + 1], B.EV[0:1, 512:513], 1.0 / (K * M), 1e-5,
        ALU.mult, ALU.max),
        reads=[B.EV[0:1, 512:513]], writes=[B.SCR[0:1, widx:widx + 1]])
    P.op("dve", lambda e: e.reciprocal(B.SCR[0:1, 16 + widx:17 + widx],
                                       B.SCR[0:1, widx:widx + 1]),
         reads=[B.SCR[0:1, widx:widx + 1]], writes=[B.SCR[0:1, 16 + widx:17 + widx]])
    P.dma("pool", scr_dram.ap()[0:1, widx:widx + 1], B.SCR[0:1, widx:widx + 1],
          "scs%d" % widx)
    P.dma("pool", scr_dram.ap()[0:1, 16 + widx:17 + widx],
          B.SCR[0:1, 16 + widx:17 + widx], "scs%d" % widx)
    emit_load_scales(P, B, scr_dram, [widx])
    bi = 0
    for kc in range(nkc):
        for mb in range(nmb):
            sl = bi % 2
            wr = B.WR[:, 0:1024] if sl == 0 else B.RT[:, 1024:2048]
            P.dma("sp", wr, wsrc[kc, :, mb * 1024:(mb + 1) * 1024], "wr%d" % sl)
            vv = B.EV[:, 0:1024] if sl == 0 else B.BC[:, 0:1024]
            P.op("dve", lambda e, wr=wr, vv=vv: e.tensor_scalar_mul(
                vv, wr, B.MWC[:, 5 + widx:6 + widx]),
                reads=[wr, B.MWC], writes=[vv])
            c1 = B.CC[:, sl * 1024:(sl + 1) * 1024]
            P.op("dve", lambda e, vv=vv, c1=c1: e.tensor_single_scalar(
                c1, vv, 0.5, ALU.is_gt), reads=[vv], writes=[c1])
            c2 = B.QT[:, sl * 1024:(sl + 1) * 1024]
            P.op("dve", lambda e, vv=vv, c2=c2: e.tensor_single_scalar(
                c2, vv, -0.5, ALU.is_lt), reads=[vv], writes=[c2])
            wb = B.WB[:, sl * 1024:(sl + 1) * 1024]
            P.op("dve", lambda e, c1=c1, c2=c2, wb=wb: e.tensor_sub(wb, c1, c2),
                 reads=[c1, c2], writes=[wb])
            P.dma("pool", wdst[kc, :, mb * 1024:(mb + 1) * 1024], wb, "tst%d" % widx)
            bi += 1


def emit_load_scales(P, B, scr_dram, widxs):
    for w in widxs:
        P.dma("sp", B.MWC[:, w:w + 1],
              scr_dram.ap()[0:1, w:w + 1].broadcast_to([128, 1]), "mwc")
        P.dma("sp", B.MWC[:, 5 + w:6 + w],
              scr_dram.ap()[0:1, 16 + w:17 + w].broadcast_to([128, 1]), "mwc")


def emit_load_bias(P, B, dins):
    P.dma("sp", B.BIA[:, 0:32], dins["b1c"].ap(), "bias")
    P.dma("sp", B.BIA[:, 32:40], dins["b2c"].ap(), "bias")
    P.op("dve", lambda e: e.tensor_scalar_mul(B.BIA[:, 40:48], B.BIA[:, 32:40], 0.5),
         reads=[B.BIA[:, 32:40]], writes=[B.BIA[:, 40:48]])



def emit_stats(P, B, ffn):
    for qt in range(4):
        for c in range(8):
            xsl = B.xs(c, qt)
            slot = c % 2
            xb = B.QT[:, slot * 512:(slot + 1) * 512]
            sq = B.QT[:, 1024 + slot * 512: 1024 + (slot + 1) * 512]
            P.op("dve", lambda e, xb=xb, xsl=xsl: e.tensor_copy(xb, xsl),
                 reads=[xsl], writes=[xb])
            P.op("act", lambda e, sq=sq, xsl=xsl: e.activation(sq, xsl, AF.Square),
                 reads=[xsl], writes=[sq])
            P.op("pe", lambda e, xb=xb, c=c: e.matmul(
                B.pb[5][0:1, :], B.ONE1, xb, start=(c == 0), stop=(c == 7)),
                reads=[xb, B.ONE1], writes=[B.pb[5]])
            P.op("pe", lambda e, sq=sq, c=c: e.matmul(
                B.pb[6][0:1, :], B.ONE1, sq, start=(c == 0), stop=(c == 7)),
                reads=[sq, B.ONE1], writes=[B.pb[6]])
        r1 = B.EV[0:1, 0:512]
        r2 = B.EV[0:1, 512:1024]
        P.op("dve", lambda e: e.tensor_copy(r1, B.pb[5][0:1, :]),
             reads=[B.pb[5]], writes=[r1])
        P.op("dve", lambda e: e.tensor_copy(r2, B.pb[6][0:1, :]),
             reads=[B.pb[6]], writes=[r2])
        for g in range(4):
            P.op("pe", lambda e, g=g: e.transpose(
                B.pb[7][:, g:g + 1], B.EV[0:1, g * 128:(g + 1) * 128], B.IDF[0:1, 0:1]),
                reads=[r1, B.IDF], writes=[B.pb[7]], inc=False)
            P.op("pe", lambda e, g=g: e.transpose(
                B.pb[7][:, 4 + g:5 + g],
                B.EV[0:1, 512 + g * 128:512 + (g + 1) * 128], B.IDF[0:1, 0:1]),
                reads=[r2, B.IDF], writes=[B.pb[7]])
        P.op("dve", lambda e, qt=qt: e.tensor_copy(
            B.col(VAR_S1, qt * 4, 4), B.pb[7][:, 0:4]),
            reads=[B.pb[7]], writes=[B.col(VAR_S1, qt * 4, 4)])
        P.op("dve", lambda e, qt=qt: e.tensor_copy(
            B.col(VAR_S2, qt * 4, 4), B.pb[7][:, 4:8]),
            reads=[B.pb[7]], writes=[B.col(VAR_S2, qt * 4, 4)])
        if ffn:
            ea = B.EV[:, 0:512]
            eb = B.EV[:, 512:1024]
            xview = B.X.rearrange("p (c t) -> p t c", c=8)[:, qt * 512:(qt + 1) * 512, :]
            xkeys = [B.xs(c2_, qt) for c2_ in range(8)] + [r1, r2]
            P.op("dve", lambda e, xview=xview: e.reduce_max(ea, xview, AX.X),
                 reads=xkeys, writes=[ea])
            P.op("dve", lambda e, xview=xview: e.tensor_reduce(
                eb, xview, AX.X, ALU.min), reads=xkeys, writes=[eb])
            for g in range(4):
                P.op("pe", lambda e, g=g: e.transpose(
                    B.pb[7][:, g * 128:(g + 1) * 128], ea[:, g * 128:(g + 1) * 128],
                    B.IDF),
                    reads=[ea, B.IDF], writes=[B.pb[7]])
            for g in range(4):
                P.op("dve", lambda e, g=g, qt=qt: e.reduce_max(
                    B.col(VAR_MAXX, qt * 4 + g), B.pb[7][:, g * 128:(g + 1) * 128],
                    AX.X),
                    reads=[B.pb[7]], writes=[B.col(VAR_MAXX, qt * 4 + g)])
            for g in range(4):
                P.op("pe", lambda e, g=g: e.transpose(
                    B.pb[7][:, g * 128:(g + 1) * 128], eb[:, g * 128:(g + 1) * 128],
                    B.IDF),
                    reads=[eb, B.IDF], writes=[B.pb[7]])
            for g in range(4):
                P.op("dve", lambda e, g=g, qt=qt: e.tensor_reduce(
                    B.col(VAR_MINX, qt * 4 + g), B.pb[7][:, g * 128:(g + 1) * 128],
                    AX.X, ALU.min),
                    reads=[B.pb[7]], writes=[B.col(VAR_MINX, qt * 4 + g)])
    m16 = B.col(VAR_M, 0, 16)
    rst = B.col(VAR_RSTD, 0, 16)
    tmp = B.col(VAR_TMP, 0, 16)
    s1 = B.col(VAR_S1, 0, 16)
    s2 = B.col(VAR_S2, 0, 16)
    P.op("dve", lambda e: e.tensor_scalar_mul(m16, s1, 1.0 / 1024.0),
         reads=[s1], writes=[m16])
    P.op("dve", lambda e: e.tensor_mul(tmp, m16, m16), reads=[m16], writes=[tmp])
    P.op("dve", lambda e: e.scalar_tensor_tensor(
        rst, s2, 1.0 / 1024.0, tmp, ALU.mult, ALU.subtract),
        reads=[s2, tmp], writes=[rst])
    P.op("act", lambda e: e.activation(rst, rst, AF.Sqrt, bias=B.KCONST[:, 0:1]),
         reads=[rst, B.KCONST], writes=[rst])
    P.op("dve", lambda e: e.reciprocal(rst, rst), reads=[rst], writes=[rst])
    if ffn:
        mx = B.col(VAR_MAXX, 0, 16)
        mn = B.col(VAR_MINX, 0, 16)
        a16 = B.col(VAR_ACOL, 0, 16)
        iv = B.col(VAR_INVSX, 0, 16)
        P.op("dve", lambda e: e.tensor_sub(mx, mx, m16), reads=[mx, m16], writes=[mx])
        P.op("dve", lambda e: e.tensor_sub(mn, m16, mn), reads=[mn, m16], writes=[mn])
        P.op("dve", lambda e: e.tensor_max(mx, mx, mn), reads=[mx, mn], writes=[mx])
        P.op("dve", lambda e: e.reciprocal(a16, mx), reads=[mx], writes=[a16])
        P.op("dve", lambda e: e.tensor_scalar_mul(a16, a16, 127.0),
             reads=[a16], writes=[a16])
        P.op("dve", lambda e: e.tensor_mul(iv, rst, mx), reads=[rst, mx], writes=[iv])
        P.op("dve", lambda e: e.tensor_scalar(iv, iv, 1e-5, 1.0 / 127.0,
                                              ALU.max, ALU.mult),
             reads=[iv], writes=[iv])


def emit_bcast(P, B, colap, slot, qt, pbi=4):
    pb = B.pb[pbi]
    for g in range(4):
        cs = colap[:, qt * 4 + g: qt * 4 + g + 1]
        P.op("act", lambda e, cs=cs: e.activation(B.CB, B.CB, AF.Identity,
                                                  bias=cs, scale=0.0),
             reads=[cs, B.CB], writes=[B.CB])
        P.op("pe", lambda e, g=g, pb=pb: e.transpose(
            pb[:, g * 128:(g + 1) * 128], B.CB, B.IDF),
            reads=[B.CB, B.IDF], writes=[pb])
    bc = B.BC[:, slot * 512:(slot + 1) * 512]
    P.op("dve", lambda e, bc=bc, pb=pb: e.tensor_copy(bc, pb),
         reads=[pb], writes=[bc])
    return bc


def emit_round(P, B, src, dst, ro, n=512):
    yi = B.RI[:, ro:ro + n]
    yf = B.RT[:, 1024 + ro:1024 + ro + n]
    c1 = B.CC[:, ro:ro + n]
    c2 = B.CC[:, 1024 + ro:1024 + ro + n]
    P.op("dve", lambda e: e.tensor_scalar(yi, src, 1.0, None, ALU.mult),
         reads=[src], writes=[yi])
    P.op("dve", lambda e: e.tensor_copy(yf, yi), reads=[yi], writes=[yf])
    P.op("dve", lambda e: e.scalar_tensor_tensor(
        c1, src, 0.5, yf, ALU.subtract, ALU.is_gt),
        reads=[src, yf], writes=[c1])
    P.op("dve", lambda e: e.scalar_tensor_tensor(
        c2, src, 0.5, yf, ALU.add, ALU.is_lt),
        reads=[src, yf], writes=[c2])
    P.op("dve", lambda e: e.tensor_add(yf, yf, c1), reads=[yf, c1], writes=[yf])
    P.op("dve", lambda e: e.tensor_sub(dst, yf, c2), reads=[yf, c2], writes=[dst])



def emit_ffn(P, B, tern_g, tern_u, tern_d, wg_i, wu_i, wd_i, l1):
    emit_stats(P, B, ffn=True)
    g3 = tern_g.ap()
    u3 = tern_u.ap() if tern_u is not None else None
    d3 = tern_d.ap().rearrange("(k p) m -> k p m", p=128)
    cgc = B.col(VAR_TMP, 0, 16)
    P.op("dve", lambda e: e.tensor_scalar_mul(
        cgc, B.col(VAR_INVSX, 0, 16), B.MWC[:, wg_i:wg_i + 1]),
        reads=[B.col(VAR_INVSX, 0, 16), B.MWC], writes=[cgc])
    for tt in FFN_TTS:
        for sub in range(2):
            qt = tt * 2 + sub
            bM = emit_bcast(P, B, B.col(VAR_M, 0, 16), 0, qt)
            bA = emit_bcast(P, B, B.col(VAR_ACOL, 0, 16), 1, qt)
            for c in range(8):
                xsl = B.xs(c, qt)
                ro = (c % 2) * 512
                xm = B.EV[:, ro:ro + 512]
                t = B.RT[:, ro:ro + 512]
                P.op("dve", lambda e, xsl=xsl, xm=xm, bM=bM: e.tensor_sub(xm, xsl, bM),
                     reads=[xsl, bM], writes=[xm])
                P.op("dve", lambda e, xm=xm, t=t, bA=bA: e.tensor_mul(t, xm, bA),
                     reads=[xm, bA], writes=[t])
                xq = B.XQ[:, c * 1024 + sub * 512: c * 1024 + sub * 512 + 512]
                emit_round(P, B, t, xq, ro)
        if tt == 1 and FFN_T1_STOP == "xq":
            continue
        bCg = [emit_bcast(P, B, cgc, 2, tt * 2), emit_bcast(P, B, cgc, 3, tt * 2 + 1)]
        am0 = B.ACC[:, 0:1024]
        an0 = B.CC[:, 1024:2048]
        def gu_mms(m):
            ws = m % 2
            gblk = B.WB[:, ws * 1024: ws * 1024 + 1024]
            P.dma("sp", gblk, g3[m], "wg%d" % ws)
            pg = B.pbw[(m % 2) * 2]
            if not l1:
                ublk = B.WB[:, 2048 + ws * 1024: 2048 + ws * 1024 + 1024]
                P.dma("sp", ublk, u3[m], "wu%d" % ws)
                pu = B.pbw[(m % 2) * 2 + 1]
            for sub in range(2):
                dstg = pg[:, sub * 512: sub * 512 + 512]
                for k in range(8):
                    wt = gblk[:, k * 128:(k + 1) * 128]
                    xq = B.XQ[:, k * 1024 + sub * 512: k * 1024 + sub * 512 + 512]
                    P.op("pe", lambda e, wt=wt, xq=xq, k=k, dstg=dstg: e.matmul(
                        dstg, wt, xq, start=(k == 0), stop=(k == 7)),
                        reads=[wt, xq], writes=[dstg])
                if not l1:
                    dstu = pu[:, sub * 512: sub * 512 + 512]
                    for k in range(8):
                        wt = ublk[:, k * 128:(k + 1) * 128]
                        xq = B.XQ[:, k * 1024 + sub * 512: k * 1024 + sub * 512 + 512]
                        P.op("pe", lambda e, wt=wt, xq=xq, k=k, dstu=dstu: e.matmul(
                            dstu, wt, xq, start=(k == 0), stop=(k == 7)),
                            reads=[wt, xq], writes=[dstu])

        def gu_chain(m):
            pg = B.pbw[(m % 2) * 2]
            pu = B.pbw[(m % 2) * 2 + 1]
            gs = B.EV[:, 0:1024]
            hsl = B.HB[:, m * 1024: m * 1024 + 1024]
            bCgp = B.BC[:, 1024:2048]
            P.op("dve", lambda e, pg=pg, gs=gs, bCgp=bCgp: e.tensor_mul(
                gs, pg, bCgp),
                reads=[B.pb[(m % 2) * 4], B.pb[(m % 2) * 4 + 1], bCgp], writes=[gs])
            if l1:
                b1c = B.BIA[:, m:m + 1]
                P.op("act", lambda e, gs=gs, hsl=hsl, b1c=b1c: e.activation(
                    hsl, gs, AF.Gelu, bias=b1c),
                    reads=[gs, B.BIA], writes=[hsl])
            else:
                P.op("act", lambda e, gs=gs: e.activation(gs, gs, AF.Silu),
                     reads=[gs], writes=[gs])
                P.op("dve", lambda e, gs=gs, hsl=hsl, pu=pu: e.tensor_mul(
                    hsl, gs, pu),
                    reads=[gs, B.pb[(m % 2) * 4 + 2], B.pb[(m % 2) * 4 + 3]],
                    writes=[hsl])

        for m in range(33):
            if m < 32:
                gu_mms(m)
            if m >= 1:
                gu_chain(m - 1)
        if True:
            if True:
                if True:
                    pass

        if tt == 1 and FFN_T1_STOP == "gu":
            continue
        hview = B.HB[:, 0:32768].rearrange("p (m t) -> p t m", m=32)
        hkeys = [B.HB[:, 0:32768]]
        P.op("dve", lambda e, hview=hview: e.reduce_max(am0, hview, AX.X),
             reads=hkeys, writes=[am0])
        P.op("dve", lambda e, hview=hview: e.tensor_reduce(an0, hview, AX.X, ALU.min),
             reads=hkeys, writes=[an0])
        shc = B.col(VAR_SH, tt * 8, 8)
        negm = B.col(VAR_MAXX, tt * 8, 8)
        for half in range(2):
            rmax = B.RT[:, 0:512]
            rmin = B.RT[:, 512:1024]
            P.op("dve", lambda e, half=half, rmax=rmax: e.tensor_copy(
                rmax, B.ACC[:, half * 512:(half + 1) * 512]),
                reads=[am0], writes=[rmax])
            P.op("dve", lambda e, half=half, rmin=rmin: e.tensor_copy(
                rmin, B.CC[:, 1024 + half * 512: 1024 + (half + 1) * 512]),
                reads=[an0], writes=[rmin])
            for g in range(4):
                P.op("pe", lambda e, g=g, rmax=rmax: e.transpose(
                    B.pb[7][:, g * 128:(g + 1) * 128],
                    rmax[:, g * 128:(g + 1) * 128], B.IDF),
                    reads=[rmax, B.IDF], writes=[B.pb[7]])
            for g in range(4):
                gg = half * 4 + g
                P.op("dve", lambda e, g=g, gg=gg: e.reduce_max(
                    B.col(VAR_SH, tt * 8 + gg), B.pb[7][:, g * 128:(g + 1) * 128],
                    AX.X),
                    reads=[B.pb[7]], writes=[B.col(VAR_SH, tt * 8 + gg)])
            for g in range(4):
                P.op("pe", lambda e, g=g, rmin=rmin: e.transpose(
                    B.pb[7][:, g * 128:(g + 1) * 128],
                    rmin[:, g * 128:(g + 1) * 128], B.IDF),
                    reads=[rmin, B.IDF], writes=[B.pb[7]])
            for g in range(4):
                gg = half * 4 + g
                P.op("dve", lambda e, g=g, gg=gg: e.tensor_reduce(
                    B.col(VAR_MAXX, tt * 8 + gg),
                    B.pb[7][:, g * 128:(g + 1) * 128], AX.X, ALU.min,
                    negate=True),
                    reads=[B.pb[7]], writes=[B.col(VAR_MAXX, tt * 8 + gg)])
        if tt == 1 and FFN_T1_STOP == "tailmid":
            continue
        P.op("dve", lambda e, shc=shc, negm=negm: e.tensor_max(shc, shc, negm),
             reads=[shc, negm], writes=[shc])
        P.op("dve", lambda e, shc=shc: e.tensor_scalar_max(shc, shc, 1e-5),
             reads=[shc], writes=[shc])
        shs = B.col(VAR_MINX, tt * 8, 8)
        P.op("dve", lambda e, shs=shs, shc=shc: e.reciprocal(shs, shc),
             reads=[shc], writes=[shs])
        P.op("dve", lambda e, shs=shs: e.tensor_scalar_mul(shs, shs, 127.0),
             reads=[shs], writes=[shs])
        sd = negm
        if l1:
            P.op("dve", lambda e, sd=sd, shc=shc: e.tensor_scalar_mul(
                sd, shc, 1.0 / 127.0), reads=[shc], writes=[sd])
        else:
            iv8 = B.col(VAR_INVSX, tt * 8, 8)
            P.op("dve", lambda e, sd=sd, iv8=iv8, shc=shc: e.tensor_mul(sd, iv8, shc),
                 reads=[iv8, shc], writes=[sd])
            P.op("dve", lambda e, sd=sd: e.tensor_scalar_mul(
                sd, sd, B.MWC[:, wu_i:wu_i + 1]),
                 reads=[sd, B.MWC], writes=[sd])
            P.op("dve", lambda e, sd=sd: e.tensor_scalar(
                sd, sd, 1e-5, 1.0 / 127.0, ALU.max, ALU.mult),
                 reads=[sd], writes=[sd])
        P.op("dve", lambda e, sd=sd: e.tensor_scalar_mul(
            sd, sd, B.MWC[:, wd_i:wd_i + 1]),
             reads=[sd, B.MWC], writes=[sd])
        P.op("dve", lambda e, sd=sd: e.tensor_scalar_mul(sd, sd, 0.5),
             reads=[sd], writes=[sd])
        if tt == 1 and FFN_T1_STOP == "tailsd":
            continue
        bSh = [emit_bcast(P, B, B.col(VAR_MINX, 0, 16), 0, tt * 2),
               emit_bcast(P, B, B.col(VAR_MINX, 0, 16), 1, tt * 2 + 1)]
        bSd = [emit_bcast(P, B, B.col(VAR_MAXX, 0, 16), 2, tt * 2),
               emit_bcast(P, B, B.col(VAR_MAXX, 0, 16), 3, tt * 2 + 1)]
        if tt == 1 and FFN_T1_STOP == "tail":
            continue
        for m in range(32):
            for sub in range(2):
                hsl = B.HB[:, m * 1024 + sub * 512: m * 1024 + sub * 512 + 512]
                t = B.RT[:, sub * 512: sub * 512 + 512]
                P.op("dve", lambda e, hsl=hsl, t=t, sub=sub: e.tensor_mul(
                    t, hsl, bSh[sub]), reads=[hsl, bSh[sub]], writes=[t])
            hfull = B.HB[:, m * 1024: m * 1024 + 1024]
            tfull = B.RT[:, 0:1024]
            emit_round(P, B, tfull, hfull, 0, n=1024)
        if tt == 1 and FFN_T1_STOP == "round":
            continue
        for sub in range(2):
            qt = tt * 2 + sub
            for k in range(32):
                ws = k % 2
                wdb = B.WB[:, ws * 1024: ws * 1024 + 1024]
                P.dma("sp", wdb, d3[k, :, :], "wg%d" % ws)
                hq = B.HB[:, k * 1024 + sub * 512: k * 1024 + sub * 512 + 512]
                for m8 in range(8):
                    wt = wdb[:, m8 * 128:(m8 + 1) * 128]
                    P.op("pe", lambda e, wt=wt, hq=hq, k=k, m8=m8: e.matmul(
                        B.pb[m8], wt, hq, start=(k == 0), stop=(k == 31)),
                        reads=[wt, hq], writes=[B.pb[m8]])
            for m8 in range(8):
                ev = B.EV[:, (m8 % 2) * 512:(m8 % 2) * 512 + 512]
                P.op("dve", lambda e, m8=m8, ev=ev, sub=sub: e.tensor_mul(
                    ev, B.pb[m8], bSd[sub]),
                    reads=[B.pb[m8], bSd[sub]], writes=[ev])
                if l1:
                    b2h = B.BIA[:, 40 + m8:41 + m8]
                    P.op("pool", lambda e, ev=ev, b2h=b2h: e.tensor_scalar_add(
                        ev, ev, b2h), reads=[ev, B.BIA], writes=[ev])
                xsl = B.xs(m8, qt)
                P.op("dve", lambda e, xsl=xsl, ev=ev: e.tensor_add(xsl, xsl, ev),
                     reads=[ev, xsl], writes=[xsl])



def emit_attn(P, B, layer, qkvT, woT, dscr):
    emit_stats(P, B, ffn=False)
    l1 = layer == 1
    qoff = 8 if l1 else 0
    koff = 0 if l1 else 8
    voff = 16
    qkv = qkvT.ap()
    wo = woT.ap()
    VOFF = 16384
    vones = B.HB[:, VOFF:VOFF + 16640].rearrange("p (s c) -> p s c", c=65)[:, :, 64:65]
    P.op("dve", lambda e: e.memset(vones, 1.0), writes=[B.HB[:, VOFF:VOFF + 16640]])

    def build_u(qt):
        bM = emit_bcast(P, B, B.col(VAR_M, 0, 16), 0, qt)
        bR = emit_bcast(P, B, B.col(VAR_RSTD, 0, 16), 1, qt)
        for c in range(8):
            xsl = B.xs(c, qt)
            ro = (c % 2) * 512
            xm = B.EV[:, ro:ro + 512]
            P.op("dve", lambda e, xsl=xsl, xm=xm, bM=bM: e.tensor_sub(xm, xsl, bM),
                 reads=[xsl, bM], writes=[xm])
            u = B.XQ[:, c * 512:(c + 1) * 512]
            P.op("dve", lambda e, xm=xm, u=u, bR=bR: e.tensor_mul(u, xm, bR),
                 reads=[xm, bR], writes=[u])

    def proj_one(qt, m, bi):
        ws = bi % 2
        wr = B.WR[:, 0:1024] if ws == 0 else B.RT[:, 0:1024]
        P.dma("sp", wr, qkv[m], "wr%d" % ws)
        wb = B.WB[:, 2048 + ws * 1024: 2048 + (ws + 1) * 1024]
        P.op("dve", lambda e, wb=wb, wr=wr: e.tensor_copy(wb, wr),
             reads=[wr], writes=[wb])
        pb = B.pb[ws]
        for k in range(8):
            wt = wb[:, k * 128:(k + 1) * 128]
            u = B.XQ[:, k * 512:(k + 1) * 512]
            P.op("pe", lambda e, wt=wt, u=u, k=k, pb=pb: e.matmul(
                pb, wt, u, start=(k == 0), stop=(k == 7)),
                reads=[wt, u], writes=[pb])
        return pb

    def rope_evac(pb, toff, dst):
        ea = B.BC[:, 1536:2048]
        ebb = B.EV[:, 0:512]
        P.op("dve", lambda e: e.tensor_copy(ea, pb), reads=[pb], writes=[ea])
        P.op("dve", lambda e: e.stream_shuffle(ebb, ea, _SHUF_PAIR),
             reads=[ea], writes=[ebb])
        P.op("dve", lambda e: e.tensor_mul(ea, ea, B.RC[:, toff:toff + 512]),
             reads=[ea, B.RC], writes=[ea])
        P.op("pool", lambda e: e.tensor_mul(ebb, ebb, B.RS[:, toff:toff + 512]),
             reads=[ebb, B.RS], writes=[ebb])
        P.op("dve", lambda e: e.tensor_add(dst, ea, ebb),
             reads=[ea, ebb], writes=[dst])

    for qt in range(4):
        build_u(qt)
        for j in range(8):
            pb = proj_one(qt, koff + j, j)
            dst = B.HB[:, j * 2048 + qt * 512: j * 2048 + qt * 512 + 512]
            if l1:
                rope_evac(pb, qt * 512, dst)
            else:
                P.op("dve", lambda e, dst=dst, pb=pb: e.tensor_copy(dst, pb),
                     reads=[pb], writes=[dst])
        for j in range(8):
            pb = proj_one(qt, voff + j, j)
            ev = B.EV[:, 0:512]
            P.op("dve", lambda e, ev=ev, pb=pb: e.tensor_copy(ev, pb),
                 reads=[pb], writes=[ev])
            for g in range(4):
                kc = qt * 4 + g
                pt = B.pb[6 + (g % 2)]
                P.op("pe", lambda e, g=g, pt=pt: e.transpose(
                    pt[:, 0:128], ev[:, g * 128:(g + 1) * 128], B.IDF),
                    reads=[ev, B.IDF], writes=[pt])
                for hh in range(2):
                    h = 2 * j + hh
                    slot = VOFF + (kc * 16 + h) * 65
                    dst = B.HB[:, slot:slot + 64]
                    P.op("dve", lambda e, dst=dst, pt=pt, hh=hh: e.tensor_copy(
                        dst, pt[:, hh * 64:hh * 64 + 64]),
                        reads=[pt], writes=[dst])
    for qt in range(4):
        build_u(qt)
        for j in range(8):
            pb = proj_one(qt, qoff + j, j)
            dst = B.QT[:, j * 512:(j + 1) * 512]
            if l1:
                rope_evac(pb, qt * 512, dst)
            else:
                P.op("dve", lambda e, dst=dst, pb=pb: e.tensor_copy(dst, pb),
                     reads=[pb], writes=[dst])
        nkc = qt * 4 + 4
        for h in range(NH):
            qslice = B.QT[64 * (h % 2):64 * (h % 2) + 64,
                          (h // 2) * 512:(h // 2) * 512 + 512]
            pav = B.pb[5]
            def sc_mm(kc):
                ps = B.pb[2 + (kc % 2)]
                ksl = B.HB[64 * (h % 2):64 * (h % 2) + 64,
                           (h // 2) * 2048 + kc * 128: (h // 2) * 2048 + kc * 128 + 128]
                P.op("pe", lambda e, ksl=ksl, qslice=qslice, ps=ps: e.matmul(
                    ps, ksl, qslice, start=True, stop=True),
                    reads=[ksl, qslice], writes=[ps])

            def sc_tail(kc):
                ps = B.pb[2 + (kc % 2)]
                p0 = B.CC[:, (kc % 2) * 512:(kc % 2) * 512 + 512]
                P.op("act", lambda e, ps=ps, p0=p0: e.activation(
                    p0, ps, AF.Exp, scale=B.KCONST[:, 1:2]),
                    reads=[ps, B.KCONST], writes=[p0])
                if (not l1) or (kc >= qt * 4):
                    moff = qt * 512 - kc * 128 + 384
                    msl = B.MASK[:, moff:moff + 512]
                    P.op("dve", lambda e, p0=p0, msl=msl: e.tensor_mul(p0, p0, msl),
                         reads=[p0, msl], writes=[p0])
                vsl = B.HB[:, VOFF + (kc * 16 + h) * 65: VOFF + (kc * 16 + h) * 65 + 65]
                P.op("pe", lambda e, vsl=vsl, p0=p0, kc=kc, nkc=nkc, pav=pav: e.matmul(
                    pav[0:65, :], vsl, p0, start=(kc == 0), stop=(kc == nkc - 1)),
                    reads=[vsl, p0], writes=[pav])

            for kc in range(nkc + 1):
                if kc < nkc:
                    sc_mm(kc)
                if kc >= 1:
                    sc_tail(kc - 1)
            drow = B.EV[64:65, 512:1024]
            P.op("dve", lambda e, pav=pav: e.reciprocal(drow, pav[64:65, 0:512]),
                 reads=[pav], writes=[drow])
            dsl = dscr.ap()[h % 4: h % 4 + 1, :]
            P.dma("act", dsl, drow, "dsc%d" % (h % 4))
            db = B.BC[0:64, 1536:2048]
            P.dma("sp", db, dsl.broadcast_to([64, 512]), "dbc")
            if h % 2 == 0:
                ao = B.XQ[0:64, (h // 2) * 512:(h // 2) * 512 + 512]
                P.op("dve", lambda e, ao=ao, db=db, pav=pav: e.tensor_mul(
                    ao, pav[0:64, :], db), reads=[pav, db], writes=[ao])
            else:
                stg = B.CC[0:64, 1024:1536]
                P.op("dve", lambda e, stg=stg, db=db, pav=pav: e.tensor_mul(
                    stg, pav[0:64, :], db), reads=[pav, db], writes=[stg])
                ao = B.XQ[64:128, (h // 2) * 512:(h // 2) * 512 + 512]
                P.dma("pool", ao, stg, "aomv")
        for mo in range(8):
            ws = mo % 2
            wr = B.WR[:, 0:1024] if ws == 0 else B.RT[:, 0:1024]
            P.dma("sp", wr, wo[mo], "wr%d" % ws)
            wb = B.WB[:, 2048 + ws * 1024: 2048 + (ws + 1) * 1024]
            P.op("dve", lambda e, wb=wb, wr=wr: e.tensor_copy(wb, wr),
                 reads=[wr], writes=[wb])
            po = B.pb[6 + ws]
            for k in range(8):
                wt = wb[:, k * 128:(k + 1) * 128]
                ao = B.XQ[:, k * 512:(k + 1) * 512]
                P.op("pe", lambda e, wt=wt, ao=ao, k=k, po=po: e.matmul(
                    po, wt, ao, start=(k == 0), stop=(k == 7)),
                    reads=[wt, ao], writes=[po])
            xsl = B.xs(mo, qt)
            P.op("dve", lambda e, xsl=xsl, po=po: e.tensor_add(xsl, xsl, po),
                 reads=[po, xsl], writes=[xsl])



def emit_input_tm(P, B, x_tm):
    xv = x_tm.ap().rearrange("(n p) d -> n p d", p=128)
    for nt in range(16):
        io = B.BC[:, 0:1024]
        P.dma("sp", io, xv[nt], "xin")
        for c in range(8):
            pt = B.pb[c % 2]
            P.op("pe", lambda e, c=c, io=io, pt=pt: e.transpose(
                pt[:, 0:128], io[:, c * 128:(c + 1) * 128], B.IDF),
                reads=[io, B.IDF], writes=[pt])
            dst = B.X[:, c * 2048 + nt * 128: c * 2048 + (nt + 1) * 128]
            P.op("dve", lambda e, dst=dst, pt=pt: e.tensor_copy(dst, pt[:, 0:128]),
                 reads=[pt], writes=[dst])


def emit_output_tm(P, B, y_tm):
    yv = y_tm.ap().rearrange("(n p) d -> n p d", p=128)
    for nt in range(16):
        io = B.BC[:, 1024:2048]
        for c in range(8):
            src = B.X[:, c * 2048 + nt * 128: c * 2048 + (nt + 1) * 128]
            pt = B.pb[2 + c % 2]
            P.op("pe", lambda e, src=src, pt=pt: e.transpose(
                pt[:, 0:128], src, B.IDF),
                reads=[src, B.IDF], writes=[pt])
            P.op("dve", lambda e, c=c, io=io, pt=pt: e.tensor_copy(
                io[:, c * 128:(c + 1) * 128], pt[:, 0:128]),
                reads=[pt], writes=[io[:, c * 128:(c + 1) * 128]])
        P.dma("pool", yv[nt], io, "yo")


def emit_input_fm(P, B, x_fm):
    xv = x_fm.ap().rearrange("(c p) n -> c p n", p=128)
    for c in range(8):
        P.dma("sp", B.X[:, c * 2048:(c + 1) * 2048], xv[c], "xin%d" % c)


def emit_output_fm(P, B, y_fm):
    yv = y_fm.ap().rearrange("(c p) n -> c p n", p=128)
    for c in range(8):
        P.dma("pool", yv[c], B.X[:, c * 2048:(c + 1) * 2048], "yo")



W_G0, W_U0, W_D0, W_1, W_2 = 0, 1, 2, 3, 4
TKEYS = ["wg3", "wu3", "wd3", "w13", "w23"]
TSHAPES = {"wg3": (1024, 4096), "wu3": (1024, 4096), "wd3": (4096, 1024),
           "w13": (1024, 4096), "w23": (4096, 1024)}
TILED = {"wg3": True, "wu3": True, "wd3": False, "w13": True, "w23": False}


def _tshape(k):
    K, M = TSHAPES[k]
    return [M // 128, 128, K] if TILED[k] else [K, M]
WIDX = {"wg3": W_G0, "wu3": W_U0, "wd3": W_D0, "w13": W_1, "w23": W_2}


def build_prep():
    nc = bass.Bass("TRN2", target_bir_lowering=False, debug=False)
    dins = {"identf": nc.dram_tensor("identf", [128, 128], F32, kind="ExternalInput")}
    wts = {}
    for k in TKEYS:
        wts[k] = (nc.dram_tensor(k + "_in", _tshape(k), F32, kind="ExternalInput"),
                  nc.dram_tensor(k, _tshape(k), BF16, kind="ExternalOutput"))
    scales = nc.dram_tensor("scales", [1, 32], F32, kind="ExternalOutput")
    P = Prog(nc)
    B = Bufs(nc)
    emit_setup(P, B, dins)
    for k in TKEYS:
        K, M = TSHAPES[k]
        emit_weight_prep(P, B, wts[k][0], wts[k][1], WIDX[k], K, M, scales,
                         tiled=TILED[k])
    P.emit()
    nc.finalize()
    return nc


def build_main(stages, x_in, x_out):
    nc = bass.Bass("TRN2", target_bir_lowering=False, debug=False)
    dins = {"identf": nc.dram_tensor("identf", [128, 128], F32, kind="ExternalInput")}
    need = set()
    for st in stages:
        if st[0] == "ffn":
            need.add("scales")
            need |= ({"wg3", "wu3", "wd3"} if st[1] == 0 else {"w13", "w23", "bias"})
        else:
            need |= {"attn%d" % st[1]}
            if st[1] == 1:
                need.add("rope")
    tern = {}
    for k in TKEYS:
        if k in need:
            tern[k] = nc.dram_tensor(k, _tshape(k), BF16, kind="ExternalInput")
    scales = (nc.dram_tensor("scales", [1, 32], F32, kind="ExternalInput")
              if "scales" in need else None)
    if "bias" in need:
        dins["b1c"] = nc.dram_tensor("b1c", [128, 32], F32, kind="ExternalInput")
        dins["b2c"] = nc.dram_tensor("b2c", [128, 8], F32, kind="ExternalInput")
    for L in (0, 1):
        if ("attn%d" % L) in need:
            w = _HC["mask%d" % L].shape[1]
            dins["mask%d" % L] = nc.dram_tensor("mask%d" % L, [128, w], BF16,
                                                kind="ExternalInput")
            dins["qkvT%d" % L] = nc.dram_tensor("qkvT%d" % L, [1024, 3072], F32,
                                                kind="ExternalInput")
            dins["woT%d" % L] = nc.dram_tensor("woT%d" % L, [1024, 1024], F32,
                                               kind="ExternalInput")
    if "rope" in need:
        dins["ropeC"] = nc.dram_tensor("ropeC", [128, 2048], BF16, kind="ExternalInput")
        dins["ropeS"] = nc.dram_tensor("ropeS", [128, 2048], BF16, kind="ExternalInput")
    x_d = nc.dram_tensor("x", [2048, 1024] if x_in == "tm" else [1024, 2048],
                         F32, kind="ExternalInput")
    y_d = nc.dram_tensor("y", [2048, 1024] if x_out == "tm" else [1024, 2048],
                         F32, kind="ExternalOutput")
    dscr = nc.dram_tensor("dscr", [4, 512], F32, kind="Internal")

    P = Prog(nc)
    B = Bufs(nc)
    emit_setup(P, B, dins)
    if scales is not None:
        emit_load_scales(P, B, scales, list(range(5)))
    if "bias" in need:
        emit_load_bias(P, B, dins)
    if "rope" in need:
        P.dma("sp", B.RC, dins["ropeC"].ap(), "ropec")
        P.dma("sp", B.RS, dins["ropeS"].ap(), "ropes")
    if x_in == "tm":
        emit_input_tm(P, B, x_d)
    else:
        emit_input_fm(P, B, x_d)
    for st in stages:
        if st[0] == "ffn":
            if st[1] == 0:
                emit_ffn(P, B, tern["wg3"], tern["wu3"], tern["wd3"],
                         W_G0, W_U0, W_D0, l1=False)
            else:
                emit_ffn(P, B, tern["w13"], None, tern["w23"],
                         W_1, None, W_2, l1=True)
        else:
            L = st[1]
            m = _HC["mask%d" % L]
            P.dma("sp", B.MASK[:, 0:m.shape[1]], dins["mask%d" % L].ap(), "mask")
            emit_attn(P, B, L, dins["qkvT%d" % L], dins["woT%d" % L], dscr)
    if x_out == "tm":
        emit_output_tm(P, B, y_d)
    else:
        emit_output_fm(P, B, y_d)
    P.emit()
    nc.finalize()
    return nc


def build_fused():
    nc = bass.Bass("TRN2", target_bir_lowering=False, debug=False)
    dins = {"identf": nc.dram_tensor("identf", [128, 128], F32, kind="ExternalInput")}
    win = {}
    tern = {}
    for k in TKEYS:
        win[k] = nc.dram_tensor(k + "_in", _tshape(k), F32, kind="ExternalInput")
        tern[k] = nc.dram_tensor(k, _tshape(k), BF16, kind="Internal")
    scales = nc.dram_tensor("scales", [1, 32], F32, kind="Internal")
    dins["b1c"] = nc.dram_tensor("b1c", [128, 32], F32, kind="ExternalInput")
    dins["b2c"] = nc.dram_tensor("b2c", [128, 8], F32, kind="ExternalInput")
    for L in (0, 1):
        w = _HC["mask%d" % L].shape[1]
        dins["mask%d" % L] = nc.dram_tensor("mask%d" % L, [128, w], BF16,
                                            kind="ExternalInput")
        dins["qkvT%d" % L] = nc.dram_tensor("qkvT%d" % L, [24, 128, 1024], F32,
                                            kind="ExternalInput")
        dins["woT%d" % L] = nc.dram_tensor("woT%d" % L, [8, 128, 1024], F32,
                                           kind="ExternalInput")
    dins["ropeC"] = nc.dram_tensor("ropeC", [128, 2048], BF16, kind="ExternalInput")
    dins["ropeS"] = nc.dram_tensor("ropeS", [128, 2048], BF16, kind="ExternalInput")
    x_d = nc.dram_tensor("x", [2048, 1024], F32, kind="ExternalInput")
    y_d = nc.dram_tensor("y", [2048, 1024], F32, kind="ExternalOutput")
    dscr = nc.dram_tensor("dscr", [4, 512], F32, kind="Internal")

    P = Prog(nc)
    B = Bufs(nc)
    emit_setup(P, B, dins)
    emit_load_bias(P, B, dins)
    emit_input_tm(P, B, x_d)
    for k in TKEYS:
        K, M = TSHAPES[k]
        emit_weight_prep(P, B, win[k], tern[k], WIDX[k], K, M, scales,
                         tiled=TILED[k])
        P.seal_tensor(k, "tst%d" % WIDX[k])
    for st in [("ffn", 0), ("attn", 0), ("ffn", 0), ("ffn", 1), ("attn", 1),
               ("ffn", 1)]:
        if st[0] == "ffn":
            if st[1] == 0:
                emit_ffn(P, B, tern["wg3"], tern["wu3"], tern["wd3"],
                         W_G0, W_U0, W_D0, l1=False)
            else:
                emit_ffn(P, B, tern["w13"], None, tern["w23"],
                         W_1, None, W_2, l1=True)
        else:
            L = st[1]
            m = _HC["mask%d" % L]
            P.dma("sp", B.MASK[:, 0:m.shape[1]], dins["mask%d" % L].ap(), "mask")
            if L == 1:
                P.dma("sp", B.RC, dins["ropeC"].ap(), "ropec")
                P.dma("sp", B.RS, dins["ropeS"].ap(), "ropes")
            emit_attn(P, B, L, dins["qkvT%d" % L], dins["woT%d" % L], dscr)
    emit_output_tm(P, B, y_d)
    P.emit()
    nc.finalize()
    return nc



_NC_CACHE = {}


def _get(key, fn):
    if key not in _NC_CACHE:
        _NC_CACHE[key] = fn()
    return _NC_CACHE[key]


def kernel(L0_attn_x, L0_attn_qkv, L0_attn_wo, L0_ffn_w_gate, L0_ffn_w_up,
           L0_ffn_w_down, L1_attn_W_KQV, L1_attn_W_O, L1_ffn_W1, L1_ffn_b1,
           L1_ffn_W2, L1_ffn_b2):
    x = np.ascontiguousarray(np.asarray(L0_attn_x, np.float32))
    tw = lambda w: np.ascontiguousarray(np.asarray(w, np.float32).T)

    def tile5(wT):
        K, M = wT.shape
        return np.ascontiguousarray(
            wT.reshape(K // 128, 128, M // 128, 128).transpose(2, 1, 0, 3)
            .reshape(M // 128, 128, K))

    wgT, wuT, wdT = tile5(tw(L0_ffn_w_gate)), tile5(tw(L0_ffn_w_up)), tw(L0_ffn_w_down)
    w1T, w2T = tile5(tw(L1_ffn_W1)), tw(L1_ffn_W2)
    qkvT0, woT0 = tile5(tw(L0_attn_qkv)), tile5(tw(L0_attn_wo))
    qkvT1, woT1 = tile5(tw(L1_attn_W_KQV)), tile5(tw(L1_attn_W_O))
    b1c = np.ascontiguousarray(np.asarray(L1_ffn_b1, np.float32).reshape(32, 128).T)
    b2c = np.ascontiguousarray(np.asarray(L1_ffn_b2, np.float32).reshape(8, 128).T)
    consts = _HC

    if FUSED:
        ncf = _get("fused", build_fused)
        in_maps = []
        for i in range(8):
            mm = {"identf": consts["identf"], "b1c": b1c, "b2c": b2c,
                  "ropeC": consts["ropeC"], "ropeS": consts["ropeS"],
                  "mask0": np.ascontiguousarray(consts["mask0"]),
                  "mask1": np.ascontiguousarray(consts["mask1"]),
                  "qkvT0": qkvT0, "woT0": woT0, "qkvT1": qkvT1, "woT1": woT1,
                  "x": np.ascontiguousarray(x[i])}
            mm.update({k + "_in": w for k, w in
                       zip(TKEYS, [wgT, wuT, wdT, w1T, w2T])})
            in_maps.append(mm)
        res = run_bass_kernel_spmd(ncf, in_maps, core_ids=list(range(8)))
        return np.stack([res.results[i]["y"] for i in range(8)], axis=0).astype(np.float32)

    ncp = _get("prep", build_prep)
    inp = {"identf": consts["identf"]}
    inp.update({k + "_in": w for k, w in zip(TKEYS, [wgT, wuT, wdT, w1T, w2T])})
    resp = run_bass_kernel_spmd(ncp, [inp] * 8, core_ids=list(range(8)))
    tern = {k: resp.results[0][k] for k in TKEYS}
    scales = resp.results[0]["scales"]

    common = {"identf": consts["identf"]}
    stage_cfgs = [
        ([("ffn", 0)], "tm", "fm"),
        ([("attn", 0)], "fm", "fm"),
        ([("ffn", 0)], "fm", "fm"),
        ([("ffn", 1)], "fm", "fm"),
        ([("attn", 1)], "fm", "fm"),
        ([("ffn", 1)], "fm", "tm"),
    ]
    cur = [np.ascontiguousarray(x[i]) for i in range(8)]
    for si, (stages, xin, xout) in enumerate(stage_cfgs):
        ncm = _get(("main", si), lambda stages=stages, xin=xin, xout=xout:
                   build_main(stages, xin, xout))
        in_maps = []
        for i in range(8):
            m = dict(common)
            st = stages[0]
            if st[0] == "ffn":
                m["scales"] = scales
                if st[1] == 0:
                    m.update({k: tern[k] for k in ("wg3", "wu3", "wd3")})
                else:
                    m.update({k: tern[k] for k in ("w13", "w23")})
                    m.update({"b1c": b1c, "b2c": b2c})
            else:
                L = st[1]
                m["mask%d" % L] = np.ascontiguousarray(consts["mask%d" % L])
                m["qkvT%d" % L] = qkvT0 if L == 0 else qkvT1
                m["woT%d" % L] = woT0 if L == 0 else woT1
                if L == 1:
                    m["ropeC"] = consts["ropeC"]
                    m["ropeS"] = consts["ropeS"]
            m["x"] = np.ascontiguousarray(cur[i])
            in_maps.append(m)
        res = run_bass_kernel_spmd(ncm, in_maps, core_ids=list(range(8)))
        cur = [res.results[i]["y"] for i in range(8)]
    return np.stack(cur, axis=0).astype(np.float32)


# version 10
# speedup vs baseline: 1.3592x; 1.0726x over previous
import numpy as np
import ml_dtypes
import concourse.bass as bass
import concourse.mybir as mybir
from concourse.bass_utils import run_bass_kernel_spmd

F32 = mybir.dt.float32
BF16 = mybir.dt.bfloat16
I32 = mybir.dt.int32
AF = mybir.ActivationFunctionType
ALU = mybir.AluOpType
AX = mybir.AxisListType
BFNP = ml_dtypes.bfloat16

S, DM, NH, DH, INNER = 2048, 1024, 16, 64, 4096
FFN_TTS = range(2)
FFN_T1_STOP = None
FUSED = True


def host_constants():
    c = {}
    c["identf"] = np.eye(128, dtype=np.float32)
    jj = np.arange(128)[:, None]
    uu = np.arange(2432)[None, :] - 384
    d = uu - jj
    allowed = ((d >= 0) & (d <= 127)) | ((d >= 132) & ((d - 128) % 4 == 0))
    c["mask0"] = allowed.astype(BFNP)
    uu1 = np.arange(896)[None, :] - 384
    c["mask1"] = ((uu1 - jj) >= 0).astype(BFNP)
    r = np.arange(0, DH, 2, dtype=np.float64)
    invf = 1.0 / (10000.0 ** (r / DH))
    fr = np.arange(S, dtype=np.float64)[:, None] * invf[None, :]
    cos = np.cos(fr); sin = np.sin(fr)
    C = np.repeat(cos, 2, axis=1)
    S2 = np.empty((S, DH)); S2[:, 0::2] = -sin; S2[:, 1::2] = sin
    c["ropeC"] = np.tile(C.T, (2, 1)).astype(BFNP)
    c["ropeS"] = np.tile(S2.T, (2, 1)).astype(BFNP)
    return c

_HC = host_constants()
_SHUF_PAIR = []
for _i in range(0, 32, 2):
    _SHUF_PAIR.extend([_i + 1, _i])


QSEM = {"sp": "qsp", "act": "qact", "pool": "qpool"}


def _apkeys(x):
    if not hasattr(x, "tensor"):
        return (x,)
    t = x.tensor
    nm = t.name
    dims = list(x.ap)
    try:
        import bass_rust
        is_dram = x.space == getattr(bass_rust, "MemorySpace").DRAM
    except Exception:
        is_dram = str(getattr(x, "space", "")).find("DRAM") >= 0
    lo = int(x.offset)
    if not is_dram:
        fw = 1
        for d in list(t.shape)[1:]:
            fw *= int(d)
        lo = lo % fw
    hi = lo
    use = dims if is_dram else dims[1:] if len(dims) > 1 else []
    for st, ct in use:
        hi += (int(ct) - 1) * int(st)
    hi += 1
    gran = 65536 if is_dram else 512
    return tuple((nm, i) for i in range(lo // gran, (hi + gran - 1) // gran))


class Prog:
    def __init__(self, nc):
        self.nc = nc
        self.sem = {}
        for e in ("pe", "dve", "act", "pool"):
            self.sem[e] = nc.alloc_semaphore("S" + e)
        self.cnt = {e: 0 for e in self.sem}
        self.q = {e: [] for e in ("pe", "dve", "act", "pool", "sp")}
        self.hwm = {e: {} for e in self.q}
        self.res = {}

    def _stream(self, name):
        key = ("dma", name)
        if key not in self.sem:
            self.sem[key] = self.nc.alloc_semaphore("Sd_" + name)
            self.cnt[key] = 0
        return key

    def _expand(self, items):
        out = []
        for it in items:
            out.extend(_apkeys(it))
        return out

    def _need(self, eng, own_sem, reads, writes):
        need = {}

        def add(s, v):
            if s == own_sem and v > self.cnt.get(own_sem, 0):
                return
            need[s] = max(need.get(s, 0), v)

        for k in reads:
            r = self.res.get(k)
            if r and r["w"] is not None:
                add(*r["w"])
        for k in writes:
            r = self.res.get(k)
            if r:
                if r["w"] is not None:
                    add(*r["w"])
                for s, v in r["r"].items():
                    add(s, v)
        hw = self.hwm[eng]
        waits = []
        for s, v in sorted(need.items(), key=lambda kv: str(kv[0])):
            if hw.get(s, 0) < v:
                waits.append((s, v))
                hw[s] = v
        return waits

    def _record(self, own_sem, tick, reads, writes):
        for k in writes:
            self.res[k] = {"w": (own_sem, tick), "r": {}}
        for k in reads:
            r = self.res.get(k)
            if r is None:
                r = {"w": None, "r": {}}
                self.res[k] = r
            r["r"][own_sem] = max(r["r"].get(own_sem, 0), tick)

    def op(self, eng, fn, reads=(), writes=(), inc=True):
        import sys as _s
        f = _s._getframe(1)
        self._dbg = "%s:%d" % (f.f_code.co_name, f.f_lineno)
        own = eng
        reads = self._expand(reads)
        writes = self._expand(writes)
        waits = self._need(eng, own, reads, writes)
        if inc:
            self.cnt[eng] += 1
            tick = self.cnt[eng]
        else:
            tick = self.cnt[eng] + 1
        self.q[eng].append((waits, fn, own if inc else None, self._dbg))
        self._record(own, tick, reads, writes)

    def dma(self, qeng, dst, src, stream, reads=(), writes=(), **kw):
        own = self._stream(stream)
        reads = self._expand(tuple(reads) + (src,))
        writes = self._expand(tuple(writes) + (dst,))
        waits = self._need(qeng, own, reads, writes)
        self.cnt[own] += 1
        tick = 16 * self.cnt[own]
        import sys as _s
        f = _s._getframe(1)
        dbg = "%s:%d" % (f.f_code.co_name, f.f_lineno)
        self.q[qeng].append(
            (waits, lambda e, d=dst, s=src, k=dict(kw): e.dma_start(d, s, **k), own, dbg))
        self._record(own, tick, reads, writes)

    def seal_tensor(self, name, stream):
        key = ("dma", stream)
        tick = 16 * self.cnt[key]
        for k, r in self.res.items():
            if isinstance(k, tuple) and len(k) == 2 and k[0] == name:
                r["w"] = (key, tick)

    def check_deadlock(self):
        pos = {e: 0 for e in self.q}
        val = {s: 0 for s in self.cnt}
        progressed = True
        while progressed:
            progressed = False
            for e, lst in self.q.items():
                while pos[e] < len(lst):
                    waits, _fn, inc_sem = lst[pos[e]][0], lst[pos[e]][1], lst[pos[e]][2]
                    if all(val.get(s, 0) >= v for s, v in waits):
                        if inc_sem is not None:
                            val[inc_sem] = val.get(inc_sem, 0) + (
                                16 if isinstance(inc_sem, tuple) else 1)
                        pos[e] += 1
                        progressed = True
                    else:
                        break
        stuck = {e: pos[e] for e in self.q if pos[e] < len(self.q[e])}
        if stuck:
            msg = []
            for e, p in stuck.items():
                waits, _f, _i, dbg = self.q[e][p]
                bad = [(s, v, val.get(s, 0)) for s, v in waits if val.get(s, 0) < v]
                msg.append(f"{e}@{p} [{dbg}] needs {bad}")
            raise RuntimeError("build-time deadlock: " + " | ".join(msg))

    def emit(self):
        nc = self.nc
        sems = self.sem
        final = dict(self.cnt)

        def mk(lst, eng_name):
            def body(engine):
                for waits, fn, inc_sem, _dbg in lst:
                    for s, v in waits:
                        engine.wait_ge(sems[s], v)
                    ins = fn(engine)
                    if inc_sem is not None:
                        amt = 16 if isinstance(inc_sem, tuple) else 1
                        ins.then_inc(sems[inc_sem], amt)
                if eng_name == "pool":
                    for s, v in final.items():
                        tgt = v * 16 if isinstance(s, tuple) else v
                        if tgt > 0:
                            engine.wait_ge(sems[s], tgt)
            return body

        self.check_deadlock()
        with nc.Block() as blk:
            blk.sync(mk(self.q["sp"], "sp"))
            blk.tensor(mk(self.q["pe"], "pe"))
            blk.vector(mk(self.q["dve"], "dve"))
            blk.scalar(mk(self.q["act"], "act"))
            blk.gpsimd(mk(self.q["pool"], "pool"))



class Bufs:
    def __init__(self, nc):
        A = nc.alloc_sbuf_tensor
        self.X = A("X", [128, 16384], F32).ap()
        self.HB = A("HB", [128, 33024], BF16).ap()
        self.XQ = A("XQb", [128, 8192], BF16).ap()
        self.QT = A("QTb", [128, 4096], BF16).ap()
        self.WB = A("WBb", [128, 4096], BF16).ap()
        self.WR = A("WRb", [128, 1024], F32).ap()
        self.BC = A("BCb", [128, 2048], F32).ap()
        self.EV = A("EVb", [128, 1280], F32).ap()
        self.ACC = A("ACCb", [128, 1024], BF16).ap()
        self.RT = A("RTb", [128, 2048], F32).ap()
        self.RI = A("RIb", [128, 1024], I32).ap()
        self.CC = A("CCb", [128, 2048], BF16).ap()
        self.AUX = A("AUXb", [128, 4992], BF16).ap()
        self.MASK = self.AUX
        self.RC = self.AUX[:, 896:2944]
        self.RS = self.AUX[:, 2944:4992]
        self.COLS = A("COLSb", [128, 160], F32).ap()
        self.CB = A("CBb", [128, 128], F32).ap()
        self.IDF = A("IDFb", [128, 128], F32).ap()
        self.ONE1 = A("ONE1b", [128, 1], BF16).ap()
        self.KCONST = A("KCONSTb", [128, 3], F32).ap()
        self.MWC = A("MWCb", [128, 10], F32).ap()
        self.BIA = A("BIAb", [128, 48], F32).ap()
        self.SCR = A("SCRb", [1, 32], F32).ap()
        self.pbw = [nc.alloc_psum_tensor(f"pw{i}", [128, 1024], F32).ap() for i in range(4)]
        self.pb = [self.pbw[i // 2][:, (i % 2) * 512:(i % 2) * 512 + 512] for i in range(8)]

    def col(self, v, idx, n=1):
        return self.COLS[:, v * 16 + idx: v * 16 + idx + n]

    def xs(self, c, qt):
        return self.X[:, c * 2048 + qt * 512: c * 2048 + qt * 512 + 512]


(VAR_M, VAR_RSTD, VAR_S1, VAR_S2, VAR_MAXX, VAR_MINX,
 VAR_ACOL, VAR_INVSX, VAR_SH, VAR_TMP) = range(10)
KC_EPS, KC_EIGHTH = 0, 1


def emit_setup(P, B, dins):
    P.dma("sp", B.IDF, dins["identf"].ap(), "ldid")
    P.op("dve", lambda e: e.memset(B.CB, 0.0), writes=[B.CB])
    P.op("dve", lambda e: e.memset(B.ONE1, 1.0), writes=[B.ONE1])
    P.op("dve", lambda e: e.memset(B.KCONST[:, 0:1], 1e-5), writes=[B.KCONST])
    P.op("dve", lambda e: e.memset(B.KCONST[:, 1:2], 0.125), writes=[B.KCONST])
    P.op("dve", lambda e: e.memset(B.KCONST[:, 2:3], 1.0 / 127.0), writes=[B.KCONST])
    P.op("dve", lambda e: e.memset(B.COLS, 0.0), writes=[B.COLS])



def emit_weight_prep(P, B, wT, scratch, widx, K, M, scr_dram, tiled=False):
    if tiled:
        nkc, nmb = M // 128, K // 1024
        wsrc = wT.ap()
        wdst = scratch.ap()
    else:
        nkc, nmb = K // 128, M // 1024
        wsrc = wT.ap().rearrange("(k p) m -> k p m", p=128)
        wdst = scratch.ap().rearrange("(k p) m -> k p m", p=128)
    acc = B.col(VAR_TMP, 0)
    tmpc = B.col(VAR_TMP, 1)
    P.op("dve", lambda e: e.memset(acc, 0.0), writes=[acc])
    bi = 0
    for kc in range(nkc):
        for mb in range(nmb):
            wr = B.WR[:, 0:1024] if bi % 2 == 0 else B.RT[:, 1024:2048]
            tm2 = B.col(VAR_TMP, 1 + bi % 2)
            P.dma("sp", wr, wsrc[kc, :, mb * 1024:(mb + 1) * 1024],
                  "wr%d" % (bi % 2))
            P.op("dve", lambda e, wr=wr, tm2=tm2: e.reduce_sum(
                tm2, wr, AX.X, apply_absolute_value=True),
                reads=[wr], writes=[tm2])
            P.op("dve", lambda e, tm2=tm2: e.tensor_add(acc, acc, tm2),
                 reads=[tm2, acc], writes=[acc])
            bi += 1
    P.op("pe", lambda e: e.transpose(B.pb[7][0:1, 0:128], acc, B.IDF),
         reads=[acc, B.IDF], writes=[B.pb[7]])
    P.op("dve", lambda e: e.tensor_copy(B.EV[0:1, 0:128], B.pb[7][0:1, 0:128]),
         reads=[B.pb[7]], writes=[B.EV[0:1, 0:128]])
    P.op("dve", lambda e: e.reduce_sum(B.EV[0:1, 512:513], B.EV[0:1, 0:128], AX.X),
         reads=[B.EV[0:1, 0:128]], writes=[B.EV[0:1, 512:513]])
    P.op("dve", lambda e: e.tensor_scalar(
        B.SCR[0:1, widx:widx + 1], B.EV[0:1, 512:513], 1.0 / (K * M), 1e-5,
        ALU.mult, ALU.max),
        reads=[B.EV[0:1, 512:513]], writes=[B.SCR[0:1, widx:widx + 1]])
    P.op("dve", lambda e: e.reciprocal(B.SCR[0:1, 16 + widx:17 + widx],
                                       B.SCR[0:1, widx:widx + 1]),
         reads=[B.SCR[0:1, widx:widx + 1]], writes=[B.SCR[0:1, 16 + widx:17 + widx]])
    P.dma("pool", scr_dram.ap()[0:1, widx:widx + 1], B.SCR[0:1, widx:widx + 1],
          "scs%d" % widx)
    P.dma("pool", scr_dram.ap()[0:1, 16 + widx:17 + widx],
          B.SCR[0:1, 16 + widx:17 + widx], "scs%d" % widx)
    emit_load_scales(P, B, scr_dram, [widx])
    bi = 0
    for kc in range(nkc):
        for mb in range(nmb):
            sl = bi % 2
            wr = B.WR[:, 0:1024] if sl == 0 else B.RT[:, 1024:2048]
            P.dma("sp", wr, wsrc[kc, :, mb * 1024:(mb + 1) * 1024], "wr%d" % sl)
            vv = B.EV[:, 0:1024] if sl == 0 else B.BC[:, 0:1024]
            P.op("dve", lambda e, wr=wr, vv=vv: e.tensor_scalar_mul(
                vv, wr, B.MWC[:, 5 + widx:6 + widx]),
                reads=[wr, B.MWC], writes=[vv])
            c1 = B.CC[:, sl * 1024:(sl + 1) * 1024]
            P.op("dve", lambda e, vv=vv, c1=c1: e.tensor_single_scalar(
                c1, vv, 0.5, ALU.is_gt), reads=[vv], writes=[c1])
            c2 = B.QT[:, sl * 1024:(sl + 1) * 1024]
            P.op("dve", lambda e, vv=vv, c2=c2: e.tensor_single_scalar(
                c2, vv, -0.5, ALU.is_lt), reads=[vv], writes=[c2])
            wb = B.WB[:, sl * 1024:(sl + 1) * 1024]
            P.op("dve", lambda e, c1=c1, c2=c2, wb=wb: e.tensor_sub(wb, c1, c2),
                 reads=[c1, c2], writes=[wb])
            P.dma("pool", wdst[kc, :, mb * 1024:(mb + 1) * 1024], wb, "tst%d" % widx)
            bi += 1


def emit_load_scales(P, B, scr_dram, widxs):
    for w in widxs:
        P.dma("sp", B.MWC[:, w:w + 1],
              scr_dram.ap()[0:1, w:w + 1].broadcast_to([128, 1]), "mwc")
        P.dma("sp", B.MWC[:, 5 + w:6 + w],
              scr_dram.ap()[0:1, 16 + w:17 + w].broadcast_to([128, 1]), "mwc")


def emit_load_bias(P, B, dins):
    P.dma("sp", B.BIA[:, 0:32], dins["b1c"].ap(), "bias")
    P.dma("sp", B.BIA[:, 32:40], dins["b2c"].ap(), "bias")
    P.op("dve", lambda e: e.tensor_scalar_mul(B.BIA[:, 40:48], B.BIA[:, 32:40], 0.5),
         reads=[B.BIA[:, 32:40]], writes=[B.BIA[:, 40:48]])



def emit_stats(P, B, ffn):
    for qt in range(4):
        for c in range(8):
            xsl = B.xs(c, qt)
            slot = c % 2
            xb = B.QT[:, slot * 512:(slot + 1) * 512]
            sq = B.QT[:, 1024 + slot * 512: 1024 + (slot + 1) * 512]
            P.op("dve", lambda e, xb=xb, xsl=xsl: e.tensor_copy(xb, xsl),
                 reads=[xsl], writes=[xb])
            P.op("act", lambda e, sq=sq, xsl=xsl: e.activation(sq, xsl, AF.Square),
                 reads=[xsl], writes=[sq])
            P.op("pe", lambda e, xb=xb, c=c: e.matmul(
                B.pb[5][0:1, :], B.ONE1, xb, start=(c == 0), stop=(c == 7)),
                reads=[xb, B.ONE1], writes=[B.pb[5]])
            P.op("pe", lambda e, sq=sq, c=c: e.matmul(
                B.pb[6][0:1, :], B.ONE1, sq, start=(c == 0), stop=(c == 7)),
                reads=[sq, B.ONE1], writes=[B.pb[6]])
        r1 = B.EV[0:1, 0:512]
        r2 = B.EV[0:1, 512:1024]
        P.op("dve", lambda e: e.tensor_copy(r1, B.pb[5][0:1, :]),
             reads=[B.pb[5]], writes=[r1])
        P.op("dve", lambda e: e.tensor_copy(r2, B.pb[6][0:1, :]),
             reads=[B.pb[6]], writes=[r2])
        for g in range(4):
            P.op("pe", lambda e, g=g: e.transpose(
                B.pb[7][:, g:g + 1], B.EV[0:1, g * 128:(g + 1) * 128], B.IDF[0:1, 0:1]),
                reads=[r1, B.IDF], writes=[B.pb[7]], inc=False)
            P.op("pe", lambda e, g=g: e.transpose(
                B.pb[7][:, 4 + g:5 + g],
                B.EV[0:1, 512 + g * 128:512 + (g + 1) * 128], B.IDF[0:1, 0:1]),
                reads=[r2, B.IDF], writes=[B.pb[7]])
        P.op("dve", lambda e, qt=qt: e.tensor_copy(
            B.col(VAR_S1, qt * 4, 4), B.pb[7][:, 0:4]),
            reads=[B.pb[7]], writes=[B.col(VAR_S1, qt * 4, 4)])
        P.op("dve", lambda e, qt=qt: e.tensor_copy(
            B.col(VAR_S2, qt * 4, 4), B.pb[7][:, 4:8]),
            reads=[B.pb[7]], writes=[B.col(VAR_S2, qt * 4, 4)])
        if ffn:
            ea = B.EV[:, 0:512]
            eb = B.EV[:, 512:1024]
            xview = B.X.rearrange("p (c t) -> p t c", c=8)[:, qt * 512:(qt + 1) * 512, :]
            xkeys = [B.xs(c2_, qt) for c2_ in range(8)] + [r1, r2]
            P.op("dve", lambda e, xview=xview: e.reduce_max(ea, xview, AX.X),
                 reads=xkeys, writes=[ea])
            P.op("dve", lambda e, xview=xview: e.tensor_reduce(
                eb, xview, AX.X, ALU.min), reads=xkeys, writes=[eb])
            for g in range(4):
                P.op("pe", lambda e, g=g: e.transpose(
                    B.pb[7][:, g * 128:(g + 1) * 128], ea[:, g * 128:(g + 1) * 128],
                    B.IDF),
                    reads=[ea, B.IDF], writes=[B.pb[7]])
            for g in range(4):
                P.op("dve", lambda e, g=g, qt=qt: e.reduce_max(
                    B.col(VAR_MAXX, qt * 4 + g), B.pb[7][:, g * 128:(g + 1) * 128],
                    AX.X),
                    reads=[B.pb[7]], writes=[B.col(VAR_MAXX, qt * 4 + g)])
            for g in range(4):
                P.op("pe", lambda e, g=g: e.transpose(
                    B.pb[7][:, g * 128:(g + 1) * 128], eb[:, g * 128:(g + 1) * 128],
                    B.IDF),
                    reads=[eb, B.IDF], writes=[B.pb[7]])
            for g in range(4):
                P.op("dve", lambda e, g=g, qt=qt: e.tensor_reduce(
                    B.col(VAR_MINX, qt * 4 + g), B.pb[7][:, g * 128:(g + 1) * 128],
                    AX.X, ALU.min),
                    reads=[B.pb[7]], writes=[B.col(VAR_MINX, qt * 4 + g)])
    m16 = B.col(VAR_M, 0, 16)
    rst = B.col(VAR_RSTD, 0, 16)
    tmp = B.col(VAR_TMP, 0, 16)
    s1 = B.col(VAR_S1, 0, 16)
    s2 = B.col(VAR_S2, 0, 16)
    P.op("dve", lambda e: e.tensor_scalar_mul(m16, s1, 1.0 / 1024.0),
         reads=[s1], writes=[m16])
    P.op("dve", lambda e: e.tensor_mul(tmp, m16, m16), reads=[m16], writes=[tmp])
    P.op("dve", lambda e: e.scalar_tensor_tensor(
        rst, s2, 1.0 / 1024.0, tmp, ALU.mult, ALU.subtract),
        reads=[s2, tmp], writes=[rst])
    P.op("act", lambda e: e.activation(rst, rst, AF.Sqrt, bias=B.KCONST[:, 0:1]),
         reads=[rst, B.KCONST], writes=[rst])
    P.op("dve", lambda e: e.reciprocal(rst, rst), reads=[rst], writes=[rst])
    if ffn:
        mx = B.col(VAR_MAXX, 0, 16)
        mn = B.col(VAR_MINX, 0, 16)
        a16 = B.col(VAR_ACOL, 0, 16)
        iv = B.col(VAR_INVSX, 0, 16)
        P.op("dve", lambda e: e.tensor_sub(mx, mx, m16), reads=[mx, m16], writes=[mx])
        P.op("dve", lambda e: e.tensor_sub(mn, m16, mn), reads=[mn, m16], writes=[mn])
        P.op("dve", lambda e: e.tensor_max(mx, mx, mn), reads=[mx, mn], writes=[mx])
        P.op("dve", lambda e: e.reciprocal(a16, mx), reads=[mx], writes=[a16])
        P.op("dve", lambda e: e.tensor_scalar_mul(a16, a16, 127.0),
             reads=[a16], writes=[a16])
        P.op("dve", lambda e: e.tensor_mul(iv, rst, mx), reads=[rst, mx], writes=[iv])
        P.op("dve", lambda e: e.tensor_scalar(iv, iv, 1e-5, 1.0 / 127.0,
                                              ALU.max, ALU.mult),
             reads=[iv], writes=[iv])


def emit_bcast(P, B, colap, slot, qt, pbi=4):
    pb = B.pb[pbi]
    for g in range(4):
        cs = colap[:, qt * 4 + g: qt * 4 + g + 1]
        P.op("act", lambda e, cs=cs: e.activation(B.CB, B.CB, AF.Identity,
                                                  bias=cs, scale=0.0),
             reads=[cs, B.CB], writes=[B.CB])
        P.op("pe", lambda e, g=g, pb=pb: e.transpose(
            pb[:, g * 128:(g + 1) * 128], B.CB, B.IDF),
            reads=[B.CB, B.IDF], writes=[pb])
    bc = B.BC[:, slot * 512:(slot + 1) * 512]
    P.op("dve", lambda e, bc=bc, pb=pb: e.tensor_copy(bc, pb),
         reads=[pb], writes=[bc])
    return bc


def emit_round(P, B, src, dst, ro, n=512, pair=False, dkeys=None):
    yi = B.RI[:, ro:ro + n]
    yf = B.RT[:, 1024 + ro:1024 + ro + n]
    c1 = B.CC[:, ro:ro + n]
    c2 = B.CC[:, 1024 + ro:1024 + ro + n]
    P.op("dve", lambda e: e.tensor_scalar(yi, src, 1.0, None, ALU.mult),
         reads=[src], writes=[yi])
    P.op("dve", lambda e: e.tensor_copy(yf, yi), reads=[yi], writes=[yf])
    P.op("dve", lambda e: e.scalar_tensor_tensor(
        c1, src, 0.5, yf, ALU.subtract, ALU.is_gt),
        reads=[src, yf], writes=[c1])
    P.op("dve", lambda e: e.scalar_tensor_tensor(
        c2, src, 0.5, yf, ALU.add, ALU.is_lt),
        reads=[src, yf], writes=[c2])
    P.op("dve", lambda e: e.tensor_add(yf, yf, c1), reads=[yf, c1], writes=[yf])
    if pair:
        yfv = yf.rearrange("p (a b) -> p a b", a=2)
        c2v = c2.rearrange("p (a b) -> p a b", a=2)
    else:
        yfv, c2v = yf, c2
    P.op("dve", lambda e: e.tensor_sub(dst, yfv, c2v), reads=[yf, c2],
         writes=(dkeys if dkeys is not None else [dst]))



def emit_ffn(P, B, tern_g, tern_u, tern_d, wg_i, wu_i, wd_i, l1):
    emit_stats(P, B, ffn=True)
    g3 = tern_g.ap()
    u3 = tern_u.ap() if tern_u is not None else None
    d3 = tern_d.ap().rearrange("(k p) m -> k p m", p=128)
    cgc = B.col(VAR_TMP, 0, 16)
    P.op("dve", lambda e: e.tensor_scalar_mul(
        cgc, B.col(VAR_INVSX, 0, 16), B.MWC[:, wg_i:wg_i + 1]),
        reads=[B.col(VAR_INVSX, 0, 16), B.MWC], writes=[cgc])
    for tt in FFN_TTS:
        for sub in range(2):
            qt = tt * 2 + sub
            bM = emit_bcast(P, B, B.col(VAR_M, 0, 16), 0, qt)
            P.op("dve", lambda e: e.tensor_copy(B.BC[:, 512:1024], B.BC[:, 0:512]),
                 reads=[B.BC[:, 0:512]], writes=[B.BC[:, 512:1024]])
            bA = emit_bcast(P, B, B.col(VAR_ACOL, 0, 16), 2, qt)
            P.op("dve", lambda e: e.tensor_copy(B.BC[:, 1536:2048],
                                                B.BC[:, 1024:1536]),
                 reads=[B.BC[:, 1024:1536]], writes=[B.BC[:, 1536:2048]])
            bMd = B.BC[:, 0:1024].rearrange("p (a b) -> p a b", a=2)
            bAd = B.BC[:, 1024:2048].rearrange("p (a b) -> p a b", a=2)
            bMk = [B.BC[:, 0:512], B.BC[:, 512:1024]]
            bAk = [B.BC[:, 1024:1536], B.BC[:, 1536:2048]]
            for c in range(0, 8, 2):
                xsl3 = B.X.rearrange("p (c t) -> p c t", c=8)[
                    :, c:c + 2, qt * 512:(qt + 1) * 512]
                xm3 = B.EV[:, 0:1024].rearrange("p (a b) -> p a b", a=2)
                t2 = B.RT[:, 0:1024]
                t3 = t2.rearrange("p (a b) -> p a b", a=2)
                xkeys = [B.xs(c, qt), B.xs(c + 1, qt)]
                P.op("dve", lambda e, xsl3=xsl3, xm3=xm3, bMd=bMd: e.tensor_sub(
                    xm3, xsl3, bMd),
                    reads=xkeys + bMk, writes=[B.EV[:, 0:1024]])
                P.op("dve", lambda e, xm3=xm3, t3=t3, bAd=bAd: e.tensor_mul(
                    t3, xm3, bAd),
                    reads=[B.EV[:, 0:1024]] + bAk, writes=[t2])
                xq3 = B.XQ.rearrange("p (k t) -> p k t", k=8)[
                    :, c:c + 2, sub * 512:(sub + 1) * 512]
                xqk = [B.XQ[:, c * 1024 + sub * 512: c * 1024 + sub * 512 + 512],
                       B.XQ[:, (c + 1) * 1024 + sub * 512:
                            (c + 1) * 1024 + sub * 512 + 512]]
                emit_round(P, B, t2, xq3, 0, n=1024, pair=True, dkeys=xqk)
        if tt == 1 and FFN_T1_STOP == "xq":
            continue
        bCg = [emit_bcast(P, B, cgc, 2, tt * 2), emit_bcast(P, B, cgc, 3, tt * 2 + 1)]
        am0 = B.ACC[:, 0:1024]
        an0 = B.CC[:, 1024:2048]
        def gu_mms(m):
            ws = m % 2
            gblk = B.WB[:, ws * 1024: ws * 1024 + 1024]
            P.dma("sp", gblk, g3[m], "wg%d" % ws)
            pg = B.pbw[(m % 2) * 2]
            if not l1:
                ublk = B.WB[:, 2048 + ws * 1024: 2048 + ws * 1024 + 1024]
                P.dma("sp", ublk, u3[m], "wu%d" % ws)
                pu = B.pbw[(m % 2) * 2 + 1]
            for sub in range(2):
                dstg = pg[:, sub * 512: sub * 512 + 512]
                for k in range(8):
                    wt = gblk[:, k * 128:(k + 1) * 128]
                    xq = B.XQ[:, k * 1024 + sub * 512: k * 1024 + sub * 512 + 512]
                    P.op("pe", lambda e, wt=wt, xq=xq, k=k, dstg=dstg: e.matmul(
                        dstg, wt, xq, start=(k == 0), stop=(k == 7)),
                        reads=[wt, xq], writes=[dstg])
                if not l1:
                    dstu = pu[:, sub * 512: sub * 512 + 512]
                    for k in range(8):
                        wt = ublk[:, k * 128:(k + 1) * 128]
                        xq = B.XQ[:, k * 1024 + sub * 512: k * 1024 + sub * 512 + 512]
                        P.op("pe", lambda e, wt=wt, xq=xq, k=k, dstu=dstu: e.matmul(
                            dstu, wt, xq, start=(k == 0), stop=(k == 7)),
                            reads=[wt, xq], writes=[dstu])

        def gu_chain(m):
            pg = B.pbw[(m % 2) * 2]
            pu = B.pbw[(m % 2) * 2 + 1]
            gs = B.EV[:, 0:1024]
            hsl = B.HB[:, m * 1024: m * 1024 + 1024]
            bCgp = B.BC[:, 1024:2048]
            P.op("dve", lambda e, pg=pg, gs=gs, bCgp=bCgp: e.tensor_mul(
                gs, pg, bCgp),
                reads=[B.pb[(m % 2) * 4], B.pb[(m % 2) * 4 + 1], bCgp], writes=[gs])
            if l1:
                b1c = B.BIA[:, m:m + 1]
                P.op("act", lambda e, gs=gs, hsl=hsl, b1c=b1c: e.activation(
                    hsl, gs, AF.Gelu, bias=b1c),
                    reads=[gs, B.BIA], writes=[hsl])
            else:
                P.op("act", lambda e, gs=gs: e.activation(gs, gs, AF.Silu),
                     reads=[gs], writes=[gs])
                P.op("dve", lambda e, gs=gs, hsl=hsl, pu=pu: e.tensor_mul(
                    hsl, gs, pu),
                    reads=[gs, B.pb[(m % 2) * 4 + 2], B.pb[(m % 2) * 4 + 3]],
                    writes=[hsl])

        for m in range(33):
            if m < 32:
                gu_mms(m)
            if m >= 1:
                gu_chain(m - 1)
        if True:
            if True:
                if True:
                    pass

        if tt == 1 and FFN_T1_STOP == "gu":
            continue
        hview = B.HB[:, 0:32768].rearrange("p (m t) -> p t m", m=32)
        hkeys = [B.HB[:, 0:32768]]
        P.op("dve", lambda e, hview=hview: e.tensor_reduce(
            am0, hview, AX.X, ALU.max, apply_absolute_value=True),
             reads=hkeys, writes=[am0])
        shc = B.col(VAR_SH, tt * 8, 8)
        for half in range(2):
            rmax = B.RT[:, half * 512:(half + 1) * 512]
            P.op("dve", lambda e, half=half, rmax=rmax: e.tensor_copy(
                rmax, B.ACC[:, half * 512:(half + 1) * 512]),
                reads=[am0], writes=[rmax])
            for g in range(4):
                P.op("pe", lambda e, g=g, rmax=rmax: e.transpose(
                    B.pb[7][:, g * 128:(g + 1) * 128],
                    rmax[:, g * 128:(g + 1) * 128], B.IDF),
                    reads=[rmax, B.IDF], writes=[B.pb[7]])
            for g in range(4):
                tg = tt * 8 + half * 4 + g
                P.op("dve", lambda e, g=g, tg=tg: e.reduce_max(
                    B.col(VAR_SH, tg), B.pb[7][:, g * 128:(g + 1) * 128],
                    AX.X),
                    reads=[B.pb[7]], writes=[B.col(VAR_SH, tg)])
        if tt == 1 and FFN_T1_STOP == "tailmid":
            continue
        P.op("dve", lambda e, shc=shc: e.tensor_scalar_max(shc, shc, 1e-5),
             reads=[shc], writes=[shc])
        shs = B.col(VAR_MINX, tt * 8, 8)
        P.op("dve", lambda e, shs=shs, shc=shc: e.reciprocal(shs, shc),
             reads=[shc], writes=[shs])
        P.op("dve", lambda e, shs=shs: e.tensor_scalar_mul(shs, shs, 127.0),
             reads=[shs], writes=[shs])
        sd = B.col(VAR_MAXX, tt * 8, 8)
        if l1:
            P.op("dve", lambda e, sd=sd, shc=shc: e.tensor_scalar_mul(
                sd, shc, 1.0 / 127.0), reads=[shc], writes=[sd])
        else:
            iv8 = B.col(VAR_INVSX, tt * 8, 8)
            P.op("dve", lambda e, sd=sd, iv8=iv8, shc=shc: e.tensor_mul(sd, iv8, shc),
                 reads=[iv8, shc], writes=[sd])
            P.op("dve", lambda e, sd=sd: e.tensor_scalar_mul(
                sd, sd, B.MWC[:, wu_i:wu_i + 1]),
                 reads=[sd, B.MWC], writes=[sd])
            P.op("dve", lambda e, sd=sd: e.tensor_scalar(
                sd, sd, 1e-5, 1.0 / 127.0, ALU.max, ALU.mult),
                 reads=[sd], writes=[sd])
        P.op("dve", lambda e, sd=sd: e.tensor_scalar_mul(
            sd, sd, B.MWC[:, wd_i:wd_i + 1]),
             reads=[sd, B.MWC], writes=[sd])
        P.op("dve", lambda e, sd=sd: e.tensor_scalar_mul(sd, sd, 0.5),
             reads=[sd], writes=[sd])
        if tt == 1 and FFN_T1_STOP == "tailsd":
            continue
        bSh = [emit_bcast(P, B, B.col(VAR_MINX, 0, 16), 0, tt * 2),
               emit_bcast(P, B, B.col(VAR_MINX, 0, 16), 1, tt * 2 + 1)]
        bSd = [emit_bcast(P, B, B.col(VAR_MAXX, 0, 16), 2, tt * 2),
               emit_bcast(P, B, B.col(VAR_MAXX, 0, 16), 3, tt * 2 + 1)]
        if tt == 1 and FFN_T1_STOP == "tail":
            continue
        bShp = B.BC[:, 0:1024]
        for m in range(32):
            hfull = B.HB[:, m * 1024: m * 1024 + 1024]
            tfull = B.RT[:, 0:1024]
            P.op("dve", lambda e, hfull=hfull, tfull=tfull: e.tensor_mul(
                tfull, hfull, bShp), reads=[hfull, bSh[0], bSh[1]], writes=[tfull])
            emit_round(P, B, tfull, hfull, 0, n=1024)
        if tt == 1 and FFN_T1_STOP == "round":
            continue
        for sub in range(2):
            qt = tt * 2 + sub
            rtd = B.RT[:, sub * 1024:(sub + 1) * 1024]
            for hh in range(2):
                P.op("dve", lambda e, rtd=rtd, hh=hh, sub=sub: e.tensor_copy(
                    rtd[:, hh * 512:(hh + 1) * 512], bSd[sub]),
                    reads=[bSd[sub]], writes=[rtd])
            for k in range(32):
                ws = k % 2
                wdb = B.WB[:, ws * 1024: ws * 1024 + 1024]
                P.dma("sp", wdb, d3[k, :, :], "wg%d" % ws)
                hq = B.HB[:, k * 1024 + sub * 512: k * 1024 + sub * 512 + 512]
                for m8 in range(8):
                    wt = wdb[:, m8 * 128:(m8 + 1) * 128]
                    P.op("pe", lambda e, wt=wt, hq=hq, k=k, m8=m8: e.matmul(
                        B.pb[m8], wt, hq, start=(k == 0), stop=(k == 31)),
                        reads=[wt, hq], writes=[B.pb[m8]])
            for p4 in range(4):
                ev = B.EV[:, 0:1024] if p4 % 2 == 0 else B.WR[:, 0:1024]
                P.op("dve", lambda e, p4=p4, ev=ev, rtd=rtd: e.tensor_mul(
                    ev, B.pbw[p4], rtd),
                    reads=[B.pb[2 * p4], B.pb[2 * p4 + 1], rtd], writes=[ev])
                if l1:
                    for hh in range(2):
                        m8 = 2 * p4 + hh
                        b2h = B.BIA[:, 40 + m8:41 + m8]
                        evh = ev[:, hh * 512:(hh + 1) * 512]
                        P.op("act", lambda e, evh=evh, b2h=b2h: e.activation(
                            evh, evh, AF.Identity, bias=b2h),
                            reads=[ev, B.BIA], writes=[ev])
                xp = B.X.rearrange("p (c t) -> p c t", c=8)[
                    :, 2 * p4:2 * p4 + 2, qt * 512:(qt + 1) * 512]
                ev3 = ev.rearrange("p (a b) -> p a b", a=2)
                xkeys = [B.xs(2 * p4, qt), B.xs(2 * p4 + 1, qt)]
                P.op("dve", lambda e, xp=xp, ev3=ev3: e.tensor_add(xp, xp, ev3),
                     reads=[ev] + xkeys, writes=xkeys)



def emit_attn(P, B, layer, qkvT, woT, dscr):
    emit_stats(P, B, ffn=False)
    l1 = layer == 1
    qoff = 8 if l1 else 0
    koff = 0 if l1 else 8
    voff = 16
    qkv = qkvT.ap()
    wo = woT.ap()
    VOFF = 16384
    vones = B.HB[:, VOFF:VOFF + 16640].rearrange("p (s c) -> p s c", c=65)[:, :, 64:65]
    P.op("dve", lambda e: e.memset(vones, 1.0), writes=[B.HB[:, VOFF:VOFF + 16640]])

    def build_u(qt):
        bM = emit_bcast(P, B, B.col(VAR_M, 0, 16), 0, qt)
        bR = emit_bcast(P, B, B.col(VAR_RSTD, 0, 16), 1, qt)
        for c in range(8):
            xsl = B.xs(c, qt)
            ro = (c % 2) * 512
            xm = B.EV[:, ro:ro + 512]
            P.op("dve", lambda e, xsl=xsl, xm=xm, bM=bM: e.tensor_sub(xm, xsl, bM),
                 reads=[xsl, bM], writes=[xm])
            u = B.XQ[:, c * 512:(c + 1) * 512]
            P.op("dve", lambda e, xm=xm, u=u, bR=bR: e.tensor_mul(u, xm, bR),
                 reads=[xm, bR], writes=[u])

    def proj_one(qt, m, bi):
        ws = bi % 2
        wr = B.WR[:, 0:1024] if ws == 0 else B.RT[:, 0:1024]
        P.dma("sp", wr, qkv[m], "wr%d" % ws)
        wb = B.WB[:, 2048 + ws * 1024: 2048 + (ws + 1) * 1024]
        P.op("dve", lambda e, wb=wb, wr=wr: e.tensor_copy(wb, wr),
             reads=[wr], writes=[wb])
        pb = B.pb[ws]
        for k in range(8):
            wt = wb[:, k * 128:(k + 1) * 128]
            u = B.XQ[:, k * 512:(k + 1) * 512]
            P.op("pe", lambda e, wt=wt, u=u, k=k, pb=pb: e.matmul(
                pb, wt, u, start=(k == 0), stop=(k == 7)),
                reads=[wt, u], writes=[pb])
        return pb

    def rope_evac(pb, toff, dst):
        ea = B.BC[:, 1536:2048]
        ebb = B.EV[:, 0:512]
        P.op("dve", lambda e: e.tensor_copy(ea, pb), reads=[pb], writes=[ea])
        P.op("dve", lambda e: e.stream_shuffle(ebb, ea, _SHUF_PAIR),
             reads=[ea], writes=[ebb])
        P.op("dve", lambda e: e.tensor_mul(ea, ea, B.RC[:, toff:toff + 512]),
             reads=[ea, B.RC], writes=[ea])
        P.op("pool", lambda e: e.tensor_mul(ebb, ebb, B.RS[:, toff:toff + 512]),
             reads=[ebb, B.RS], writes=[ebb])
        P.op("dve", lambda e: e.tensor_add(dst, ea, ebb),
             reads=[ea, ebb], writes=[dst])

    for qt in range(4):
        build_u(qt)
        for j in range(8):
            pb = proj_one(qt, koff + j, j)
            dst = B.HB[:, j * 2048 + qt * 512: j * 2048 + qt * 512 + 512]
            if l1:
                rope_evac(pb, qt * 512, dst)
            else:
                P.op("dve", lambda e, dst=dst, pb=pb: e.tensor_copy(dst, pb),
                     reads=[pb], writes=[dst])
        for j in range(8):
            pb = proj_one(qt, voff + j, j)
            ev = B.EV[:, 0:512]
            P.op("dve", lambda e, ev=ev, pb=pb: e.tensor_copy(ev, pb),
                 reads=[pb], writes=[ev])
            for g in range(4):
                kc = qt * 4 + g
                pt = B.pb[6 + (g % 2)]
                P.op("pe", lambda e, g=g, pt=pt: e.transpose(
                    pt[:, 0:128], ev[:, g * 128:(g + 1) * 128], B.IDF),
                    reads=[ev, B.IDF], writes=[pt])
                for hh in range(2):
                    h = 2 * j + hh
                    slot = VOFF + (kc * 16 + h) * 65
                    dst = B.HB[:, slot:slot + 64]
                    P.op("dve", lambda e, dst=dst, pt=pt, hh=hh: e.tensor_copy(
                        dst, pt[:, hh * 64:hh * 64 + 64]),
                        reads=[pt], writes=[dst])
    for qt in range(4):
        build_u(qt)
        for j in range(8):
            pb = proj_one(qt, qoff + j, j)
            dst = B.QT[:, j * 512:(j + 1) * 512]
            if l1:
                rope_evac(pb, qt * 512, dst)
            else:
                P.op("dve", lambda e, dst=dst, pb=pb: e.tensor_copy(dst, pb),
                     reads=[pb], writes=[dst])
        nkc = qt * 4 + 4
        for h in range(NH):
            qslice = B.QT[64 * (h % 2):64 * (h % 2) + 64,
                          (h // 2) * 512:(h // 2) * 512 + 512]
            pav = B.pb[5]
            def sc_mm(kc):
                ps = B.pb[2 + (kc % 2)]
                ksl = B.HB[64 * (h % 2):64 * (h % 2) + 64,
                           (h // 2) * 2048 + kc * 128: (h // 2) * 2048 + kc * 128 + 128]
                P.op("pe", lambda e, ksl=ksl, qslice=qslice, ps=ps: e.matmul(
                    ps, ksl, qslice, start=True, stop=True),
                    reads=[ksl, qslice], writes=[ps])

            def sc_tail(kc):
                ps = B.pb[2 + (kc % 2)]
                p0 = B.CC[:, (kc % 2) * 512:(kc % 2) * 512 + 512]
                P.op("act", lambda e, ps=ps, p0=p0: e.activation(
                    p0, ps, AF.Exp, scale=B.KCONST[:, 1:2]),
                    reads=[ps, B.KCONST], writes=[p0])
                if (not l1) or (kc >= qt * 4):
                    moff = qt * 512 - kc * 128 + 384
                    msl = B.MASK[:, moff:moff + 512]
                    P.op("dve", lambda e, p0=p0, msl=msl: e.tensor_mul(p0, p0, msl),
                         reads=[p0, msl], writes=[p0])
                vsl = B.HB[:, VOFF + (kc * 16 + h) * 65: VOFF + (kc * 16 + h) * 65 + 65]
                P.op("pe", lambda e, vsl=vsl, p0=p0, kc=kc, nkc=nkc, pav=pav: e.matmul(
                    pav[0:65, :], vsl, p0, start=(kc == 0), stop=(kc == nkc - 1)),
                    reads=[vsl, p0], writes=[pav])

            for kc in range(nkc + 1):
                if kc < nkc:
                    sc_mm(kc)
                if kc >= 1:
                    sc_tail(kc - 1)
            drow = B.EV[64:65, 512:1024]
            P.op("dve", lambda e, pav=pav: e.reciprocal(drow, pav[64:65, 0:512]),
                 reads=[pav], writes=[drow])
            dsl = dscr.ap()[h % 4: h % 4 + 1, :]
            P.dma("act", dsl, drow, "dsc%d" % (h % 4))
            db = B.BC[0:64, 1536:2048]
            P.dma("sp", db, dsl.broadcast_to([64, 512]), "dbc")
            if h % 2 == 0:
                ao = B.XQ[0:64, (h // 2) * 512:(h // 2) * 512 + 512]
                P.op("dve", lambda e, ao=ao, db=db, pav=pav: e.tensor_mul(
                    ao, pav[0:64, :], db), reads=[pav, db], writes=[ao])
            else:
                stg = B.CC[0:64, 1024:1536]
                P.op("dve", lambda e, stg=stg, db=db, pav=pav: e.tensor_mul(
                    stg, pav[0:64, :], db), reads=[pav, db], writes=[stg])
                ao = B.XQ[64:128, (h // 2) * 512:(h // 2) * 512 + 512]
                P.dma("pool", ao, stg, "aomv")
        for mo in range(8):
            ws = mo % 2
            wr = B.WR[:, 0:1024] if ws == 0 else B.RT[:, 0:1024]
            P.dma("sp", wr, wo[mo], "wr%d" % ws)
            wb = B.WB[:, 2048 + ws * 1024: 2048 + (ws + 1) * 1024]
            P.op("dve", lambda e, wb=wb, wr=wr: e.tensor_copy(wb, wr),
                 reads=[wr], writes=[wb])
            po = B.pb[6 + ws]
            for k in range(8):
                wt = wb[:, k * 128:(k + 1) * 128]
                ao = B.XQ[:, k * 512:(k + 1) * 512]
                P.op("pe", lambda e, wt=wt, ao=ao, k=k, po=po: e.matmul(
                    po, wt, ao, start=(k == 0), stop=(k == 7)),
                    reads=[wt, ao], writes=[po])
            xsl = B.xs(mo, qt)
            P.op("dve", lambda e, xsl=xsl, po=po: e.tensor_add(xsl, xsl, po),
                 reads=[po, xsl], writes=[xsl])



def emit_input_tm(P, B, x_tm):
    xv = x_tm.ap().rearrange("(n p) d -> n p d", p=128)
    for nt in range(16):
        io = B.BC[:, 0:1024]
        P.dma("sp", io, xv[nt], "xin")
        for c in range(8):
            pt = B.pb[c % 2]
            P.op("pe", lambda e, c=c, io=io, pt=pt: e.transpose(
                pt[:, 0:128], io[:, c * 128:(c + 1) * 128], B.IDF),
                reads=[io, B.IDF], writes=[pt])
            dst = B.X[:, c * 2048 + nt * 128: c * 2048 + (nt + 1) * 128]
            P.op("dve", lambda e, dst=dst, pt=pt: e.tensor_copy(dst, pt[:, 0:128]),
                 reads=[pt], writes=[dst])


def emit_output_tm(P, B, y_tm):
    yv = y_tm.ap().rearrange("(n p) d -> n p d", p=128)
    for nt in range(16):
        io = B.BC[:, 1024:2048]
        for c in range(8):
            src = B.X[:, c * 2048 + nt * 128: c * 2048 + (nt + 1) * 128]
            pt = B.pb[2 + c % 2]
            P.op("pe", lambda e, src=src, pt=pt: e.transpose(
                pt[:, 0:128], src, B.IDF),
                reads=[src, B.IDF], writes=[pt])
            P.op("dve", lambda e, c=c, io=io, pt=pt: e.tensor_copy(
                io[:, c * 128:(c + 1) * 128], pt[:, 0:128]),
                reads=[pt], writes=[io[:, c * 128:(c + 1) * 128]])
        P.dma("pool", yv[nt], io, "yo")


def emit_input_fm(P, B, x_fm):
    xv = x_fm.ap().rearrange("(c p) n -> c p n", p=128)
    for c in range(8):
        P.dma("sp", B.X[:, c * 2048:(c + 1) * 2048], xv[c], "xin%d" % c)


def emit_output_fm(P, B, y_fm):
    yv = y_fm.ap().rearrange("(c p) n -> c p n", p=128)
    for c in range(8):
        P.dma("pool", yv[c], B.X[:, c * 2048:(c + 1) * 2048], "yo")



W_G0, W_U0, W_D0, W_1, W_2 = 0, 1, 2, 3, 4
TKEYS = ["wg3", "wu3", "wd3", "w13", "w23"]
TSHAPES = {"wg3": (1024, 4096), "wu3": (1024, 4096), "wd3": (4096, 1024),
           "w13": (1024, 4096), "w23": (4096, 1024)}
TILED = {"wg3": True, "wu3": True, "wd3": False, "w13": True, "w23": False}


def _tshape(k):
    K, M = TSHAPES[k]
    return [M // 128, 128, K] if TILED[k] else [K, M]
WIDX = {"wg3": W_G0, "wu3": W_U0, "wd3": W_D0, "w13": W_1, "w23": W_2}


def build_prep():
    nc = bass.Bass("TRN2", target_bir_lowering=False, debug=False)
    dins = {"identf": nc.dram_tensor("identf", [128, 128], F32, kind="ExternalInput")}
    wts = {}
    for k in TKEYS:
        wts[k] = (nc.dram_tensor(k + "_in", _tshape(k), F32, kind="ExternalInput"),
                  nc.dram_tensor(k, _tshape(k), BF16, kind="ExternalOutput"))
    scales = nc.dram_tensor("scales", [1, 32], F32, kind="ExternalOutput")
    P = Prog(nc)
    B = Bufs(nc)
    emit_setup(P, B, dins)
    for k in TKEYS:
        K, M = TSHAPES[k]
        emit_weight_prep(P, B, wts[k][0], wts[k][1], WIDX[k], K, M, scales,
                         tiled=TILED[k])
    P.emit()
    nc.finalize()
    return nc


def build_main(stages, x_in, x_out):
    nc = bass.Bass("TRN2", target_bir_lowering=False, debug=False)
    dins = {"identf": nc.dram_tensor("identf", [128, 128], F32, kind="ExternalInput")}
    need = set()
    for st in stages:
        if st[0] == "ffn":
            need.add("scales")
            need |= ({"wg3", "wu3", "wd3"} if st[1] == 0 else {"w13", "w23", "bias"})
        else:
            need |= {"attn%d" % st[1]}
            if st[1] == 1:
                need.add("rope")
    tern = {}
    for k in TKEYS:
        if k in need:
            tern[k] = nc.dram_tensor(k, _tshape(k), BF16, kind="ExternalInput")
    scales = (nc.dram_tensor("scales", [1, 32], F32, kind="ExternalInput")
              if "scales" in need else None)
    if "bias" in need:
        dins["b1c"] = nc.dram_tensor("b1c", [128, 32], F32, kind="ExternalInput")
        dins["b2c"] = nc.dram_tensor("b2c", [128, 8], F32, kind="ExternalInput")
    for L in (0, 1):
        if ("attn%d" % L) in need:
            w = _HC["mask%d" % L].shape[1]
            dins["mask%d" % L] = nc.dram_tensor("mask%d" % L, [128, w], BF16,
                                                kind="ExternalInput")
            dins["qkvT%d" % L] = nc.dram_tensor("qkvT%d" % L, [1024, 3072], F32,
                                                kind="ExternalInput")
            dins["woT%d" % L] = nc.dram_tensor("woT%d" % L, [1024, 1024], F32,
                                               kind="ExternalInput")
    if "rope" in need:
        dins["ropeC"] = nc.dram_tensor("ropeC", [128, 2048], BF16, kind="ExternalInput")
        dins["ropeS"] = nc.dram_tensor("ropeS", [128, 2048], BF16, kind="ExternalInput")
    x_d = nc.dram_tensor("x", [2048, 1024] if x_in == "tm" else [1024, 2048],
                         F32, kind="ExternalInput")
    y_d = nc.dram_tensor("y", [2048, 1024] if x_out == "tm" else [1024, 2048],
                         F32, kind="ExternalOutput")
    dscr = nc.dram_tensor("dscr", [4, 512], F32, kind="Internal")

    P = Prog(nc)
    B = Bufs(nc)
    emit_setup(P, B, dins)
    if scales is not None:
        emit_load_scales(P, B, scales, list(range(5)))
    if "bias" in need:
        emit_load_bias(P, B, dins)
    if "rope" in need:
        P.dma("sp", B.RC, dins["ropeC"].ap(), "ropec")
        P.dma("sp", B.RS, dins["ropeS"].ap(), "ropes")
    if x_in == "tm":
        emit_input_tm(P, B, x_d)
    else:
        emit_input_fm(P, B, x_d)
    for st in stages:
        if st[0] == "ffn":
            if st[1] == 0:
                emit_ffn(P, B, tern["wg3"], tern["wu3"], tern["wd3"],
                         W_G0, W_U0, W_D0, l1=False)
            else:
                emit_ffn(P, B, tern["w13"], None, tern["w23"],
                         W_1, None, W_2, l1=True)
        else:
            L = st[1]
            m = _HC["mask%d" % L]
            P.dma("sp", B.MASK[:, 0:m.shape[1]], dins["mask%d" % L].ap(), "mask")
            emit_attn(P, B, L, dins["qkvT%d" % L], dins["woT%d" % L], dscr)
    if x_out == "tm":
        emit_output_tm(P, B, y_d)
    else:
        emit_output_fm(P, B, y_d)
    P.emit()
    nc.finalize()
    return nc


def build_fused():
    nc = bass.Bass("TRN2", target_bir_lowering=False, debug=False)
    dins = {"identf": nc.dram_tensor("identf", [128, 128], F32, kind="ExternalInput")}
    win = {}
    tern = {}
    for k in TKEYS:
        win[k] = nc.dram_tensor(k + "_in", _tshape(k), F32, kind="ExternalInput")
        tern[k] = nc.dram_tensor(k, _tshape(k), BF16, kind="Internal")
    scales = nc.dram_tensor("scales", [1, 32], F32, kind="Internal")
    dins["b1c"] = nc.dram_tensor("b1c", [128, 32], F32, kind="ExternalInput")
    dins["b2c"] = nc.dram_tensor("b2c", [128, 8], F32, kind="ExternalInput")
    for L in (0, 1):
        w = _HC["mask%d" % L].shape[1]
        dins["mask%d" % L] = nc.dram_tensor("mask%d" % L, [128, w], BF16,
                                            kind="ExternalInput")
        dins["qkvT%d" % L] = nc.dram_tensor("qkvT%d" % L, [24, 128, 1024], F32,
                                            kind="ExternalInput")
        dins["woT%d" % L] = nc.dram_tensor("woT%d" % L, [8, 128, 1024], F32,
                                           kind="ExternalInput")
    dins["ropeC"] = nc.dram_tensor("ropeC", [128, 2048], BF16, kind="ExternalInput")
    dins["ropeS"] = nc.dram_tensor("ropeS", [128, 2048], BF16, kind="ExternalInput")
    x_d = nc.dram_tensor("x", [2048, 1024], F32, kind="ExternalInput")
    y_d = nc.dram_tensor("y", [2048, 1024], F32, kind="ExternalOutput")
    dscr = nc.dram_tensor("dscr", [4, 512], F32, kind="Internal")

    P = Prog(nc)
    B = Bufs(nc)
    emit_setup(P, B, dins)
    emit_load_bias(P, B, dins)
    emit_input_tm(P, B, x_d)
    for k in TKEYS:
        K, M = TSHAPES[k]
        emit_weight_prep(P, B, win[k], tern[k], WIDX[k], K, M, scales,
                         tiled=TILED[k])
        P.seal_tensor(k, "tst%d" % WIDX[k])
    for st in [("ffn", 0), ("attn", 0), ("ffn", 0), ("ffn", 1), ("attn", 1),
               ("ffn", 1)]:
        if st[0] == "ffn":
            if st[1] == 0:
                emit_ffn(P, B, tern["wg3"], tern["wu3"], tern["wd3"],
                         W_G0, W_U0, W_D0, l1=False)
            else:
                emit_ffn(P, B, tern["w13"], None, tern["w23"],
                         W_1, None, W_2, l1=True)
        else:
            L = st[1]
            m = _HC["mask%d" % L]
            P.dma("sp", B.MASK[:, 0:m.shape[1]], dins["mask%d" % L].ap(), "mask")
            if L == 1:
                P.dma("sp", B.RC, dins["ropeC"].ap(), "ropec")
                P.dma("sp", B.RS, dins["ropeS"].ap(), "ropes")
            emit_attn(P, B, L, dins["qkvT%d" % L], dins["woT%d" % L], dscr)
    emit_output_tm(P, B, y_d)
    P.emit()
    nc.finalize()
    return nc



_NC_CACHE = {}


def _get(key, fn):
    if key not in _NC_CACHE:
        _NC_CACHE[key] = fn()
    return _NC_CACHE[key]


def kernel(L0_attn_x, L0_attn_qkv, L0_attn_wo, L0_ffn_w_gate, L0_ffn_w_up,
           L0_ffn_w_down, L1_attn_W_KQV, L1_attn_W_O, L1_ffn_W1, L1_ffn_b1,
           L1_ffn_W2, L1_ffn_b2):
    x = np.ascontiguousarray(np.asarray(L0_attn_x, np.float32))
    tw = lambda w: np.ascontiguousarray(np.asarray(w, np.float32).T)

    def tile5(wT):
        K, M = wT.shape
        return np.ascontiguousarray(
            wT.reshape(K // 128, 128, M // 128, 128).transpose(2, 1, 0, 3)
            .reshape(M // 128, 128, K))

    wgT, wuT, wdT = tile5(tw(L0_ffn_w_gate)), tile5(tw(L0_ffn_w_up)), tw(L0_ffn_w_down)
    w1T, w2T = tile5(tw(L1_ffn_W1)), tw(L1_ffn_W2)
    qkvT0, woT0 = tile5(tw(L0_attn_qkv)), tile5(tw(L0_attn_wo))
    qkvT1, woT1 = tile5(tw(L1_attn_W_KQV)), tile5(tw(L1_attn_W_O))
    b1c = np.ascontiguousarray(np.asarray(L1_ffn_b1, np.float32).reshape(32, 128).T)
    b2c = np.ascontiguousarray(np.asarray(L1_ffn_b2, np.float32).reshape(8, 128).T)
    consts = _HC

    if FUSED:
        ncf = _get("fused", build_fused)
        in_maps = []
        for i in range(8):
            mm = {"identf": consts["identf"], "b1c": b1c, "b2c": b2c,
                  "ropeC": consts["ropeC"], "ropeS": consts["ropeS"],
                  "mask0": np.ascontiguousarray(consts["mask0"]),
                  "mask1": np.ascontiguousarray(consts["mask1"]),
                  "qkvT0": qkvT0, "woT0": woT0, "qkvT1": qkvT1, "woT1": woT1,
                  "x": np.ascontiguousarray(x[i])}
            mm.update({k + "_in": w for k, w in
                       zip(TKEYS, [wgT, wuT, wdT, w1T, w2T])})
            in_maps.append(mm)
        res = run_bass_kernel_spmd(ncf, in_maps, core_ids=list(range(8)))
        return np.stack([res.results[i]["y"] for i in range(8)], axis=0).astype(np.float32)

    ncp = _get("prep", build_prep)
    inp = {"identf": consts["identf"]}
    inp.update({k + "_in": w for k, w in zip(TKEYS, [wgT, wuT, wdT, w1T, w2T])})
    resp = run_bass_kernel_spmd(ncp, [inp] * 8, core_ids=list(range(8)))
    tern = {k: resp.results[0][k] for k in TKEYS}
    scales = resp.results[0]["scales"]

    common = {"identf": consts["identf"]}
    stage_cfgs = [
        ([("ffn", 0)], "tm", "fm"),
        ([("attn", 0)], "fm", "fm"),
        ([("ffn", 0)], "fm", "fm"),
        ([("ffn", 1)], "fm", "fm"),
        ([("attn", 1)], "fm", "fm"),
        ([("ffn", 1)], "fm", "tm"),
    ]
    cur = [np.ascontiguousarray(x[i]) for i in range(8)]
    for si, (stages, xin, xout) in enumerate(stage_cfgs):
        ncm = _get(("main", si), lambda stages=stages, xin=xin, xout=xout:
                   build_main(stages, xin, xout))
        in_maps = []
        for i in range(8):
            m = dict(common)
            st = stages[0]
            if st[0] == "ffn":
                m["scales"] = scales
                if st[1] == 0:
                    m.update({k: tern[k] for k in ("wg3", "wu3", "wd3")})
                else:
                    m.update({k: tern[k] for k in ("w13", "w23")})
                    m.update({"b1c": b1c, "b2c": b2c})
            else:
                L = st[1]
                m["mask%d" % L] = np.ascontiguousarray(consts["mask%d" % L])
                m["qkvT%d" % L] = qkvT0 if L == 0 else qkvT1
                m["woT%d" % L] = woT0 if L == 0 else woT1
                if L == 1:
                    m["ropeC"] = consts["ropeC"]
                    m["ropeS"] = consts["ropeS"]
            m["x"] = np.ascontiguousarray(cur[i])
            in_maps.append(m)
        res = run_bass_kernel_spmd(ncm, in_maps, core_ids=list(range(8)))
        cur = [res.results[i]["y"] for i in range(8)]
    return np.stack(cur, axis=0).astype(np.float32)


# version 12
# speedup vs baseline: 1.6358x; 1.2035x over previous
import numpy as np
import ml_dtypes
import concourse.bass as bass
import concourse.mybir as mybir
from concourse.bass_utils import run_bass_kernel_spmd

F32 = mybir.dt.float32
BF16 = mybir.dt.bfloat16
I32 = mybir.dt.int32
AF = mybir.ActivationFunctionType
ALU = mybir.AluOpType
AX = mybir.AxisListType
BFNP = ml_dtypes.bfloat16

S, DM, NH, DH, INNER = 2048, 1024, 16, 64, 4096
FFN_TTS = range(2)
FFN_T1_STOP = None
FUSED = True


def host_constants():
    c = {}
    c["identf"] = np.eye(128, dtype=np.float32)
    jj = np.arange(128)[:, None]
    uu = np.arange(2432)[None, :] - 384
    d = uu - jj
    allowed = ((d >= 0) & (d <= 127)) | ((d >= 132) & ((d - 128) % 4 == 0))
    c["mask0"] = allowed.astype(BFNP)
    uu1 = np.arange(896)[None, :] - 384
    c["mask1"] = ((uu1 - jj) >= 0).astype(BFNP)
    r = np.arange(0, DH, 2, dtype=np.float64)
    invf = 1.0 / (10000.0 ** (r / DH))
    fr = np.arange(S, dtype=np.float64)[:, None] * invf[None, :]
    cos = np.cos(fr); sin = np.sin(fr)
    C = np.repeat(cos, 2, axis=1)
    S2 = np.empty((S, DH)); S2[:, 0::2] = -sin; S2[:, 1::2] = sin
    c["ropeC"] = np.tile(C.T, (2, 1)).astype(BFNP)
    c["ropeS"] = np.tile(S2.T, (2, 1)).astype(BFNP)
    return c

_HC = host_constants()
_SHUF_PAIR = []
for _i in range(0, 32, 2):
    _SHUF_PAIR.extend([_i + 1, _i])


QSEM = {"sp": "qsp", "act": "qact", "pool": "qpool"}


def _apkeys(x):
    if not hasattr(x, "tensor"):
        return (x,)
    t = x.tensor
    nm = t.name
    dims = list(x.ap)
    try:
        import bass_rust
        is_dram = x.space == getattr(bass_rust, "MemorySpace").DRAM
    except Exception:
        is_dram = str(getattr(x, "space", "")).find("DRAM") >= 0
    lo = int(x.offset)
    if not is_dram:
        fw = 1
        for d in list(t.shape)[1:]:
            fw *= int(d)
        lo = lo % fw
    hi = lo
    use = dims if is_dram else dims[1:] if len(dims) > 1 else []
    for st, ct in use:
        hi += (int(ct) - 1) * int(st)
    hi += 1
    gran = 65536 if is_dram else 512
    return tuple((nm, i) for i in range(lo // gran, (hi + gran - 1) // gran))


class Prog:
    def __init__(self, nc):
        self.nc = nc
        self.sem = {}
        for e in ("pe", "dve", "act", "pool"):
            self.sem[e] = nc.alloc_semaphore("S" + e)
        self.cnt = {e: 0 for e in self.sem}
        self.q = {e: [] for e in ("pe", "dve", "act", "pool", "sp")}
        self.hwm = {e: {} for e in self.q}
        self.res = {}

    def _stream(self, name):
        key = ("dma", name)
        if key not in self.sem:
            self.sem[key] = self.nc.alloc_semaphore("Sd_" + name)
            self.cnt[key] = 0
        return key

    def _expand(self, items):
        out = []
        for it in items:
            out.extend(_apkeys(it))
        return out

    def _need(self, eng, own_sem, reads, writes):
        need = {}

        def add(s, v):
            if s == own_sem and v > self.cnt.get(own_sem, 0):
                return
            need[s] = max(need.get(s, 0), v)

        for k in reads:
            r = self.res.get(k)
            if r and r["w"] is not None:
                add(*r["w"])
        for k in writes:
            r = self.res.get(k)
            if r:
                if r["w"] is not None:
                    add(*r["w"])
                for s, v in r["r"].items():
                    add(s, v)
        hw = self.hwm[eng]
        waits = []
        for s, v in sorted(need.items(), key=lambda kv: str(kv[0])):
            if hw.get(s, 0) < v:
                waits.append((s, v))
                hw[s] = v
        return waits

    def _record(self, own_sem, tick, reads, writes):
        for k in writes:
            self.res[k] = {"w": (own_sem, tick), "r": {}}
        for k in reads:
            r = self.res.get(k)
            if r is None:
                r = {"w": None, "r": {}}
                self.res[k] = r
            r["r"][own_sem] = max(r["r"].get(own_sem, 0), tick)

    def op(self, eng, fn, reads=(), writes=(), inc=True):
        import sys as _s
        f = _s._getframe(1)
        self._dbg = "%s:%d" % (f.f_code.co_name, f.f_lineno)
        own = eng
        reads = self._expand(reads)
        writes = self._expand(writes)
        waits = self._need(eng, own, reads, writes)
        if inc:
            self.cnt[eng] += 1
            tick = self.cnt[eng]
        else:
            tick = self.cnt[eng] + 1
        self.q[eng].append((waits, fn, own if inc else None, self._dbg))
        self._record(own, tick, reads, writes)

    def dma(self, qeng, dst, src, stream, reads=(), writes=(), **kw):
        own = self._stream(stream)
        reads = self._expand(tuple(reads) + (src,))
        writes = self._expand(tuple(writes) + (dst,))
        waits = self._need(qeng, own, reads, writes)
        self.cnt[own] += 1
        tick = 16 * self.cnt[own]
        import sys as _s
        f = _s._getframe(1)
        dbg = "%s:%d" % (f.f_code.co_name, f.f_lineno)
        self.q[qeng].append(
            (waits, lambda e, d=dst, s=src, k=dict(kw): e.dma_start(d, s, **k), own, dbg))
        self._record(own, tick, reads, writes)

    def seal_tensor(self, name, stream):
        key = ("dma", stream)
        tick = 16 * self.cnt[key]
        for k, r in self.res.items():
            if isinstance(k, tuple) and len(k) == 2 and k[0] == name:
                r["w"] = (key, tick)

    def check_deadlock(self):
        pos = {e: 0 for e in self.q}
        val = {s: 0 for s in self.cnt}
        progressed = True
        while progressed:
            progressed = False
            for e, lst in self.q.items():
                while pos[e] < len(lst):
                    waits, _fn, inc_sem = lst[pos[e]][0], lst[pos[e]][1], lst[pos[e]][2]
                    if all(val.get(s, 0) >= v for s, v in waits):
                        if inc_sem is not None:
                            val[inc_sem] = val.get(inc_sem, 0) + (
                                16 if isinstance(inc_sem, tuple) else 1)
                        pos[e] += 1
                        progressed = True
                    else:
                        break
        stuck = {e: pos[e] for e in self.q if pos[e] < len(self.q[e])}
        if stuck:
            msg = []
            for e, p in stuck.items():
                waits, _f, _i, dbg = self.q[e][p]
                bad = [(s, v, val.get(s, 0)) for s, v in waits if val.get(s, 0) < v]
                msg.append(f"{e}@{p} [{dbg}] needs {bad}")
            raise RuntimeError("build-time deadlock: " + " | ".join(msg))

    def emit(self):
        nc = self.nc
        sems = self.sem
        final = dict(self.cnt)

        def mk(lst, eng_name):
            def body(engine):
                for waits, fn, inc_sem, _dbg in lst:
                    for s, v in waits:
                        engine.wait_ge(sems[s], v)
                    ins = fn(engine)
                    if inc_sem is not None:
                        amt = 16 if isinstance(inc_sem, tuple) else 1
                        ins.then_inc(sems[inc_sem], amt)
                if eng_name == "pool":
                    for s, v in final.items():
                        tgt = v * 16 if isinstance(s, tuple) else v
                        if tgt > 0:
                            engine.wait_ge(sems[s], tgt)
            return body

        self.check_deadlock()
        with nc.Block() as blk:
            blk.sync(mk(self.q["sp"], "sp"))
            blk.tensor(mk(self.q["pe"], "pe"))
            blk.vector(mk(self.q["dve"], "dve"))
            blk.scalar(mk(self.q["act"], "act"))
            blk.gpsimd(mk(self.q["pool"], "pool"))



class Bufs:
    def __init__(self, nc):
        A = nc.alloc_sbuf_tensor
        self.X = A("X", [128, 16384], F32).ap()
        self.HB = A("HB", [128, 33024], BF16).ap()
        self.XQ = A("XQb", [128, 8192], BF16).ap()
        self.QT = A("QTb", [128, 4096], BF16).ap()
        self.WB = A("WBb", [128, 4096], BF16).ap()
        self.WR = A("WRb", [128, 1024], F32).ap()
        self.BC = A("BCb", [128, 2048], F32).ap()
        self.EV = A("EVb", [128, 1280], F32).ap()
        self.ACC = A("ACCb", [128, 1024], BF16).ap()
        self.RT = A("RTb", [128, 2048], F32).ap()
        self.RI = A("RIb", [128, 1024], I32).ap()
        self.CC = A("CCb", [128, 2048], BF16).ap()
        self.AUX = A("AUXb", [128, 4992], BF16).ap()
        self.MASK = self.AUX
        self.RC = self.AUX[:, 896:2944]
        self.RS = self.AUX[:, 2944:4992]
        self.COLS = A("COLSb", [128, 160], F32).ap()
        self.CB = A("CBb", [128, 128], F32).ap()
        self.IDF = A("IDFb", [128, 128], F32).ap()
        self.ONE1 = A("ONE1b", [128, 1], BF16).ap()
        self.KCONST = A("KCONSTb", [128, 3], F32).ap()
        self.MWC = A("MWCb", [128, 10], F32).ap()
        self.BIA = A("BIAb", [128, 48], F32).ap()
        self.SCR = A("SCRb", [1, 32], F32).ap()
        self.pbw = [nc.alloc_psum_tensor(f"pw{i}", [128, 1024], F32).ap() for i in range(4)]
        self.pb = [self.pbw[i // 2][:, (i % 2) * 512:(i % 2) * 512 + 512] for i in range(8)]

    def col(self, v, idx, n=1):
        return self.COLS[:, v * 16 + idx: v * 16 + idx + n]

    def xs(self, c, qt):
        return self.X[:, c * 2048 + qt * 512: c * 2048 + qt * 512 + 512]


(VAR_M, VAR_RSTD, VAR_S1, VAR_S2, VAR_MAXX, VAR_MINX,
 VAR_ACOL, VAR_INVSX, VAR_SH, VAR_TMP) = range(10)
KC_EPS, KC_EIGHTH = 0, 1


def emit_setup(P, B, dins):
    P.dma("sp", B.IDF, dins["identf"].ap(), "ldid")
    P.op("dve", lambda e: e.memset(B.CB, 0.0), writes=[B.CB])
    P.op("dve", lambda e: e.memset(B.ONE1, 1.0), writes=[B.ONE1])
    P.op("dve", lambda e: e.memset(B.KCONST[:, 0:1], 1e-5), writes=[B.KCONST])
    P.op("dve", lambda e: e.memset(B.KCONST[:, 1:2], 0.125), writes=[B.KCONST])
    P.op("dve", lambda e: e.memset(B.KCONST[:, 2:3], 1.0 / 127.0), writes=[B.KCONST])
    P.op("dve", lambda e: e.memset(B.COLS, 0.0), writes=[B.COLS])



def emit_weight_prep(P, B, wT, scratch, widx, K, M, scr_dram, tiled=False):
    if tiled:
        nkc, nmb = M // 128, K // 1024
        wsrc = wT.ap()
        wdst = scratch.ap()
    else:
        nkc, nmb = K // 128, M // 1024
        wsrc = wT.ap().rearrange("(k p) m -> k p m", p=128)
        wdst = scratch.ap().rearrange("(k p) m -> k p m", p=128)
    acc = B.col(VAR_TMP, 0)
    tmpc = B.col(VAR_TMP, 1)
    P.op("dve", lambda e: e.memset(acc, 0.0), writes=[acc])
    bi = 0
    for kc in range(nkc):
        for mb in range(nmb):
            wr = B.WR[:, 0:1024] if bi % 2 == 0 else B.RT[:, 1024:2048]
            tm2 = B.col(VAR_TMP, 1 + bi % 2)
            P.dma("sp", wr, wsrc[kc, :, mb * 1024:(mb + 1) * 1024],
                  "wr%d" % (bi % 2))
            P.op("dve", lambda e, wr=wr, tm2=tm2: e.reduce_sum(
                tm2, wr, AX.X, apply_absolute_value=True),
                reads=[wr], writes=[tm2])
            P.op("dve", lambda e, tm2=tm2: e.tensor_add(acc, acc, tm2),
                 reads=[tm2, acc], writes=[acc])
            bi += 1
    P.op("pe", lambda e: e.transpose(B.pb[7][0:1, 0:128], acc, B.IDF),
         reads=[acc, B.IDF], writes=[B.pb[7]])
    P.op("dve", lambda e: e.tensor_copy(B.EV[0:1, 0:128], B.pb[7][0:1, 0:128]),
         reads=[B.pb[7]], writes=[B.EV[0:1, 0:128]])
    P.op("dve", lambda e: e.reduce_sum(B.EV[0:1, 512:513], B.EV[0:1, 0:128], AX.X),
         reads=[B.EV[0:1, 0:128]], writes=[B.EV[0:1, 512:513]])
    P.op("dve", lambda e: e.tensor_scalar(
        B.SCR[0:1, widx:widx + 1], B.EV[0:1, 512:513], 1.0 / (K * M), 1e-5,
        ALU.mult, ALU.max),
        reads=[B.EV[0:1, 512:513]], writes=[B.SCR[0:1, widx:widx + 1]])
    P.op("dve", lambda e: e.reciprocal(B.SCR[0:1, 16 + widx:17 + widx],
                                       B.SCR[0:1, widx:widx + 1]),
         reads=[B.SCR[0:1, widx:widx + 1]], writes=[B.SCR[0:1, 16 + widx:17 + widx]])
    P.dma("pool", scr_dram.ap()[0:1, widx:widx + 1], B.SCR[0:1, widx:widx + 1],
          "scs%d" % widx)
    P.dma("pool", scr_dram.ap()[0:1, 16 + widx:17 + widx],
          B.SCR[0:1, 16 + widx:17 + widx], "scs%d" % widx)
    emit_load_scales(P, B, scr_dram, [widx])
    bi = 0
    for kc in range(nkc):
        for mb in range(nmb):
            sl = bi % 2
            wr = B.WR[:, 0:1024] if sl == 0 else B.RT[:, 1024:2048]
            P.dma("sp", wr, wsrc[kc, :, mb * 1024:(mb + 1) * 1024], "wr%d" % sl)
            vv = B.EV[:, 0:1024] if sl == 0 else B.BC[:, 0:1024]
            P.op("dve", lambda e, wr=wr, vv=vv: e.tensor_scalar_mul(
                vv, wr, B.MWC[:, 5 + widx:6 + widx]),
                reads=[wr, B.MWC], writes=[vv])
            c1 = B.CC[:, sl * 1024:(sl + 1) * 1024]
            P.op("dve", lambda e, vv=vv, c1=c1: e.tensor_single_scalar(
                c1, vv, 0.5, ALU.is_gt), reads=[vv], writes=[c1])
            c2 = B.QT[:, sl * 1024:(sl + 1) * 1024]
            P.op("dve", lambda e, vv=vv, c2=c2: e.tensor_single_scalar(
                c2, vv, -0.5, ALU.is_lt), reads=[vv], writes=[c2])
            wb = B.WB[:, sl * 1024:(sl + 1) * 1024]
            P.op("dve", lambda e, c1=c1, c2=c2, wb=wb: e.tensor_sub(wb, c1, c2),
                 reads=[c1, c2], writes=[wb])
            P.dma("pool", wdst[kc, :, mb * 1024:(mb + 1) * 1024], wb, "tst%d" % widx)
            bi += 1


def emit_load_scales(P, B, scr_dram, widxs):
    for w in widxs:
        P.dma("sp", B.MWC[:, w:w + 1],
              scr_dram.ap()[0:1, w:w + 1].broadcast_to([128, 1]), "mwc")
        P.dma("sp", B.MWC[:, 5 + w:6 + w],
              scr_dram.ap()[0:1, 16 + w:17 + w].broadcast_to([128, 1]), "mwc")


def emit_load_bias(P, B, dins):
    P.dma("sp", B.BIA[:, 0:32], dins["b1c"].ap(), "bias")
    P.dma("sp", B.BIA[:, 32:40], dins["b2c"].ap(), "bias")
    P.op("dve", lambda e: e.tensor_scalar_mul(B.BIA[:, 40:48], B.BIA[:, 32:40], 0.5),
         reads=[B.BIA[:, 32:40]], writes=[B.BIA[:, 40:48]])



def emit_stats(P, B, ffn):
    for qt in range(4):
        for c in range(8):
            xsl = B.xs(c, qt)
            slot = c % 2
            xb = B.QT[:, slot * 512:(slot + 1) * 512]
            sq = B.QT[:, 1024 + slot * 512: 1024 + (slot + 1) * 512]
            P.op("dve", lambda e, xb=xb, xsl=xsl: e.tensor_copy(xb, xsl),
                 reads=[xsl], writes=[xb])
            P.op("act", lambda e, sq=sq, xsl=xsl: e.activation(sq, xsl, AF.Square),
                 reads=[xsl], writes=[sq])
            P.op("pe", lambda e, xb=xb, c=c: e.matmul(
                B.pb[5][0:1, :], B.ONE1, xb, start=(c == 0), stop=(c == 7)),
                reads=[xb, B.ONE1], writes=[B.pb[5]])
            P.op("pe", lambda e, sq=sq, c=c: e.matmul(
                B.pb[6][0:1, :], B.ONE1, sq, start=(c == 0), stop=(c == 7)),
                reads=[sq, B.ONE1], writes=[B.pb[6]])
        r1 = B.EV[0:1, 0:512]
        r2 = B.EV[0:1, 512:1024]
        P.op("dve", lambda e: e.tensor_copy(r1, B.pb[5][0:1, :]),
             reads=[B.pb[5]], writes=[r1])
        P.op("dve", lambda e: e.tensor_copy(r2, B.pb[6][0:1, :]),
             reads=[B.pb[6]], writes=[r2])
        for g in range(4):
            P.op("pe", lambda e, g=g: e.transpose(
                B.pb[7][:, g:g + 1], B.EV[0:1, g * 128:(g + 1) * 128], B.IDF[0:1, 0:1]),
                reads=[r1, B.IDF], writes=[B.pb[7]], inc=False)
            P.op("pe", lambda e, g=g: e.transpose(
                B.pb[7][:, 4 + g:5 + g],
                B.EV[0:1, 512 + g * 128:512 + (g + 1) * 128], B.IDF[0:1, 0:1]),
                reads=[r2, B.IDF], writes=[B.pb[7]])
        P.op("dve", lambda e, qt=qt: e.tensor_copy(
            B.col(VAR_S1, qt * 4, 4), B.pb[7][:, 0:4]),
            reads=[B.pb[7]], writes=[B.col(VAR_S1, qt * 4, 4)])
        P.op("dve", lambda e, qt=qt: e.tensor_copy(
            B.col(VAR_S2, qt * 4, 4), B.pb[7][:, 4:8]),
            reads=[B.pb[7]], writes=[B.col(VAR_S2, qt * 4, 4)])
        if ffn:
            ea = B.EV[:, 0:512]
            eb = B.EV[:, 512:1024]
            xview = B.X.rearrange("p (c t) -> p t c", c=8)[:, qt * 512:(qt + 1) * 512, :]
            xkeys = [B.xs(c2_, qt) for c2_ in range(8)] + [r1, r2]
            P.op("dve", lambda e, xview=xview: e.reduce_max(ea, xview, AX.X),
                 reads=xkeys, writes=[ea])
            P.op("dve", lambda e, xview=xview: e.tensor_reduce(
                eb, xview, AX.X, ALU.min), reads=xkeys, writes=[eb])
            for g in range(4):
                P.op("pe", lambda e, g=g: e.transpose(
                    B.pb[7][:, g * 128:(g + 1) * 128], ea[:, g * 128:(g + 1) * 128],
                    B.IDF),
                    reads=[ea, B.IDF], writes=[B.pb[7]])
            for g in range(4):
                P.op("dve", lambda e, g=g, qt=qt: e.reduce_max(
                    B.col(VAR_MAXX, qt * 4 + g), B.pb[7][:, g * 128:(g + 1) * 128],
                    AX.X),
                    reads=[B.pb[7]], writes=[B.col(VAR_MAXX, qt * 4 + g)])
            for g in range(4):
                P.op("pe", lambda e, g=g: e.transpose(
                    B.pb[7][:, g * 128:(g + 1) * 128], eb[:, g * 128:(g + 1) * 128],
                    B.IDF),
                    reads=[eb, B.IDF], writes=[B.pb[7]])
            for g in range(4):
                P.op("dve", lambda e, g=g, qt=qt: e.tensor_reduce(
                    B.col(VAR_MINX, qt * 4 + g), B.pb[7][:, g * 128:(g + 1) * 128],
                    AX.X, ALU.min),
                    reads=[B.pb[7]], writes=[B.col(VAR_MINX, qt * 4 + g)])
    m16 = B.col(VAR_M, 0, 16)
    rst = B.col(VAR_RSTD, 0, 16)
    tmp = B.col(VAR_TMP, 0, 16)
    s1 = B.col(VAR_S1, 0, 16)
    s2 = B.col(VAR_S2, 0, 16)
    P.op("dve", lambda e: e.tensor_scalar_mul(m16, s1, 1.0 / 1024.0),
         reads=[s1], writes=[m16])
    P.op("dve", lambda e: e.tensor_mul(tmp, m16, m16), reads=[m16], writes=[tmp])
    P.op("dve", lambda e: e.scalar_tensor_tensor(
        rst, s2, 1.0 / 1024.0, tmp, ALU.mult, ALU.subtract),
        reads=[s2, tmp], writes=[rst])
    P.op("act", lambda e: e.activation(rst, rst, AF.Sqrt, bias=B.KCONST[:, 0:1]),
         reads=[rst, B.KCONST], writes=[rst])
    P.op("dve", lambda e: e.reciprocal(rst, rst), reads=[rst], writes=[rst])
    if ffn:
        mx = B.col(VAR_MAXX, 0, 16)
        mn = B.col(VAR_MINX, 0, 16)
        a16 = B.col(VAR_ACOL, 0, 16)
        iv = B.col(VAR_INVSX, 0, 16)
        P.op("dve", lambda e: e.tensor_sub(mx, mx, m16), reads=[mx, m16], writes=[mx])
        P.op("dve", lambda e: e.tensor_sub(mn, m16, mn), reads=[mn, m16], writes=[mn])
        P.op("dve", lambda e: e.tensor_max(mx, mx, mn), reads=[mx, mn], writes=[mx])
        P.op("dve", lambda e: e.reciprocal(a16, mx), reads=[mx], writes=[a16])
        P.op("dve", lambda e: e.tensor_scalar_mul(a16, a16, 127.0),
             reads=[a16], writes=[a16])
        P.op("dve", lambda e: e.tensor_mul(iv, rst, mx), reads=[rst, mx], writes=[iv])
        P.op("dve", lambda e: e.tensor_scalar(iv, iv, 1e-5, 1.0 / 127.0,
                                              ALU.max, ALU.mult),
             reads=[iv], writes=[iv])


def emit_bcast(P, B, colap, slot, qt, pbi=4):
    pb = B.pb[pbi]
    for g in range(4):
        cs = colap[:, qt * 4 + g: qt * 4 + g + 1]
        P.op("act", lambda e, cs=cs: e.activation(B.CB, B.CB, AF.Identity,
                                                  bias=cs, scale=0.0),
             reads=[cs, B.CB], writes=[B.CB])
        P.op("pe", lambda e, g=g, pb=pb: e.transpose(
            pb[:, g * 128:(g + 1) * 128], B.CB, B.IDF),
            reads=[B.CB, B.IDF], writes=[pb])
    bc = B.BC[:, slot * 512:(slot + 1) * 512]
    P.op("dve", lambda e, bc=bc, pb=pb: e.tensor_copy(bc, pb),
         reads=[pb], writes=[bc])
    return bc


def emit_round(P, B, src, dst, ro, n=512, pair=False, dkeys=None):
    yi = B.RI[:, ro:ro + n]
    yf = B.RT[:, 1024 + ro:1024 + ro + n]
    c1 = B.CC[:, ro:ro + n]
    c2 = B.CC[:, 1024 + ro:1024 + ro + n]
    P.op("dve", lambda e: e.tensor_scalar(yi, src, 1.0, None, ALU.mult),
         reads=[src], writes=[yi])
    P.op("dve", lambda e: e.tensor_copy(yf, yi), reads=[yi], writes=[yf])
    P.op("dve", lambda e: e.scalar_tensor_tensor(
        c1, src, 0.5, yf, ALU.subtract, ALU.is_gt),
        reads=[src, yf], writes=[c1])
    P.op("dve", lambda e: e.scalar_tensor_tensor(
        c2, src, 0.5, yf, ALU.add, ALU.is_lt),
        reads=[src, yf], writes=[c2])
    P.op("dve", lambda e: e.tensor_add(yf, yf, c1), reads=[yf, c1], writes=[yf])
    if pair:
        yfv = yf.rearrange("p (a b) -> p a b", a=2)
        c2v = c2.rearrange("p (a b) -> p a b", a=2)
    else:
        yfv, c2v = yf, c2
    P.op("dve", lambda e: e.tensor_sub(dst, yfv, c2v), reads=[yf, c2],
         writes=(dkeys if dkeys is not None else [dst]))



def emit_ffn(P, B, tern_g, tern_u, tern_d, wg_i, wu_i, wd_i, l1):
    emit_stats(P, B, ffn=True)
    g3 = tern_g.ap()
    u3 = tern_u.ap() if tern_u is not None else None
    d3 = tern_d.ap().rearrange("(k p) m -> k p m", p=128)
    cgc = B.col(VAR_TMP, 0, 16)
    P.op("dve", lambda e: e.tensor_scalar_mul(
        cgc, B.col(VAR_INVSX, 0, 16), B.MWC[:, wg_i:wg_i + 1]),
        reads=[B.col(VAR_INVSX, 0, 16), B.MWC], writes=[cgc])
    for tt in FFN_TTS:
        for sub in range(2):
            qt = tt * 2 + sub
            bM = emit_bcast(P, B, B.col(VAR_M, 0, 16), 0, qt)
            P.op("dve", lambda e: e.tensor_copy(B.BC[:, 512:1024], B.BC[:, 0:512]),
                 reads=[B.BC[:, 0:512]], writes=[B.BC[:, 512:1024]])
            bA = emit_bcast(P, B, B.col(VAR_ACOL, 0, 16), 2, qt)
            P.op("dve", lambda e: e.tensor_copy(B.BC[:, 1536:2048],
                                                B.BC[:, 1024:1536]),
                 reads=[B.BC[:, 1024:1536]], writes=[B.BC[:, 1536:2048]])
            bMd = B.BC[:, 0:1024].rearrange("p (a b) -> p a b", a=2)
            bAd = B.BC[:, 1024:2048].rearrange("p (a b) -> p a b", a=2)
            bMk = [B.BC[:, 0:512], B.BC[:, 512:1024]]
            bAk = [B.BC[:, 1024:1536], B.BC[:, 1536:2048]]
            for c in range(0, 8, 2):
                xsl3 = B.X.rearrange("p (c t) -> p c t", c=8)[
                    :, c:c + 2, qt * 512:(qt + 1) * 512]
                xm3 = B.EV[:, 0:1024].rearrange("p (a b) -> p a b", a=2)
                t2 = B.RT[:, 0:1024]
                t3 = t2.rearrange("p (a b) -> p a b", a=2)
                xkeys = [B.xs(c, qt), B.xs(c + 1, qt)]
                P.op("dve", lambda e, xsl3=xsl3, xm3=xm3, bMd=bMd: e.tensor_sub(
                    xm3, xsl3, bMd),
                    reads=xkeys + bMk, writes=[B.EV[:, 0:1024]])
                P.op("dve", lambda e, xm3=xm3, t3=t3, bAd=bAd: e.tensor_mul(
                    t3, xm3, bAd),
                    reads=[B.EV[:, 0:1024]] + bAk, writes=[t2])
                xq3 = B.XQ.rearrange("p (k t) -> p k t", k=8)[
                    :, c:c + 2, sub * 512:(sub + 1) * 512]
                xqk = [B.XQ[:, c * 1024 + sub * 512: c * 1024 + sub * 512 + 512],
                       B.XQ[:, (c + 1) * 1024 + sub * 512:
                            (c + 1) * 1024 + sub * 512 + 512]]
                emit_round(P, B, t2, xq3, 0, n=1024, pair=True, dkeys=xqk)
        if tt == 1 and FFN_T1_STOP == "xq":
            continue
        bCg = [emit_bcast(P, B, cgc, 2, tt * 2), emit_bcast(P, B, cgc, 3, tt * 2 + 1)]
        am0 = B.ACC[:, 0:1024]
        an0 = B.CC[:, 1024:2048]
        def gu_mms(m):
            ws = m % 2
            gblk = B.WB[:, ws * 1024: ws * 1024 + 1024]
            P.dma("sp", gblk, g3[m], "wg%d" % ws)
            pg = B.pbw[(m % 2) * 2]
            pu = B.pbw[(m % 2) * 2 + 1]
            if not l1:
                ublk = B.WB[:, 2048 + ws * 1024: 2048 + ws * 1024 + 1024]
                P.dma("sp", ublk, u3[m], "wu%d" % ws)
                for k in range(8):
                    wt = gblk[:, k * 128:(k + 1) * 128]
                    wu = ublk[:, k * 128:(k + 1) * 128]
                    for sub in range(2):
                        xq = B.XQ[:, k * 1024 + sub * 512:
                                  k * 1024 + sub * 512 + 512]
                        dstg = pg[:, sub * 512: sub * 512 + 512]
                        P.op("pe", lambda e, wt=wt, xq=xq, k=k, dstg=dstg: e.matmul(
                            dstg, wt, xq, start=(k == 0), stop=(k == 7)),
                            reads=[wt, xq], writes=[dstg])
                    for sub in range(2):
                        xq = B.XQ[:, k * 1024 + sub * 512:
                                  k * 1024 + sub * 512 + 512]
                        dstu = pu[:, sub * 512: sub * 512 + 512]
                        P.op("pe", lambda e, wu=wu, xq=xq, k=k, dstu=dstu: e.matmul(
                            dstu, wu, xq, start=(k == 0), stop=(k == 7)),
                            reads=[wu, xq], writes=[dstu])
            else:
                for k in range(4):
                    wa = gblk[:, k * 128:(k + 1) * 128]
                    wb = gblk[:, (k + 4) * 128:(k + 5) * 128]
                    for sub in range(2):
                        xq = B.XQ[:, k * 1024 + sub * 512:
                                  k * 1024 + sub * 512 + 512]
                        dsta = pg[:, sub * 512: sub * 512 + 512]
                        P.op("pe", lambda e, wa=wa, xq=xq, k=k, dsta=dsta: e.matmul(
                            dsta, wa, xq, start=(k == 0), stop=(k == 3)),
                            reads=[wa, xq], writes=[dsta])
                    for sub in range(2):
                        xq = B.XQ[:, (k + 4) * 1024 + sub * 512:
                                  (k + 4) * 1024 + sub * 512 + 512]
                        dstb = pu[:, sub * 512: sub * 512 + 512]
                        P.op("pe", lambda e, wb=wb, xq=xq, k=k, dstb=dstb: e.matmul(
                            dstb, wb, xq, start=(k == 0), stop=(k == 3)),
                            reads=[wb, xq], writes=[dstb])

        def gu_chain(m):
            pg = B.pbw[(m % 2) * 2]
            pu = B.pbw[(m % 2) * 2 + 1]
            gs = B.EV[:, 0:1024]
            hsl = B.HB[:, m * 1024: m * 1024 + 1024]
            bCgp = B.BC[:, 1024:2048]
            pbk = [B.pb[(m % 2) * 4], B.pb[(m % 2) * 4 + 1],
                   B.pb[(m % 2) * 4 + 2], B.pb[(m % 2) * 4 + 3]]
            if l1:
                tmpu = B.RT[:, (m % 2) * 1024:(m % 2) * 1024 + 1024]
                P.op("act", lambda e, pu=pu, tmpu=tmpu: e.activation(
                    tmpu, pu, AF.Identity),
                    reads=[pbk[2], pbk[3]], writes=[tmpu])
                P.op("dve", lambda e, pg=pg, tmpu=tmpu, gs=gs: e.scalar_tensor_tensor(
                    gs, pg, 1.0, tmpu, ALU.mult, ALU.add),
                    reads=[pbk[0], pbk[1], tmpu], writes=[gs])
                P.op("dve", lambda e, gs=gs, bCgp=bCgp: e.tensor_mul(
                    gs, gs, bCgp), reads=[gs, bCgp], writes=[gs])
                b1c = B.BIA[:, m:m + 1]
                P.op("act", lambda e, gs=gs, hsl=hsl, b1c=b1c: e.activation(
                    hsl, gs, AF.Gelu, bias=b1c),
                    reads=[gs, B.BIA], writes=[hsl])
            else:
                P.op("dve", lambda e, pg=pg, gs=gs, bCgp=bCgp: e.tensor_mul(
                    gs, pg, bCgp),
                    reads=[pbk[0], pbk[1], bCgp], writes=[gs])
                P.op("act", lambda e, gs=gs: e.activation(gs, gs, AF.Silu),
                     reads=[gs], writes=[gs])
                P.op("dve", lambda e, gs=gs, hsl=hsl, pu=pu: e.tensor_mul(
                    hsl, gs, pu),
                    reads=[gs, pbk[2], pbk[3]],
                    writes=[hsl])

        for m in range(33):
            if m < 32:
                gu_mms(m)
            if m >= 1:
                gu_chain(m - 1)
        if True:
            if True:
                if True:
                    pass

        if tt == 1 and FFN_T1_STOP == "gu":
            continue
        hview = B.HB[:, 0:32768].rearrange("p (m t) -> p t m", m=32)
        hkeys = [B.HB[:, 0:32768]]
        P.op("dve", lambda e, hview=hview: e.tensor_reduce(
            am0, hview, AX.X, ALU.max, apply_absolute_value=True),
             reads=hkeys, writes=[am0])
        shc = B.col(VAR_SH, tt * 8, 8)
        for half in range(2):
            rmax = B.RT[:, half * 512:(half + 1) * 512]
            P.op("dve", lambda e, half=half, rmax=rmax: e.tensor_copy(
                rmax, B.ACC[:, half * 512:(half + 1) * 512]),
                reads=[am0], writes=[rmax])
            for g in range(4):
                P.op("pe", lambda e, g=g, rmax=rmax: e.transpose(
                    B.pb[7][:, g * 128:(g + 1) * 128],
                    rmax[:, g * 128:(g + 1) * 128], B.IDF),
                    reads=[rmax, B.IDF], writes=[B.pb[7]])
            for g in range(4):
                tg = tt * 8 + half * 4 + g
                P.op("dve", lambda e, g=g, tg=tg: e.reduce_max(
                    B.col(VAR_SH, tg), B.pb[7][:, g * 128:(g + 1) * 128],
                    AX.X),
                    reads=[B.pb[7]], writes=[B.col(VAR_SH, tg)])
        if tt == 1 and FFN_T1_STOP == "tailmid":
            continue
        P.op("dve", lambda e, shc=shc: e.tensor_scalar_max(shc, shc, 1e-5),
             reads=[shc], writes=[shc])
        shs = B.col(VAR_MINX, tt * 8, 8)
        P.op("dve", lambda e, shs=shs, shc=shc: e.reciprocal(shs, shc),
             reads=[shc], writes=[shs])
        P.op("dve", lambda e, shs=shs: e.tensor_scalar_mul(shs, shs, 127.0),
             reads=[shs], writes=[shs])
        sd = B.col(VAR_MAXX, tt * 8, 8)
        if l1:
            P.op("dve", lambda e, sd=sd, shc=shc: e.tensor_scalar_mul(
                sd, shc, 1.0 / 127.0), reads=[shc], writes=[sd])
        else:
            iv8 = B.col(VAR_INVSX, tt * 8, 8)
            P.op("dve", lambda e, sd=sd, iv8=iv8, shc=shc: e.tensor_mul(sd, iv8, shc),
                 reads=[iv8, shc], writes=[sd])
            P.op("dve", lambda e, sd=sd: e.tensor_scalar_mul(
                sd, sd, B.MWC[:, wu_i:wu_i + 1]),
                 reads=[sd, B.MWC], writes=[sd])
            P.op("dve", lambda e, sd=sd: e.tensor_scalar(
                sd, sd, 1e-5, 1.0 / 127.0, ALU.max, ALU.mult),
                 reads=[sd], writes=[sd])
        P.op("dve", lambda e, sd=sd: e.tensor_scalar_mul(
            sd, sd, B.MWC[:, wd_i:wd_i + 1]),
             reads=[sd, B.MWC], writes=[sd])
        P.op("dve", lambda e, sd=sd: e.tensor_scalar_mul(sd, sd, 0.5),
             reads=[sd], writes=[sd])
        if tt == 1 and FFN_T1_STOP == "tailsd":
            continue
        bSh = [emit_bcast(P, B, B.col(VAR_MINX, 0, 16), 0, tt * 2),
               emit_bcast(P, B, B.col(VAR_MINX, 0, 16), 1, tt * 2 + 1)]
        bSd = [emit_bcast(P, B, B.col(VAR_MAXX, 0, 16), 2, tt * 2),
               emit_bcast(P, B, B.col(VAR_MAXX, 0, 16), 3, tt * 2 + 1)]
        if tt == 1 and FFN_T1_STOP == "tail":
            continue
        bShp = B.BC[:, 0:1024]
        for m in range(32):
            hfull = B.HB[:, m * 1024: m * 1024 + 1024]
            tfull = B.RT[:, 0:1024]
            P.op("dve", lambda e, hfull=hfull, tfull=tfull: e.tensor_mul(
                tfull, hfull, bShp), reads=[hfull, bSh[0], bSh[1]], writes=[tfull])
            emit_round(P, B, tfull, hfull, 0, n=1024)
        if tt == 1 and FFN_T1_STOP == "round":
            continue
        for sub in range(2):
            qt = tt * 2 + sub
            rtd = B.RT[:, sub * 1024:(sub + 1) * 1024]
            for hh in range(2):
                P.op("dve", lambda e, rtd=rtd, hh=hh, sub=sub: e.tensor_copy(
                    rtd[:, hh * 512:(hh + 1) * 512], bSd[sub]),
                    reads=[bSd[sub]], writes=[rtd])
            for k in range(32):
                ws = k % 2
                wdb = B.WB[:, ws * 1024: ws * 1024 + 1024]
                P.dma("sp", wdb, d3[k, :, :], "wg%d" % ws)
                hq = B.HB[:, k * 1024 + sub * 512: k * 1024 + sub * 512 + 512]
                for m8 in range(8):
                    wt = wdb[:, m8 * 128:(m8 + 1) * 128]
                    P.op("pe", lambda e, wt=wt, hq=hq, k=k, m8=m8: e.matmul(
                        B.pb[m8], wt, hq, start=(k == 0), stop=(k == 31)),
                        reads=[wt, hq], writes=[B.pb[m8]])
            for p4 in range(4):
                ev = B.EV[:, 0:1024] if p4 % 2 == 0 else B.WR[:, 0:1024]
                P.op("dve", lambda e, p4=p4, ev=ev, rtd=rtd: e.tensor_mul(
                    ev, B.pbw[p4], rtd),
                    reads=[B.pb[2 * p4], B.pb[2 * p4 + 1], rtd], writes=[ev])
                if l1:
                    for hh in range(2):
                        m8 = 2 * p4 + hh
                        b2h = B.BIA[:, 40 + m8:41 + m8]
                        evh = ev[:, hh * 512:(hh + 1) * 512]
                        P.op("act", lambda e, evh=evh, b2h=b2h: e.activation(
                            evh, evh, AF.Identity, bias=b2h),
                            reads=[ev, B.BIA], writes=[ev])
                xp = B.X.rearrange("p (c t) -> p c t", c=8)[
                    :, 2 * p4:2 * p4 + 2, qt * 512:(qt + 1) * 512]
                ev3 = ev.rearrange("p (a b) -> p a b", a=2)
                xkeys = [B.xs(2 * p4, qt), B.xs(2 * p4 + 1, qt)]
                P.op("dve", lambda e, xp=xp, ev3=ev3: e.tensor_add(xp, xp, ev3),
                     reads=[ev] + xkeys, writes=xkeys)



def emit_attn(P, B, layer, qkvT, woT, dscr):
    emit_stats(P, B, ffn=False)
    l1 = layer == 1
    qoff = 8 if l1 else 0
    koff = 0 if l1 else 8
    voff = 16
    qkv = qkvT.ap()
    wo = woT.ap()
    VOFF = 16384
    vones = B.HB[:, VOFF:VOFF + 16640].rearrange("p (s c) -> p s c", c=65)[:, :, 64:65]
    P.op("dve", lambda e: e.memset(vones, 1.0), writes=[B.HB[:, VOFF:VOFF + 16640]])

    def build_u(qt):
        bM = emit_bcast(P, B, B.col(VAR_M, 0, 16), 0, qt)
        bR = emit_bcast(P, B, B.col(VAR_RSTD, 0, 16), 1, qt)
        for c in range(8):
            xsl = B.xs(c, qt)
            ro = (c % 2) * 512
            xm = B.EV[:, ro:ro + 512]
            P.op("dve", lambda e, xsl=xsl, xm=xm, bM=bM: e.tensor_sub(xm, xsl, bM),
                 reads=[xsl, bM], writes=[xm])
            u = B.XQ[:, c * 512:(c + 1) * 512]
            P.op("dve", lambda e, xm=xm, u=u, bR=bR: e.tensor_mul(u, xm, bR),
                 reads=[xm, bR], writes=[u])

    def proj_one(qt, m, bi):
        ws = bi % 2
        wr = B.WR[:, 0:1024] if ws == 0 else B.RT[:, 0:1024]
        P.dma("sp", wr, qkv[m], "wr%d" % ws)
        wb = B.WB[:, 2048 + ws * 1024: 2048 + (ws + 1) * 1024]
        P.op("dve", lambda e, wb=wb, wr=wr: e.tensor_copy(wb, wr),
             reads=[wr], writes=[wb])
        pb = B.pb[ws]
        for k in range(8):
            wt = wb[:, k * 128:(k + 1) * 128]
            u = B.XQ[:, k * 512:(k + 1) * 512]
            P.op("pe", lambda e, wt=wt, u=u, k=k, pb=pb: e.matmul(
                pb, wt, u, start=(k == 0), stop=(k == 7)),
                reads=[wt, u], writes=[pb])
        return pb

    def rope_evac(pb, toff, dst):
        ea = B.BC[:, 1536:2048]
        ebb = B.EV[:, 0:512]
        P.op("dve", lambda e: e.tensor_copy(ea, pb), reads=[pb], writes=[ea])
        P.op("dve", lambda e: e.stream_shuffle(ebb, ea, _SHUF_PAIR),
             reads=[ea], writes=[ebb])
        P.op("dve", lambda e: e.tensor_mul(ea, ea, B.RC[:, toff:toff + 512]),
             reads=[ea, B.RC], writes=[ea])
        P.op("pool", lambda e: e.tensor_mul(ebb, ebb, B.RS[:, toff:toff + 512]),
             reads=[ebb, B.RS], writes=[ebb])
        P.op("dve", lambda e: e.tensor_add(dst, ea, ebb),
             reads=[ea, ebb], writes=[dst])

    for qt in range(4):
        build_u(qt)
        for j in range(8):
            pb = proj_one(qt, koff + j, j)
            dst = B.HB[:, j * 2048 + qt * 512: j * 2048 + qt * 512 + 512]
            if l1:
                rope_evac(pb, qt * 512, dst)
            else:
                P.op("dve", lambda e, dst=dst, pb=pb: e.tensor_copy(dst, pb),
                     reads=[pb], writes=[dst])
        for j in range(8):
            pb = proj_one(qt, voff + j, j)
            ev = B.EV[:, 0:512]
            P.op("dve", lambda e, ev=ev, pb=pb: e.tensor_copy(ev, pb),
                 reads=[pb], writes=[ev])
            for g in range(4):
                kc = qt * 4 + g
                pt = B.pb[6 + (g % 2)]
                P.op("pe", lambda e, g=g, pt=pt: e.transpose(
                    pt[:, 0:128], ev[:, g * 128:(g + 1) * 128], B.IDF),
                    reads=[ev, B.IDF], writes=[pt])
                for hh in range(2):
                    h = 2 * j + hh
                    slot = VOFF + (kc * 16 + h) * 65
                    dst = B.HB[:, slot:slot + 64]
                    P.op("dve", lambda e, dst=dst, pt=pt, hh=hh: e.tensor_copy(
                        dst, pt[:, hh * 64:hh * 64 + 64]),
                        reads=[pt], writes=[dst])
    for qt in range(4):
        build_u(qt)
        for j in range(8):
            pb = proj_one(qt, qoff + j, j)
            dst = B.QT[:, j * 512:(j + 1) * 512]
            if l1:
                rope_evac(pb, qt * 512, dst)
            else:
                P.op("dve", lambda e, dst=dst, pb=pb: e.tensor_copy(dst, pb),
                     reads=[pb], writes=[dst])
        nkc = qt * 4 + 4
        for h in range(NH):
            qslice = B.QT[64 * (h % 2):64 * (h % 2) + 64,
                          (h // 2) * 512:(h // 2) * 512 + 512]
            pav = B.pb[5]
            def sc_mm(kc):
                ps = B.pb[2 + (kc % 2)]
                ksl = B.HB[64 * (h % 2):64 * (h % 2) + 64,
                           (h // 2) * 2048 + kc * 128: (h // 2) * 2048 + kc * 128 + 128]
                P.op("pe", lambda e, ksl=ksl, qslice=qslice, ps=ps: e.matmul(
                    ps, ksl, qslice, start=True, stop=True),
                    reads=[ksl, qslice], writes=[ps])

            def sc_tail(kc):
                ps = B.pb[2 + (kc % 2)]
                p0 = B.CC[:, (kc % 2) * 512:(kc % 2) * 512 + 512]
                P.op("act", lambda e, ps=ps, p0=p0: e.activation(
                    p0, ps, AF.Exp, scale=B.KCONST[:, 1:2]),
                    reads=[ps, B.KCONST], writes=[p0])
                if (not l1) or (kc >= qt * 4):
                    moff = qt * 512 - kc * 128 + 384
                    msl = B.MASK[:, moff:moff + 512]
                    P.op("dve", lambda e, p0=p0, msl=msl: e.tensor_mul(p0, p0, msl),
                         reads=[p0, msl], writes=[p0])
                vsl = B.HB[:, VOFF + (kc * 16 + h) * 65: VOFF + (kc * 16 + h) * 65 + 65]
                P.op("pe", lambda e, vsl=vsl, p0=p0, kc=kc, nkc=nkc, pav=pav: e.matmul(
                    pav[0:65, :], vsl, p0, start=(kc == 0), stop=(kc == nkc - 1)),
                    reads=[vsl, p0], writes=[pav])

            for kc in range(nkc + 1):
                if kc < nkc:
                    sc_mm(kc)
                if kc >= 1:
                    sc_tail(kc - 1)
            drow = B.EV[64:65, 512:1024]
            P.op("dve", lambda e, pav=pav: e.reciprocal(drow, pav[64:65, 0:512]),
                 reads=[pav], writes=[drow])
            dsl = dscr.ap()[h % 4: h % 4 + 1, :]
            P.dma("act", dsl, drow, "dsc%d" % (h % 4))
            db = B.BC[0:64, 1536:2048]
            P.dma("sp", db, dsl.broadcast_to([64, 512]), "dbc")
            if h % 2 == 0:
                ao = B.XQ[0:64, (h // 2) * 512:(h // 2) * 512 + 512]
                P.op("dve", lambda e, ao=ao, db=db, pav=pav: e.tensor_mul(
                    ao, pav[0:64, :], db), reads=[pav, db], writes=[ao])
            else:
                stg = B.CC[0:64, 1024:1536]
                P.op("dve", lambda e, stg=stg, db=db, pav=pav: e.tensor_mul(
                    stg, pav[0:64, :], db), reads=[pav, db], writes=[stg])
                ao = B.XQ[64:128, (h // 2) * 512:(h // 2) * 512 + 512]
                P.dma("pool", ao, stg, "aomv")
        for mo in range(8):
            ws = mo % 2
            wr = B.WR[:, 0:1024] if ws == 0 else B.RT[:, 0:1024]
            P.dma("sp", wr, wo[mo], "wr%d" % ws)
            wb = B.WB[:, 2048 + ws * 1024: 2048 + (ws + 1) * 1024]
            P.op("dve", lambda e, wb=wb, wr=wr: e.tensor_copy(wb, wr),
                 reads=[wr], writes=[wb])
            po = B.pb[6 + ws]
            for k in range(8):
                wt = wb[:, k * 128:(k + 1) * 128]
                ao = B.XQ[:, k * 512:(k + 1) * 512]
                P.op("pe", lambda e, wt=wt, ao=ao, k=k, po=po: e.matmul(
                    po, wt, ao, start=(k == 0), stop=(k == 7)),
                    reads=[wt, ao], writes=[po])
            xsl = B.xs(mo, qt)
            P.op("dve", lambda e, xsl=xsl, po=po: e.tensor_add(xsl, xsl, po),
                 reads=[po, xsl], writes=[xsl])



def emit_input_tm(P, B, x_tm):
    xv = x_tm.ap().rearrange("(n p) d -> n p d", p=128)
    for nt in range(16):
        io = B.BC[:, 0:1024]
        P.dma("sp", io, xv[nt], "xin")
        for c in range(8):
            pt = B.pb[c % 2]
            P.op("pe", lambda e, c=c, io=io, pt=pt: e.transpose(
                pt[:, 0:128], io[:, c * 128:(c + 1) * 128], B.IDF),
                reads=[io, B.IDF], writes=[pt])
            dst = B.X[:, c * 2048 + nt * 128: c * 2048 + (nt + 1) * 128]
            P.op("dve", lambda e, dst=dst, pt=pt: e.tensor_copy(dst, pt[:, 0:128]),
                 reads=[pt], writes=[dst])


def emit_output_tm(P, B, y_tm):
    yv = y_tm.ap().rearrange("(n p) d -> n p d", p=128)
    for nt in range(16):
        io = B.BC[:, 1024:2048]
        for c in range(8):
            src = B.X[:, c * 2048 + nt * 128: c * 2048 + (nt + 1) * 128]
            pt = B.pb[2 + c % 2]
            P.op("pe", lambda e, src=src, pt=pt: e.transpose(
                pt[:, 0:128], src, B.IDF),
                reads=[src, B.IDF], writes=[pt])
            P.op("dve", lambda e, c=c, io=io, pt=pt: e.tensor_copy(
                io[:, c * 128:(c + 1) * 128], pt[:, 0:128]),
                reads=[pt], writes=[io[:, c * 128:(c + 1) * 128]])
        P.dma("pool", yv[nt], io, "yo")


def emit_input_fm(P, B, x_fm):
    xv = x_fm.ap().rearrange("(c p) n -> c p n", p=128)
    for c in range(8):
        P.dma("sp", B.X[:, c * 2048:(c + 1) * 2048], xv[c], "xin%d" % c)


def emit_output_fm(P, B, y_fm):
    yv = y_fm.ap().rearrange("(c p) n -> c p n", p=128)
    for c in range(8):
        P.dma("pool", yv[c], B.X[:, c * 2048:(c + 1) * 2048], "yo")



W_G0, W_U0, W_D0, W_1, W_2 = 0, 1, 2, 3, 4
TKEYS = ["wg3", "wu3", "wd3", "w13", "w23"]
TSHAPES = {"wg3": (1024, 4096), "wu3": (1024, 4096), "wd3": (4096, 1024),
           "w13": (1024, 4096), "w23": (4096, 1024)}
TILED = {"wg3": True, "wu3": True, "wd3": False, "w13": True, "w23": False}


def _tshape(k):
    K, M = TSHAPES[k]
    return [M // 128, 128, K] if TILED[k] else [K, M]
WIDX = {"wg3": W_G0, "wu3": W_U0, "wd3": W_D0, "w13": W_1, "w23": W_2}


def build_prep():
    nc = bass.Bass("TRN2", target_bir_lowering=False, debug=False)
    dins = {"identf": nc.dram_tensor("identf", [128, 128], F32, kind="ExternalInput")}
    wts = {}
    for k in TKEYS:
        wts[k] = (nc.dram_tensor(k + "_in", _tshape(k), F32, kind="ExternalInput"),
                  nc.dram_tensor(k, _tshape(k), BF16, kind="ExternalOutput"))
    scales = nc.dram_tensor("scales", [1, 32], F32, kind="ExternalOutput")
    P = Prog(nc)
    B = Bufs(nc)
    emit_setup(P, B, dins)
    for k in TKEYS:
        K, M = TSHAPES[k]
        emit_weight_prep(P, B, wts[k][0], wts[k][1], WIDX[k], K, M, scales,
                         tiled=TILED[k])
    P.emit()
    nc.finalize()
    return nc


def build_main(stages, x_in, x_out):
    nc = bass.Bass("TRN2", target_bir_lowering=False, debug=False)
    dins = {"identf": nc.dram_tensor("identf", [128, 128], F32, kind="ExternalInput")}
    need = set()
    for st in stages:
        if st[0] == "ffn":
            need.add("scales")
            need |= ({"wg3", "wu3", "wd3"} if st[1] == 0 else {"w13", "w23", "bias"})
        else:
            need |= {"attn%d" % st[1]}
            if st[1] == 1:
                need.add("rope")
    tern = {}
    for k in TKEYS:
        if k in need:
            tern[k] = nc.dram_tensor(k, _tshape(k), BF16, kind="ExternalInput")
    scales = (nc.dram_tensor("scales", [1, 32], F32, kind="ExternalInput")
              if "scales" in need else None)
    if "bias" in need:
        dins["b1c"] = nc.dram_tensor("b1c", [128, 32], F32, kind="ExternalInput")
        dins["b2c"] = nc.dram_tensor("b2c", [128, 8], F32, kind="ExternalInput")
    for L in (0, 1):
        if ("attn%d" % L) in need:
            w = _HC["mask%d" % L].shape[1]
            dins["mask%d" % L] = nc.dram_tensor("mask%d" % L, [128, w], BF16,
                                                kind="ExternalInput")
            dins["qkvT%d" % L] = nc.dram_tensor("qkvT%d" % L, [1024, 3072], F32,
                                                kind="ExternalInput")
            dins["woT%d" % L] = nc.dram_tensor("woT%d" % L, [1024, 1024], F32,
                                               kind="ExternalInput")
    if "rope" in need:
        dins["ropeC"] = nc.dram_tensor("ropeC", [128, 2048], BF16, kind="ExternalInput")
        dins["ropeS"] = nc.dram_tensor("ropeS", [128, 2048], BF16, kind="ExternalInput")
    x_d = nc.dram_tensor("x", [2048, 1024] if x_in == "tm" else [1024, 2048],
                         F32, kind="ExternalInput")
    y_d = nc.dram_tensor("y", [2048, 1024] if x_out == "tm" else [1024, 2048],
                         F32, kind="ExternalOutput")
    dscr = nc.dram_tensor("dscr", [4, 512], F32, kind="Internal")

    P = Prog(nc)
    B = Bufs(nc)
    emit_setup(P, B, dins)
    if scales is not None:
        emit_load_scales(P, B, scales, list(range(5)))
    if "bias" in need:
        emit_load_bias(P, B, dins)
    if "rope" in need:
        P.dma("sp", B.RC, dins["ropeC"].ap(), "ropec")
        P.dma("sp", B.RS, dins["ropeS"].ap(), "ropes")
    if x_in == "tm":
        emit_input_tm(P, B, x_d)
    else:
        emit_input_fm(P, B, x_d)
    for st in stages:
        if st[0] == "ffn":
            if st[1] == 0:
                emit_ffn(P, B, tern["wg3"], tern["wu3"], tern["wd3"],
                         W_G0, W_U0, W_D0, l1=False)
            else:
                emit_ffn(P, B, tern["w13"], None, tern["w23"],
                         W_1, None, W_2, l1=True)
        else:
            L = st[1]
            m = _HC["mask%d" % L]
            P.dma("sp", B.MASK[:, 0:m.shape[1]], dins["mask%d" % L].ap(), "mask")
            emit_attn(P, B, L, dins["qkvT%d" % L], dins["woT%d" % L], dscr)
    if x_out == "tm":
        emit_output_tm(P, B, y_d)
    else:
        emit_output_fm(P, B, y_d)
    P.emit()
    nc.finalize()
    return nc


def build_fused():
    nc = bass.Bass("TRN2", target_bir_lowering=False, debug=False)
    dins = {"identf": nc.dram_tensor("identf", [128, 128], F32, kind="ExternalInput")}
    win = {}
    tern = {}
    for k in TKEYS:
        win[k] = nc.dram_tensor(k + "_in", _tshape(k), F32, kind="ExternalInput")
        tern[k] = nc.dram_tensor(k, _tshape(k), BF16, kind="Internal")
    scales = nc.dram_tensor("scales", [1, 32], F32, kind="Internal")
    dins["b1c"] = nc.dram_tensor("b1c", [128, 32], F32, kind="ExternalInput")
    dins["b2c"] = nc.dram_tensor("b2c", [128, 8], F32, kind="ExternalInput")
    for L in (0, 1):
        w = _HC["mask%d" % L].shape[1]
        dins["mask%d" % L] = nc.dram_tensor("mask%d" % L, [128, w], BF16,
                                            kind="ExternalInput")
        dins["qkvT%d" % L] = nc.dram_tensor("qkvT%d" % L, [24, 128, 1024], F32,
                                            kind="ExternalInput")
        dins["woT%d" % L] = nc.dram_tensor("woT%d" % L, [8, 128, 1024], F32,
                                           kind="ExternalInput")
    dins["ropeC"] = nc.dram_tensor("ropeC", [128, 2048], BF16, kind="ExternalInput")
    dins["ropeS"] = nc.dram_tensor("ropeS", [128, 2048], BF16, kind="ExternalInput")
    x_d = nc.dram_tensor("x", [2048, 1024], F32, kind="ExternalInput")
    y_d = nc.dram_tensor("y", [2048, 1024], F32, kind="ExternalOutput")
    dscr = nc.dram_tensor("dscr", [4, 512], F32, kind="Internal")

    P = Prog(nc)
    B = Bufs(nc)
    emit_setup(P, B, dins)
    emit_load_bias(P, B, dins)
    emit_input_tm(P, B, x_d)
    for k in TKEYS:
        K, M = TSHAPES[k]
        emit_weight_prep(P, B, win[k], tern[k], WIDX[k], K, M, scales,
                         tiled=TILED[k])
        P.seal_tensor(k, "tst%d" % WIDX[k])
    for st in [("ffn", 0), ("attn", 0), ("ffn", 0), ("ffn", 1), ("attn", 1),
               ("ffn", 1)]:
        if st[0] == "ffn":
            if st[1] == 0:
                emit_ffn(P, B, tern["wg3"], tern["wu3"], tern["wd3"],
                         W_G0, W_U0, W_D0, l1=False)
            else:
                emit_ffn(P, B, tern["w13"], None, tern["w23"],
                         W_1, None, W_2, l1=True)
        else:
            L = st[1]
            m = _HC["mask%d" % L]
            P.dma("sp", B.MASK[:, 0:m.shape[1]], dins["mask%d" % L].ap(), "mask")
            if L == 1:
                P.dma("sp", B.RC, dins["ropeC"].ap(), "ropec")
                P.dma("sp", B.RS, dins["ropeS"].ap(), "ropes")
            emit_attn(P, B, L, dins["qkvT%d" % L], dins["woT%d" % L], dscr)
    emit_output_tm(P, B, y_d)
    P.emit()
    nc.finalize()
    return nc



_NC_CACHE = {}


def _get(key, fn):
    if key not in _NC_CACHE:
        _NC_CACHE[key] = fn()
    return _NC_CACHE[key]


def kernel(L0_attn_x, L0_attn_qkv, L0_attn_wo, L0_ffn_w_gate, L0_ffn_w_up,
           L0_ffn_w_down, L1_attn_W_KQV, L1_attn_W_O, L1_ffn_W1, L1_ffn_b1,
           L1_ffn_W2, L1_ffn_b2):
    x = np.ascontiguousarray(np.asarray(L0_attn_x, np.float32))
    tw = lambda w: np.ascontiguousarray(np.asarray(w, np.float32).T)

    def tile5(wT):
        K, M = wT.shape
        return np.ascontiguousarray(
            wT.reshape(K // 128, 128, M // 128, 128).transpose(2, 1, 0, 3)
            .reshape(M // 128, 128, K))

    wgT, wuT, wdT = tile5(tw(L0_ffn_w_gate)), tile5(tw(L0_ffn_w_up)), tw(L0_ffn_w_down)
    w1T, w2T = tile5(tw(L1_ffn_W1)), tw(L1_ffn_W2)
    qkvT0, woT0 = tile5(tw(L0_attn_qkv)), tile5(tw(L0_attn_wo))
    qkvT1, woT1 = tile5(tw(L1_attn_W_KQV)), tile5(tw(L1_attn_W_O))
    b1c = np.ascontiguousarray(np.asarray(L1_ffn_b1, np.float32).reshape(32, 128).T)
    b2c = np.ascontiguousarray(np.asarray(L1_ffn_b2, np.float32).reshape(8, 128).T)
    consts = _HC

    if FUSED:
        ncf = _get("fused", build_fused)
        in_maps = []
        for i in range(8):
            mm = {"identf": consts["identf"], "b1c": b1c, "b2c": b2c,
                  "ropeC": consts["ropeC"], "ropeS": consts["ropeS"],
                  "mask0": np.ascontiguousarray(consts["mask0"]),
                  "mask1": np.ascontiguousarray(consts["mask1"]),
                  "qkvT0": qkvT0, "woT0": woT0, "qkvT1": qkvT1, "woT1": woT1,
                  "x": np.ascontiguousarray(x[i])}
            mm.update({k + "_in": w for k, w in
                       zip(TKEYS, [wgT, wuT, wdT, w1T, w2T])})
            in_maps.append(mm)
        res = run_bass_kernel_spmd(ncf, in_maps, core_ids=list(range(8)))
        return np.stack([res.results[i]["y"] for i in range(8)], axis=0).astype(np.float32)

    ncp = _get("prep", build_prep)
    inp = {"identf": consts["identf"]}
    inp.update({k + "_in": w for k, w in zip(TKEYS, [wgT, wuT, wdT, w1T, w2T])})
    resp = run_bass_kernel_spmd(ncp, [inp] * 8, core_ids=list(range(8)))
    tern = {k: resp.results[0][k] for k in TKEYS}
    scales = resp.results[0]["scales"]

    common = {"identf": consts["identf"]}
    stage_cfgs = [
        ([("ffn", 0)], "tm", "fm"),
        ([("attn", 0)], "fm", "fm"),
        ([("ffn", 0)], "fm", "fm"),
        ([("ffn", 1)], "fm", "fm"),
        ([("attn", 1)], "fm", "fm"),
        ([("ffn", 1)], "fm", "tm"),
    ]
    cur = [np.ascontiguousarray(x[i]) for i in range(8)]
    for si, (stages, xin, xout) in enumerate(stage_cfgs):
        ncm = _get(("main", si), lambda stages=stages, xin=xin, xout=xout:
                   build_main(stages, xin, xout))
        in_maps = []
        for i in range(8):
            m = dict(common)
            st = stages[0]
            if st[0] == "ffn":
                m["scales"] = scales
                if st[1] == 0:
                    m.update({k: tern[k] for k in ("wg3", "wu3", "wd3")})
                else:
                    m.update({k: tern[k] for k in ("w13", "w23")})
                    m.update({"b1c": b1c, "b2c": b2c})
            else:
                L = st[1]
                m["mask%d" % L] = np.ascontiguousarray(consts["mask%d" % L])
                m["qkvT%d" % L] = qkvT0 if L == 0 else qkvT1
                m["woT%d" % L] = woT0 if L == 0 else woT1
                if L == 1:
                    m["ropeC"] = consts["ropeC"]
                    m["ropeS"] = consts["ropeS"]
            m["x"] = np.ascontiguousarray(cur[i])
            in_maps.append(m)
        res = run_bass_kernel_spmd(ncm, in_maps, core_ids=list(range(8)))
        cur = [res.results[i]["y"] for i in range(8)]
    return np.stack(cur, axis=0).astype(np.float32)


# version 13
# speedup vs baseline: 1.8449x; 1.1279x over previous
import numpy as np
import ml_dtypes
import concourse.bass as bass
import concourse.mybir as mybir
from concourse.bass_utils import run_bass_kernel_spmd

F32 = mybir.dt.float32
BF16 = mybir.dt.bfloat16
I32 = mybir.dt.int32
AF = mybir.ActivationFunctionType
ALU = mybir.AluOpType
AX = mybir.AxisListType
BFNP = ml_dtypes.bfloat16

S, DM, NH, DH, INNER = 2048, 1024, 16, 64, 4096
FFN_TTS = range(2)
FFN_T1_STOP = None
FUSED = True


def host_constants():
    c = {}
    c["identf"] = np.eye(128, dtype=np.float32)
    jj = np.arange(128)[:, None]
    uu = np.arange(2432)[None, :] - 384
    d = uu - jj
    allowed = ((d >= 0) & (d <= 127)) | ((d >= 132) & ((d - 128) % 4 == 0))
    c["mask0"] = allowed.astype(BFNP)
    uu1 = np.arange(896)[None, :] - 384
    c["mask1"] = ((uu1 - jj) >= 0).astype(BFNP)
    r = np.arange(0, DH, 2, dtype=np.float64)
    invf = 1.0 / (10000.0 ** (r / DH))
    fr = np.arange(S, dtype=np.float64)[:, None] * invf[None, :]
    cos = np.cos(fr); sin = np.sin(fr)
    C = np.repeat(cos, 2, axis=1)
    S2 = np.empty((S, DH)); S2[:, 0::2] = -sin; S2[:, 1::2] = sin
    c["ropeC"] = np.tile(C.T, (2, 1)).astype(BFNP)
    c["ropeS"] = np.tile(S2.T, (2, 1)).astype(BFNP)
    return c

_HC = host_constants()
_SHUF_PAIR = []
for _i in range(0, 32, 2):
    _SHUF_PAIR.extend([_i + 1, _i])


QSEM = {"sp": "qsp", "act": "qact", "pool": "qpool"}


def _apkeys(x):
    if not hasattr(x, "tensor"):
        return (x,)
    t = x.tensor
    nm = t.name
    dims = list(x.ap)
    try:
        import bass_rust
        is_dram = x.space == getattr(bass_rust, "MemorySpace").DRAM
    except Exception:
        is_dram = str(getattr(x, "space", "")).find("DRAM") >= 0
    lo = int(x.offset)
    if not is_dram:
        fw = 1
        for d in list(t.shape)[1:]:
            fw *= int(d)
        lo = lo % fw
    hi = lo
    use = dims if is_dram else dims[1:] if len(dims) > 1 else []
    for st, ct in use:
        hi += (int(ct) - 1) * int(st)
    hi += 1
    gran = 65536 if is_dram else 512
    return tuple((nm, i) for i in range(lo // gran, (hi + gran - 1) // gran))


class Prog:
    def __init__(self, nc):
        self.nc = nc
        self.sem = {}
        for e in ("pe", "dve", "act", "pool"):
            self.sem[e] = nc.alloc_semaphore("S" + e)
        self.cnt = {e: 0 for e in self.sem}
        self.q = {e: [] for e in ("pe", "dve", "act", "pool", "sp")}
        self.hwm = {e: {} for e in self.q}
        self.res = {}

    def _stream(self, name):
        key = ("dma", name)
        if key not in self.sem:
            self.sem[key] = self.nc.alloc_semaphore("Sd_" + name)
            self.cnt[key] = 0
        return key

    def _expand(self, items):
        out = []
        for it in items:
            out.extend(_apkeys(it))
        return out

    def _need(self, eng, own_sem, reads, writes):
        need = {}

        def add(s, v):
            if s == own_sem and v > self.cnt.get(own_sem, 0):
                return
            need[s] = max(need.get(s, 0), v)

        for k in reads:
            r = self.res.get(k)
            if r and r["w"] is not None:
                add(*r["w"])
        for k in writes:
            r = self.res.get(k)
            if r:
                if r["w"] is not None:
                    add(*r["w"])
                for s, v in r["r"].items():
                    add(s, v)
        hw = self.hwm[eng]
        waits = []
        for s, v in sorted(need.items(), key=lambda kv: str(kv[0])):
            if hw.get(s, 0) < v:
                waits.append((s, v))
                hw[s] = v
        return waits

    def _record(self, own_sem, tick, reads, writes):
        for k in writes:
            self.res[k] = {"w": (own_sem, tick), "r": {}}
        for k in reads:
            r = self.res.get(k)
            if r is None:
                r = {"w": None, "r": {}}
                self.res[k] = r
            r["r"][own_sem] = max(r["r"].get(own_sem, 0), tick)

    def op(self, eng, fn, reads=(), writes=(), inc=True):
        import sys as _s
        f = _s._getframe(1)
        self._dbg = "%s:%d" % (f.f_code.co_name, f.f_lineno)
        own = eng
        reads = self._expand(reads)
        writes = self._expand(writes)
        waits = self._need(eng, own, reads, writes)
        if inc:
            self.cnt[eng] += 1
            tick = self.cnt[eng]
        else:
            tick = self.cnt[eng] + 1
        self.q[eng].append((waits, fn, own if inc else None, self._dbg))
        self._record(own, tick, reads, writes)

    def dma(self, qeng, dst, src, stream, reads=(), writes=(), **kw):
        own = self._stream(stream)
        reads = self._expand(tuple(reads) + (src,))
        writes = self._expand(tuple(writes) + (dst,))
        waits = self._need(qeng, own, reads, writes)
        self.cnt[own] += 1
        tick = 16 * self.cnt[own]
        import sys as _s
        f = _s._getframe(1)
        dbg = "%s:%d" % (f.f_code.co_name, f.f_lineno)
        self.q[qeng].append(
            (waits, lambda e, d=dst, s=src, k=dict(kw): e.dma_start(d, s, **k), own, dbg))
        self._record(own, tick, reads, writes)

    def seal_tensor(self, name, stream):
        key = ("dma", stream)
        tick = 16 * self.cnt[key]
        for k, r in self.res.items():
            if isinstance(k, tuple) and len(k) == 2 and k[0] == name:
                r["w"] = (key, tick)

    def check_deadlock(self):
        pos = {e: 0 for e in self.q}
        val = {s: 0 for s in self.cnt}
        progressed = True
        while progressed:
            progressed = False
            for e, lst in self.q.items():
                while pos[e] < len(lst):
                    waits, _fn, inc_sem = lst[pos[e]][0], lst[pos[e]][1], lst[pos[e]][2]
                    if all(val.get(s, 0) >= v for s, v in waits):
                        if inc_sem is not None:
                            val[inc_sem] = val.get(inc_sem, 0) + (
                                16 if isinstance(inc_sem, tuple) else 1)
                        pos[e] += 1
                        progressed = True
                    else:
                        break
        stuck = {e: pos[e] for e in self.q if pos[e] < len(self.q[e])}
        if stuck:
            msg = []
            for e, p in stuck.items():
                waits, _f, _i, dbg = self.q[e][p]
                bad = [(s, v, val.get(s, 0)) for s, v in waits if val.get(s, 0) < v]
                msg.append(f"{e}@{p} [{dbg}] needs {bad}")
            raise RuntimeError("build-time deadlock: " + " | ".join(msg))

    def emit(self):
        nc = self.nc
        sems = self.sem
        final = dict(self.cnt)

        def mk(lst, eng_name):
            def body(engine):
                for waits, fn, inc_sem, _dbg in lst:
                    for s, v in waits:
                        engine.wait_ge(sems[s], v)
                    ins = fn(engine)
                    if inc_sem is not None:
                        amt = 16 if isinstance(inc_sem, tuple) else 1
                        ins.then_inc(sems[inc_sem], amt)
                if eng_name == "pool":
                    for s, v in final.items():
                        tgt = v * 16 if isinstance(s, tuple) else v
                        if tgt > 0:
                            engine.wait_ge(sems[s], tgt)
            return body

        self.check_deadlock()
        with nc.Block() as blk:
            blk.sync(mk(self.q["sp"], "sp"))
            blk.tensor(mk(self.q["pe"], "pe"))
            blk.vector(mk(self.q["dve"], "dve"))
            blk.scalar(mk(self.q["act"], "act"))
            blk.gpsimd(mk(self.q["pool"], "pool"))



class Bufs:
    def __init__(self, nc):
        A = nc.alloc_sbuf_tensor
        self.X = A("X", [128, 16384], F32).ap()
        self.HB = A("HB", [128, 33024], BF16).ap()
        self.XQ = A("XQb", [128, 8192], BF16).ap()
        self.QT = A("QTb", [128, 4096], BF16).ap()
        self.WB = A("WBb", [128, 4096], BF16).ap()
        self.WR = A("WRb", [128, 1024], F32).ap()
        self.BC = A("BCb", [128, 2048], F32).ap()
        self.EV = A("EVb", [128, 1280], F32).ap()
        self.ACC = A("ACCb", [128, 1024], BF16).ap()
        self.RT = A("RTb", [128, 2048], F32).ap()
        self.RI = A("RIb", [128, 1024], I32).ap()
        self.CC = A("CCb", [128, 2048], BF16).ap()
        self.AUX = A("AUXb", [128, 4992], BF16).ap()
        self.MASK = self.AUX
        self.RC = self.AUX[:, 896:2944]
        self.RS = self.AUX[:, 2944:4992]
        self.COLS = A("COLSb", [128, 160], F32).ap()
        self.CB = A("CBb", [128, 128], F32).ap()
        self.IDF = A("IDFb", [128, 128], F32).ap()
        self.ONE1 = A("ONE1b", [128, 1], BF16).ap()
        self.KCONST = A("KCONSTb", [128, 3], F32).ap()
        self.MWC = A("MWCb", [128, 10], F32).ap()
        self.BIA = A("BIAb", [128, 48], F32).ap()
        self.SCR = A("SCRb", [1, 32], F32).ap()
        self.pbw = [nc.alloc_psum_tensor(f"pw{i}", [128, 1024], F32).ap() for i in range(4)]
        self.pb = [self.pbw[i // 2][:, (i % 2) * 512:(i % 2) * 512 + 512] for i in range(8)]

    def col(self, v, idx, n=1):
        return self.COLS[:, v * 16 + idx: v * 16 + idx + n]

    def xs(self, c, qt):
        return self.X[:, c * 2048 + qt * 512: c * 2048 + qt * 512 + 512]


(VAR_M, VAR_RSTD, VAR_S1, VAR_S2, VAR_MAXX, VAR_MINX,
 VAR_ACOL, VAR_INVSX, VAR_SH, VAR_TMP) = range(10)
KC_EPS, KC_EIGHTH = 0, 1


def emit_setup(P, B, dins):
    P.dma("sp", B.IDF, dins["identf"].ap(), "ldid")
    P.op("dve", lambda e: e.memset(B.CB, 0.0), writes=[B.CB])
    P.op("dve", lambda e: e.memset(B.ONE1, 1.0), writes=[B.ONE1])
    P.op("dve", lambda e: e.memset(B.KCONST[:, 0:1], 1e-5), writes=[B.KCONST])
    P.op("dve", lambda e: e.memset(B.KCONST[:, 1:2], 0.125), writes=[B.KCONST])
    P.op("dve", lambda e: e.memset(B.KCONST[:, 2:3], 1.0 / 127.0), writes=[B.KCONST])
    P.op("dve", lambda e: e.memset(B.COLS, 0.0), writes=[B.COLS])



def emit_weight_prep(P, B, wT, scratch, widx, K, M, scr_dram, tiled=False):
    if tiled:
        nkc, nmb = M // 128, K // 1024
        wsrc = wT.ap()
        wdst = scratch.ap()
    else:
        nkc, nmb = K // 128, M // 1024
        wsrc = wT.ap().rearrange("(k p) m -> k p m", p=128)
        wdst = scratch.ap().rearrange("(k p) m -> k p m", p=128)
    acc = B.col(VAR_TMP, 0)
    tmpc = B.col(VAR_TMP, 1)
    P.op("dve", lambda e: e.memset(acc, 0.0), writes=[acc])
    bi = 0
    for kc in range(nkc):
        for mb in range(nmb):
            wr = B.WR[:, 0:1024] if bi % 2 == 0 else B.RT[:, 1024:2048]
            tm2 = B.col(VAR_TMP, 1 + bi % 2)
            P.dma("sp", wr, wsrc[kc, :, mb * 1024:(mb + 1) * 1024],
                  "wr%d" % (bi % 2))
            P.op("dve", lambda e, wr=wr, tm2=tm2: e.reduce_sum(
                tm2, wr, AX.X, apply_absolute_value=True),
                reads=[wr], writes=[tm2])
            P.op("dve", lambda e, tm2=tm2: e.tensor_add(acc, acc, tm2),
                 reads=[tm2, acc], writes=[acc])
            bi += 1
    P.op("pe", lambda e: e.transpose(B.pb[7][0:1, 0:128], acc, B.IDF),
         reads=[acc, B.IDF], writes=[B.pb[7]])
    P.op("dve", lambda e: e.tensor_copy(B.EV[0:1, 0:128], B.pb[7][0:1, 0:128]),
         reads=[B.pb[7]], writes=[B.EV[0:1, 0:128]])
    P.op("dve", lambda e: e.reduce_sum(B.EV[0:1, 512:513], B.EV[0:1, 0:128], AX.X),
         reads=[B.EV[0:1, 0:128]], writes=[B.EV[0:1, 512:513]])
    P.op("dve", lambda e: e.tensor_scalar(
        B.SCR[0:1, widx:widx + 1], B.EV[0:1, 512:513], 1.0 / (K * M), 1e-5,
        ALU.mult, ALU.max),
        reads=[B.EV[0:1, 512:513]], writes=[B.SCR[0:1, widx:widx + 1]])
    P.op("dve", lambda e: e.reciprocal(B.SCR[0:1, 16 + widx:17 + widx],
                                       B.SCR[0:1, widx:widx + 1]),
         reads=[B.SCR[0:1, widx:widx + 1]], writes=[B.SCR[0:1, 16 + widx:17 + widx]])
    P.dma("pool", scr_dram.ap()[0:1, widx:widx + 1], B.SCR[0:1, widx:widx + 1],
          "scs%d" % widx)
    P.dma("pool", scr_dram.ap()[0:1, 16 + widx:17 + widx],
          B.SCR[0:1, 16 + widx:17 + widx], "scs%d" % widx)
    emit_load_scales(P, B, scr_dram, [widx])
    bi = 0
    for kc in range(nkc):
        for mb in range(nmb):
            sl = bi % 2
            wr = B.WR[:, 0:1024] if sl == 0 else B.RT[:, 1024:2048]
            P.dma("sp", wr, wsrc[kc, :, mb * 1024:(mb + 1) * 1024], "wr%d" % sl)
            vv = B.EV[:, 0:1024] if sl == 0 else B.BC[:, 0:1024]
            P.op("dve", lambda e, wr=wr, vv=vv: e.tensor_scalar_mul(
                vv, wr, B.MWC[:, 5 + widx:6 + widx]),
                reads=[wr, B.MWC], writes=[vv])
            c1 = B.CC[:, sl * 1024:(sl + 1) * 1024]
            P.op("dve", lambda e, vv=vv, c1=c1: e.tensor_single_scalar(
                c1, vv, 0.5, ALU.is_gt), reads=[vv], writes=[c1])
            c2 = B.QT[:, sl * 1024:(sl + 1) * 1024]
            P.op("dve", lambda e, vv=vv, c2=c2: e.tensor_single_scalar(
                c2, vv, -0.5, ALU.is_lt), reads=[vv], writes=[c2])
            wb = B.WB[:, sl * 1024:(sl + 1) * 1024]
            P.op("dve", lambda e, c1=c1, c2=c2, wb=wb: e.tensor_sub(wb, c1, c2),
                 reads=[c1, c2], writes=[wb])
            P.dma("pool", wdst[kc, :, mb * 1024:(mb + 1) * 1024], wb, "tst%d" % widx)
            bi += 1


def emit_load_scales(P, B, scr_dram, widxs):
    for w in widxs:
        P.dma("sp", B.MWC[:, w:w + 1],
              scr_dram.ap()[0:1, w:w + 1].broadcast_to([128, 1]), "mwc")
        P.dma("sp", B.MWC[:, 5 + w:6 + w],
              scr_dram.ap()[0:1, 16 + w:17 + w].broadcast_to([128, 1]), "mwc")


def emit_load_bias(P, B, dins):
    P.dma("sp", B.BIA[:, 0:32], dins["b1c"].ap(), "bias")
    P.dma("sp", B.BIA[:, 32:40], dins["b2c"].ap(), "bias")
    P.op("dve", lambda e: e.tensor_scalar_mul(B.BIA[:, 40:48], B.BIA[:, 32:40], 0.5),
         reads=[B.BIA[:, 32:40]], writes=[B.BIA[:, 40:48]])



def emit_stats(P, B, ffn):
    for qt in range(4):
        for c in range(8):
            xsl = B.xs(c, qt)
            slot = c % 2
            xb = B.QT[:, slot * 512:(slot + 1) * 512]
            sq = B.QT[:, 1024 + slot * 512: 1024 + (slot + 1) * 512]
            P.op("dve", lambda e, xb=xb, xsl=xsl: e.tensor_copy(xb, xsl),
                 reads=[xsl], writes=[xb])
            P.op("act", lambda e, sq=sq, xsl=xsl: e.activation(sq, xsl, AF.Square),
                 reads=[xsl], writes=[sq])
            P.op("pe", lambda e, xb=xb, c=c: e.matmul(
                B.pb[5][0:1, :], B.ONE1, xb, start=(c == 0), stop=(c == 7)),
                reads=[xb, B.ONE1], writes=[B.pb[5]])
            P.op("pe", lambda e, sq=sq, c=c: e.matmul(
                B.pb[6][0:1, :], B.ONE1, sq, start=(c == 0), stop=(c == 7)),
                reads=[sq, B.ONE1], writes=[B.pb[6]])
        r1 = B.EV[0:1, 0:512]
        r2 = B.EV[0:1, 512:1024]
        P.op("dve", lambda e: e.tensor_copy(r1, B.pb[5][0:1, :]),
             reads=[B.pb[5]], writes=[r1])
        P.op("dve", lambda e: e.tensor_copy(r2, B.pb[6][0:1, :]),
             reads=[B.pb[6]], writes=[r2])
        for g in range(4):
            P.op("pe", lambda e, g=g: e.transpose(
                B.pb[7][:, g:g + 1], B.EV[0:1, g * 128:(g + 1) * 128], B.IDF[0:1, 0:1]),
                reads=[r1, B.IDF], writes=[B.pb[7]], inc=False)
            P.op("pe", lambda e, g=g: e.transpose(
                B.pb[7][:, 4 + g:5 + g],
                B.EV[0:1, 512 + g * 128:512 + (g + 1) * 128], B.IDF[0:1, 0:1]),
                reads=[r2, B.IDF], writes=[B.pb[7]])
        P.op("dve", lambda e, qt=qt: e.tensor_copy(
            B.col(VAR_S1, qt * 4, 4), B.pb[7][:, 0:4]),
            reads=[B.pb[7]], writes=[B.col(VAR_S1, qt * 4, 4)])
        P.op("dve", lambda e, qt=qt: e.tensor_copy(
            B.col(VAR_S2, qt * 4, 4), B.pb[7][:, 4:8]),
            reads=[B.pb[7]], writes=[B.col(VAR_S2, qt * 4, 4)])
        if ffn:
            ea = B.EV[:, 0:512]
            eb = B.EV[:, 512:1024]
            xview = B.X.rearrange("p (c t) -> p t c", c=8)[:, qt * 512:(qt + 1) * 512, :]
            xkeys = [B.xs(c2_, qt) for c2_ in range(8)] + [r1, r2]
            P.op("dve", lambda e, xview=xview: e.reduce_max(ea, xview, AX.X),
                 reads=xkeys, writes=[ea])
            P.op("dve", lambda e, xview=xview: e.tensor_reduce(
                eb, xview, AX.X, ALU.min), reads=xkeys, writes=[eb])
            for g in range(4):
                P.op("pe", lambda e, g=g: e.transpose(
                    B.pb[7][:, g * 128:(g + 1) * 128], ea[:, g * 128:(g + 1) * 128],
                    B.IDF),
                    reads=[ea, B.IDF], writes=[B.pb[7]])
            for g in range(4):
                P.op("dve", lambda e, g=g, qt=qt: e.reduce_max(
                    B.col(VAR_MAXX, qt * 4 + g), B.pb[7][:, g * 128:(g + 1) * 128],
                    AX.X),
                    reads=[B.pb[7]], writes=[B.col(VAR_MAXX, qt * 4 + g)])
            for g in range(4):
                P.op("pe", lambda e, g=g: e.transpose(
                    B.pb[7][:, g * 128:(g + 1) * 128], eb[:, g * 128:(g + 1) * 128],
                    B.IDF),
                    reads=[eb, B.IDF], writes=[B.pb[7]])
            for g in range(4):
                P.op("dve", lambda e, g=g, qt=qt: e.tensor_reduce(
                    B.col(VAR_MINX, qt * 4 + g), B.pb[7][:, g * 128:(g + 1) * 128],
                    AX.X, ALU.min),
                    reads=[B.pb[7]], writes=[B.col(VAR_MINX, qt * 4 + g)])
    m16 = B.col(VAR_M, 0, 16)
    rst = B.col(VAR_RSTD, 0, 16)
    tmp = B.col(VAR_TMP, 0, 16)
    s1 = B.col(VAR_S1, 0, 16)
    s2 = B.col(VAR_S2, 0, 16)
    P.op("dve", lambda e: e.tensor_scalar_mul(m16, s1, 1.0 / 1024.0),
         reads=[s1], writes=[m16])
    P.op("dve", lambda e: e.tensor_mul(tmp, m16, m16), reads=[m16], writes=[tmp])
    P.op("dve", lambda e: e.scalar_tensor_tensor(
        rst, s2, 1.0 / 1024.0, tmp, ALU.mult, ALU.subtract),
        reads=[s2, tmp], writes=[rst])
    P.op("act", lambda e: e.activation(rst, rst, AF.Sqrt, bias=B.KCONST[:, 0:1]),
         reads=[rst, B.KCONST], writes=[rst])
    P.op("dve", lambda e: e.reciprocal(rst, rst), reads=[rst], writes=[rst])
    if ffn:
        mx = B.col(VAR_MAXX, 0, 16)
        mn = B.col(VAR_MINX, 0, 16)
        a16 = B.col(VAR_ACOL, 0, 16)
        iv = B.col(VAR_INVSX, 0, 16)
        P.op("dve", lambda e: e.tensor_sub(mx, mx, m16), reads=[mx, m16], writes=[mx])
        P.op("dve", lambda e: e.tensor_sub(mn, m16, mn), reads=[mn, m16], writes=[mn])
        P.op("dve", lambda e: e.tensor_max(mx, mx, mn), reads=[mx, mn], writes=[mx])
        P.op("dve", lambda e: e.reciprocal(a16, mx), reads=[mx], writes=[a16])
        P.op("dve", lambda e: e.tensor_scalar_mul(a16, a16, 127.0),
             reads=[a16], writes=[a16])
        P.op("dve", lambda e: e.tensor_mul(iv, rst, mx), reads=[rst, mx], writes=[iv])
        P.op("dve", lambda e: e.tensor_scalar(iv, iv, 1e-5, 1.0 / 127.0,
                                              ALU.max, ALU.mult),
             reads=[iv], writes=[iv])


def emit_bcast(P, B, colap, slot, qt, pbi=4):
    pb = B.pb[pbi]
    for g in range(4):
        cs = colap[:, qt * 4 + g: qt * 4 + g + 1]
        P.op("act", lambda e, cs=cs: e.activation(B.CB, B.CB, AF.Identity,
                                                  bias=cs, scale=0.0),
             reads=[cs, B.CB], writes=[B.CB])
        P.op("pe", lambda e, g=g, pb=pb: e.transpose(
            pb[:, g * 128:(g + 1) * 128], B.CB, B.IDF),
            reads=[B.CB, B.IDF], writes=[pb])
    bc = B.BC[:, slot * 512:(slot + 1) * 512]
    P.op("dve", lambda e, bc=bc, pb=pb: e.tensor_copy(bc, pb),
         reads=[pb], writes=[bc])
    return bc


def emit_round(P, B, src, dst, ro, n=512, pair=False, dkeys=None):
    yi = B.RI[:, ro:ro + n]
    yf = B.RT[:, 1024 + ro:1024 + ro + n]
    c1 = B.CC[:, ro:ro + n]
    c2 = B.CC[:, 1024 + ro:1024 + ro + n]
    P.op("dve", lambda e: e.tensor_scalar(yi, src, 1.0, None, ALU.mult),
         reads=[src], writes=[yi])
    P.op("dve", lambda e: e.tensor_copy(yf, yi), reads=[yi], writes=[yf])
    P.op("dve", lambda e: e.scalar_tensor_tensor(
        c1, src, 0.5, yf, ALU.subtract, ALU.is_gt),
        reads=[src, yf], writes=[c1])
    P.op("dve", lambda e: e.scalar_tensor_tensor(
        c2, src, 0.5, yf, ALU.add, ALU.is_lt),
        reads=[src, yf], writes=[c2])
    P.op("dve", lambda e: e.tensor_add(yf, yf, c1), reads=[yf, c1], writes=[yf])
    if pair:
        yfv = yf.rearrange("p (a b) -> p a b", a=2)
        c2v = c2.rearrange("p (a b) -> p a b", a=2)
    else:
        yfv, c2v = yf, c2
    P.op("dve", lambda e: e.tensor_sub(dst, yfv, c2v), reads=[yf, c2],
         writes=(dkeys if dkeys is not None else [dst]))



def emit_ffn(P, B, tern_g, tern_u, tern_d, wg_i, wu_i, wd_i, l1):
    emit_stats(P, B, ffn=True)
    g3 = tern_g.ap()
    u3 = tern_u.ap() if tern_u is not None else None
    d3 = tern_d.ap().rearrange("(k p) m -> k p m", p=128)
    cgc = B.col(VAR_TMP, 0, 16)
    P.op("dve", lambda e: e.tensor_scalar_mul(
        cgc, B.col(VAR_INVSX, 0, 16), B.MWC[:, wg_i:wg_i + 1]),
        reads=[B.col(VAR_INVSX, 0, 16), B.MWC], writes=[cgc])
    for tt in FFN_TTS:
        for sub in range(2):
            qt = tt * 2 + sub
            bM = emit_bcast(P, B, B.col(VAR_M, 0, 16), 0, qt)
            P.op("dve", lambda e: e.tensor_copy(B.BC[:, 512:1024], B.BC[:, 0:512]),
                 reads=[B.BC[:, 0:512]], writes=[B.BC[:, 512:1024]])
            bA = emit_bcast(P, B, B.col(VAR_ACOL, 0, 16), 2, qt)
            P.op("dve", lambda e: e.tensor_copy(B.BC[:, 1536:2048],
                                                B.BC[:, 1024:1536]),
                 reads=[B.BC[:, 1024:1536]], writes=[B.BC[:, 1536:2048]])
            bMd = B.BC[:, 0:1024].rearrange("p (a b) -> p a b", a=2)
            bAd = B.BC[:, 1024:2048].rearrange("p (a b) -> p a b", a=2)
            bMk = [B.BC[:, 0:512], B.BC[:, 512:1024]]
            bAk = [B.BC[:, 1024:1536], B.BC[:, 1536:2048]]
            for c in range(0, 8, 2):
                xsl3 = B.X.rearrange("p (c t) -> p c t", c=8)[
                    :, c:c + 2, qt * 512:(qt + 1) * 512]
                xm3 = B.EV[:, 0:1024].rearrange("p (a b) -> p a b", a=2)
                t2 = B.RT[:, 0:1024]
                t3 = t2.rearrange("p (a b) -> p a b", a=2)
                xkeys = [B.xs(c, qt), B.xs(c + 1, qt)]
                P.op("dve", lambda e, xsl3=xsl3, xm3=xm3, bMd=bMd: e.tensor_sub(
                    xm3, xsl3, bMd),
                    reads=xkeys + bMk, writes=[B.EV[:, 0:1024]])
                P.op("dve", lambda e, xm3=xm3, t3=t3, bAd=bAd: e.tensor_mul(
                    t3, xm3, bAd),
                    reads=[B.EV[:, 0:1024]] + bAk, writes=[t2])
                xq3 = B.XQ.rearrange("p (k t) -> p k t", k=8)[
                    :, c:c + 2, sub * 512:(sub + 1) * 512]
                xqk = [B.XQ[:, c * 1024 + sub * 512: c * 1024 + sub * 512 + 512],
                       B.XQ[:, (c + 1) * 1024 + sub * 512:
                            (c + 1) * 1024 + sub * 512 + 512]]
                emit_round(P, B, t2, xq3, 0, n=1024, pair=True, dkeys=xqk)
        if tt == 1 and FFN_T1_STOP == "xq":
            continue
        bCg = [emit_bcast(P, B, cgc, 2, tt * 2), emit_bcast(P, B, cgc, 3, tt * 2 + 1)]
        am0 = B.ACC[:, 0:1024]
        an0 = B.CC[:, 1024:2048]
        def gu_mms(m):
            ws = m % 2
            gblk = B.WB[:, ws * 1024: ws * 1024 + 1024]
            P.dma("sp", gblk, g3[m], "wg%d" % ws)
            pg = B.pbw[(m % 2) * 2]
            pu = B.pbw[(m % 2) * 2 + 1]
            if not l1:
                ublk = B.WB[:, 2048 + ws * 1024: 2048 + ws * 1024 + 1024]
                P.dma("sp", ublk, u3[m], "wu%d" % ws)
                for k in range(8):
                    wt = gblk[:, k * 128:(k + 1) * 128]
                    wu = ublk[:, k * 128:(k + 1) * 128]
                    for sub in range(2):
                        xq = B.XQ[:, k * 1024 + sub * 512:
                                  k * 1024 + sub * 512 + 512]
                        dstg = pg[:, sub * 512: sub * 512 + 512]
                        P.op("pe", lambda e, wt=wt, xq=xq, k=k, dstg=dstg: e.matmul(
                            dstg, wt, xq, start=(k == 0), stop=(k == 7)),
                            reads=[wt, xq], writes=[dstg])
                    for sub in range(2):
                        xq = B.XQ[:, k * 1024 + sub * 512:
                                  k * 1024 + sub * 512 + 512]
                        dstu = pu[:, sub * 512: sub * 512 + 512]
                        P.op("pe", lambda e, wu=wu, xq=xq, k=k, dstu=dstu: e.matmul(
                            dstu, wu, xq, start=(k == 0), stop=(k == 7)),
                            reads=[wu, xq], writes=[dstu])
            else:
                for k in range(4):
                    wa = gblk[:, k * 128:(k + 1) * 128]
                    wb = gblk[:, (k + 4) * 128:(k + 5) * 128]
                    for sub in range(2):
                        xq = B.XQ[:, k * 1024 + sub * 512:
                                  k * 1024 + sub * 512 + 512]
                        dsta = pg[:, sub * 512: sub * 512 + 512]
                        P.op("pe", lambda e, wa=wa, xq=xq, k=k, dsta=dsta: e.matmul(
                            dsta, wa, xq, start=(k == 0), stop=(k == 3)),
                            reads=[wa, xq], writes=[dsta])
                    for sub in range(2):
                        xq = B.XQ[:, (k + 4) * 1024 + sub * 512:
                                  (k + 4) * 1024 + sub * 512 + 512]
                        dstb = pu[:, sub * 512: sub * 512 + 512]
                        P.op("pe", lambda e, wb=wb, xq=xq, k=k, dstb=dstb: e.matmul(
                            dstb, wb, xq, start=(k == 0), stop=(k == 3)),
                            reads=[wb, xq], writes=[dstb])

        def gu_chain(m):
            pg = B.pbw[(m % 2) * 2]
            pu = B.pbw[(m % 2) * 2 + 1]
            gs = B.EV[:, 0:1024]
            hsl = B.HB[:, m * 1024: m * 1024 + 1024]
            bCgp = B.BC[:, 1024:2048]
            pbk = [B.pb[(m % 2) * 4], B.pb[(m % 2) * 4 + 1],
                   B.pb[(m % 2) * 4 + 2], B.pb[(m % 2) * 4 + 3]]
            if l1:
                tmpu = B.RT[:, (m % 2) * 1024:(m % 2) * 1024 + 1024]
                P.op("act", lambda e, pu=pu, tmpu=tmpu: e.activation(
                    tmpu, pu, AF.Identity),
                    reads=[pbk[2], pbk[3]], writes=[tmpu])
                P.op("dve", lambda e, pg=pg, tmpu=tmpu, gs=gs: e.scalar_tensor_tensor(
                    gs, pg, 1.0, tmpu, ALU.mult, ALU.add),
                    reads=[pbk[0], pbk[1], tmpu], writes=[gs])
                P.op("dve", lambda e, gs=gs, bCgp=bCgp: e.tensor_mul(
                    gs, gs, bCgp), reads=[gs, bCgp], writes=[gs])
                b1c = B.BIA[:, m:m + 1]
                P.op("act", lambda e, gs=gs, hsl=hsl, b1c=b1c: e.activation(
                    hsl, gs, AF.Gelu, bias=b1c),
                    reads=[gs, B.BIA], writes=[hsl])
            else:
                P.op("dve", lambda e, pg=pg, gs=gs, bCgp=bCgp: e.tensor_mul(
                    gs, pg, bCgp),
                    reads=[pbk[0], pbk[1], bCgp], writes=[gs])
                P.op("act", lambda e, gs=gs: e.activation(gs, gs, AF.Silu),
                     reads=[gs], writes=[gs])
                P.op("dve", lambda e, gs=gs, hsl=hsl, pu=pu: e.tensor_mul(
                    hsl, gs, pu),
                    reads=[gs, pbk[2], pbk[3]],
                    writes=[hsl])

        for m in range(33):
            if m < 32:
                gu_mms(m)
            if m >= 1:
                gu_chain(m - 1)
        if True:
            if True:
                if True:
                    pass

        if tt == 1 and FFN_T1_STOP == "gu":
            continue
        hview = B.HB[:, 0:32768].rearrange("p (m t) -> p t m", m=32)
        hkeys = [B.HB[:, 0:32768]]
        P.op("dve", lambda e, hview=hview: e.tensor_reduce(
            am0, hview, AX.X, ALU.max, apply_absolute_value=True),
             reads=hkeys, writes=[am0])
        shc = B.col(VAR_SH, tt * 8, 8)
        for half in range(2):
            rmax = B.RT[:, half * 512:(half + 1) * 512]
            P.op("dve", lambda e, half=half, rmax=rmax: e.tensor_copy(
                rmax, B.ACC[:, half * 512:(half + 1) * 512]),
                reads=[am0], writes=[rmax])
            for g in range(4):
                P.op("pe", lambda e, g=g, rmax=rmax: e.transpose(
                    B.pb[7][:, g * 128:(g + 1) * 128],
                    rmax[:, g * 128:(g + 1) * 128], B.IDF),
                    reads=[rmax, B.IDF], writes=[B.pb[7]])
            for g in range(4):
                tg = tt * 8 + half * 4 + g
                P.op("dve", lambda e, g=g, tg=tg: e.reduce_max(
                    B.col(VAR_SH, tg), B.pb[7][:, g * 128:(g + 1) * 128],
                    AX.X),
                    reads=[B.pb[7]], writes=[B.col(VAR_SH, tg)])
        if tt == 1 and FFN_T1_STOP == "tailmid":
            continue
        P.op("dve", lambda e, shc=shc: e.tensor_scalar_max(shc, shc, 1e-5),
             reads=[shc], writes=[shc])
        shs = B.col(VAR_MINX, tt * 8, 8)
        P.op("dve", lambda e, shs=shs, shc=shc: e.reciprocal(shs, shc),
             reads=[shc], writes=[shs])
        P.op("dve", lambda e, shs=shs: e.tensor_scalar_mul(shs, shs, 127.0),
             reads=[shs], writes=[shs])
        sd = B.col(VAR_MAXX, tt * 8, 8)
        if l1:
            P.op("dve", lambda e, sd=sd, shc=shc: e.tensor_scalar_mul(
                sd, shc, 1.0 / 127.0), reads=[shc], writes=[sd])
        else:
            iv8 = B.col(VAR_INVSX, tt * 8, 8)
            P.op("dve", lambda e, sd=sd, iv8=iv8, shc=shc: e.tensor_mul(sd, iv8, shc),
                 reads=[iv8, shc], writes=[sd])
            P.op("dve", lambda e, sd=sd: e.tensor_scalar_mul(
                sd, sd, B.MWC[:, wu_i:wu_i + 1]),
                 reads=[sd, B.MWC], writes=[sd])
            P.op("dve", lambda e, sd=sd: e.tensor_scalar(
                sd, sd, 1e-5, 1.0 / 127.0, ALU.max, ALU.mult),
                 reads=[sd], writes=[sd])
        P.op("dve", lambda e, sd=sd: e.tensor_scalar_mul(
            sd, sd, B.MWC[:, wd_i:wd_i + 1]),
             reads=[sd, B.MWC], writes=[sd])
        P.op("dve", lambda e, sd=sd: e.tensor_scalar_mul(sd, sd, 0.5),
             reads=[sd], writes=[sd])
        if tt == 1 and FFN_T1_STOP == "tailsd":
            continue
        bSh = [emit_bcast(P, B, B.col(VAR_MINX, 0, 16), 0, tt * 2),
               emit_bcast(P, B, B.col(VAR_MINX, 0, 16), 1, tt * 2 + 1)]
        bSd = [emit_bcast(P, B, B.col(VAR_MAXX, 0, 16), 2, tt * 2),
               emit_bcast(P, B, B.col(VAR_MAXX, 0, 16), 3, tt * 2 + 1)]
        if tt == 1 and FFN_T1_STOP == "tail":
            continue
        bShp = B.BC[:, 0:1024]
        for m in range(32):
            hfull = B.HB[:, m * 1024: m * 1024 + 1024]
            tfull = B.RT[:, 0:1024]
            P.op("dve", lambda e, hfull=hfull, tfull=tfull: e.tensor_mul(
                tfull, hfull, bShp), reads=[hfull, bSh[0], bSh[1]], writes=[tfull])
            emit_round(P, B, tfull, hfull, 0, n=1024)
        if tt == 1 and FFN_T1_STOP == "round":
            continue
        for sub in range(2):
            qt = tt * 2 + sub
            rtd = B.RT[:, sub * 1024:(sub + 1) * 1024]
            for hh in range(2):
                P.op("dve", lambda e, rtd=rtd, hh=hh, sub=sub: e.tensor_copy(
                    rtd[:, hh * 512:(hh + 1) * 512], bSd[sub]),
                    reads=[bSd[sub]], writes=[rtd])
            for k in range(32):
                ws = k % 2
                wdb = B.WB[:, ws * 1024: ws * 1024 + 1024]
                P.dma("sp", wdb, d3[k, :, :], "wg%d" % ws)
                hq = B.HB[:, k * 1024 + sub * 512: k * 1024 + sub * 512 + 512]
                for m8 in range(8):
                    wt = wdb[:, m8 * 128:(m8 + 1) * 128]
                    P.op("pe", lambda e, wt=wt, hq=hq, k=k, m8=m8: e.matmul(
                        B.pb[m8], wt, hq, start=(k == 0), stop=(k == 31)),
                        reads=[wt, hq], writes=[B.pb[m8]])
            for p4 in range(4):
                ev = B.EV[:, 0:1024] if p4 % 2 == 0 else B.WR[:, 0:1024]
                P.op("dve", lambda e, p4=p4, ev=ev, rtd=rtd: e.tensor_mul(
                    ev, B.pbw[p4], rtd),
                    reads=[B.pb[2 * p4], B.pb[2 * p4 + 1], rtd], writes=[ev])
                if l1:
                    for hh in range(2):
                        m8 = 2 * p4 + hh
                        b2h = B.BIA[:, 40 + m8:41 + m8]
                        evh = ev[:, hh * 512:(hh + 1) * 512]
                        P.op("act", lambda e, evh=evh, b2h=b2h: e.activation(
                            evh, evh, AF.Identity, bias=b2h),
                            reads=[ev, B.BIA], writes=[ev])
                xp = B.X.rearrange("p (c t) -> p c t", c=8)[
                    :, 2 * p4:2 * p4 + 2, qt * 512:(qt + 1) * 512]
                ev3 = ev.rearrange("p (a b) -> p a b", a=2)
                xkeys = [B.xs(2 * p4, qt), B.xs(2 * p4 + 1, qt)]
                P.op("dve", lambda e, xp=xp, ev3=ev3: e.tensor_add(xp, xp, ev3),
                     reads=[ev] + xkeys, writes=xkeys)



def emit_attn(P, B, layer, qkvT, woT, dscr):
    emit_stats(P, B, ffn=False)
    l1 = layer == 1
    qoff = 8 if l1 else 0
    koff = 0 if l1 else 8
    voff = 16
    qkv = qkvT.ap()
    wo = woT.ap()
    VOFF = 16384
    vones = B.HB[:, VOFF:VOFF + 16640].rearrange("p (s c) -> p s c", c=65)[:, :, 64:65]
    P.op("dve", lambda e: e.memset(vones, 1.0), writes=[B.HB[:, VOFF:VOFF + 16640]])

    def build_u(qt):
        bM = emit_bcast(P, B, B.col(VAR_M, 0, 16), 0, qt)
        bR = emit_bcast(P, B, B.col(VAR_RSTD, 0, 16), 1, qt)
        for c in range(8):
            xsl = B.xs(c, qt)
            ro = (c % 2) * 512
            xm = B.EV[:, ro:ro + 512]
            P.op("dve", lambda e, xsl=xsl, xm=xm, bM=bM: e.tensor_sub(xm, xsl, bM),
                 reads=[xsl, bM], writes=[xm])
            u = B.XQ[:, c * 512:(c + 1) * 512]
            P.op("dve", lambda e, xm=xm, u=u, bR=bR: e.tensor_mul(u, xm, bR),
                 reads=[xm, bR], writes=[u])

    STG4 = [B.WR[:, 0:1024], B.RT[:, 0:1024], B.RT[:, 1024:2048],
            B.BC[:, 0:1024]]

    def proj_quad(src4, ms):
        wbs = []
        for ws, m in enumerate(ms):
            wr = STG4[ws]
            P.dma("sp", wr, src4[m], "wr%d" % ws)
            wb = B.WB[:, ws * 1024:(ws + 1) * 1024]
            P.op("dve", lambda e, wb=wb, wr=wr: e.tensor_copy(wb, wr),
                 reads=[wr], writes=[wb])
            wbs.append(wb)
        for k in range(8):
            u = B.XQ[:, k * 512:(k + 1) * 512]
            for ws in range(len(ms)):
                wt = wbs[ws][:, k * 128:(k + 1) * 128]
                pb = B.pb[ws]
                P.op("pe", lambda e, wt=wt, u=u, k=k, pb=pb: e.matmul(
                    pb, wt, u, start=(k == 0), stop=(k == 7)),
                    reads=[wt, u], writes=[pb])
        return [B.pb[ws] for ws in range(len(ms))]

    def rope_evac(pb, toff, dst):
        ea = B.BC[:, 1536:2048]
        ebb = B.EV[:, 0:512]
        P.op("dve", lambda e: e.tensor_copy(ea, pb), reads=[pb], writes=[ea])
        P.op("dve", lambda e: e.stream_shuffle(ebb, ea, _SHUF_PAIR),
             reads=[ea], writes=[ebb])
        P.op("dve", lambda e: e.tensor_mul(ea, ea, B.RC[:, toff:toff + 512]),
             reads=[ea, B.RC], writes=[ea])
        P.op("pool", lambda e: e.tensor_mul(ebb, ebb, B.RS[:, toff:toff + 512]),
             reads=[ebb, B.RS], writes=[ebb])
        P.op("dve", lambda e: e.tensor_add(dst, ea, ebb),
             reads=[ea, ebb], writes=[dst])

    for qt in range(4):
        build_u(qt)
        for j0 in range(0, 8, 4):
            pbs = proj_quad(qkv, [koff + j0 + i for i in range(4)])
            for i in range(4):
                j = j0 + i
                dst = B.HB[:, j * 2048 + qt * 512: j * 2048 + qt * 512 + 512]
                if l1:
                    rope_evac(pbs[i], qt * 512, dst)
                else:
                    P.op("dve", lambda e, dst=dst, pb=pbs[i]: e.tensor_copy(
                        dst, pb), reads=[pbs[i]], writes=[dst])
        for j0 in range(0, 8, 4):
            pbs = proj_quad(qkv, [voff + j0 + i for i in range(4)])
            for i in range(4):
                j = j0 + i
                ev = B.EV[:, 0:512]
                P.op("dve", lambda e, ev=ev, pb=pbs[i]: e.tensor_copy(ev, pb),
                     reads=[pbs[i]], writes=[ev])
                for g in range(4):
                    kc = qt * 4 + g
                    pt = B.pb[6 + (g % 2)]
                    P.op("pe", lambda e, g=g, pt=pt: e.transpose(
                        pt[:, 0:128], ev[:, g * 128:(g + 1) * 128], B.IDF),
                        reads=[ev, B.IDF], writes=[pt])
                    for hh in range(2):
                        h = 2 * j + hh
                        slot = VOFF + (kc * 16 + h) * 65
                        dst = B.HB[:, slot:slot + 64]
                        P.op("dve", lambda e, dst=dst, pt=pt, hh=hh: e.tensor_copy(
                            dst, pt[:, hh * 64:hh * 64 + 64]),
                            reads=[pt], writes=[dst])
    for qt in range(4):
        build_u(qt)
        for j0 in range(0, 8, 4):
            pbs = proj_quad(qkv, [qoff + j0 + i for i in range(4)])
            for i in range(4):
                j = j0 + i
                dst = B.QT[:, j * 512:(j + 1) * 512]
                if l1:
                    rope_evac(pbs[i], qt * 512, dst)
                else:
                    P.op("dve", lambda e, dst=dst, pb=pbs[i]: e.tensor_copy(
                        dst, pb), reads=[pbs[i]], writes=[dst])
        nkc = qt * 4 + 4
        for j2 in range(NH // 2):
            pavs = [B.pb[4], B.pb[5]]

            def sc_mm(hh, kc, j2=j2, qt=qt):
                h = 2 * j2 + hh
                ps = B.pb[(2 if hh == 0 else 6) + (kc % 2)]
                qslice = B.QT[64 * hh:64 * hh + 64, j2 * 512:(j2 + 1) * 512]
                ksl = B.HB[64 * hh:64 * hh + 64,
                           j2 * 2048 + kc * 128: j2 * 2048 + kc * 128 + 128]
                P.op("pe", lambda e, ksl=ksl, qslice=qslice, ps=ps: e.matmul(
                    ps, ksl, qslice, start=True, stop=True),
                    reads=[ksl, qslice], writes=[ps])

            def sc_tail(hh, kc, j2=j2, qt=qt, nkc=nkc):
                h = 2 * j2 + hh
                pav = pavs[hh]
                ps = B.pb[(2 if hh == 0 else 6) + (kc % 2)]
                p0 = B.CC[:, (hh * 2 + kc % 2) * 512:(hh * 2 + kc % 2) * 512 + 512]
                P.op("act", lambda e, ps=ps, p0=p0: e.activation(
                    p0, ps, AF.Exp, scale=B.KCONST[:, 1:2]),
                    reads=[ps, B.KCONST], writes=[p0])
                if (not l1) or (kc >= qt * 4):
                    moff = qt * 512 - kc * 128 + 384
                    msl = B.MASK[:, moff:moff + 512]
                    P.op("dve", lambda e, p0=p0, msl=msl: e.tensor_mul(p0, p0, msl),
                         reads=[p0, msl], writes=[p0])
                vsl = B.HB[:, VOFF + (kc * 16 + h) * 65: VOFF + (kc * 16 + h) * 65 + 65]
                P.op("pe", lambda e, vsl=vsl, p0=p0, kc=kc, nkc=nkc, pav=pav: e.matmul(
                    pav[0:65, :], vsl, p0, start=(kc == 0), stop=(kc == nkc - 1)),
                    reads=[vsl, p0], writes=[pav])

            for kc in range(nkc + 1):
                if kc < nkc:
                    sc_mm(0, kc)
                    sc_mm(1, kc)
                if kc >= 1:
                    sc_tail(0, kc - 1)
                    sc_tail(1, kc - 1)
            for hh in range(2):
                h = 2 * j2 + hh
                pav = pavs[hh]
                drow = B.EV[64:65, hh * 512:hh * 512 + 512]
                P.op("dve", lambda e, pav=pav, drow=drow: e.reciprocal(
                    drow, pav[64:65, 0:512]), reads=[pav], writes=[drow])
                dsl = dscr.ap()[h % 4: h % 4 + 1, :]
                P.dma("act", dsl, drow, "dsc%d" % (h % 4))
                db = B.BC[0:64, (3 - hh) * 512:(4 - hh) * 512]
                P.dma("sp", db, dsl.broadcast_to([64, 512]),
                      "dbc" if hh == 0 else "dbc2")
                if hh == 0:
                    ao = B.XQ[0:64, j2 * 512:(j2 + 1) * 512]
                    P.op("dve", lambda e, ao=ao, db=db, pav=pav: e.tensor_mul(
                        ao, pav[0:64, :], db), reads=[pav, db], writes=[ao])
                else:
                    stg = B.ACC[0:64, 0:512]
                    P.op("dve", lambda e, stg=stg, db=db, pav=pav: e.tensor_mul(
                        stg, pav[0:64, :], db), reads=[pav, db], writes=[stg])
                    ao = B.XQ[64:128, j2 * 512:(j2 + 1) * 512]
                    P.dma("pool", ao, stg, "aomv")
        for mo0 in (0, 4):
            pbs = proj_quad(wo, [mo0 + i for i in range(4)])
            for i in range(4):
                xsl = B.xs(mo0 + i, qt)
                P.op("dve", lambda e, xsl=xsl, po=pbs[i]: e.tensor_add(
                    xsl, xsl, po), reads=[pbs[i], xsl], writes=[xsl])



def emit_input_tm(P, B, x_tm):
    xv = x_tm.ap().rearrange("(n p) d -> n p d", p=128)
    for nt in range(16):
        io = B.BC[:, 0:1024]
        P.dma("sp", io, xv[nt], "xin")
        for c in range(8):
            pt = B.pb[c % 2]
            P.op("pe", lambda e, c=c, io=io, pt=pt: e.transpose(
                pt[:, 0:128], io[:, c * 128:(c + 1) * 128], B.IDF),
                reads=[io, B.IDF], writes=[pt])
            dst = B.X[:, c * 2048 + nt * 128: c * 2048 + (nt + 1) * 128]
            P.op("dve", lambda e, dst=dst, pt=pt: e.tensor_copy(dst, pt[:, 0:128]),
                 reads=[pt], writes=[dst])


def emit_output_tm(P, B, y_tm):
    yv = y_tm.ap().rearrange("(n p) d -> n p d", p=128)
    for nt in range(16):
        io = B.BC[:, 1024:2048]
        for c in range(8):
            src = B.X[:, c * 2048 + nt * 128: c * 2048 + (nt + 1) * 128]
            pt = B.pb[2 + c % 2]
            P.op("pe", lambda e, src=src, pt=pt: e.transpose(
                pt[:, 0:128], src, B.IDF),
                reads=[src, B.IDF], writes=[pt])
            P.op("dve", lambda e, c=c, io=io, pt=pt: e.tensor_copy(
                io[:, c * 128:(c + 1) * 128], pt[:, 0:128]),
                reads=[pt], writes=[io[:, c * 128:(c + 1) * 128]])
        P.dma("pool", yv[nt], io, "yo")


def emit_input_fm(P, B, x_fm):
    xv = x_fm.ap().rearrange("(c p) n -> c p n", p=128)
    for c in range(8):
        P.dma("sp", B.X[:, c * 2048:(c + 1) * 2048], xv[c], "xin%d" % c)


def emit_output_fm(P, B, y_fm):
    yv = y_fm.ap().rearrange("(c p) n -> c p n", p=128)
    for c in range(8):
        P.dma("pool", yv[c], B.X[:, c * 2048:(c + 1) * 2048], "yo")



W_G0, W_U0, W_D0, W_1, W_2 = 0, 1, 2, 3, 4
TKEYS = ["wg3", "wu3", "wd3", "w13", "w23"]
TSHAPES = {"wg3": (1024, 4096), "wu3": (1024, 4096), "wd3": (4096, 1024),
           "w13": (1024, 4096), "w23": (4096, 1024)}
TILED = {"wg3": True, "wu3": True, "wd3": False, "w13": True, "w23": False}


def _tshape(k):
    K, M = TSHAPES[k]
    return [M // 128, 128, K] if TILED[k] else [K, M]
WIDX = {"wg3": W_G0, "wu3": W_U0, "wd3": W_D0, "w13": W_1, "w23": W_2}


def build_prep():
    nc = bass.Bass("TRN2", target_bir_lowering=False, debug=False)
    dins = {"identf": nc.dram_tensor("identf", [128, 128], F32, kind="ExternalInput")}
    wts = {}
    for k in TKEYS:
        wts[k] = (nc.dram_tensor(k + "_in", _tshape(k), F32, kind="ExternalInput"),
                  nc.dram_tensor(k, _tshape(k), BF16, kind="ExternalOutput"))
    scales = nc.dram_tensor("scales", [1, 32], F32, kind="ExternalOutput")
    P = Prog(nc)
    B = Bufs(nc)
    emit_setup(P, B, dins)
    for k in TKEYS:
        K, M = TSHAPES[k]
        emit_weight_prep(P, B, wts[k][0], wts[k][1], WIDX[k], K, M, scales,
                         tiled=TILED[k])
    P.emit()
    nc.finalize()
    return nc


def build_main(stages, x_in, x_out):
    nc = bass.Bass("TRN2", target_bir_lowering=False, debug=False)
    dins = {"identf": nc.dram_tensor("identf", [128, 128], F32, kind="ExternalInput")}
    need = set()
    for st in stages:
        if st[0] == "ffn":
            need.add("scales")
            need |= ({"wg3", "wu3", "wd3"} if st[1] == 0 else {"w13", "w23", "bias"})
        else:
            need |= {"attn%d" % st[1]}
            if st[1] == 1:
                need.add("rope")
    tern = {}
    for k in TKEYS:
        if k in need:
            tern[k] = nc.dram_tensor(k, _tshape(k), BF16, kind="ExternalInput")
    scales = (nc.dram_tensor("scales", [1, 32], F32, kind="ExternalInput")
              if "scales" in need else None)
    if "bias" in need:
        dins["b1c"] = nc.dram_tensor("b1c", [128, 32], F32, kind="ExternalInput")
        dins["b2c"] = nc.dram_tensor("b2c", [128, 8], F32, kind="ExternalInput")
    for L in (0, 1):
        if ("attn%d" % L) in need:
            w = _HC["mask%d" % L].shape[1]
            dins["mask%d" % L] = nc.dram_tensor("mask%d" % L, [128, w], BF16,
                                                kind="ExternalInput")
            dins["qkvT%d" % L] = nc.dram_tensor("qkvT%d" % L, [24, 128, 1024], F32,
                                                kind="ExternalInput")
            dins["woT%d" % L] = nc.dram_tensor("woT%d" % L, [8, 128, 1024], F32,
                                               kind="ExternalInput")
    if "rope" in need:
        dins["ropeC"] = nc.dram_tensor("ropeC", [128, 2048], BF16, kind="ExternalInput")
        dins["ropeS"] = nc.dram_tensor("ropeS", [128, 2048], BF16, kind="ExternalInput")
    x_d = nc.dram_tensor("x", [2048, 1024] if x_in == "tm" else [1024, 2048],
                         F32, kind="ExternalInput")
    y_d = nc.dram_tensor("y", [2048, 1024] if x_out == "tm" else [1024, 2048],
                         F32, kind="ExternalOutput")
    dscr = nc.dram_tensor("dscr", [4, 512], F32, kind="Internal")

    P = Prog(nc)
    B = Bufs(nc)
    emit_setup(P, B, dins)
    if scales is not None:
        emit_load_scales(P, B, scales, list(range(5)))
    if "bias" in need:
        emit_load_bias(P, B, dins)
    if "rope" in need:
        P.dma("sp", B.RC, dins["ropeC"].ap(), "ropec")
        P.dma("sp", B.RS, dins["ropeS"].ap(), "ropes")
    if x_in == "tm":
        emit_input_tm(P, B, x_d)
    else:
        emit_input_fm(P, B, x_d)
    for st in stages:
        if st[0] == "ffn":
            if st[1] == 0:
                emit_ffn(P, B, tern["wg3"], tern["wu3"], tern["wd3"],
                         W_G0, W_U0, W_D0, l1=False)
            else:
                emit_ffn(P, B, tern["w13"], None, tern["w23"],
                         W_1, None, W_2, l1=True)
        else:
            L = st[1]
            m = _HC["mask%d" % L]
            P.dma("sp", B.MASK[:, 0:m.shape[1]], dins["mask%d" % L].ap(), "mask")
            emit_attn(P, B, L, dins["qkvT%d" % L], dins["woT%d" % L], dscr)
    if x_out == "tm":
        emit_output_tm(P, B, y_d)
    else:
        emit_output_fm(P, B, y_d)
    P.emit()
    nc.finalize()
    return nc


def build_fused():
    nc = bass.Bass("TRN2", target_bir_lowering=False, debug=False)
    dins = {"identf": nc.dram_tensor("identf", [128, 128], F32, kind="ExternalInput")}
    win = {}
    tern = {}
    for k in TKEYS:
        win[k] = nc.dram_tensor(k + "_in", _tshape(k), F32, kind="ExternalInput")
        tern[k] = nc.dram_tensor(k, _tshape(k), BF16, kind="Internal")
    scales = nc.dram_tensor("scales", [1, 32], F32, kind="Internal")
    dins["b1c"] = nc.dram_tensor("b1c", [128, 32], F32, kind="ExternalInput")
    dins["b2c"] = nc.dram_tensor("b2c", [128, 8], F32, kind="ExternalInput")
    for L in (0, 1):
        w = _HC["mask%d" % L].shape[1]
        dins["mask%d" % L] = nc.dram_tensor("mask%d" % L, [128, w], BF16,
                                            kind="ExternalInput")
        dins["qkvT%d" % L] = nc.dram_tensor("qkvT%d" % L, [24, 128, 1024], F32,
                                            kind="ExternalInput")
        dins["woT%d" % L] = nc.dram_tensor("woT%d" % L, [8, 128, 1024], F32,
                                           kind="ExternalInput")
    dins["ropeC"] = nc.dram_tensor("ropeC", [128, 2048], BF16, kind="ExternalInput")
    dins["ropeS"] = nc.dram_tensor("ropeS", [128, 2048], BF16, kind="ExternalInput")
    x_d = nc.dram_tensor("x", [2048, 1024], F32, kind="ExternalInput")
    y_d = nc.dram_tensor("y", [2048, 1024], F32, kind="ExternalOutput")
    dscr = nc.dram_tensor("dscr", [4, 512], F32, kind="Internal")

    P = Prog(nc)
    B = Bufs(nc)
    emit_setup(P, B, dins)
    emit_load_bias(P, B, dins)
    emit_input_tm(P, B, x_d)
    for k in TKEYS:
        K, M = TSHAPES[k]
        emit_weight_prep(P, B, win[k], tern[k], WIDX[k], K, M, scales,
                         tiled=TILED[k])
        P.seal_tensor(k, "tst%d" % WIDX[k])
    for st in [("ffn", 0), ("attn", 0), ("ffn", 0), ("ffn", 1), ("attn", 1),
               ("ffn", 1)]:
        if st[0] == "ffn":
            if st[1] == 0:
                emit_ffn(P, B, tern["wg3"], tern["wu3"], tern["wd3"],
                         W_G0, W_U0, W_D0, l1=False)
            else:
                emit_ffn(P, B, tern["w13"], None, tern["w23"],
                         W_1, None, W_2, l1=True)
        else:
            L = st[1]
            m = _HC["mask%d" % L]
            P.dma("sp", B.MASK[:, 0:m.shape[1]], dins["mask%d" % L].ap(), "mask")
            if L == 1:
                P.dma("sp", B.RC, dins["ropeC"].ap(), "ropec")
                P.dma("sp", B.RS, dins["ropeS"].ap(), "ropes")
            emit_attn(P, B, L, dins["qkvT%d" % L], dins["woT%d" % L], dscr)
    emit_output_tm(P, B, y_d)
    P.emit()
    nc.finalize()
    return nc



_NC_CACHE = {}


def _get(key, fn):
    if key not in _NC_CACHE:
        _NC_CACHE[key] = fn()
    return _NC_CACHE[key]


def kernel(L0_attn_x, L0_attn_qkv, L0_attn_wo, L0_ffn_w_gate, L0_ffn_w_up,
           L0_ffn_w_down, L1_attn_W_KQV, L1_attn_W_O, L1_ffn_W1, L1_ffn_b1,
           L1_ffn_W2, L1_ffn_b2):
    x = np.ascontiguousarray(np.asarray(L0_attn_x, np.float32))
    tw = lambda w: np.ascontiguousarray(np.asarray(w, np.float32).T)

    def tile5(wT):
        K, M = wT.shape
        return np.ascontiguousarray(
            wT.reshape(K // 128, 128, M // 128, 128).transpose(2, 1, 0, 3)
            .reshape(M // 128, 128, K))

    wgT, wuT, wdT = tile5(tw(L0_ffn_w_gate)), tile5(tw(L0_ffn_w_up)), tw(L0_ffn_w_down)
    w1T, w2T = tile5(tw(L1_ffn_W1)), tw(L1_ffn_W2)
    qkvT0, woT0 = tile5(tw(L0_attn_qkv)), tile5(tw(L0_attn_wo))
    qkvT1, woT1 = tile5(tw(L1_attn_W_KQV)), tile5(tw(L1_attn_W_O))
    b1c = np.ascontiguousarray(np.asarray(L1_ffn_b1, np.float32).reshape(32, 128).T)
    b2c = np.ascontiguousarray(np.asarray(L1_ffn_b2, np.float32).reshape(8, 128).T)
    consts = _HC

    if FUSED:
        ncf = _get("fused", build_fused)
        in_maps = []
        for i in range(8):
            mm = {"identf": consts["identf"], "b1c": b1c, "b2c": b2c,
                  "ropeC": consts["ropeC"], "ropeS": consts["ropeS"],
                  "mask0": np.ascontiguousarray(consts["mask0"]),
                  "mask1": np.ascontiguousarray(consts["mask1"]),
                  "qkvT0": qkvT0, "woT0": woT0, "qkvT1": qkvT1, "woT1": woT1,
                  "x": np.ascontiguousarray(x[i])}
            mm.update({k + "_in": w for k, w in
                       zip(TKEYS, [wgT, wuT, wdT, w1T, w2T])})
            in_maps.append(mm)
        res = run_bass_kernel_spmd(ncf, in_maps, core_ids=list(range(8)))
        return np.stack([res.results[i]["y"] for i in range(8)], axis=0).astype(np.float32)

    ncp = _get("prep", build_prep)
    inp = {"identf": consts["identf"]}
    inp.update({k + "_in": w for k, w in zip(TKEYS, [wgT, wuT, wdT, w1T, w2T])})
    resp = run_bass_kernel_spmd(ncp, [inp] * 8, core_ids=list(range(8)))
    tern = {k: resp.results[0][k] for k in TKEYS}
    scales = resp.results[0]["scales"]

    common = {"identf": consts["identf"]}
    stage_cfgs = [
        ([("ffn", 0)], "tm", "fm"),
        ([("attn", 0)], "fm", "fm"),
        ([("ffn", 0)], "fm", "fm"),
        ([("ffn", 1)], "fm", "fm"),
        ([("attn", 1)], "fm", "fm"),
        ([("ffn", 1)], "fm", "tm"),
    ]
    cur = [np.ascontiguousarray(x[i]) for i in range(8)]
    for si, (stages, xin, xout) in enumerate(stage_cfgs):
        ncm = _get(("main", si), lambda stages=stages, xin=xin, xout=xout:
                   build_main(stages, xin, xout))
        in_maps = []
        for i in range(8):
            m = dict(common)
            st = stages[0]
            if st[0] == "ffn":
                m["scales"] = scales
                if st[1] == 0:
                    m.update({k: tern[k] for k in ("wg3", "wu3", "wd3")})
                else:
                    m.update({k: tern[k] for k in ("w13", "w23")})
                    m.update({"b1c": b1c, "b2c": b2c})
            else:
                L = st[1]
                m["mask%d" % L] = np.ascontiguousarray(consts["mask%d" % L])
                m["qkvT%d" % L] = qkvT0 if L == 0 else qkvT1
                m["woT%d" % L] = woT0 if L == 0 else woT1
                if L == 1:
                    m["ropeC"] = consts["ropeC"]
                    m["ropeS"] = consts["ropeS"]
            m["x"] = np.ascontiguousarray(cur[i])
            in_maps.append(m)
        res = run_bass_kernel_spmd(ncm, in_maps, core_ids=list(range(8)))
        cur = [res.results[i]["y"] for i in range(8)]
    return np.stack(cur, axis=0).astype(np.float32)
